# version 01
# baseline (speedup 1.0000x reference)
import numpy as np
from contextlib import ExitStack

import concourse.bass as bass
import concourse.mybir as mybir
from concourse.bass_utils import run_bass_kernel_spmd
from concourse.bass import IndirectOffsetOnAxis

F32 = mybir.dt.float32
BF16 = mybir.dt.bfloat16
I32 = mybir.dt.int32
AF = mybir.ActivationFunctionType
ALU = mybir.AluOpType
AX = mybir.AxisListType

NCORES = 8
M = 16384
DM = 1024
LIVE = 4096
TOK = 2048
NH = 16
NKV = 1024
NSEG = 16
VSEGW = 16 * 65
DFF = 2048
NEXP = 8
NT = 4
TW = 512
NBLK = 8


class Sem:
    def __init__(self, es, nc, name):
        self.h = es.enter_context(nc.semaphore(name))
        self.n = 0

    def inc(self, instr, k=1):
        instr.then_inc(self.h, k)
        self.n += k
        return instr


def _fmload(eng, dst, src, nch):
    return eng.dma_start(out=dst[:].rearrange("p (c o) -> p c o", c=nch),
                         in_=src.rearrange("(c p) o -> p c o", p=128))


def build_program(phases=("A", "C", "B", "D", "E")):
    nc = bass.Bass("TRN2", target_bir_lowering=False, enable_asserts=False)

    x_liv = nc.dram_tensor("x_liv", [LIVE, DM], F32, kind="ExternalInput").ap()
    x_own = nc.dram_tensor("x_own", [TOK, DM], F32, kind="ExternalInput").ap()
    xT_own = nc.dram_tensor("xT_own", [DM, TOK], F32, kind="ExternalInput").ap()
    mapping = nc.dram_tensor("mapping", [LIVE], I32, kind="ExternalInput").ap()
    qpos = nc.dram_tensor("qpos", [128, 16], I32, kind="ExternalInput").ap()
    Wqkv = nc.dram_tensor("Wqkv", [DM, 3 * DM], F32, kind="ExternalInput").ap()
    bqkv = nc.dram_tensor("bqkv", [3 * DM], F32, kind="ExternalInput").ap()
    Wproj = nc.dram_tensor("Wproj", [DM, DM], F32, kind="ExternalInput").ap()
    bproj = nc.dram_tensor("bproj", [DM], F32, kind="ExternalInput").ap()
    gate_w = nc.dram_tensor("gate_w", [DM, NEXP], F32, kind="ExternalInput").ap()
    w_gate = nc.dram_tensor("w_gate", [NEXP, DM, DFF], F32, kind="ExternalInput").ap()
    w_up = nc.dram_tensor("w_up", [NEXP, DM, DFF], F32, kind="ExternalInput").ap()
    w_down = nc.dram_tensor("w_down", [NEXP, DFF, DM], F32, kind="ExternalInput").ap()
    ws_gate = nc.dram_tensor("ws_gate", [DM, DFF], F32, kind="ExternalInput").ap()
    ws_up = nc.dram_tensor("ws_up", [DM, DFF], F32, kind="ExternalInput").ap()
    ws_down = nc.dram_tensor("ws_down", [DFF, DM], F32, kind="ExternalInput").ap()
    outT = nc.dram_tensor("outT", [DM, TOK], F32, kind="ExternalOutput").ap()
    K_dram = nc.dram_tensor("K_dram", [128 * NSEG, 512], BF16).ap()
    V_dram = nc.dram_tensor("V_dram", [64 * NSEG, VSEGW], BF16).ap()

    es = ExitStack()

    def sbuf(name, shape, dt):
        return es.enter_context(nc.sbuf_tensor(name, shape, dt))

    identf = sbuf("identf", [128, 128], F32)
    identb = sbuf("identb", [128, 128], BF16)
    ones64f = sbuf("ones64f", [1, 64], F32)
    ones128f = sbuf("ones128f", [1, 128], F32)
    iota16f = sbuf("iota16f", [128, 1], F32)
    qpos_sb = sbuf("qpos_sb", [128, 16], I32)
    kvidx = sbuf("kvidx", [128, 8], I32)
    qmap = sbuf("qmap", [128, 16], I32)
    segraw = sbuf("segraw", [1, 32], I32)
    segf = sbuf("segf", [1, 32], F32)
    idxK = sbuf("idxK", [128, 32], I32)
    bqkv_sb = sbuf("bqkv_sb", [128, 24], F32)
    bqs = sbuf("bqs", [128, 8], F32)
    bproj_sb = sbuf("bproj_sb", [128, 8], F32)
    bv_row = sbuf("bv_row", [1, DM], F32)
    acc = sbuf("acc", [128, 8 * TOK], F32)
    bn6 = sbuf("bn6", [128, 12], F32)
    bn2 = sbuf("bn2", [128, 2], F32)
    rstd = sbuf("rstd", [128, 1], F32)
    nmr = sbuf("nmr", [128, 1], F32)
    stg = [sbuf(f"stg{i}", [128, DM], F32) for i in range(2)]
    stgn = [sbuf(f"stgn{i}", [128, DM], BF16) for i in range(2)]

    pb = [es.enter_context(nc.psum_tensor(f"pb{i}", [128, 512], F32)) for i in range(6)]
    tb = [es.enter_context(nc.psum_tensor(f"tb{i}", [128, 1024], BF16)) for i in range(2)]

    d_in = Sem(es, nc, "d_in")
    d_iny = Sem(es, nc, "d_iny")
    d_idx = Sem(es, nc, "d_idx")
    d_g = [Sem(es, nc, f"d_g{i}") for i in range(2)]
    d_kst = [Sem(es, nc, f"d_kst{i}") for i in range(2)]
    d_own = [Sem(es, nc, f"d_own{i}") for i in range(2)]
    d_kv = Sem(es, nc, "d_kv")
    d_wtf = Sem(es, nc, "d_wtf")
    d_ffn = [Sem(es, nc, f"d_ffn{i}") for i in range(2)]
    d_acc = Sem(es, nc, "d_acc")
    d_out = Sem(es, nc, "d_out")
    pe = Sem(es, nc, "pe")
    ve = Sem(es, nc, "ve")
    se = Sem(es, nc, "se")
    gp = Sem(es, nc, "gp")

    blk = es.enter_context(nc.Block())
    g_, s_, t_, v_, a_ = blk.gpsimd, blk.sync, blk.tensor, blk.vector, blk.scalar

    def f(g):
        gp.inc(g.iota(identf[:], pattern=[[1, 128]], base=0, channel_multiplier=-1,
               allow_small_or_imprecise_dtypes=True))
        gp.inc(g.iota(iota16f[:], pattern=[[0, 1]], base=0, channel_multiplier=16,
               allow_small_or_imprecise_dtypes=True))
        gp.inc(g.memset(ones64f[:], 1.0))
        gp.inc(g.memset(ones128f[:], 1.0))
        d_in.inc(g.dma_start(
            out=kvidx[:],
            in_=mapping.rearrange("(gi p f) -> p gi f", gi=8, p=128)[:, :, 0:1].squeeze(2),
            allow_slow_non_contiguous=True), 16)
    g_(f)

    def f(s):
        d_iny.inc(s.dma_start(out=qpos_sb[:], in_=qpos), 16)
        d_iny.inc(s.dma_start(out=bqkv_sb[:], in_=bqkv.rearrange("(c p) -> p c", p=128),
                              allow_slow_non_contiguous=True), 16)
        d_iny.inc(s.dma_start(out=bproj_sb[:], in_=bproj.rearrange("(c p) -> p c", p=128),
                              allow_slow_non_contiguous=True), 16)
        d_iny.inc(s.dma_start(out=bv_row[:], in_=bqkv[None, 2 * DM:3 * DM]), 16)
    s_(f)
    setup_iny = d_iny.n
    setup_in = d_in.n

    def f(v):
        v.wait_ge(gp.h, 1)
        ve.inc(v.tensor_scalar(identf[:], identf[:], 0.0, None, op0=ALU.is_equal))
        v.wait_ge(ve.h, ve.n)
        ve.inc(v.tensor_copy(identb[:], identf[:]))
        v.wait_ge(d_iny.h, setup_iny)
        ve.inc(v.tensor_scalar(bqs[:], bqkv_sb[:, 0:8], 0.125, None, op0=ALU.mult))
    v_(f)
    ident_ready = ve.n

    def f(g):
        g.wait_ge(d_iny.h, setup_iny)
        for gi in range(16):
            d_idx.inc(g.indirect_dma_start(
                out=qmap[:, gi:gi + 1], out_offset=None,
                in_=mapping[None, :],
                in_offset=IndirectOffsetOnAxis(qpos_sb[:, gi:gi + 1], 1)), 16)
        g.wait_ge(d_idx.h, d_idx.n)
        d_idx.inc(g.dma_start(out=segraw[:], in_=qmap[0:128:64, 0:16]), 16)
    g_(f)
    qmap_done = d_idx.n

    def f(v):
        v.wait_ge(d_idx.h, d_idx.n)
        ve.inc(v.tensor_scalar(segraw[:], segraw[:], 8, None, op0=ALU.arith_shift_right))
        v.wait_ge(ve.h, ve.n)
        ve.inc(v.tensor_copy(segf[:], segraw[:]))
    v_(f)
    segf_ready = ve.n

    def f(t):
        t.wait_ge(ve.h, segf_ready)
        t.wait_ge(gp.h, 4)
        pe.inc(t.matmul(pb[0][:, 0:32], ones128f[:], segf[:], start=True, stop=True))
    t_(f)
    def f(v):
        v.wait_ge(pe.h, pe.n)
        ve.inc(v.tensor_scalar(pb[0][:, 0:32], pb[0][:, 0:32], 1.0, iota16f[:],
                               op0=ALU.mult, op1=ALU.add))
        v.wait_ge(ve.h, ve.n)
        ve.inc(v.tensor_copy(idxK[:], pb[0][:, 0:32]))
    v_(f)
    idx_ready = ve.n

    all_sems = [d_in, d_iny, d_idx, d_g[0], d_g[1], d_kst[0], d_kst[1],
                d_own[0], d_own[1], d_kv, d_wtf, d_ffn[0], d_ffn[1],
                d_acc, d_out, pe, ve, se, gp]

    def phase_barrier():
        vals = [(sm.h, sm.n) for sm in all_sems if sm.n > 0]
        for section in (g_, s_, t_, v_, a_):
            def f(e, vals=vals):
                for h, n in vals:
                    e.wait_ge(h, n)
            section(f)

    def blkcol(b):
        return 16 * (b & 1) + (b >> 1)

    def emit_ln(src, dst, src_wait):
        def f(v):
            v.wait_ge(*src_wait)
            v.wait_ge(se.h, se.n)
            ve.inc(v.bn_stats(bn6[:, 0:6], src[:, 0:512]))
            ve.inc(v.bn_stats(bn6[:, 6:12], src[:, 512:1024]))
            v.wait_ge(ve.h, ve.n)
            ve.inc(v.bn_aggr(bn2[:], bn6[:]))
            v.wait_ge(ve.h, ve.n)
            ve.inc(v.tensor_scalar(rstd[:], bn2[:, 1:2], 1e-5, None, op0=ALU.add))
            v.wait_ge(ve.h, ve.n)
            ve.inc(v.reciprocal(rstd[:], rstd[:]))
        v_(f)
        def f(a):
            a.wait_ge(ve.h, ve.n)
            se.inc(a.activation(rstd[:], rstd[:], AF.Sqrt, bias=0.0, scale=1.0))
        a_(f)
        def f(v):
            v.wait_ge(se.h, se.n)
            ve.inc(v.tensor_scalar(nmr[:], bn2[:, 0:1], rstd[:], -1.0,
                                   op0=ALU.mult, op1=ALU.mult))
        v_(f)
        def f(a):
            a.wait_ge(ve.h, ve.n)
            se.inc(a.activation(dst[:], src[:], AF.Identity, bias=nmr[:], scale=rstd[:]))
        a_(f)
        return (se.h, se.n)

    tb_free = [0, 0]

    def emit_tr8(src, src_ready, dst, dst_col, col_stride):
        for ch in range(8):
            k = ch % 2
            def f(t, ch=ch, k=k):
                t.wait_ge(src_ready[0], src_ready[1])
                t.wait_ge(ve.h, max(tb_free[k], ident_ready))
                pe.inc(t.transpose(tb[k][0:128, 0:128],
                                   src[:, ch * 128:ch * 128 + 128], identb[:]))
            t_(f)
            def f(v, ch=ch, k=k):
                v.wait_ge(pe.h, pe.n)
                ve.inc(v.tensor_copy(dst[:, dst_col + ch * col_stride:
                                         dst_col + ch * col_stride + 128],
                                     tb[k][0:128, 0:128]))
            v_(f)
            tb_free[k] = ve.n

    st_A = ExitStack()
    wk_sb = st_A.enter_context(nc.sbuf_tensor("wk_sb", [128, 8 * DM], BF16))
    wv_sb = st_A.enter_context(nc.sbuf_tensor("wv_sb", [128, 8 * DM], BF16))
    xn_kvT = st_A.enter_context(nc.sbuf_tensor("xn_kvT", [128, 8 * NKV], BF16))
    bv_b = st_A.enter_context(nc.sbuf_tensor("bv_b", [64, DM], F32))
    K_T2 = st_A.enter_context(nc.sbuf_tensor("K_T2", [128, NSEG * 512], BF16))
    V_seg = st_A.enter_context(nc.sbuf_tensor("V_seg", [64, NSEG * VSEGW], BF16))

    if "A" in phases:
        def f(g):
            d_in.inc(_fmload(g, wk_sb, Wqkv[:, DM:2 * DM], 8), 16)
            d_in.inc(_fmload(g, wv_sb, Wqkv[:, 2 * DM:3 * DM], 8), 16)
            vs4 = V_seg[:].rearrange("p (s h d) -> p s h d", s=NSEG, h=16)
            gp.inc(g.memset(vs4[:, :, :, 64:65], 1.0))
        g_(f)
        wkwv_loaded = d_in.n
        vones_gp = gp.n

        def f(t):
            t.wait_ge(d_iny.h, setup_iny)
            t.wait_ge(gp.h, 4)
            pe.inc(t.matmul(pb[1][0:64, :], ones64f[:], bv_row[0:1, 0:512],
                            start=True, stop=True))
            pe.inc(t.matmul(pb[2][0:64, :], ones64f[:], bv_row[0:1, 512:1024],
                            start=True, stop=True))
        t_(f)
        def f(v):
            v.wait_ge(pe.h, pe.n)
            ve.inc(v.tensor_copy(bv_b[:, 0:512], pb[1][0:64, :]))
            ve.inc(v.tensor_copy(bv_b[:, 512:1024], pb[2][0:64, :]))
        v_(f)
        bvb_ready = ve.n

        stg_free = [(se.h, 0), (se.h, 0)]
        for gi in range(8):
            k = gi % 2
            def f(g, gi=gi, k=k, din=None):
                g.wait_ge(d_in.h, d_in.n)
                g.wait_ge(*stg_free[k])
                d_g[k].inc(g.indirect_dma_start(
                    out=stg[k][:], out_offset=None,
                    in_=x_liv,
                    in_offset=IndirectOffsetOnAxis(kvidx[:, gi:gi + 1], 0)), 16)
            g_(f)
            ready = emit_ln(stg[k], stgn[k], (d_g[k].h, d_g[k].n))
            stg_free[k] = ready
            emit_tr8(stgn[k], ready, xn_kvT, gi * 128, NKV)
        xnkvT_ready = ve.n

        k_evacs = []
        for kc in range(8):
            for half in range(2):
                i = kc * 2 + half
                bank = pb[i % 4]
                def f(t, kc=kc, half=half, bank=bank, i=i):
                    t.wait_ge(d_in.h, wkwv_loaded)
                    t.wait_ge(ve.h, max(xnkvT_ready, idx_ready))
                    if i >= 4:
                        t.wait_ge(se.h, k_evacs[i - 4])
                    for dc in range(8):
                        ins = t.matmul(bank[:, :],
                                       wk_sb[:, dc * DM + kc * 128:dc * DM + kc * 128 + 128],
                                       xn_kvT[:, dc * NKV + half * 512:dc * NKV + half * 512 + 512],
                                       start=(dc == 0), stop=(dc == 7))
                    pe.inc(ins)
                t_(f)
                def f(a, kc=kc, half=half, bank=bank):
                    a.wait_ge(pe.h, pe.n)
                    dst = K_T2[:].rearrange("p (sg r) -> p sg r", sg=NSEG)[
                        :, half * 8:half * 8 + 8, kc * 64:kc * 64 + 64]
                    srcp = bank[:, :].rearrange("p (sg s) -> p sg s", sg=8)
                    se.inc(a.activation(dst, srcp, AF.Identity,
                                        bias=bqkv_sb[:, 8 + kc:9 + kc], scale=1.0))
                a_(f)
                k_evacs.append(se.n)
        KT_done = se.n

        v_evacs = []
        for sg in range(NSEG):
            for vh in range(2):
                i = sg * 2 + vh
                bank = pb[4 + (i % 2)]
                def f(t, sg=sg, vh=vh, bank=bank, i=i):
                    t.wait_ge(ve.h, max(xnkvT_ready, bvb_ready))
                    if i >= 2:
                        t.wait_ge(ve.h, v_evacs[i - 2])
                    for dc in range(8):
                        ins = t.matmul(bank[0:64, :],
                                       xn_kvT[:, dc * NKV + sg * 64:dc * NKV + sg * 64 + 64],
                                       wv_sb[:, dc * DM + vh * 512:dc * DM + vh * 512 + 512],
                                       start=(dc == 0), stop=(dc == 7))
                    pe.inc(ins)
                t_(f)
                def f(v, sg=sg, vh=vh, bank=bank):
                    v.wait_ge(pe.h, pe.n)
                    base = sg * VSEGW + vh * 520
                    dst = V_seg[0:64, base:base + 520].rearrange(
                        "p (h d) -> p h d", h=8)[:, :, 0:64]
                    srcv = bank[0:64, :].rearrange("p (h d) -> p h d", h=8)
                    ve.inc(v.scalar_tensor_tensor(
                        dst, srcv, 0.0,
                        bv_b[:, vh * 512:vh * 512 + 512].rearrange("p (h d) -> p h d", h=8),
                        op0=ALU.bypass, op1=ALU.add))
                v_(f)
                v_evacs.append(ve.n)
        Vseg_done = ve.n

        def f(g):
            g.wait_ge(se.h, KT_done)
            d_kv.inc(g.dma_start(out=K_dram.rearrange("(p e) r -> p e r", p=128),
                                 in_=K_T2[:].rearrange("p (e r) -> p e r", e=NSEG)), 16)
            g.wait_ge(ve.h, Vseg_done)
            g.wait_ge(gp.h, vones_gp)
            d_kv.inc(g.dma_start(out=V_dram.rearrange("(p e) r -> p e r", p=64),
                                 in_=V_seg[:].rearrange("p (e r) -> p e r", e=NSEG)), 16)
        g_(f)
        kv_dram_done = d_kv.n
    else:
        kv_dram_done = d_kv.n
        def f(v):
            ve.inc(v.memset(acc[:], 0.0))
        v_(f)

    st_A.close()
    phase_barrier()

    st_C = ExitStack()
    wq_sb = st_C.enter_context(nc.sbuf_tensor("wq_sb", [128, 8 * DM], BF16))
    wp_sb = st_C.enter_context(nc.sbuf_tensor("wp_sb", [128, 8 * DM], BF16))
    xnqT = st_C.enter_context(nc.sbuf_tensor("xnqT", [128, 8 * TW], BF16))
    QT = st_C.enter_context(nc.sbuf_tensor("QT", [128, 8 * TW], BF16))
    attn_tm = st_C.enter_context(nc.sbuf_tensor("attn_tm", [64, NBLK * DM], BF16))
    attn_fm = st_C.enter_context(nc.sbuf_tensor("attn_fm", [128, 8 * TW], BF16))
    Kst = [st_C.enter_context(nc.sbuf_tensor(f"Kst{i}", [128, NBLK * 512], BF16)) for i in range(2)]
    Vst = [st_C.enter_context(nc.sbuf_tensor(f"Vst{i}", [64, NBLK * VSEGW], BF16)) for i in range(2)]
    S_sb = [st_C.enter_context(nc.sbuf_tensor(f"S_sb{i}", [64, 64], F32)) for i in range(3)]
    P_sb = [st_C.enter_context(nc.sbuf_tensor(f"P_sb{i}", [64, 64], BF16)) for i in range(3)]
    negC = [st_C.enter_context(nc.sbuf_tensor(f"negC{i}", [64, 1], F32)) for i in range(2)]
    recb = [st_C.enter_context(nc.sbuf_tensor(f"recb{i}", [64, 1], F32)) for i in range(2)]

    if "C" in phases:
        def f(g):
            d_in.inc(_fmload(g, wq_sb, Wqkv[:, 0:DM], 8), 16)
            d_in.inc(_fmload(g, wp_sb, Wproj, 8), 16)
        g_(f)
        wqwp_loaded = d_in.n

        stg_free = [(se.h, se.n), (se.h, se.n)]
        prev_proj_ve = ve.n
        prev_pipe_pe = 0
        for j in range(NT):
            kb = j % 2
            def f(g, j=j, kb=kb):
                g.wait_ge(d_kv.h, kv_dram_done)
                g.wait_ge(ve.h, idx_ready)
                g.wait_ge(pe.h, prev_pipe_pe)
                for bb in range(NBLK):
                    col = blkcol(8 * j + bb)
                    d_kst[kb].inc(g.indirect_dma_start(
                        out=Kst[kb][:, bb * 512:bb * 512 + 512], out_offset=None,
                        in_=K_dram,
                        in_offset=IndirectOffsetOnAxis(idxK[:, col:col + 1], 0)), 16)
                    d_kst[kb].inc(g.indirect_dma_start(
                        out=Vst[kb][0:64, bb * VSEGW:bb * VSEGW + VSEGW], out_offset=None,
                        in_=V_dram,
                        in_offset=IndirectOffsetOnAxis(idxK[0:64, col:col + 1], 0)), 16)
            g_(f)
            kv_st_done = d_kst[kb].n

            for gsub in range(4):
                k = gsub % 2
                gi = 4 * j + gsub
                def f(g, gi=gi, k=k):
                    g.wait_ge(d_idx.h, qmap_done)
                    g.wait_ge(*stg_free[k])
                    d_g[k].inc(g.indirect_dma_start(
                        out=stg[k][:], out_offset=None,
                        in_=x_liv,
                        in_offset=IndirectOffsetOnAxis(qmap[:, gi:gi + 1], 0)), 16)
                g_(f)
                ready = emit_ln(stg[k], stgn[k], (d_g[k].h, d_g[k].n))
                stg_free[k] = ready
                emit_tr8(stgn[k], ready, xnqT, gsub * 128, TW)
            xnqT_ready = ve.n

            q_evacs = []
            for qc in range(8):
                bank = pb[qc % 2]
                def f(t, qc=qc, bank=bank):
                    t.wait_ge(ve.h, xnqT_ready)
                    t.wait_ge(d_in.h, wqwp_loaded)
                    if len(q_evacs) >= 2:
                        t.wait_ge(se.h, q_evacs[qc - 2])
                    for dc in range(8):
                        ins = t.matmul(bank[:, :],
                                       wq_sb[:, dc * DM + qc * 128:dc * DM + qc * 128 + 128],
                                       xnqT[:, dc * TW:dc * TW + 512],
                                       start=(dc == 0), stop=(dc == 7))
                    pe.inc(ins)
                t_(f)
                def f(a, qc=qc, bank=bank):
                    a.wait_ge(pe.h, pe.n)
                    se.inc(a.activation(QT[:, qc * TW:qc * TW + 512], bank[:, :],
                                        AF.Identity, bias=bqs[:, qc:qc + 1], scale=0.125))
                a_(f)
                q_evacs.append(se.n)
            QT_ready = se.n

            NIT = NBLK * NH
            c_sc, c_ev, c_tr, c_rd, c_ex, c_av, c_rc, c_no = ({} for _ in range(8))
            pe_c, ve_c, se_c = pe.n, ve.n, se.n
            for slot in range(NIT + 5):
                if slot < NIT:
                    pe_c += 1; c_sc[slot] = pe_c
                if 0 <= slot - 1 < NIT:
                    ve_c += 1; c_ev[slot - 1] = ve_c
                if 0 <= slot - 2 < NIT:
                    pe_c += 1; c_tr[slot - 2] = pe_c
                    ve_c += 1; c_rd[slot - 2] = ve_c
                if 0 <= slot - 3 < NIT:
                    se_c += 1; c_ex[slot - 3] = se_c
                if 0 <= slot - 4 < NIT:
                    pe_c += 1; c_av[slot - 4] = pe_c
                if 0 <= slot - 5 < NIT:
                    ve_c += 3; c_rc[slot - 5] = ve_c - 1; c_no[slot - 5] = ve_c

            def f(t, j=j, kb=kb, QT_ready=QT_ready, kv_st_done=kv_st_done,
                  ppv=prev_proj_ve):
                for slot in range(NIT + 5):
                    i = slot
                    if i < NIT:
                        bb, h = divmod(i, NH)
                        t.wait_ge(se.h, QT_ready)
                        if i >= 2:
                            t.wait_ge(ve.h, c_ev[i - 2])
                        else:
                            t.wait_ge(d_kst[kb].h, kv_st_done)
                            t.wait_ge(ve.h, ppv)
                        lhs = QT[(h & 1) * 64:(h & 1) * 64 + 64,
                                 (h >> 1) * TW + 64 * bb:(h >> 1) * TW + 64 * bb + 64]
                        rhs = Kst[kb][(h & 1) * 64:(h & 1) * 64 + 64,
                                      bb * 512 + (h >> 1) * 64:bb * 512 + (h >> 1) * 64 + 64]
                        pe.inc(t.matmul(pb[2 + (i % 2)][0:64, 0:64], lhs, rhs,
                                        start=True, stop=True))
                        assert pe.n == c_sc[i]
                    i = slot - 2
                    if 0 <= i < NIT:
                        t.wait_ge(ve.h, c_ev[i])
                        if i >= 2:
                            t.wait_ge(se.h, c_ex[i - 2])
                        pe.inc(t.transpose(pb[4 + (i % 2)][0:64, 0:64], S_sb[i % 3][:],
                                           identf[0:64, 0:64]))
                        assert pe.n == c_tr[i]
                    i = slot - 4
                    if 0 <= i < NIT:
                        bb, h = divmod(i, NH)
                        t.wait_ge(se.h, c_ex[i])
                        if i >= 2:
                            t.wait_ge(ve.h, c_no[i - 2])
                        rhs = Vst[kb][0:64, bb * VSEGW + 65 * h:bb * VSEGW + 65 * h + 65]
                        pe.inc(t.matmul(pb[(i % 2)][0:64, 0:65], P_sb[i % 3][:], rhs,
                                        start=True, stop=True))
                        assert pe.n == c_av[i]
            t_(f)
            prev_pipe_pe = pe.n

            def f(v):
                for slot in range(NIT + 5):
                    i = slot - 1
                    if 0 <= i < NIT:
                        v.wait_ge(pe.h, c_sc[i])
                        if i >= 3:
                            v.wait_ge(pe.h, c_tr[i - 3])
                        ve.inc(v.tensor_copy(S_sb[i % 3][:], pb[2 + (i % 2)][0:64, 0:64]))
                        assert ve.n == c_ev[i]
                    i = slot - 2
                    if 0 <= i < NIT:
                        v.wait_ge(pe.h, c_tr[i])
                        ve.inc(v.reduce_max(negC[i % 2][:], pb[4 + (i % 2)][0:64, 0:64],
                                            axis=AX.X, negate=True))
                        assert ve.n == c_rd[i]
                    i = slot - 5
                    if 0 <= i < NIT:
                        bb, h = divmod(i, NH)
                        v.wait_ge(pe.h, c_av[i])
                        ve.inc(v.tensor_scalar(recb[i % 2][:], pb[(i % 2)][0:64, 64:65],
                                               1e-6, None, op0=ALU.add))
                        v.wait_ge(ve.h, ve.n)
                        ve.inc(v.reciprocal(recb[i % 2][:], recb[i % 2][:]))
                        assert ve.n == c_rc[i]
                        v.wait_ge(ve.h, ve.n)
                        ve.inc(v.tensor_scalar(
                            attn_tm[0:64, bb * DM + h * 64:bb * DM + h * 64 + 64],
                            pb[(i % 2)][0:64, 0:64], recb[i % 2][:], None, op0=ALU.mult))
                        assert ve.n == c_no[i]
            v_(f)

            def f(a):
                for slot in range(NIT + 5):
                    i = slot - 3
                    if 0 <= i < NIT:
                        a.wait_ge(ve.h, c_rd[i])
                        if i >= 3:
                            a.wait_ge(pe.h, c_av[i - 3])
                        se.inc(a.activation(P_sb[i % 3][:], pb[4 + (i % 2)][0:64, 0:64],
                                            AF.Exp, bias=negC[i % 2][:], scale=1.0))
                        assert se.n == c_ex[i]
            a_(f)

            for bb in range(NBLK):
                for ch in range(8):
                    k = ch % 2
                    def f(t, bb=bb, ch=ch, k=k):
                        t.wait_ge(ve.h, c_no[bb * NH + NH - 1])
                        t.wait_ge(ve.h, tb_free[k])
                        pe.inc(t.transpose(tb[k][0:128, 0:64],
                                           attn_tm[0:64, bb * DM + ch * 128:bb * DM + ch * 128 + 128],
                                           identb[0:64, 0:64]))
                    t_(f)
                    def f(v, bb=bb, ch=ch, k=k):
                        v.wait_ge(pe.h, pe.n)
                        ve.inc(v.tensor_copy(
                            attn_fm[:, ch * TW + bb * 64:ch * TW + bb * 64 + 64],
                            tb[k][0:128, 0:64]))
                    v_(f)
                    tb_free[k] = ve.n
            attn_fm_ready = ve.n

            p_evacs = []
            for oc in range(8):
                bank = pb[2 + (oc % 2)]
                def f(t, oc=oc, bank=bank):
                    t.wait_ge(ve.h, attn_fm_ready)
                    if len(p_evacs) >= 2:
                        t.wait_ge(ve.h, p_evacs[oc - 2])
                    for fc in range(8):
                        ins = t.matmul(bank[:, :],
                                       wp_sb[:, fc * DM + oc * 128:fc * DM + oc * 128 + 128],
                                       attn_fm[:, fc * TW:fc * TW + 512],
                                       start=(fc == 0), stop=(fc == 7))
                    pe.inc(ins)
                t_(f)
                def f(v, oc=oc, bank=bank, j=j):
                    v.wait_ge(pe.h, pe.n)
                    ve.inc(v.tensor_scalar(acc[:, oc * TOK + j * TW:oc * TOK + j * TW + 512],
                                           bank[:, :], bproj_sb[:, oc:oc + 1], None,
                                           op0=ALU.add))
                v_(f)
                p_evacs.append(ve.n)
            prev_proj_ve = ve.n
        attn_done_ve = ve.n
    else:
        attn_done_ve = ve.n

    st_C.close()
    phase_barrier()

    st_BD = ExitStack()
    xn_ownT = st_BD.enter_context(nc.sbuf_tensor("xn_ownT", [128, 8 * TOK], BF16))

    if "B" in phases or "D" in phases:
        stg_free2 = [(se.h, se.n), (se.h, se.n)]
        for tt in range(16):
            k = tt % 2
            def f(s, tt=tt, k=k):
                s.wait_ge(*stg_free2[k])
                d_own[k].inc(s.dma_start(out=stg[k][:], in_=x_own[tt * 128:tt * 128 + 128, :]), 16)
            s_(f)
            ready = emit_ln(stg[k], stgn[k], (d_own[k].h, d_own[k].n))
            stg_free2[k] = ready
            emit_tr8(stgn[k], ready, xn_ownT, tt * 128, TOK)
    xnownT_ready = ve.n

    st_D = ExitStack()
    gate_sb = st_D.enter_context(nc.sbuf_tensor("gate_sb", [128, 8 * NEXP], BF16))
    w_tm = st_D.enter_context(nc.sbuf_tensor("w_tm", [128, 16 * NEXP], F32))
    wT8 = st_D.enter_context(nc.sbuf_tensor("wT8", [NEXP, TOK], F32))
    wTflat = st_D.enter_context(nc.sbuf_tensor("wTflat", [1, TOK], F32))
    sgs = st_D.enter_context(nc.sbuf_tensor("sgs", [128, NEXP], F32))
    sgm = st_D.enter_context(nc.sbuf_tensor("sgm", [128, 1], F32))
    sgm2 = st_D.enter_context(nc.sbuf_tensor("sgm2", [128, 1], F32))
    sgtmp = st_D.enter_context(nc.sbuf_tensor("sgtmp", [128, NEXP], F32))
    wbcast = st_D.enter_context(nc.sbuf_tensor("wbcast", [128, TOK], F32))
    h_q = st_D.enter_context(nc.sbuf_tensor("h_q", [128, 4 * 512], BF16))
    hsil = [st_D.enter_context(nc.sbuf_tensor(f"hsil{i}", [128, 512], BF16)) for i in range(2)]
    tmpd = st_D.enter_context(nc.sbuf_tensor("tmpd", [128, 4 * 512], F32))
    wgb = [st_D.enter_context(nc.sbuf_tensor(f"wgb{i}", [128, 8 * 512], BF16)) for i in range(2)]
    wub = [st_D.enter_context(nc.sbuf_tensor(f"wub{i}", [128, 8 * 512], BF16)) for i in range(2)]
    wdb = [st_D.enter_context(nc.sbuf_tensor(f"wdb{i}", [128, 4 * DM], BF16)) for i in range(2)]

    if "D" in phases:
        def f(g):
            d_in.inc(_fmload(g, gate_sb, gate_w, 8), 16)
        g_(f)
        gate_loaded = d_in.n

        zg_done = []
        for tt in range(16):
            bank = pb[tt % 2]
            def f(t, tt=tt, bank=bank):
                t.wait_ge(d_in.h, gate_loaded)
                t.wait_ge(ve.h, xnownT_ready)
                if len(zg_done) >= 2:
                    t.wait_ge(ve.h, zg_done[tt - 2])
                for dc in range(8):
                    ins = t.matmul(bank[:, 0:NEXP],
                                   xn_ownT[:, dc * TOK + tt * 128:dc * TOK + tt * 128 + 128],
                                   gate_sb[:, dc * NEXP:dc * NEXP + NEXP],
                                   start=(dc == 0), stop=(dc == 7))
                pe.inc(ins)
            t_(f)
            def f(v, tt=tt, bank=bank):
                v.wait_ge(pe.h, pe.n)
                v.wait_ge(ve.h, ve.n)
                v.wait_ge(se.h, se.n)
                ve.inc(v.reduce_max(sgm[:], bank[:, 0:NEXP], axis=AX.X, negate=True))
            v_(f)
            def f(a, bank=bank):
                a.wait_ge(ve.h, ve.n)
                se.inc(a.activation(sgs[:], bank[:, 0:NEXP], AF.Exp, bias=sgm[:], scale=1.0))
            a_(f)
            def f(v, tt=tt):
                v.wait_ge(se.h, se.n)
                ve.inc(v.reduce_sum(sgm[:], sgs[:], axis=AX.X))
                v.wait_ge(ve.h, ve.n)
                ve.inc(v.reciprocal(sgm[:], sgm[:]))
                v.wait_ge(ve.h, ve.n)
                ve.inc(v.tensor_scalar(sgs[:], sgs[:], sgm[:], None, op0=ALU.mult))
                v.wait_ge(ve.h, ve.n)
                ve.inc(v.reduce_max(sgm[:], sgs[:], axis=AX.X))
                v.wait_ge(ve.h, ve.n)
                ve.inc(v.tensor_scalar(sgtmp[:], sgs[:], sgm[:], None, op0=ALU.is_equal))
                v.wait_ge(ve.h, ve.n)
                ve.inc(v.scalar_tensor_tensor(sgtmp[:], sgtmp[:], -1e30, sgs[:],
                                              op0=ALU.mult, op1=ALU.add))
                v.wait_ge(ve.h, ve.n)
                ve.inc(v.reduce_max(sgm2[:], sgtmp[:], axis=AX.X))
                v.wait_ge(ve.h, ve.n)
                ve.inc(v.tensor_scalar(sgtmp[:], sgs[:], sgm2[:], None, op0=ALU.is_ge))
                v.wait_ge(ve.h, ve.n)
                ve.inc(v.tensor_mul(sgtmp[:], sgtmp[:], sgs[:]))
                ve.inc(v.tensor_add(sgm[:], sgm[:], sgm2[:]))
                v.wait_ge(ve.h, ve.n)
                ve.inc(v.tensor_scalar(sgm[:], sgm[:], 1e-20, None, op0=ALU.add))
                v.wait_ge(ve.h, ve.n)
                ve.inc(v.reciprocal(sgm[:], sgm[:]))
                v.wait_ge(ve.h, ve.n)
                ve.inc(v.tensor_scalar(w_tm[:, tt * NEXP:tt * NEXP + NEXP], sgtmp[:],
                                       sgm[:], None, op0=ALU.mult))
            v_(f)
            zg_done.append(ve.n)

        for tt in range(16):
            def f(t, tt=tt):
                t.wait_ge(ve.h, ve.n)
                pe.inc(t.transpose(pb[2][0:NEXP, 0:128],
                                   w_tm[:, tt * NEXP:tt * NEXP + NEXP], identf[:]))
            t_(f)
            def f(v, tt=tt):
                v.wait_ge(pe.h, pe.n)
                ve.inc(v.tensor_copy(wT8[:, tt * 128:tt * 128 + 128], pb[2][0:NEXP, 0:128]))
            v_(f)
        wT8_ready = ve.n
        prev_wb_pe = 0

        def wload(g, e, q):
            kbuf = (e * 4 + q) % 2
            if e < NEXP:
                srcs = (w_gate[e, :, 512 * q:512 * q + 512],
                        w_up[e, :, 512 * q:512 * q + 512],
                        w_down[e, 512 * q:512 * q + 512, :])
            else:
                srcs = (ws_gate[:, 512 * q:512 * q + 512],
                        ws_up[:, 512 * q:512 * q + 512],
                        ws_down[512 * q:512 * q + 512, :])
            d_ffn[kbuf].inc(_fmload(g, wgb[kbuf], srcs[0], 8), 16)
            d_ffn[kbuf].inc(_fmload(g, wub[kbuf], srcs[1], 8), 16)
            d_ffn[kbuf].inc(_fmload(g, wdb[kbuf], srcs[2], 4), 16)

        eq_list = [(e, q) for e in range(NEXP + 1) for q in range(4)]
        eq_loaded = {}
        eq_consumed = {}

        def f(g):
            for i in range(2):
                wload(g, *eq_list[i])
                eq_loaded[i] = d_ffn[i % 2].n
        g_(f)

        wb_ready = 0
        for i, (e, q) in enumerate(eq_list):
            if q == 0 and e < NEXP:
                def f(g, e=e, pwp=prev_wb_pe):
                    g.wait_ge(ve.h, wT8_ready)
                    g.wait_ge(pe.h, pwp)
                    d_wtf.inc(g.dma_start(out=wTflat[:], in_=wT8[e:e + 1, :]), 16)
                g_(f)
                wtf_done = d_wtf.n
                for tq in range(4):
                    bank = pb[4 + tq % 2]
                    def f(t, tq=tq, bank=bank, wtf_done=wtf_done):
                        t.wait_ge(d_wtf.h, wtf_done)
                        t.wait_ge(gp.h, 4)
                        t.wait_ge(ve.h, ve.n)
                        pe.inc(t.matmul(bank[:, :], ones128f[:],
                                        wTflat[0:1, tq * 512:tq * 512 + 512],
                                        start=True, stop=True))
                    t_(f)
                    def f(v, tq=tq, bank=bank):
                        v.wait_ge(pe.h, pe.n)
                        ve.inc(v.tensor_copy(wbcast[:, tq * 512:tq * 512 + 512], bank[:, :]))
                    v_(f)
                wb_ready = ve.n
                prev_wb_pe = pe.n

            prev_dn_ve = ve.n
            for tq in range(4):
                mul_done = []
                for sl in range(4):
                    kb2 = sl % 2
                    def f(t, i=i, sl=sl, kb2=kb2, tq=tq, pdv=prev_dn_ve):
                        t.wait_ge(d_ffn[i % 2].h, eq_loaded[i])
                        t.wait_ge(ve.h, xnownT_ready)
                        if len(mul_done) >= 2:
                            t.wait_ge(ve.h, mul_done[sl - 2])
                        else:
                            t.wait_ge(ve.h, pdv)
                        buf = wgb[i % 2]
                        for dc in range(8):
                            ins = t.matmul(pb[kb2][:, :],
                                           buf[:, dc * 512 + sl * 128:dc * 512 + sl * 128 + 128],
                                           xn_ownT[:, dc * TOK + tq * 512:dc * TOK + tq * 512 + 512],
                                           start=(dc == 0), stop=(dc == 7))
                        pe.inc(ins)
                        buf = wub[i % 2]
                        for dc in range(8):
                            ins = t.matmul(pb[2 + kb2][:, :],
                                           buf[:, dc * 512 + sl * 128:dc * 512 + sl * 128 + 128],
                                           xn_ownT[:, dc * TOK + tq * 512:dc * TOK + tq * 512 + 512],
                                           start=(dc == 0), stop=(dc == 7))
                        pe.inc(ins)
                    t_(f)
                    def f(a, sl=sl, kb2=kb2):
                        a.wait_ge(pe.h, pe.n)
                        se.inc(a.activation(hsil[kb2][:], pb[kb2][:, :], AF.Silu,
                                            bias=0.0, scale=1.0))
                    a_(f)
                    def f(v, sl=sl, kb2=kb2):
                        v.wait_ge(se.h, se.n)
                        ve.inc(v.tensor_tensor(h_q[:, sl * 512:sl * 512 + 512],
                                               hsil[kb2][:], pb[2 + kb2][:, :], op=ALU.mult))
                    v_(f)
                    mul_done.append(ve.n)
                for hh in range(2):
                    def f(t, i=i, hh=hh, tq=tq):
                        t.wait_ge(ve.h, ve.n)
                        for sl in range(4):
                            for c in range(4):
                                ins = t.matmul(
                                    pb[c][:, :],
                                    wdb[i % 2][:, sl * DM + (4 * hh + c) * 128:
                                               sl * DM + (4 * hh + c) * 128 + 128],
                                    h_q[:, sl * 512:sl * 512 + 512],
                                    start=(sl == 0), stop=(sl == 3))
                        pe.inc(ins)
                    t_(f)
                    def f(v, e=e, hh=hh, tq=tq):
                        v.wait_ge(pe.h, pe.n)
                        accv = acc[:].rearrange("p (c t2) -> p c t2", c=8)[
                            :, 4 * hh:4 * hh + 4, tq * 512:tq * 512 + 512]
                        if e < NEXP:
                            v.wait_ge(ve.h, wb_ready)
                            for c in range(4):
                                ve.inc(v.tensor_tensor(tmpd[:, c * 512:c * 512 + 512],
                                                       pb[c][:, :],
                                                       wbcast[:, tq * 512:tq * 512 + 512],
                                                       op=ALU.mult))
                            v.wait_ge(ve.h, ve.n)
                            ve.inc(v.tensor_tensor(
                                accv, accv,
                                tmpd[:].rearrange("p (c t2) -> p c t2", c=4), op=ALU.add))
                        else:
                            for c in range(4):
                                ve.inc(v.tensor_tensor(
                                    accv[:, c:c + 1, :].squeeze(1), accv[:, c:c + 1, :].squeeze(1),
                                    pb[c][:, :], op=ALU.add))
                    v_(f)
                prev_dn_ve = ve.n
            eq_consumed[i] = pe.n
            if i + 2 < len(eq_list):
                def f(g, i=i):
                    g.wait_ge(pe.h, eq_consumed[i])
                    wload(g, *eq_list[i + 2])
                    eq_loaded[i + 2] = d_ffn[i % 2].n
                g_(f)

    st_D.close()
    st_BD.close()

    ffn_done_ve = ve.n

    def f(g):
        g.wait_ge(ve.h, ffn_done_ve)
        d_acc.inc(g.dma_start(out=acc[:].rearrange("p (c t2) -> p c t2", c=8),
                              in_=xT_own.rearrange("(c p) t2 -> p c t2", p=128),
                              accum_op=ALU.add), 16)
    g_(f)

    def f(s):
        s.wait_ge(d_acc.h, d_acc.n)
        d_out.inc(s.dma_start(out=outT.rearrange("(c p) t2 -> p c t2", p=128),
                              in_=acc[:].rearrange("p (c t2) -> p c t2", c=8)), 16)
        s.wait_ge(d_out.h, d_out.n)
    s_(f)

    es.close()
    return nc


_CACHED = {}


def _get_program():
    if "nc" not in _CACHED:
        _CACHED["nc"] = build_program()
    return _CACHED["nc"]


def kernel(**inputs):
    x = np.asarray(inputs["L0_attn_x"], dtype=np.float32)[0]
    xT = np.ascontiguousarray(x.T)
    x_live = np.ascontiguousarray(x[:LIVE])

    base = {
        "x_liv": x_live,
        "mapping": np.ascontiguousarray(np.asarray(inputs["L0_attn_mapping"], dtype=np.int32)),
        "Wqkv": np.ascontiguousarray(np.asarray(inputs["L0_attn_Wqkv"], dtype=np.float32)),
        "bqkv": np.ascontiguousarray(np.asarray(inputs["L0_attn_bqkv"], dtype=np.float32)),
        "Wproj": np.ascontiguousarray(np.asarray(inputs["L0_attn_Wproj"], dtype=np.float32)),
        "bproj": np.ascontiguousarray(np.asarray(inputs["L0_attn_bproj"], dtype=np.float32)),
        "gate_w": np.ascontiguousarray(np.asarray(inputs["L0_ffn_gate_w"], dtype=np.float32)),
        "w_gate": np.ascontiguousarray(np.asarray(inputs["L0_ffn_w_gate"], dtype=np.float32)),
        "w_up": np.ascontiguousarray(np.asarray(inputs["L0_ffn_w_up"], dtype=np.float32)),
        "w_down": np.ascontiguousarray(np.asarray(inputs["L0_ffn_w_down"], dtype=np.float32)),
        "ws_gate": np.ascontiguousarray(np.asarray(inputs["L0_ffn_ws_gate"], dtype=np.float32)),
        "ws_up": np.ascontiguousarray(np.asarray(inputs["L0_ffn_ws_up"], dtype=np.float32)),
        "ws_down": np.ascontiguousarray(np.asarray(inputs["L0_ffn_ws_down"], dtype=np.float32)),
    }

    in_maps = []
    p = np.arange(128)[:, None]
    g = np.arange(16)[None, :]
    for c in range(NCORES):
        qp = np.minimum(c * TOK + 128 * g + p, LIVE - 1).astype(np.int32)
        m = dict(base)
        m["x_own"] = np.ascontiguousarray(x[c * TOK:(c + 1) * TOK])
        m["xT_own"] = np.ascontiguousarray(xT[:, c * TOK:(c + 1) * TOK])
        m["qpos"] = np.ascontiguousarray(qp)
        in_maps.append(m)

    nc = _get_program()
    res = run_bass_kernel_spmd(nc, in_maps, core_ids=list(range(NCORES)))
    out = np.empty((M, DM), dtype=np.float32)
    for c in range(NCORES):
        out[c * TOK:(c + 1) * TOK] = res.results[c]["outT"].T
    return out.reshape(1, M, DM)


# version 02
# speedup vs baseline: 1.3120x; 1.3120x over previous
import numpy as np
from contextlib import ExitStack

import concourse.bass as bass
import concourse.mybir as mybir
from concourse.bass_utils import run_bass_kernel_spmd
from concourse.bass import IndirectOffsetOnAxis

F32 = mybir.dt.float32
BF16 = mybir.dt.bfloat16
I32 = mybir.dt.int32
AF = mybir.ActivationFunctionType
ALU = mybir.AluOpType
AX = mybir.AxisListType

NCORES = 8
M = 16384
DM = 1024
LIVE = 4096
TOK = 2048
NH = 16
NKV = 1024
NSEG = 16
VSEGW = 16 * 65
DFF = 2048
NEXP = 8
NT = 4
TW = 512
NBLK = 8


class Sem:
    def __init__(self, es, nc, name):
        self.h = es.enter_context(nc.semaphore(name))
        self.n = 0

    def inc(self, instr, k=1):
        instr.then_inc(self.h, k)
        self.n += k
        return instr


def _fmload(eng, dst, src, nch):
    return eng.dma_start(out=dst[:].rearrange("p (c o) -> p c o", c=nch),
                         in_=src.rearrange("(c p) o -> p c o", p=128))


def build_program(phases=("A", "C", "B", "D", "E")):
    nc = bass.Bass("TRN2", target_bir_lowering=False, enable_asserts=False)

    x_liv = nc.dram_tensor("x_liv", [LIVE, DM], F32, kind="ExternalInput").ap()
    x_own = nc.dram_tensor("x_own", [TOK, DM], F32, kind="ExternalInput").ap()
    xT_own = nc.dram_tensor("xT_own", [DM, TOK], F32, kind="ExternalInput").ap()
    mapping = nc.dram_tensor("mapping", [LIVE], I32, kind="ExternalInput").ap()
    qpos = nc.dram_tensor("qpos", [128, 16], I32, kind="ExternalInput").ap()
    Wqkv = nc.dram_tensor("Wqkv", [DM, 3 * DM], F32, kind="ExternalInput").ap()
    bqkv = nc.dram_tensor("bqkv", [3 * DM], F32, kind="ExternalInput").ap()
    Wproj = nc.dram_tensor("Wproj", [DM, DM], F32, kind="ExternalInput").ap()
    bproj = nc.dram_tensor("bproj", [DM], F32, kind="ExternalInput").ap()
    gate_w = nc.dram_tensor("gate_w", [DM, NEXP], F32, kind="ExternalInput").ap()
    w_gate = nc.dram_tensor("w_gate", [NEXP, DM, DFF], F32, kind="ExternalInput").ap()
    w_up = nc.dram_tensor("w_up", [NEXP, DM, DFF], F32, kind="ExternalInput").ap()
    w_down = nc.dram_tensor("w_down", [NEXP, DFF, DM], F32, kind="ExternalInput").ap()
    ws_gate = nc.dram_tensor("ws_gate", [DM, DFF], F32, kind="ExternalInput").ap()
    ws_up = nc.dram_tensor("ws_up", [DM, DFF], F32, kind="ExternalInput").ap()
    ws_down = nc.dram_tensor("ws_down", [DFF, DM], F32, kind="ExternalInput").ap()
    outT = nc.dram_tensor("outT", [DM, TOK], F32, kind="ExternalOutput").ap()
    K_dram = nc.dram_tensor("K_dram", [128 * NSEG, 512], BF16).ap()
    V_dram = nc.dram_tensor("V_dram", [64 * NSEG, VSEGW], BF16).ap()
    W_wb = nc.dram_tensor("W_wb", [NEXP, 128, TOK], BF16).ap()

    es = ExitStack()

    def sbuf(name, shape, dt):
        return es.enter_context(nc.sbuf_tensor(name, shape, dt))

    identf = sbuf("identf", [128, 128], F32)
    identb = sbuf("identb", [128, 128], BF16)
    ones64f = sbuf("ones64f", [1, 64], F32)
    ones128f = sbuf("ones128f", [1, 128], F32)
    iota16f = sbuf("iota16f", [128, 1], F32)
    qpos_sb = sbuf("qpos_sb", [128, 16], I32)
    kvidx = sbuf("kvidx", [128, 8], I32)
    qmap = sbuf("qmap", [128, 16], I32)
    segraw = sbuf("segraw", [1, 32], I32)
    segf = sbuf("segf", [1, 32], F32)
    idxK = sbuf("idxK", [128, 32], I32)
    bqkv_sb = sbuf("bqkv_sb", [128, 24], F32)
    bqs = sbuf("bqs", [128, 8], F32)
    bproj_sb = sbuf("bproj_sb", [128, 8], F32)
    acc = sbuf("acc", [128, 8 * TOK], F32)
    bn6 = [sbuf(f"bn6_{i}", [128, 12], F32) for i in range(2)]
    bn2 = [sbuf(f"bn2_{i}", [128, 2], F32) for i in range(2)]
    rstd = [sbuf(f"rstd{i}", [128, 1], F32) for i in range(2)]
    nmr = [sbuf(f"nmr{i}", [128, 1], F32) for i in range(2)]
    stg = [sbuf(f"stg{i}", [128, DM], F32) for i in range(2)]
    stgn = [sbuf(f"stgn{i}", [128, DM], BF16) for i in range(2)]

    pb = [es.enter_context(nc.psum_tensor(f"pb{i}", [128, 512], F32)) for i in range(4)]
    st_TB = ExitStack()
    pb += [st_TB.enter_context(nc.psum_tensor(f"pb{i}", [128, 512], F32)) for i in range(4, 6)]
    tb = [st_TB.enter_context(nc.psum_tensor(f"tb{i}", [128, 1024], BF16)) for i in range(2)]

    d_in = Sem(es, nc, "d_in")
    d_iny = Sem(es, nc, "d_iny")
    d_idx = Sem(es, nc, "d_idx")
    d_g = [Sem(es, nc, f"d_g{i}") for i in range(2)]
    d_kst = [Sem(es, nc, f"d_kst{i}") for i in range(2)]
    d_own = [Sem(es, nc, f"d_own{i}") for i in range(2)]
    d_kv = Sem(es, nc, "d_kv")
    d_wtf = Sem(es, nc, "d_wtf")
    d_wb = [Sem(es, nc, f"d_wb{i}") for i in range(2)]
    d_ffn = [Sem(es, nc, f"d_ffn{i}") for i in range(2)]
    d_bv = Sem(es, nc, "d_bv")
    d_acc = Sem(es, nc, "d_acc")
    d_out = Sem(es, nc, "d_out")
    pe = Sem(es, nc, "pe")
    ve = Sem(es, nc, "ve")
    se = Sem(es, nc, "se")
    gp = Sem(es, nc, "gp")

    blk = es.enter_context(nc.Block())
    g_, s_, t_, v_, a_ = blk.gpsimd, blk.sync, blk.tensor, blk.vector, blk.scalar

    def f(g):
        gp.inc(g.iota(identf[:], pattern=[[1, 128]], base=0, channel_multiplier=-1,
               allow_small_or_imprecise_dtypes=True))
        gp.inc(g.iota(iota16f[:], pattern=[[0, 1]], base=0, channel_multiplier=16,
               allow_small_or_imprecise_dtypes=True))
        gp.inc(g.memset(ones64f[:], 1.0))
        gp.inc(g.memset(ones128f[:], 1.0))
        d_in.inc(g.dma_start(
            out=kvidx[:],
            in_=mapping.rearrange("(gi p f) -> p gi f", gi=8, p=128)[:, :, 0:1].squeeze(2),
            allow_slow_non_contiguous=True), 16)
    g_(f)

    def f(s):
        d_iny.inc(s.dma_start(out=qpos_sb[:], in_=qpos), 16)
        d_iny.inc(s.dma_start(out=bqkv_sb[:], in_=bqkv.rearrange("(c p) -> p c", p=128),
                              allow_slow_non_contiguous=True), 16)
        d_iny.inc(s.dma_start(out=bproj_sb[:], in_=bproj.rearrange("(c p) -> p c", p=128),
                              allow_slow_non_contiguous=True), 16)
    s_(f)
    setup_iny = d_iny.n
    setup_in = d_in.n

    def f(v):
        v.wait_ge(gp.h, 1)
        ve.inc(v.tensor_scalar(identf[:], identf[:], 0.0, None, op0=ALU.is_equal))
        v.wait_ge(ve.h, ve.n)
        ve.inc(v.tensor_copy(identb[:], identf[:]))
        v.wait_ge(d_iny.h, setup_iny)
        ve.inc(v.tensor_scalar(bqs[:], bqkv_sb[:, 0:8], 0.125, None, op0=ALU.mult))
    v_(f)
    ident_ready = ve.n

    def f(g):
        g.wait_ge(d_iny.h, setup_iny)
        for gi in range(16):
            d_idx.inc(g.indirect_dma_start(
                out=qmap[:, gi:gi + 1], out_offset=None,
                in_=mapping[None, :],
                in_offset=IndirectOffsetOnAxis(qpos_sb[:, gi:gi + 1], 1)), 16)
        g.wait_ge(d_idx.h, d_idx.n)
        d_idx.inc(g.dma_start(out=segraw[:], in_=qmap[0:128:64, 0:16]), 16)
    g_(f)
    qmap_done = d_idx.n

    def f(v):
        v.wait_ge(d_idx.h, d_idx.n)
        ve.inc(v.tensor_scalar(segraw[:], segraw[:], 8, None, op0=ALU.arith_shift_right))
        v.wait_ge(ve.h, ve.n)
        ve.inc(v.tensor_copy(segf[:], segraw[:]))
    v_(f)
    segf_ready = ve.n

    def f(t):
        t.wait_ge(ve.h, segf_ready)
        t.wait_ge(gp.h, 4)
        pe.inc(t.matmul(pb[0][:, 0:32], ones128f[:], segf[:], start=True, stop=True))
    t_(f)
    def f(v):
        v.wait_ge(pe.h, pe.n)
        ve.inc(v.tensor_scalar(pb[0][:, 0:32], pb[0][:, 0:32], 1.0, iota16f[:],
                               op0=ALU.mult, op1=ALU.add))
        v.wait_ge(ve.h, ve.n)
        ve.inc(v.tensor_copy(idxK[:], pb[0][:, 0:32]))
    v_(f)
    idx_ready = ve.n

    all_sems = [d_in, d_iny, d_idx, d_g[0], d_g[1], d_kst[0], d_kst[1],
                d_own[0], d_own[1], d_kv, d_bv, d_wtf, d_wb[0], d_wb[1], d_ffn[0], d_ffn[1],
                d_acc, d_out, pe, ve, se, gp]

    def phase_barrier():
        vals = [(sm.h, sm.n) for sm in all_sems if sm.n > 0]
        for section in (g_, s_, t_, v_, a_):
            def f(e, vals=vals):
                for h, n in vals:
                    e.wait_ge(h, n)
            section(f)

    def blkcol(b):
        return 16 * (b & 1) + (b >> 1)

    ln_se = [0, 0]

    def emit_ln(src, dst, src_wait, k):
        def f(v, k=k):
            v.wait_ge(*src_wait)
            ve.inc(v.bn_stats(bn6[k][:, 0:6], src[:, 0:512]))
            ve.inc(v.bn_stats(bn6[k][:, 6:12], src[:, 512:1024]))
            v.wait_ge(ve.h, ve.n)
            ve.inc(v.bn_aggr(bn2[k][:], bn6[k][:]))
            v.wait_ge(ve.h, ve.n)
            v.wait_ge(se.h, ln_se[k])
            ve.inc(v.tensor_scalar(rstd[k][:], bn2[k][:, 1:2], 1e-5, None, op0=ALU.add))
            v.wait_ge(ve.h, ve.n)
            ve.inc(v.reciprocal(rstd[k][:], rstd[k][:]))
        v_(f)
        def f(a, k=k):
            a.wait_ge(ve.h, ve.n)
            se.inc(a.activation(rstd[k][:], rstd[k][:], AF.Sqrt, bias=0.0, scale=1.0))
        a_(f)
        ln_se[k] = se.n
        def f(v, k=k):
            v.wait_ge(se.h, se.n)
            ve.inc(v.tensor_scalar(nmr[k][:], bn2[k][:, 0:1], rstd[k][:], -1.0,
                                   op0=ALU.mult, op1=ALU.mult))
            v.wait_ge(ve.h, ve.n)
            ve.inc(v.tensor_scalar(dst[:], src[:], rstd[k][:], nmr[k][:],
                                   op0=ALU.mult, op1=ALU.add))
        v_(f)
        return (ve.h, ve.n)

    tb_free = [0, 0]

    def emit_tr8(src, src_ready, dst, dst_col, col_stride):
        for ch in range(8):
            k = ch % 2
            def f(t, ch=ch, k=k):
                t.wait_ge(src_ready[0], src_ready[1])
                t.wait_ge(ve.h, max(tb_free[k], ident_ready))
                pe.inc(t.transpose(tb[k][0:128, 0:128],
                                   src[:, ch * 128:ch * 128 + 128], identb[:]))
            t_(f)
            def f(v, ch=ch, k=k):
                v.wait_ge(pe.h, pe.n)
                ve.inc(v.tensor_copy(dst[:, dst_col + ch * col_stride:
                                         dst_col + ch * col_stride + 128],
                                     tb[k][0:128, 0:128]))
            v_(f)
            tb_free[k] = ve.n

    st_A = ExitStack()
    wk_sb = st_A.enter_context(nc.sbuf_tensor("wk_sb", [128, 8 * DM], BF16))
    wv_sb = st_A.enter_context(nc.sbuf_tensor("wv_sb", [128, 8 * DM], BF16))
    xn_kvT = st_A.enter_context(nc.sbuf_tensor("xn_kvT", [128, 8 * NKV], BF16))
    bv_b = st_A.enter_context(nc.sbuf_tensor("bv_b", [64, DM], F32))
    bv_row = st_A.enter_context(nc.sbuf_tensor("bv_row", [1, DM], F32))
    K_T2 = st_A.enter_context(nc.sbuf_tensor("K_T2", [128, NSEG * 512], BF16))
    V_seg = st_A.enter_context(nc.sbuf_tensor("V_seg", [64, NSEG * VSEGW], BF16))

    if "A" in phases:
        def f(g):
            d_in.inc(_fmload(g, wk_sb, Wqkv[:, DM:2 * DM], 8), 16)
            d_in.inc(_fmload(g, wv_sb, Wqkv[:, 2 * DM:3 * DM], 8), 16)
            vs4 = V_seg[:].rearrange("p (s h d) -> p s h d", s=NSEG, h=16)
            gp.inc(g.memset(vs4[:, :, :, 64:65], 1.0))
        g_(f)
        wkwv_loaded = d_in.n
        vones_gp = gp.n

        def f(s2):
            d_bv.inc(s2.dma_start(out=bv_row[:], in_=bqkv[None, 2 * DM:3 * DM]), 16)
        s_(f)

        def f(t):
            t.wait_ge(d_bv.h, d_bv.n)
            t.wait_ge(gp.h, 4)
            pe.inc(t.matmul(pb[1][0:64, :], ones64f[:], bv_row[0:1, 0:512],
                            start=True, stop=True))
            pe.inc(t.matmul(pb[2][0:64, :], ones64f[:], bv_row[0:1, 512:1024],
                            start=True, stop=True))
        t_(f)
        def f(v):
            v.wait_ge(pe.h, pe.n)
            ve.inc(v.tensor_copy(bv_b[:, 0:512], pb[1][0:64, :]))
            ve.inc(v.tensor_copy(bv_b[:, 512:1024], pb[2][0:64, :]))
        v_(f)
        bvb_ready = ve.n

        stg_free = [(se.h, 0), (se.h, 0)]
        for gi in range(8):
            k = gi % 2
            def f(g, gi=gi, k=k, din=None):
                g.wait_ge(d_in.h, d_in.n)
                g.wait_ge(*stg_free[k])
                d_g[k].inc(g.indirect_dma_start(
                    out=stg[k][:], out_offset=None,
                    in_=x_liv,
                    in_offset=IndirectOffsetOnAxis(kvidx[:, gi:gi + 1], 0)), 16)
            g_(f)
            ready = emit_ln(stg[k], stgn[k], (d_g[k].h, d_g[k].n), k)
            stg_free[k] = ready
            emit_tr8(stgn[k], ready, xn_kvT, gi * 128, NKV)
        xnkvT_ready = ve.n

        k_evacs = []
        for kc in range(8):
            for half in range(2):
                i = kc * 2 + half
                bank = pb[i % 4]
                def f(t, kc=kc, half=half, bank=bank, i=i):
                    t.wait_ge(d_in.h, wkwv_loaded)
                    t.wait_ge(ve.h, max(xnkvT_ready, idx_ready))
                    if i >= 4:
                        t.wait_ge(ve.h, k_evacs[i - 4])
                    for dc in range(8):
                        ins = t.matmul(bank[:, :],
                                       wk_sb[:, dc * DM + kc * 128:dc * DM + kc * 128 + 128],
                                       xn_kvT[:, dc * NKV + half * 512:dc * NKV + half * 512 + 512],
                                       start=(dc == 0), stop=(dc == 7))
                    pe.inc(ins)
                t_(f)
                def f(v, kc=kc, half=half, bank=bank):
                    v.wait_ge(pe.h, pe.n)
                    dst = K_T2[:].rearrange("p (sg r) -> p sg r", sg=NSEG)[
                        :, half * 8:half * 8 + 8, kc * 64:kc * 64 + 64]
                    srcp = bank[:, :].rearrange("p (sg s) -> p sg s", sg=8)
                    ve.inc(v.tensor_scalar(dst, srcp, bqkv_sb[:, 8 + kc:9 + kc], None,
                                           op0=ALU.add))
                v_(f)
                k_evacs.append(ve.n)
        KT_done = ve.n

        v_evacs = []
        for sg in range(NSEG):
            for vh in range(2):
                i = sg * 2 + vh
                bank = pb[4 + (i % 2)]
                def f(t, sg=sg, vh=vh, bank=bank, i=i):
                    t.wait_ge(ve.h, max(xnkvT_ready, bvb_ready))
                    if i >= 2:
                        t.wait_ge(ve.h, v_evacs[i - 2])
                    for dc in range(8):
                        ins = t.matmul(bank[0:64, :],
                                       xn_kvT[:, dc * NKV + sg * 64:dc * NKV + sg * 64 + 64],
                                       wv_sb[:, dc * DM + vh * 512:dc * DM + vh * 512 + 512],
                                       start=(dc == 0), stop=(dc == 7))
                    pe.inc(ins)
                t_(f)
                def f(v, sg=sg, vh=vh, bank=bank):
                    v.wait_ge(pe.h, pe.n)
                    base = sg * VSEGW + vh * 520
                    dst = V_seg[0:64, base:base + 520].rearrange(
                        "p (h d) -> p h d", h=8)[:, :, 0:64]
                    srcv = bank[0:64, :].rearrange("p (h d) -> p h d", h=8)
                    ve.inc(v.scalar_tensor_tensor(
                        dst, srcv, 0.0,
                        bv_b[:, vh * 512:vh * 512 + 512].rearrange("p (h d) -> p h d", h=8),
                        op0=ALU.bypass, op1=ALU.add))
                v_(f)
                v_evacs.append(ve.n)
        Vseg_done = ve.n

        def f(g):
            g.wait_ge(ve.h, KT_done)
            d_kv.inc(g.dma_start(out=K_dram.rearrange("(p e) r -> p e r", p=128),
                                 in_=K_T2[:].rearrange("p (e r) -> p e r", e=NSEG)), 16)
            g.wait_ge(ve.h, Vseg_done)
            g.wait_ge(gp.h, vones_gp)
            d_kv.inc(g.dma_start(out=V_dram.rearrange("(p e) r -> p e r", p=64),
                                 in_=V_seg[:].rearrange("p (e r) -> p e r", e=NSEG)), 16)
        g_(f)
        kv_dram_done = d_kv.n
    else:
        kv_dram_done = d_kv.n
        def f(v):
            ve.inc(v.memset(acc[:], 0.0))
        v_(f)

    st_A.close()
    phase_barrier()

    st_C = ExitStack()
    wq_sb = st_C.enter_context(nc.sbuf_tensor("wq_sb", [128, 8 * DM], BF16))
    wp_sb = st_C.enter_context(nc.sbuf_tensor("wp_sb", [128, 8 * DM], BF16))
    xnqT = st_C.enter_context(nc.sbuf_tensor("xnqT", [128, 8 * TW], BF16))
    QT = st_C.enter_context(nc.sbuf_tensor("QT", [128, 8 * TW], BF16))
    attn_tm = st_C.enter_context(nc.sbuf_tensor("attn_tm", [64, 2 * DM], BF16))
    attn_fm = st_C.enter_context(nc.sbuf_tensor("attn_fm", [128, 8 * TW], BF16))
    Kst = [st_C.enter_context(nc.sbuf_tensor(f"Kst{i}", [128, NBLK * 512], BF16)) for i in range(2)]
    Vst = [st_C.enter_context(nc.sbuf_tensor(f"Vst{i}", [64, NBLK * VSEGW], BF16)) for i in range(2)]
    P_sb = [st_C.enter_context(nc.sbuf_tensor(f"P_sb{i}", [64, 64], BF16)) for i in range(3)]
    negC = [st_C.enter_context(nc.sbuf_tensor(f"negC{i}", [64, 1], F32)) for i in range(2)]
    aU = [st_C.enter_context(nc.sbuf_tensor(f"aU{i}", [64, VSEGW], F32)) for i in range(2)]
    rec16 = st_C.enter_context(nc.sbuf_tensor("rec16", [64, 16], F32))

    if "C" in phases:
        def f(g):
            d_in.inc(_fmload(g, wq_sb, Wqkv[:, 0:DM], 8), 16)
            d_in.inc(_fmload(g, wp_sb, Wproj, 8), 16)
        g_(f)
        wqwp_loaded = d_in.n

        stg_free = [(se.h, se.n), (se.h, se.n)]
        prev_proj_ve = ve.n
        prev_pipe_pe = 0
        for j in range(NT):
            kb = j % 2
            def f(g, j=j, kb=kb):
                g.wait_ge(d_kv.h, kv_dram_done)
                g.wait_ge(ve.h, idx_ready)
                g.wait_ge(pe.h, prev_pipe_pe)
                for bb in range(NBLK):
                    col = blkcol(8 * j + bb)
                    d_kst[kb].inc(g.indirect_dma_start(
                        out=Kst[kb][:, bb * 512:bb * 512 + 512], out_offset=None,
                        in_=K_dram,
                        in_offset=IndirectOffsetOnAxis(idxK[:, col:col + 1], 0)), 16)
                    d_kst[kb].inc(g.indirect_dma_start(
                        out=Vst[kb][0:64, bb * VSEGW:bb * VSEGW + VSEGW], out_offset=None,
                        in_=V_dram,
                        in_offset=IndirectOffsetOnAxis(idxK[0:64, col:col + 1], 0)), 16)
            g_(f)
            kv_st_done = d_kst[kb].n

            for gsub in range(4):
                k = gsub % 2
                gi = 4 * j + gsub
                def f(g, gi=gi, k=k):
                    g.wait_ge(d_idx.h, qmap_done)
                    g.wait_ge(*stg_free[k])
                    d_g[k].inc(g.indirect_dma_start(
                        out=stg[k][:], out_offset=None,
                        in_=x_liv,
                        in_offset=IndirectOffsetOnAxis(qmap[:, gi:gi + 1], 0)), 16)
                g_(f)
                ready = emit_ln(stg[k], stgn[k], (d_g[k].h, d_g[k].n), k)
                stg_free[k] = ready
                emit_tr8(stgn[k], ready, xnqT, gsub * 128, TW)
            xnqT_ready = ve.n

            q_evacs = []
            for qc in range(8):
                bank = pb[qc % 2]
                def f(t, qc=qc, bank=bank):
                    t.wait_ge(ve.h, xnqT_ready)
                    t.wait_ge(d_in.h, wqwp_loaded)
                    if len(q_evacs) >= 2:
                        t.wait_ge(ve.h, q_evacs[qc - 2])
                    for dc in range(8):
                        ins = t.matmul(bank[:, :],
                                       wq_sb[:, dc * DM + qc * 128:dc * DM + qc * 128 + 128],
                                       xnqT[:, dc * TW:dc * TW + 512],
                                       start=(dc == 0), stop=(dc == 7))
                    pe.inc(ins)
                t_(f)
                def f(v, qc=qc, bank=bank):
                    v.wait_ge(pe.h, pe.n)
                    ve.inc(v.tensor_scalar(QT[:, qc * TW:qc * TW + 512], bank[:, :],
                                           bqkv_sb[:, qc:qc + 1], 0.125,
                                           op0=ALU.add, op1=ALU.mult))
                v_(f)
                q_evacs.append(ve.n)
            QT_ready = ve.n

            NIT = NBLK * NH
            c_sc, c_rd, c_ex, c_av, c_ea = ({} for _ in range(5))
            c_no = {}
            c_tp = {}
            c_tc = {}
            pe_c, ve_c, se_c = pe.n, ve.n, se.n
            NSLOT = NIT + 6 + 16
            for slot in range(NSLOT):
                if slot < NIT:
                    pe_c += 1; c_sc[slot] = pe_c
                if 0 <= slot - 1 < NIT:
                    ve_c += 1; c_rd[slot - 1] = ve_c
                if 0 <= slot - 2 < NIT:
                    se_c += 1; c_ex[slot - 2] = se_c
                if 0 <= slot - 3 < NIT:
                    pe_c += 1; c_av[slot - 3] = pe_c
                if 0 <= slot - 4 < NIT:
                    i4 = slot - 4
                    ve_c += 1; c_ea[i4] = ve_c
                    if i4 % NH == NH - 1:
                        ve_c += 3; c_no[i4 // NH] = ve_c
                if slot >= 5 and (slot - 5) % NH == NH - 1:
                    bb2 = (slot - 5) // NH
                    if 0 <= bb2 < NBLK:
                        for ch in range(8):
                            pe_c += 1; c_tp[(bb2, ch)] = pe_c
                            ve_c += 1; c_tc[(bb2, ch)] = ve_c
            prev_se = se.n

            def f(t, j=j, kb=kb, QT_ready=QT_ready, kv_st_done=kv_st_done,
                  ppv=prev_proj_ve, prev_se=prev_se, tbf=tuple(tb_free)):
                for slot in range(NSLOT):
                    i = slot
                    if i < NIT:
                        bb, h = divmod(i, NH)
                        if i == 0:
                            t.wait_ge(ve.h, QT_ready)
                            t.wait_ge(d_kst[kb].h, kv_st_done)
                            t.wait_ge(ve.h, ppv)
                            t.wait_ge(se.h, prev_se)
                        if i >= 2:
                            t.wait_ge(ve.h, c_rd[i - 2])
                            t.wait_ge(se.h, c_ex[i - 2])
                        lhs = Kst[kb][(h & 1) * 64:(h & 1) * 64 + 64,
                                      bb * 512 + (h >> 1) * 64:bb * 512 + (h >> 1) * 64 + 64]
                        rhs = QT[(h & 1) * 64:(h & 1) * 64 + 64,
                                 (h >> 1) * TW + 64 * bb:(h >> 1) * TW + 64 * bb + 64]
                        pe.inc(t.matmul(pb[2 + (i % 2)][0:64, 0:64], lhs, rhs,
                                        start=True, stop=True))
                        assert pe.n == c_sc[i]
                    i = slot - 3
                    if 0 <= i < NIT:
                        bb, h = divmod(i, NH)
                        t.wait_ge(se.h, c_ex[i])
                        if i >= 2:
                            t.wait_ge(ve.h, c_ea[i - 2])
                        rhs = Vst[kb][0:64, bb * VSEGW + 65 * h:bb * VSEGW + 65 * h + 65]
                        pe.inc(t.matmul(pb[(i % 2)][0:64, 0:65], P_sb[i % 3][:], rhs,
                                        start=True, stop=True))
                        assert pe.n == c_av[i]
                    if slot >= 5 and (slot - 5) % NH == NH - 1:
                        bb2 = (slot - 5) // NH
                        if 0 <= bb2 < NBLK:
                            for ch in range(8):
                                if ch == 0:
                                    t.wait_ge(ve.h, c_no[bb2])
                                if ch >= 2:
                                    t.wait_ge(ve.h, c_tc[(bb2, ch - 2)])
                                elif bb2 >= 1:
                                    t.wait_ge(ve.h, c_tc[(bb2 - 1, 6 + ch)])
                                elif ch < 2:
                                    t.wait_ge(ve.h, tbf[ch])
                                pe.inc(t.transpose(
                                    tb[ch % 2][0:128, 0:64],
                                    attn_tm[0:64, (bb2 % 2) * DM + ch * 128:
                                            (bb2 % 2) * DM + ch * 128 + 128],
                                    identb[0:64, 0:64]))
                                assert pe.n == c_tp[(bb2, ch)]
            t_(f)
            prev_pipe_pe = pe.n

            def f(v, kb=kb):
                for slot in range(NSLOT):
                    i = slot - 1
                    if 0 <= i < NIT:
                        v.wait_ge(pe.h, c_sc[i])
                        if i >= 2:
                            v.wait_ge(se.h, c_ex[i - 2])
                        ve.inc(v.reduce_max(negC[i % 2][:], pb[2 + (i % 2)][0:64, 0:64],
                                            axis=AX.X, negate=True))
                        assert ve.n == c_rd[i]
                    i = slot - 4
                    if 0 <= i < NIT:
                        bb, h = divmod(i, NH)
                        v.wait_ge(pe.h, c_av[i])
                        ve.inc(v.tensor_copy(aU[bb % 2][:, 65 * h:65 * h + 65],
                                             pb[(i % 2)][0:64, 0:65]))
                        assert ve.n == c_ea[i]
                        if h == NH - 1:
                            if bb >= 2:
                                v.wait_ge(pe.h, c_tp[(bb - 2, 7)])
                            v.wait_ge(ve.h, ve.n)
                            a3 = aU[bb % 2][:].rearrange("p (hh c) -> p hh c", hh=16)
                            ve.inc(v.tensor_scalar(rec16[:], a3[:, :, 64:65].squeeze(2),
                                                   1e-6, None, op0=ALU.add))
                            v.wait_ge(ve.h, ve.n)
                            ve.inc(v.reciprocal(rec16[:], rec16[:]))
                            v.wait_ge(ve.h, ve.n)
                            dstb = attn_tm[0:64, (bb % 2) * DM:(bb % 2) * DM + DM].rearrange(
                                "p (hh d) -> p hh d", hh=16)
                            ve.inc(v.tensor_tensor(
                                dstb, a3[:, :, 0:64],
                                rec16[:].unsqueeze(2).broadcast_to([64, 16, 64]),
                                op=ALU.mult))
                            assert ve.n == c_no[bb]
                    if slot >= 5 and (slot - 5) % NH == NH - 1:
                        bb2 = (slot - 5) // NH
                        if 0 <= bb2 < NBLK:
                            for ch in range(8):
                                v.wait_ge(pe.h, c_tp[(bb2, ch)])
                                ve.inc(v.tensor_copy(
                                    attn_fm[:, ch * TW + bb2 * 64:ch * TW + bb2 * 64 + 64],
                                    tb[ch % 2][0:128, 0:64]))
                                assert ve.n == c_tc[(bb2, ch)]
            v_(f)
            tb_free[0] = c_tc[(NBLK - 1, 6)]
            tb_free[1] = c_tc[(NBLK - 1, 7)]
            attn_fm_ready = ve.n

            def f(a):
                for slot in range(NSLOT):
                    i = slot - 2
                    if 0 <= i < NIT:
                        a.wait_ge(ve.h, c_rd[i])
                        if i >= 3:
                            a.wait_ge(pe.h, c_av[i - 3])
                        se.inc(a.activation(P_sb[i % 3][:], pb[2 + (i % 2)][0:64, 0:64],
                                            AF.Exp, bias=negC[i % 2][:], scale=1.0))
                        assert se.n == c_ex[i]
            a_(f)

            p_evacs = []
            for oc in range(8):
                bank = pb[2 + (oc % 2)]
                def f(t, oc=oc, bank=bank):
                    t.wait_ge(ve.h, attn_fm_ready)
                    if len(p_evacs) >= 2:
                        t.wait_ge(ve.h, p_evacs[oc - 2])
                    for fc in range(8):
                        ins = t.matmul(bank[:, :],
                                       wp_sb[:, fc * DM + oc * 128:fc * DM + oc * 128 + 128],
                                       attn_fm[:, fc * TW:fc * TW + 512],
                                       start=(fc == 0), stop=(fc == 7))
                    pe.inc(ins)
                t_(f)
                def f(v, oc=oc, bank=bank, j=j):
                    v.wait_ge(pe.h, pe.n)
                    ve.inc(v.tensor_scalar(acc[:, oc * TOK + j * TW:oc * TOK + j * TW + 512],
                                           bank[:, :], bproj_sb[:, oc:oc + 1], None,
                                           op0=ALU.add))
                v_(f)
                p_evacs.append(ve.n)
            prev_proj_ve = ve.n
        attn_done_ve = ve.n
    else:
        attn_done_ve = ve.n

    st_C.close()
    phase_barrier()

    st_BD = ExitStack()
    xn_ownT = st_BD.enter_context(nc.sbuf_tensor("xn_ownT", [128, 8 * TOK], BF16))

    if "B" in phases or "D" in phases:
        stg_free2 = [(se.h, se.n), (se.h, se.n)]
        for tt in range(16):
            k = tt % 2
            def f(s, tt=tt, k=k):
                s.wait_ge(*stg_free2[k])
                d_own[k].inc(s.dma_start(out=stg[k][:], in_=x_own[tt * 128:tt * 128 + 128, :]), 16)
            s_(f)
            ready = emit_ln(stg[k], stgn[k], (d_own[k].h, d_own[k].n), k)
            stg_free2[k] = ready
            emit_tr8(stgn[k], ready, xn_ownT, tt * 128, TOK)
    xnownT_ready = ve.n

    st_D = ExitStack()
    gate_sb = st_D.enter_context(nc.sbuf_tensor("gate_sb", [128, 8 * NEXP], BF16))
    w_tm = st_D.enter_context(nc.sbuf_tensor("w_tm", [128, 16 * NEXP], F32))
    sgs = st_D.enter_context(nc.sbuf_tensor("sgs", [128, NEXP], F32))
    sgm = st_D.enter_context(nc.sbuf_tensor("sgm", [128, 1], F32))
    sgm2 = st_D.enter_context(nc.sbuf_tensor("sgm2", [128, 1], F32))
    sgtmp = st_D.enter_context(nc.sbuf_tensor("sgtmp", [128, NEXP], F32))
    wbcast = [st_D.enter_context(nc.sbuf_tensor(f"wbcast{i}", [128, TOK], BF16)) for i in range(2)]
    h_q = [st_D.enter_context(nc.sbuf_tensor(f"h_q{i}", [128, 4 * 512], BF16)) for i in range(2)]
    h32 = st_D.enter_context(nc.sbuf_tensor("h32", [128, 512], F32))
    hsil = [st_D.enter_context(nc.sbuf_tensor(f"hsil{i}", [128, 512], BF16)) for i in range(2)]
    wgb = [st_D.enter_context(nc.sbuf_tensor(f"wgb{i}", [128, 8 * 512], BF16)) for i in range(2)]
    wub = [st_D.enter_context(nc.sbuf_tensor(f"wub{i}", [128, 8 * 512], BF16)) for i in range(2)]
    wdb = [st_D.enter_context(nc.sbuf_tensor(f"wdb{i}", [128, 4 * DM], BF16)) for i in range(2)]

    if "D" in phases:
        def f(g):
            d_in.inc(_fmload(g, gate_sb, gate_w, 8), 16)
        g_(f)
        gate_loaded = d_in.n

        zg_done = []
        for tt in range(16):
            bank = pb[tt % 2]
            def f(t, tt=tt, bank=bank):
                t.wait_ge(d_in.h, gate_loaded)
                t.wait_ge(ve.h, xnownT_ready)
                if len(zg_done) >= 2:
                    t.wait_ge(ve.h, zg_done[tt - 2])
                for dc in range(8):
                    ins = t.matmul(bank[:, 0:NEXP],
                                   xn_ownT[:, dc * TOK + tt * 128:dc * TOK + tt * 128 + 128],
                                   gate_sb[:, dc * NEXP:dc * NEXP + NEXP],
                                   start=(dc == 0), stop=(dc == 7))
                pe.inc(ins)
            t_(f)
            def f(v, tt=tt, bank=bank):
                v.wait_ge(pe.h, pe.n)
                v.wait_ge(ve.h, ve.n)
                v.wait_ge(se.h, se.n)
                ve.inc(v.reduce_max(sgm[:], bank[:, 0:NEXP], axis=AX.X, negate=True))
            v_(f)
            def f(a, bank=bank):
                a.wait_ge(ve.h, ve.n)
                se.inc(a.activation(sgs[:], bank[:, 0:NEXP], AF.Exp, bias=sgm[:], scale=1.0))
            a_(f)
            def f(v, tt=tt):
                v.wait_ge(se.h, se.n)
                ve.inc(v.reduce_sum(sgm[:], sgs[:], axis=AX.X))
                v.wait_ge(ve.h, ve.n)
                ve.inc(v.reciprocal(sgm[:], sgm[:]))
                v.wait_ge(ve.h, ve.n)
                ve.inc(v.tensor_scalar(sgs[:], sgs[:], sgm[:], None, op0=ALU.mult))
                v.wait_ge(ve.h, ve.n)
                ve.inc(v.reduce_max(sgm[:], sgs[:], axis=AX.X))
                v.wait_ge(ve.h, ve.n)
                ve.inc(v.tensor_scalar(sgtmp[:], sgs[:], sgm[:], None, op0=ALU.is_equal))
                v.wait_ge(ve.h, ve.n)
                ve.inc(v.scalar_tensor_tensor(sgtmp[:], sgtmp[:], -1e30, sgs[:],
                                              op0=ALU.mult, op1=ALU.add))
                v.wait_ge(ve.h, ve.n)
                ve.inc(v.reduce_max(sgm2[:], sgtmp[:], axis=AX.X))
                v.wait_ge(ve.h, ve.n)
                ve.inc(v.tensor_scalar(sgtmp[:], sgs[:], sgm2[:], None, op0=ALU.is_ge))
                v.wait_ge(ve.h, ve.n)
                ve.inc(v.tensor_mul(sgtmp[:], sgtmp[:], sgs[:]))
                ve.inc(v.tensor_add(sgm[:], sgm[:], sgm2[:]))
                v.wait_ge(ve.h, ve.n)
                ve.inc(v.tensor_scalar(sgm[:], sgm[:], 1e-20, None, op0=ALU.add))
                v.wait_ge(ve.h, ve.n)
                ve.inc(v.reciprocal(sgm[:], sgm[:]))
                v.wait_ge(ve.h, ve.n)
                ve.inc(v.tensor_scalar(w_tm[:, tt * NEXP:tt * NEXP + NEXP], sgtmp[:],
                                       sgm[:], None, op0=ALU.mult))
            v_(f)
            zg_done.append(ve.n)

        st_WB = ExitStack()
        wT8 = st_WB.enter_context(nc.sbuf_tensor("wT8", [NEXP, TOK], F32))
        wTflat = st_WB.enter_context(nc.sbuf_tensor("wTflat", [1, TOK], F32))
        wbstage = st_WB.enter_context(nc.sbuf_tensor("wbstage", [128, TOK], BF16))

        for tt in range(16):
            def f(t, tt=tt):
                t.wait_ge(ve.h, ve.n)
                pe.inc(t.transpose(pb[2][0:NEXP, 0:128],
                                   w_tm[:, tt * NEXP:tt * NEXP + NEXP], identf[:]))
            t_(f)
            def f(v, tt=tt):
                v.wait_ge(pe.h, pe.n)
                ve.inc(v.tensor_copy(wT8[:, tt * 128:tt * 128 + 128], pb[2][0:NEXP, 0:128]))
            v_(f)
        wT8_ready = ve.n

        for e in range(NEXP):
            def f(g, e=e):
                g.wait_ge(ve.h, wT8_ready)
                g.wait_ge(pe.h, pe.n)
                d_wtf.inc(g.dma_start(out=wTflat[:], in_=wT8[e:e + 1, :]), 16)
            g_(f)
            wtf_done = d_wtf.n
            for tq in range(4):
                bank = pb[4 + tq % 2]
                def f(t, tq=tq, bank=bank, wtf_done=wtf_done):
                    t.wait_ge(d_wtf.h, wtf_done)
                    t.wait_ge(gp.h, 4)
                    t.wait_ge(ve.h, ve.n)
                    t.wait_ge(d_kv.h, d_kv.n)
                    pe.inc(t.matmul(bank[:, :], ones128f[:],
                                    wTflat[0:1, tq * 512:tq * 512 + 512],
                                    start=True, stop=True))
                t_(f)
                def f(v, tq=tq, bank=bank):
                    v.wait_ge(pe.h, pe.n)
                    ve.inc(v.tensor_copy(wbstage[:, tq * 512:tq * 512 + 512], bank[:, :]))
                v_(f)
            def f(g, e=e):
                g.wait_ge(ve.h, ve.n)
                d_kv.inc(g.dma_start(out=W_wb[e], in_=wbstage[:]), 16)
            g_(f)
        wb_built = d_kv.n

        st_WB.close()
        st_TB.close()
        st_Dp = ExitStack()
        pdown = st_Dp.enter_context(nc.psum_tensor("pdown", [128, 4, 512], F32))

        def wload(g, e, q):
            kbuf = (e * 4 + q) % 2
            if e < NEXP:
                srcs = (w_gate[e, :, 512 * q:512 * q + 512],
                        w_up[e, :, 512 * q:512 * q + 512],
                        w_down[e, 512 * q:512 * q + 512, :])
            else:
                srcs = (ws_gate[:, 512 * q:512 * q + 512],
                        ws_up[:, 512 * q:512 * q + 512],
                        ws_down[512 * q:512 * q + 512, :])
            d_ffn[kbuf].inc(_fmload(g, wgb[kbuf], srcs[0], 8), 16)
            d_ffn[kbuf].inc(_fmload(g, wub[kbuf], srcs[1], 8), 16)
            d_ffn[kbuf].inc(_fmload(g, wdb[kbuf], srcs[2], 4), 16)

        eq_list = [(e, q) for e in range(NEXP + 1) for q in range(4)]
        eq_loaded = {}
        eq_consumed = {}

        def f(g):
            for i in range(2):
                wload(g, *eq_list[i])
                eq_loaded[i] = d_ffn[i % 2].n
        g_(f)

        def f(s2):
            s2.wait_ge(d_kv.h, wb_built)
            d_wb[0].inc(s2.dma_start(out=wbcast[0][:], in_=W_wb[0]), 16)
        s_(f)
        wb_loaded = {0: d_wb[0].n}
        exp_mult_ve = {}
        dn_pe = {}
        prev_add_ve = ve.n

        for i, (e, q) in enumerate(eq_list):
            if q == 0 and e + 1 < NEXP:
                def f(s2, e=e):
                    s2.wait_ge(d_kv.h, wb_built)
                    if e >= 1:
                        s2.wait_ge(ve.h, exp_mult_ve[e - 1])
                    d_wb[(e + 1) % 2].inc(
                        s2.dma_start(out=wbcast[(e + 1) % 2][:], in_=W_wb[e + 1]), 16)
                s_(f)
                wb_loaded[e + 1] = d_wb[(e + 1) % 2].n

            for tq in range(4):
                tqp = tq % 2
                mul1_done = []
                mul2_done = []
                silu_done = []
                for sl in range(4):
                    kb2 = sl % 2
                    def f(t, i=i, sl=sl, kb2=kb2, tq=tq):
                        t.wait_ge(d_ffn[i % 2].h, eq_loaded[i])
                        if i == 0 and sl == 0 and tq == 0:
                            t.wait_ge(ve.h, xnownT_ready)
                        if len(silu_done) >= 2:
                            t.wait_ge(se.h, silu_done[sl - 2])
                        buf = wgb[i % 2]
                        for dc in range(8):
                            ins = t.matmul(pb[kb2][:, :],
                                           buf[:, dc * 512 + sl * 128:dc * 512 + sl * 128 + 128],
                                           xn_ownT[:, dc * TOK + tq * 512:dc * TOK + tq * 512 + 512],
                                           start=(dc == 0), stop=(dc == 7))
                        pe.inc(ins)
                        if len(mul1_done) >= 2:
                            t.wait_ge(ve.h, mul1_done[sl - 2])
                        buf = wub[i % 2]
                        for dc in range(8):
                            ins = t.matmul(pb[2 + kb2][:, :],
                                           buf[:, dc * 512 + sl * 128:dc * 512 + sl * 128 + 128],
                                           xn_ownT[:, dc * TOK + tq * 512:dc * TOK + tq * 512 + 512],
                                           start=(dc == 0), stop=(dc == 7))
                        pe.inc(ins)
                    t_(f)
                    gate_mm = pe.n - 1
                    def f(a, sl=sl, kb2=kb2, gate_mm=gate_mm):
                        a.wait_ge(pe.h, gate_mm)
                        if len(mul1_done) >= 2:
                            a.wait_ge(ve.h, mul1_done[sl - 2])
                        se.inc(a.activation(hsil[kb2][:], pb[kb2][:, :], AF.Silu,
                                            bias=0.0, scale=1.0))
                    a_(f)
                    silu_done.append(se.n)
                    def f(v, i=i, e=e, sl=sl, kb2=kb2, tq=tq, tqp=tqp):
                        v.wait_ge(se.h, se.n)
                        v.wait_ge(pe.h, pe.n)
                        v.wait_ge(ve.h, ve.n)
                        ve.inc(v.tensor_tensor(h32[:], hsil[kb2][:], pb[2 + kb2][:, :],
                                               op=ALU.mult))
                        mul1_done.append(ve.n)
                        v.wait_ge(ve.h, ve.n)
                        if (i // 4) * 4 + 8 <= len(eq_list):
                            pass
                        dn_key = (i, tq - 2)
                        if tq >= 2:
                            v.wait_ge(pe.h, dn_pe[(i, tq - 2)])
                        elif i >= 1 and tq < 2:
                            v.wait_ge(pe.h, dn_pe[(i - 1, tq + 2)])
                        if e < NEXP:
                            v.wait_ge(d_wb[e % 2].h, wb_loaded[e])
                            ve.inc(v.tensor_tensor(h_q[tqp][:, sl * 512:sl * 512 + 512],
                                                   h32[:], wbcast[e % 2][:, tq * 512:tq * 512 + 512],
                                                   op=ALU.mult))
                        else:
                            ve.inc(v.tensor_copy(h_q[tqp][:, sl * 512:sl * 512 + 512],
                                                 h32[:]))
                        mul2_done.append(ve.n)
                    v_(f)
                for hh in range(2):
                    def f(t, i=i, hh=hh, tq=tq, tqp=tqp, pav=prev_add_ve):
                        t.wait_ge(ve.h, mul2_done[3])
                        t.wait_ge(ve.h, pav)
                        for sl in range(4):
                            for c in range(4):
                                ins = t.matmul(
                                    pdown[:, c, :],
                                    wdb[i % 2][:, sl * DM + (4 * hh + c) * 128:
                                               sl * DM + (4 * hh + c) * 128 + 128],
                                    h_q[tqp][:, sl * 512:sl * 512 + 512],
                                    start=(sl == 0), stop=(sl == 3))
                        pe.inc(ins)
                    t_(f)
                    def f(v, hh=hh, tq=tq):
                        v.wait_ge(pe.h, pe.n)
                        accv = acc[:].rearrange("p (c t2) -> p c t2", c=8)[
                            :, 4 * hh:4 * hh + 4, tq * 512:tq * 512 + 512]
                        ve.inc(v.tensor_tensor(accv, accv, pdown[:, :, :], op=ALU.add))
                    v_(f)
                    prev_add_ve = ve.n
                dn_pe[(i, tq)] = pe.n
            exp_mult_ve[e] = ve.n
            eq_consumed[i] = pe.n
            if i + 2 < len(eq_list):
                def f(g, i=i):
                    g.wait_ge(pe.h, eq_consumed[i])
                    wload(g, *eq_list[i + 2])
                    eq_loaded[i + 2] = d_ffn[i % 2].n
                g_(f)
        st_Dp.close()

    st_D.close()
    st_BD.close()

    ffn_done_ve = ve.n

    def f(g):
        g.wait_ge(ve.h, ffn_done_ve)
        d_acc.inc(g.dma_start(out=acc[:].rearrange("p (c t2) -> p c t2", c=8),
                              in_=xT_own.rearrange("(c p) t2 -> p c t2", p=128),
                              accum_op=ALU.add), 16)
    g_(f)

    def f(s):
        s.wait_ge(d_acc.h, d_acc.n)
        d_out.inc(s.dma_start(out=outT.rearrange("(c p) t2 -> p c t2", p=128),
                              in_=acc[:].rearrange("p (c t2) -> p c t2", c=8)), 16)
        s.wait_ge(d_out.h, d_out.n)
    s_(f)

    es.close()
    return nc


_CACHED = {}


def _get_program():
    if "nc" not in _CACHED:
        _CACHED["nc"] = build_program()
    return _CACHED["nc"]


def kernel(**inputs):
    x = np.asarray(inputs["L0_attn_x"], dtype=np.float32)[0]
    xT = np.ascontiguousarray(x.T)
    x_live = np.ascontiguousarray(x[:LIVE])

    base = {
        "x_liv": x_live,
        "mapping": np.ascontiguousarray(np.asarray(inputs["L0_attn_mapping"], dtype=np.int32)),
        "Wqkv": np.ascontiguousarray(np.asarray(inputs["L0_attn_Wqkv"], dtype=np.float32)),
        "bqkv": np.ascontiguousarray(np.asarray(inputs["L0_attn_bqkv"], dtype=np.float32)),
        "Wproj": np.ascontiguousarray(np.asarray(inputs["L0_attn_Wproj"], dtype=np.float32)),
        "bproj": np.ascontiguousarray(np.asarray(inputs["L0_attn_bproj"], dtype=np.float32)),
        "gate_w": np.ascontiguousarray(np.asarray(inputs["L0_ffn_gate_w"], dtype=np.float32)),
        "w_gate": np.ascontiguousarray(np.asarray(inputs["L0_ffn_w_gate"], dtype=np.float32)),
        "w_up": np.ascontiguousarray(np.asarray(inputs["L0_ffn_w_up"], dtype=np.float32)),
        "w_down": np.ascontiguousarray(np.asarray(inputs["L0_ffn_w_down"], dtype=np.float32)),
        "ws_gate": np.ascontiguousarray(np.asarray(inputs["L0_ffn_ws_gate"], dtype=np.float32)),
        "ws_up": np.ascontiguousarray(np.asarray(inputs["L0_ffn_ws_up"], dtype=np.float32)),
        "ws_down": np.ascontiguousarray(np.asarray(inputs["L0_ffn_ws_down"], dtype=np.float32)),
    }

    in_maps = []
    p = np.arange(128)[:, None]
    g = np.arange(16)[None, :]
    for c in range(NCORES):
        qp = np.minimum(c * TOK + 128 * g + p, LIVE - 1).astype(np.int32)
        m = dict(base)
        m["x_own"] = np.ascontiguousarray(x[c * TOK:(c + 1) * TOK])
        m["xT_own"] = np.ascontiguousarray(xT[:, c * TOK:(c + 1) * TOK])
        m["qpos"] = np.ascontiguousarray(qp)
        in_maps.append(m)

    nc = _get_program()
    res = run_bass_kernel_spmd(nc, in_maps, core_ids=list(range(NCORES)))
    out = np.empty((M, DM), dtype=np.float32)
    for c in range(NCORES):
        out[c * TOK:(c + 1) * TOK] = res.results[c]["outT"].T
    return out.reshape(1, M, DM)


# version 03
# speedup vs baseline: 1.3591x; 1.0360x over previous
import numpy as np
from contextlib import ExitStack

import concourse.bass as bass
import concourse.mybir as mybir
from concourse.bass_utils import run_bass_kernel_spmd
from concourse.bass import IndirectOffsetOnAxis

F32 = mybir.dt.float32
BF16 = mybir.dt.bfloat16
I32 = mybir.dt.int32
AF = mybir.ActivationFunctionType
ALU = mybir.AluOpType
AX = mybir.AxisListType

NCORES = 8
M = 16384
DM = 1024
LIVE = 4096
TOK = 2048
NH = 16
NKV = 1024
NSEG = 16
VSEGW = 16 * 65
DFF = 2048
NEXP = 8
NT = 4
TW = 512
NBLK = 8


class Sem:
    def __init__(self, es, nc, name):
        self.h = es.enter_context(nc.semaphore(name))
        self.n = 0

    def inc(self, instr, k=1):
        instr.then_inc(self.h, k)
        self.n += k
        return instr


def _fmload(eng, dst, src, nch):
    return eng.dma_start(out=dst[:].rearrange("p (c o) -> p c o", c=nch),
                         in_=src.rearrange("(c p) o -> p c o", p=128))


def build_program(phases=("A", "C", "B", "D", "E")):
    nc = bass.Bass("TRN2", target_bir_lowering=False, enable_asserts=False)

    x_liv = nc.dram_tensor("x_liv", [LIVE, DM], F32, kind="ExternalInput").ap()
    x_own = nc.dram_tensor("x_own", [TOK, DM], F32, kind="ExternalInput").ap()
    xT_own = nc.dram_tensor("xT_own", [DM, TOK], F32, kind="ExternalInput").ap()
    mapping = nc.dram_tensor("mapping", [LIVE], I32, kind="ExternalInput").ap()
    qpos = nc.dram_tensor("qpos", [128, 16], I32, kind="ExternalInput").ap()
    Wqkv = nc.dram_tensor("Wqkv", [DM, 3 * DM], F32, kind="ExternalInput").ap()
    bqkv = nc.dram_tensor("bqkv", [3 * DM], F32, kind="ExternalInput").ap()
    Wproj = nc.dram_tensor("Wproj", [DM, DM], F32, kind="ExternalInput").ap()
    bproj = nc.dram_tensor("bproj", [DM], F32, kind="ExternalInput").ap()
    gate_w = nc.dram_tensor("gate_w", [DM, NEXP], F32, kind="ExternalInput").ap()
    w_gate = nc.dram_tensor("w_gate", [NEXP, DM, DFF], F32, kind="ExternalInput").ap()
    w_up = nc.dram_tensor("w_up", [NEXP, DM, DFF], F32, kind="ExternalInput").ap()
    w_down = nc.dram_tensor("w_down", [NEXP, DFF, DM], F32, kind="ExternalInput").ap()
    ws_gate = nc.dram_tensor("ws_gate", [DM, DFF], F32, kind="ExternalInput").ap()
    ws_up = nc.dram_tensor("ws_up", [DM, DFF], F32, kind="ExternalInput").ap()
    ws_down = nc.dram_tensor("ws_down", [DFF, DM], F32, kind="ExternalInput").ap()
    outT = nc.dram_tensor("outT", [DM, TOK], F32, kind="ExternalOutput").ap()
    K_dram = nc.dram_tensor("K_dram", [128 * NSEG, 512], BF16).ap()
    V_dram = nc.dram_tensor("V_dram", [64 * NSEG, VSEGW], BF16).ap()
    W_wb = nc.dram_tensor("W_wb", [NEXP, 128, TOK], BF16).ap()

    es = ExitStack()

    def sbuf(name, shape, dt):
        return es.enter_context(nc.sbuf_tensor(name, shape, dt))

    identf = sbuf("identf", [128, 128], F32)
    identb = sbuf("identb", [128, 128], BF16)
    ones64f = sbuf("ones64f", [1, 64], F32)
    ones128f = sbuf("ones128f", [1, 128], F32)
    ones128b = sbuf("ones128b", [1, 128], BF16)
    iota16f = sbuf("iota16f", [128, 1], F32)
    qpos_sb = sbuf("qpos_sb", [128, 16], I32)
    kvidx = sbuf("kvidx", [128, 8], I32)
    qmap = sbuf("qmap", [128, 16], I32)
    segraw = sbuf("segraw", [1, 32], I32)
    segf = sbuf("segf", [1, 32], F32)
    idxK = sbuf("idxK", [128, 32], I32)
    bqkv_sb = sbuf("bqkv_sb", [128, 24], F32)
    bqs = sbuf("bqs", [128, 8], F32)
    bproj_sb = sbuf("bproj_sb", [128, 8], F32)
    acc = sbuf("acc", [128, 8 * TOK], F32)
    NLN = 4
    bn6 = [sbuf(f"bn6_{i}", [128, 12], F32) for i in range(NLN)]
    bn2 = [sbuf(f"bn2_{i}", [128, 2], F32) for i in range(NLN)]
    rstd = [sbuf(f"rstd{i}", [128, 1], F32) for i in range(NLN)]
    stg = [sbuf(f"stg{i}", [128, DM], F32) for i in range(NLN)]
    stgn = [sbuf(f"stgn{i}", [128, DM], BF16) for i in range(NLN)]

    pb = [es.enter_context(nc.psum_tensor(f"pb{i}", [128, 512], F32)) for i in range(4)]
    st_TB = ExitStack()
    pb += [st_TB.enter_context(nc.psum_tensor(f"pb{i}", [128, 512], F32)) for i in range(4, 6)]
    tb = [st_TB.enter_context(nc.psum_tensor(f"tb{i}", [128, 1024], BF16)) for i in range(2)]

    d_in = Sem(es, nc, "d_in")
    d_iny = Sem(es, nc, "d_iny")
    d_idx = Sem(es, nc, "d_idx")
    d_g = [Sem(es, nc, f"d_g{i}") for i in range(4)]
    d_kst = [Sem(es, nc, f"d_kst{i}") for i in range(2)]
    d_own = [Sem(es, nc, f"d_own{i}") for i in range(4)]
    d_kv = Sem(es, nc, "d_kv")
    d_wtf = Sem(es, nc, "d_wtf")
    d_wb = [Sem(es, nc, f"d_wb{i}") for i in range(2)]
    d_ffn = [Sem(es, nc, f"d_ffn{i}") for i in range(2)]
    d_bv = Sem(es, nc, "d_bv")
    d_acc = Sem(es, nc, "d_acc")
    d_out = Sem(es, nc, "d_out")
    pe = Sem(es, nc, "pe")
    ve = Sem(es, nc, "ve")
    se = Sem(es, nc, "se")
    gp = Sem(es, nc, "gp")

    blk = es.enter_context(nc.Block())
    g_, s_, t_, v_, a_ = blk.gpsimd, blk.sync, blk.tensor, blk.vector, blk.scalar

    def f(g):
        gp.inc(g.iota(identf[:], pattern=[[1, 128]], base=0, channel_multiplier=-1,
               allow_small_or_imprecise_dtypes=True))
        gp.inc(g.iota(iota16f[:], pattern=[[0, 1]], base=0, channel_multiplier=16,
               allow_small_or_imprecise_dtypes=True))
        gp.inc(g.memset(ones64f[:], 1.0))
        gp.inc(g.memset(ones128f[:], 1.0))
        gp.inc(g.memset(ones128b[:], 1.0))
        d_in.inc(g.dma_start(
            out=kvidx[:],
            in_=mapping.rearrange("(gi p f) -> p gi f", gi=8, p=128)[:, :, 0:1].squeeze(2),
            allow_slow_non_contiguous=True), 16)
    g_(f)

    def f(s):
        d_iny.inc(s.dma_start(out=qpos_sb[:], in_=qpos), 16)
        d_iny.inc(s.dma_start(out=bqkv_sb[:], in_=bqkv.rearrange("(c p) -> p c", p=128),
                              allow_slow_non_contiguous=True), 16)
        d_iny.inc(s.dma_start(out=bproj_sb[:], in_=bproj.rearrange("(c p) -> p c", p=128),
                              allow_slow_non_contiguous=True), 16)
    s_(f)
    setup_iny = d_iny.n
    setup_in = d_in.n

    def f(v):
        v.wait_ge(gp.h, 1)
        ve.inc(v.tensor_scalar(identf[:], identf[:], 0.0, None, op0=ALU.is_equal))
        v.wait_ge(ve.h, ve.n)
        ve.inc(v.tensor_copy(identb[:], identf[:]))
        v.wait_ge(d_iny.h, setup_iny)
        ve.inc(v.tensor_scalar(bqs[:], bqkv_sb[:, 0:8], 0.125, None, op0=ALU.mult))
    v_(f)
    ident_ready = ve.n

    def f(g):
        g.wait_ge(d_iny.h, setup_iny)
        for gi in range(16):
            d_idx.inc(g.indirect_dma_start(
                out=qmap[:, gi:gi + 1], out_offset=None,
                in_=mapping[None, :],
                in_offset=IndirectOffsetOnAxis(qpos_sb[:, gi:gi + 1], 1)), 16)
        g.wait_ge(d_idx.h, d_idx.n)
        d_idx.inc(g.dma_start(out=segraw[:], in_=qmap[0:128:64, 0:16]), 16)
    g_(f)
    qmap_done = d_idx.n

    def f(v):
        v.wait_ge(d_idx.h, d_idx.n)
        ve.inc(v.tensor_scalar(segraw[:], segraw[:], 8, None, op0=ALU.arith_shift_right))
        v.wait_ge(ve.h, ve.n)
        ve.inc(v.tensor_copy(segf[:], segraw[:]))
    v_(f)
    segf_ready = ve.n

    def f(t):
        t.wait_ge(ve.h, segf_ready)
        t.wait_ge(gp.h, 4)
        pe.inc(t.matmul(pb[0][:, 0:32], ones128f[:], segf[:], start=True, stop=True))
    t_(f)
    def f(v):
        v.wait_ge(pe.h, pe.n)
        ve.inc(v.tensor_scalar(pb[0][:, 0:32], pb[0][:, 0:32], 1.0, iota16f[:],
                               op0=ALU.mult, op1=ALU.add))
        v.wait_ge(ve.h, ve.n)
        ve.inc(v.tensor_copy(idxK[:], pb[0][:, 0:32]))
    v_(f)
    idx_ready = ve.n

    all_sems = [d_in, d_iny, d_idx, d_g[0], d_g[1], d_g[2], d_g[3], d_kst[0], d_kst[1],
                d_own[0], d_own[1], d_own[2], d_own[3], d_kv, d_bv, d_wtf, d_wb[0], d_wb[1], d_ffn[0], d_ffn[1],
                d_acc, d_out, pe, ve, se, gp]

    def phase_barrier():
        vals = [(sm.h, sm.n) for sm in all_sems if sm.n > 0]
        for section in (g_, s_, t_, v_, a_):
            def f(e, vals=vals):
                for h, n in vals:
                    e.wait_ge(h, n)
            section(f)

    def blkcol(b):
        return 16 * (b & 1) + (b >> 1)

    ln_sqrt_se = [0] * 4
    ln_norm_ve = [0] * 4
    tb_free = [0, 0]

    def emit_ln_tr_batch(items):
        n = len(items)
        ve_c, se_c, pe_c = ve.n, se.n, pe.n
        p_rec, p_sqrt, p_norm = {}, {}, {}
        for idx, (k, gw, dfm, dcol, cstr) in enumerate(items):
            ve_c += 5; p_rec[idx] = ve_c
        for idx in range(n):
            se_c += 1; p_sqrt[idx] = se_c
        for idx in range(n):
            ve_c += 1; p_norm[idx] = ve_c
        p_tp, p_tc = {}, {}
        for idx in range(n):
            for ch in range(8):
                pe_c += 1; p_tp[(idx, ch)] = pe_c
                ve_c += 1; p_tc[(idx, ch)] = ve_c

        def f(v):
            for idx, (k, gw, dfm, dcol, cstr) in enumerate(items):
                v.wait_ge(*gw)
                v.wait_ge(se.h, ln_sqrt_se[k])
                ve.inc(v.bn_stats(bn6[k][:, 0:6], stg[k][:, 0:512]))
                ve.inc(v.bn_stats(bn6[k][:, 6:12], stg[k][:, 512:1024]))
                v.wait_ge(ve.h, ve.n)
                ve.inc(v.bn_aggr(bn2[k][:], bn6[k][:]))
                v.wait_ge(ve.h, ve.n)
                ve.inc(v.tensor_scalar(rstd[k][:], bn2[k][:, 1:2], 1e-5, None, op0=ALU.add))
                v.wait_ge(ve.h, ve.n)
                ve.inc(v.reciprocal(rstd[k][:], rstd[k][:]))
                assert ve.n == p_rec[idx]
        v_(f)
        def f(a):
            for idx, (k, gw, dfm, dcol, cstr) in enumerate(items):
                a.wait_ge(ve.h, p_rec[idx])
                se.inc(a.activation(rstd[k][:], rstd[k][:], AF.Sqrt, bias=0.0, scale=1.0))
                assert se.n == p_sqrt[idx]
        a_(f)
        def f(v):
            for idx, (k, gw, dfm, dcol, cstr) in enumerate(items):
                v.wait_ge(se.h, p_sqrt[idx])
                ve.inc(v.tensor_scalar(stgn[k][:], stg[k][:], bn2[k][:, 0:1], rstd[k][:],
                                       op0=ALU.subtract, op1=ALU.mult))
                assert ve.n == p_norm[idx]
        v_(f)
        def f(t):
            for idx, (k, gw, dfm, dcol, cstr) in enumerate(items):
                for ch in range(8):
                    gi2 = idx * 8 + ch
                    if ch == 0:
                        t.wait_ge(ve.h, p_norm[idx])
                    if gi2 >= 2:
                        t.wait_ge(ve.h, p_tc[((gi2 - 2) // 8, (gi2 - 2) % 8)])
                    else:
                        t.wait_ge(ve.h, max(tb_free[gi2 % 2], ident_ready))
                    pe.inc(t.transpose(tb[gi2 % 2][0:128, 0:128],
                                       stgn[k][:, ch * 128:ch * 128 + 128], identb[:]))
                    assert pe.n == p_tp[(idx, ch)]
        t_(f)
        def f(v):
            for idx, (k, gw, dfm, dcol, cstr) in enumerate(items):
                for ch in range(8):
                    gi2 = idx * 8 + ch
                    v.wait_ge(pe.h, p_tp[(idx, ch)])
                    ve.inc(v.tensor_copy(dfm[:, dcol + ch * cstr:dcol + ch * cstr + 128],
                                         tb[gi2 % 2][0:128, 0:128]))
                    assert ve.n == p_tc[(idx, ch)]
        v_(f)
        for idx, (k, gw, dfm, dcol, cstr) in enumerate(items):
            ln_sqrt_se[k] = p_sqrt[idx]
            ln_norm_ve[k] = p_norm[idx]
        tb_free[0] = p_tc[(n - 1, 6)]
        tb_free[1] = p_tc[(n - 1, 7)]
        return ve.n

    st_A = ExitStack()
    wk_sb = st_A.enter_context(nc.sbuf_tensor("wk_sb", [128, 8 * DM], BF16))
    wv_sb = st_A.enter_context(nc.sbuf_tensor("wv_sb", [128, 8 * DM], BF16))
    xn_kvT = st_A.enter_context(nc.sbuf_tensor("xn_kvT", [128, 8 * NKV], BF16))
    bv_b = st_A.enter_context(nc.sbuf_tensor("bv_b", [64, DM], F32))
    bv_row = st_A.enter_context(nc.sbuf_tensor("bv_row", [1, DM], F32))
    K_T2 = st_A.enter_context(nc.sbuf_tensor("K_T2", [128, NSEG * 512], BF16))
    V_seg = st_A.enter_context(nc.sbuf_tensor("V_seg", [64, NSEG * VSEGW], BF16))

    if "A" in phases:
        def f(g):
            d_in.inc(_fmload(g, wk_sb, Wqkv[:, DM:2 * DM], 8), 16)
            d_in.inc(_fmload(g, wv_sb, Wqkv[:, 2 * DM:3 * DM], 8), 16)
            vs4 = V_seg[:].rearrange("p (s h d) -> p s h d", s=NSEG, h=16)
            gp.inc(g.memset(vs4[:, :, :, 64:65], 1.0))
        g_(f)
        wkwv_loaded = d_in.n
        vones_gp = gp.n

        def f(s2):
            d_bv.inc(s2.dma_start(out=bv_row[:], in_=bqkv[None, 2 * DM:3 * DM]), 16)
        s_(f)

        def f(t):
            t.wait_ge(d_bv.h, d_bv.n)
            t.wait_ge(gp.h, 4)
            pe.inc(t.matmul(pb[1][0:64, :], ones64f[:], bv_row[0:1, 0:512],
                            start=True, stop=True))
            pe.inc(t.matmul(pb[2][0:64, :], ones64f[:], bv_row[0:1, 512:1024],
                            start=True, stop=True))
        t_(f)
        def f(v):
            v.wait_ge(pe.h, pe.n)
            ve.inc(v.tensor_copy(bv_b[:, 0:512], pb[1][0:64, :]))
            ve.inc(v.tensor_copy(bv_b[:, 512:1024], pb[2][0:64, :]))
        v_(f)
        bvb_ready = ve.n

        for bat in range(2):
            items = []
            for sub in range(4):
                gi = bat * 4 + sub
                k = gi % NLN
                def f(g, gi=gi, k=k):
                    g.wait_ge(d_in.h, d_in.n)
                    g.wait_ge(ve.h, ln_norm_ve[k])
                    d_g[k].inc(g.indirect_dma_start(
                        out=stg[k][:], out_offset=None,
                        in_=x_liv,
                        in_offset=IndirectOffsetOnAxis(kvidx[:, gi:gi + 1], 0)), 16)
                g_(f)
                items.append((k, (d_g[k].h, d_g[k].n), xn_kvT, gi * 128, NKV))
            emit_ln_tr_batch(items)
        xnkvT_ready = ve.n

        k_evacs = []
        for kc in range(8):
            for half in range(2):
                i = kc * 2 + half
                bank = pb[i % 4]
                def f(t, kc=kc, half=half, bank=bank, i=i):
                    t.wait_ge(d_in.h, wkwv_loaded)
                    t.wait_ge(ve.h, max(xnkvT_ready, idx_ready))
                    if i >= 4:
                        t.wait_ge(ve.h, k_evacs[i - 4])
                    for dc in range(8):
                        ins = t.matmul(bank[:, :],
                                       wk_sb[:, dc * DM + kc * 128:dc * DM + kc * 128 + 128],
                                       xn_kvT[:, dc * NKV + half * 512:dc * NKV + half * 512 + 512],
                                       start=(dc == 0), stop=(dc == 7))
                    pe.inc(ins)
                t_(f)
                def f(v, kc=kc, half=half, bank=bank):
                    v.wait_ge(pe.h, pe.n)
                    dst = K_T2[:].rearrange("p (sg r) -> p sg r", sg=NSEG)[
                        :, half * 8:half * 8 + 8, kc * 64:kc * 64 + 64]
                    srcp = bank[:, :].rearrange("p (sg s) -> p sg s", sg=8)
                    ve.inc(v.tensor_scalar(dst, srcp, bqkv_sb[:, 8 + kc:9 + kc], None,
                                           op0=ALU.add))
                v_(f)
                k_evacs.append(ve.n)
        KT_done = ve.n

        v_evacs = []
        for sg in range(NSEG):
            for vh in range(2):
                i = sg * 2 + vh
                bank = pb[4 + (i % 2)]
                def f(t, sg=sg, vh=vh, bank=bank, i=i):
                    t.wait_ge(ve.h, max(xnkvT_ready, bvb_ready))
                    if i >= 2:
                        t.wait_ge(ve.h, v_evacs[i - 2])
                    for dc in range(8):
                        ins = t.matmul(bank[0:64, :],
                                       xn_kvT[:, dc * NKV + sg * 64:dc * NKV + sg * 64 + 64],
                                       wv_sb[:, dc * DM + vh * 512:dc * DM + vh * 512 + 512],
                                       start=(dc == 0), stop=(dc == 7))
                    pe.inc(ins)
                t_(f)
                def f(v, sg=sg, vh=vh, bank=bank):
                    v.wait_ge(pe.h, pe.n)
                    base = sg * VSEGW + vh * 520
                    dst = V_seg[0:64, base:base + 520].rearrange(
                        "p (h d) -> p h d", h=8)[:, :, 0:64]
                    srcv = bank[0:64, :].rearrange("p (h d) -> p h d", h=8)
                    ve.inc(v.scalar_tensor_tensor(
                        dst, srcv, 0.0,
                        bv_b[:, vh * 512:vh * 512 + 512].rearrange("p (h d) -> p h d", h=8),
                        op0=ALU.bypass, op1=ALU.add))
                v_(f)
                v_evacs.append(ve.n)
        Vseg_done = ve.n

        def f(g):
            g.wait_ge(ve.h, KT_done)
            d_kv.inc(g.dma_start(out=K_dram.rearrange("(p e) r -> p e r", p=128),
                                 in_=K_T2[:].rearrange("p (e r) -> p e r", e=NSEG)), 16)
            g.wait_ge(ve.h, Vseg_done)
            g.wait_ge(gp.h, vones_gp)
            d_kv.inc(g.dma_start(out=V_dram.rearrange("(p e) r -> p e r", p=64),
                                 in_=V_seg[:].rearrange("p (e r) -> p e r", e=NSEG)), 16)
        g_(f)
        kv_dram_done = d_kv.n
    else:
        kv_dram_done = d_kv.n
        def f(v):
            ve.inc(v.memset(acc[:], 0.0))
        v_(f)

    st_A.close()
    phase_barrier()

    st_C = ExitStack()
    wq_sb = st_C.enter_context(nc.sbuf_tensor("wq_sb", [128, 8 * DM], BF16))
    wp_sb = st_C.enter_context(nc.sbuf_tensor("wp_sb", [128, 8 * DM], BF16))
    xnqT = st_C.enter_context(nc.sbuf_tensor("xnqT", [128, 8 * TW], BF16))
    QT = st_C.enter_context(nc.sbuf_tensor("QT", [128, 8 * TW], BF16))
    attn_tm = st_C.enter_context(nc.sbuf_tensor("attn_tm", [64, 2 * DM], BF16))
    attn_fm = st_C.enter_context(nc.sbuf_tensor("attn_fm", [128, 8 * TW], BF16))
    Kst = [st_C.enter_context(nc.sbuf_tensor(f"Kst{i}", [128, NBLK * 512], BF16)) for i in range(2)]
    Vst = [st_C.enter_context(nc.sbuf_tensor(f"Vst{i}", [64, NBLK * VSEGW], BF16)) for i in range(2)]
    P_sb = [st_C.enter_context(nc.sbuf_tensor(f"P_sb{i}", [64, 64], BF16)) for i in range(3)]
    negC = [st_C.enter_context(nc.sbuf_tensor(f"negC{i}", [64, 1], F32)) for i in range(2)]
    aU = [st_C.enter_context(nc.sbuf_tensor(f"aU{i}", [64, VSEGW], BF16)) for i in range(2)]
    rec16 = st_C.enter_context(nc.sbuf_tensor("rec16", [64, 16], F32))

    if "C" in phases:
        def f(g):
            d_in.inc(_fmload(g, wq_sb, Wqkv[:, 0:DM], 8), 16)
            d_in.inc(_fmload(g, wp_sb, Wproj, 8), 16)
        g_(f)
        wqwp_loaded = d_in.n

        stg_free = [(se.h, se.n), (se.h, se.n)]
        prev_proj_ve = ve.n
        prev_pipe_pe = 0
        for j in range(NT):
            kb = j % 2
            def f(g, j=j, kb=kb):
                g.wait_ge(d_kv.h, kv_dram_done)
                g.wait_ge(ve.h, idx_ready)
                g.wait_ge(pe.h, prev_pipe_pe)
                for bb in range(NBLK):
                    col = blkcol(8 * j + bb)
                    d_kst[kb].inc(g.indirect_dma_start(
                        out=Kst[kb][:, bb * 512:bb * 512 + 512], out_offset=None,
                        in_=K_dram,
                        in_offset=IndirectOffsetOnAxis(idxK[:, col:col + 1], 0)), 16)
                    d_kst[kb].inc(g.indirect_dma_start(
                        out=Vst[kb][0:64, bb * VSEGW:bb * VSEGW + VSEGW], out_offset=None,
                        in_=V_dram,
                        in_offset=IndirectOffsetOnAxis(idxK[0:64, col:col + 1], 0)), 16)
            g_(f)
            kv_st_done = d_kst[kb].n

            items = []
            for gsub in range(4):
                k = gsub % NLN
                gi = 4 * j + gsub
                def f(g, gi=gi, k=k):
                    g.wait_ge(d_idx.h, qmap_done)
                    g.wait_ge(ve.h, ln_norm_ve[k])
                    d_g[k].inc(g.indirect_dma_start(
                        out=stg[k][:], out_offset=None,
                        in_=x_liv,
                        in_offset=IndirectOffsetOnAxis(qmap[:, gi:gi + 1], 0)), 16)
                g_(f)
                items.append((k, (d_g[k].h, d_g[k].n), xnqT, gsub * 128, TW))
            emit_ln_tr_batch(items)
            xnqT_ready = ve.n

            q_evacs = []
            for qc in range(8):
                bank = pb[qc % 2]
                def f(t, qc=qc, bank=bank):
                    t.wait_ge(ve.h, xnqT_ready)
                    t.wait_ge(d_in.h, wqwp_loaded)
                    if len(q_evacs) >= 2:
                        t.wait_ge(ve.h, q_evacs[qc - 2])
                    for dc in range(8):
                        ins = t.matmul(bank[:, :],
                                       wq_sb[:, dc * DM + qc * 128:dc * DM + qc * 128 + 128],
                                       xnqT[:, dc * TW:dc * TW + 512],
                                       start=(dc == 0), stop=(dc == 7))
                    pe.inc(ins)
                t_(f)
                def f(v, qc=qc, bank=bank):
                    v.wait_ge(pe.h, pe.n)
                    ve.inc(v.tensor_scalar(QT[:, qc * TW:qc * TW + 512], bank[:, :],
                                           bqkv_sb[:, qc:qc + 1], 0.125,
                                           op0=ALU.add, op1=ALU.mult))
                v_(f)
                q_evacs.append(ve.n)
            QT_ready = ve.n

            NIT = NBLK * NH
            c_sc, c_rd, c_ex, c_av, c_ea = ({} for _ in range(5))
            c_no = {}
            c_tp = {}
            c_tc = {}
            pe_c, ve_c, se_c = pe.n, ve.n, se.n
            NSLOT = NIT + 6 + 16
            for slot in range(NSLOT):
                if slot < NIT:
                    pe_c += 1; c_sc[slot] = pe_c
                if 0 <= slot - 1 < NIT:
                    ve_c += 1; c_rd[slot - 1] = ve_c
                if 0 <= slot - 2 < NIT:
                    se_c += 1; c_ex[slot - 2] = se_c
                if 0 <= slot - 3 < NIT:
                    pe_c += 1; c_av[slot - 3] = pe_c
                if 0 <= slot - 4 < NIT:
                    i4 = slot - 4
                    ve_c += 1; c_ea[i4] = ve_c
                    if i4 % NH == NH - 1:
                        ve_c += 3; c_no[i4 // NH] = ve_c
                if slot >= 5 and (slot - 5) % NH == NH - 1:
                    bb2 = (slot - 5) // NH
                    if 0 <= bb2 < NBLK:
                        for ch in range(8):
                            pe_c += 1; c_tp[(bb2, ch)] = pe_c
                            ve_c += 1; c_tc[(bb2, ch)] = ve_c
            prev_se = se.n

            def f(t, j=j, kb=kb, QT_ready=QT_ready, kv_st_done=kv_st_done,
                  ppv=prev_proj_ve, prev_se=prev_se, tbf=tuple(tb_free)):
                for slot in range(NSLOT):
                    i = slot
                    if i < NIT:
                        bb, h = divmod(i, NH)
                        if i == 0:
                            t.wait_ge(ve.h, QT_ready)
                            t.wait_ge(d_kst[kb].h, kv_st_done)
                            t.wait_ge(ve.h, ppv)
                            t.wait_ge(se.h, prev_se)
                        if i >= 2:
                            t.wait_ge(ve.h, c_rd[i - 2])
                            t.wait_ge(se.h, c_ex[i - 2])
                        lhs = Kst[kb][(h & 1) * 64:(h & 1) * 64 + 64,
                                      bb * 512 + (h >> 1) * 64:bb * 512 + (h >> 1) * 64 + 64]
                        rhs = QT[(h & 1) * 64:(h & 1) * 64 + 64,
                                 (h >> 1) * TW + 64 * bb:(h >> 1) * TW + 64 * bb + 64]
                        pe.inc(t.matmul(pb[2 + (i % 2)][0:64, 0:64], lhs, rhs,
                                        start=True, stop=True))
                        assert pe.n == c_sc[i]
                    i = slot - 3
                    if 0 <= i < NIT:
                        bb, h = divmod(i, NH)
                        t.wait_ge(se.h, c_ex[i])
                        if i >= 2:
                            t.wait_ge(ve.h, c_ea[i - 2])
                        rhs = Vst[kb][0:64, bb * VSEGW + 65 * h:bb * VSEGW + 65 * h + 65]
                        pe.inc(t.matmul(pb[(i % 2)][0:64, 0:65], P_sb[i % 3][:], rhs,
                                        start=True, stop=True))
                        assert pe.n == c_av[i]
                    if slot >= 5 and (slot - 5) % NH == NH - 1:
                        bb2 = (slot - 5) // NH
                        if 0 <= bb2 < NBLK:
                            for ch in range(8):
                                if ch == 0:
                                    t.wait_ge(ve.h, c_no[bb2])
                                if ch >= 2:
                                    t.wait_ge(ve.h, c_tc[(bb2, ch - 2)])
                                elif bb2 >= 1:
                                    t.wait_ge(ve.h, c_tc[(bb2 - 1, 6 + ch)])
                                elif ch < 2:
                                    t.wait_ge(ve.h, tbf[ch])
                                pe.inc(t.transpose(
                                    tb[ch % 2][0:128, 0:64],
                                    attn_tm[0:64, (bb2 % 2) * DM + ch * 128:
                                            (bb2 % 2) * DM + ch * 128 + 128],
                                    identb[0:64, 0:64]))
                                assert pe.n == c_tp[(bb2, ch)]
            t_(f)
            prev_pipe_pe = pe.n

            def f(v, kb=kb):
                for slot in range(NSLOT):
                    i = slot - 1
                    if 0 <= i < NIT:
                        v.wait_ge(pe.h, c_sc[i])
                        if i >= 2:
                            v.wait_ge(se.h, c_ex[i - 2])
                        ve.inc(v.reduce_max(negC[i % 2][:], pb[2 + (i % 2)][0:64, 0:64],
                                            axis=AX.X, negate=True))
                        assert ve.n == c_rd[i]
                    i = slot - 4
                    if 0 <= i < NIT:
                        bb, h = divmod(i, NH)
                        v.wait_ge(pe.h, c_av[i])
                        ve.inc(v.tensor_copy(aU[bb % 2][:, 65 * h:65 * h + 65],
                                             pb[(i % 2)][0:64, 0:65]))
                        assert ve.n == c_ea[i]
                        if h == NH - 1:
                            if bb >= 2:
                                v.wait_ge(pe.h, c_tp[(bb - 2, 7)])
                            v.wait_ge(ve.h, ve.n)
                            a3 = aU[bb % 2][:].rearrange("p (hh c) -> p hh c", hh=16)
                            ve.inc(v.tensor_scalar(rec16[:], a3[:, :, 64:65].squeeze(2),
                                                   1e-6, None, op0=ALU.add))
                            v.wait_ge(ve.h, ve.n)
                            ve.inc(v.reciprocal(rec16[:], rec16[:]))
                            v.wait_ge(ve.h, ve.n)
                            dstb = attn_tm[0:64, (bb % 2) * DM:(bb % 2) * DM + DM].rearrange(
                                "p (hh d) -> p hh d", hh=16)
                            ve.inc(v.tensor_tensor(
                                dstb, a3[:, :, 0:64],
                                rec16[:].unsqueeze(2).broadcast_to([64, 16, 64]),
                                op=ALU.mult))
                            assert ve.n == c_no[bb]
                    if slot >= 5 and (slot - 5) % NH == NH - 1:
                        bb2 = (slot - 5) // NH
                        if 0 <= bb2 < NBLK:
                            for ch in range(8):
                                v.wait_ge(pe.h, c_tp[(bb2, ch)])
                                ve.inc(v.tensor_copy(
                                    attn_fm[:, ch * TW + bb2 * 64:ch * TW + bb2 * 64 + 64],
                                    tb[ch % 2][0:128, 0:64]))
                                assert ve.n == c_tc[(bb2, ch)]
            v_(f)
            tb_free[0] = c_tc[(NBLK - 1, 6)]
            tb_free[1] = c_tc[(NBLK - 1, 7)]
            attn_fm_ready = ve.n

            def f(a):
                for slot in range(NSLOT):
                    i = slot - 2
                    if 0 <= i < NIT:
                        a.wait_ge(ve.h, c_rd[i])
                        if i >= 3:
                            a.wait_ge(pe.h, c_av[i - 3])
                        se.inc(a.activation(P_sb[i % 3][:], pb[2 + (i % 2)][0:64, 0:64],
                                            AF.Exp, bias=negC[i % 2][:], scale=1.0))
                        assert se.n == c_ex[i]
            a_(f)

            p_evacs = []
            for oc in range(8):
                bank = pb[2 + (oc % 2)]
                def f(t, oc=oc, bank=bank):
                    t.wait_ge(ve.h, attn_fm_ready)
                    if len(p_evacs) >= 2:
                        t.wait_ge(ve.h, p_evacs[oc - 2])
                    for fc in range(8):
                        ins = t.matmul(bank[:, :],
                                       wp_sb[:, fc * DM + oc * 128:fc * DM + oc * 128 + 128],
                                       attn_fm[:, fc * TW:fc * TW + 512],
                                       start=(fc == 0), stop=(fc == 7))
                    pe.inc(ins)
                t_(f)
                def f(v, oc=oc, bank=bank, j=j):
                    v.wait_ge(pe.h, pe.n)
                    ve.inc(v.tensor_scalar(acc[:, oc * TOK + j * TW:oc * TOK + j * TW + 512],
                                           bank[:, :], bproj_sb[:, oc:oc + 1], None,
                                           op0=ALU.add))
                v_(f)
                p_evacs.append(ve.n)
            prev_proj_ve = ve.n
        attn_done_ve = ve.n
    else:
        attn_done_ve = ve.n

    st_C.close()
    phase_barrier()

    st_BD = ExitStack()
    xn_ownT = st_BD.enter_context(nc.sbuf_tensor("xn_ownT", [128, 8 * TOK], BF16))

    if "B" in phases or "D" in phases:
        for bat in range(4):
            items = []
            for sub in range(4):
                tt = bat * 4 + sub
                k = tt % NLN
                def f(s, tt=tt, k=k):
                    s.wait_ge(ve.h, ln_norm_ve[k])
                    d_own[k].inc(s.dma_start(out=stg[k][:],
                                             in_=x_own[tt * 128:tt * 128 + 128, :]), 16)
                s_(f)
                items.append((k, (d_own[k].h, d_own[k].n), xn_ownT, tt * 128, TOK))
            emit_ln_tr_batch(items)
    xnownT_ready = ve.n

    st_D = ExitStack()
    gate_sb = st_D.enter_context(nc.sbuf_tensor("gate_sb", [128, 8 * NEXP], BF16))
    w_tm = st_D.enter_context(nc.sbuf_tensor("w_tm", [128, 16 * NEXP], F32))
    sgs = st_D.enter_context(nc.sbuf_tensor("sgs", [128, NEXP], F32))
    sgm = st_D.enter_context(nc.sbuf_tensor("sgm", [128, 1], F32))
    sgm2 = st_D.enter_context(nc.sbuf_tensor("sgm2", [128, 1], F32))
    sgtmp = st_D.enter_context(nc.sbuf_tensor("sgtmp", [128, NEXP], F32))
    wbcast = [st_D.enter_context(nc.sbuf_tensor(f"wbcast{i}", [128, TOK], BF16)) for i in range(2)]
    h_q = [st_D.enter_context(nc.sbuf_tensor(f"h_q{i}", [128, 4 * 512], BF16)) for i in range(2)]
    h32 = st_D.enter_context(nc.sbuf_tensor("h32", [128, 512], F32))
    hsil = [st_D.enter_context(nc.sbuf_tensor(f"hsil{i}", [128, 512], BF16)) for i in range(2)]
    wgb = [st_D.enter_context(nc.sbuf_tensor(f"wgb{i}", [128, 8 * 512], BF16)) for i in range(2)]
    wub = [st_D.enter_context(nc.sbuf_tensor(f"wub{i}", [128, 8 * 512], BF16)) for i in range(2)]
    wdb = [st_D.enter_context(nc.sbuf_tensor(f"wdb{i}", [128, 4 * DM], BF16)) for i in range(2)]

    if "D" in phases:
        def f(g):
            d_in.inc(_fmload(g, gate_sb, gate_w, 8), 16)
        g_(f)
        gate_loaded = d_in.n

        zg_done = []
        for tt in range(16):
            bank = pb[tt % 2]
            def f(t, tt=tt, bank=bank):
                t.wait_ge(d_in.h, gate_loaded)
                t.wait_ge(ve.h, xnownT_ready)
                if len(zg_done) >= 2:
                    t.wait_ge(ve.h, zg_done[tt - 2])
                for dc in range(8):
                    ins = t.matmul(bank[:, 0:NEXP],
                                   xn_ownT[:, dc * TOK + tt * 128:dc * TOK + tt * 128 + 128],
                                   gate_sb[:, dc * NEXP:dc * NEXP + NEXP],
                                   start=(dc == 0), stop=(dc == 7))
                pe.inc(ins)
            t_(f)
            def f(v, tt=tt, bank=bank):
                v.wait_ge(pe.h, pe.n)
                v.wait_ge(ve.h, ve.n)
                v.wait_ge(se.h, se.n)
                ve.inc(v.reduce_max(sgm[:], bank[:, 0:NEXP], axis=AX.X, negate=True))
            v_(f)
            def f(a, bank=bank):
                a.wait_ge(ve.h, ve.n)
                se.inc(a.activation(sgs[:], bank[:, 0:NEXP], AF.Exp, bias=sgm[:], scale=1.0))
            a_(f)
            def f(v, tt=tt):
                v.wait_ge(se.h, se.n)
                ve.inc(v.reduce_sum(sgm[:], sgs[:], axis=AX.X))
                v.wait_ge(ve.h, ve.n)
                ve.inc(v.reciprocal(sgm[:], sgm[:]))
                v.wait_ge(ve.h, ve.n)
                ve.inc(v.tensor_scalar(sgs[:], sgs[:], sgm[:], None, op0=ALU.mult))
                v.wait_ge(ve.h, ve.n)
                ve.inc(v.reduce_max(sgm[:], sgs[:], axis=AX.X))
                v.wait_ge(ve.h, ve.n)
                ve.inc(v.tensor_scalar(sgtmp[:], sgs[:], sgm[:], None, op0=ALU.is_equal))
                v.wait_ge(ve.h, ve.n)
                ve.inc(v.scalar_tensor_tensor(sgtmp[:], sgtmp[:], -1e30, sgs[:],
                                              op0=ALU.mult, op1=ALU.add))
                v.wait_ge(ve.h, ve.n)
                ve.inc(v.reduce_max(sgm2[:], sgtmp[:], axis=AX.X))
                v.wait_ge(ve.h, ve.n)
                ve.inc(v.tensor_scalar(sgtmp[:], sgs[:], sgm2[:], None, op0=ALU.is_ge))
                v.wait_ge(ve.h, ve.n)
                ve.inc(v.tensor_mul(sgtmp[:], sgtmp[:], sgs[:]))
                ve.inc(v.tensor_add(sgm[:], sgm[:], sgm2[:]))
                v.wait_ge(ve.h, ve.n)
                ve.inc(v.tensor_scalar(sgm[:], sgm[:], 1e-20, None, op0=ALU.add))
                v.wait_ge(ve.h, ve.n)
                ve.inc(v.reciprocal(sgm[:], sgm[:]))
                v.wait_ge(ve.h, ve.n)
                ve.inc(v.tensor_scalar(w_tm[:, tt * NEXP:tt * NEXP + NEXP], sgtmp[:],
                                       sgm[:], None, op0=ALU.mult))
            v_(f)
            zg_done.append(ve.n)

        st_WB = ExitStack()
        wT8 = st_WB.enter_context(nc.sbuf_tensor("wT8", [NEXP, TOK], BF16))
        wTflat = st_WB.enter_context(nc.sbuf_tensor("wTflat", [1, TOK], BF16))
        wbstage = st_WB.enter_context(nc.sbuf_tensor("wbstage", [128, TOK], BF16))

        for tt in range(16):
            def f(t, tt=tt):
                t.wait_ge(ve.h, ve.n)
                pe.inc(t.transpose(pb[2][0:NEXP, 0:128],
                                   w_tm[:, tt * NEXP:tt * NEXP + NEXP], identf[:]))
            t_(f)
            def f(v, tt=tt):
                v.wait_ge(pe.h, pe.n)
                ve.inc(v.tensor_copy(wT8[:, tt * 128:tt * 128 + 128], pb[2][0:NEXP, 0:128]))
            v_(f)
        wT8_ready = ve.n

        for e in range(NEXP):
            def f(g, e=e):
                g.wait_ge(ve.h, wT8_ready)
                g.wait_ge(pe.h, pe.n)
                d_wtf.inc(g.dma_start(out=wTflat[:], in_=wT8[e:e + 1, :]), 16)
            g_(f)
            wtf_done = d_wtf.n
            for tq in range(4):
                bank = pb[4 + tq % 2]
                def f(t, tq=tq, bank=bank, wtf_done=wtf_done):
                    t.wait_ge(d_wtf.h, wtf_done)
                    t.wait_ge(gp.h, 5)
                    t.wait_ge(ve.h, ve.n)
                    t.wait_ge(d_kv.h, d_kv.n)
                    pe.inc(t.matmul(bank[:, :], ones128b[:],
                                    wTflat[0:1, tq * 512:tq * 512 + 512],
                                    start=True, stop=True))
                t_(f)
                def f(v, tq=tq, bank=bank):
                    v.wait_ge(pe.h, pe.n)
                    ve.inc(v.tensor_copy(wbstage[:, tq * 512:tq * 512 + 512], bank[:, :]))
                v_(f)
            def f(g, e=e):
                g.wait_ge(ve.h, ve.n)
                d_kv.inc(g.dma_start(out=W_wb[e], in_=wbstage[:]), 16)
            g_(f)
        wb_built = d_kv.n

        st_WB.close()
        st_TB.close()
        st_Dp = ExitStack()
        pdown = st_Dp.enter_context(nc.psum_tensor("pdown", [128, 4, 512], F32))

        def wload(g, e, q):
            kbuf = (e * 4 + q) % 2
            if e < NEXP:
                srcs = (w_gate[e, :, 512 * q:512 * q + 512],
                        w_up[e, :, 512 * q:512 * q + 512],
                        w_down[e, 512 * q:512 * q + 512, :])
            else:
                srcs = (ws_gate[:, 512 * q:512 * q + 512],
                        ws_up[:, 512 * q:512 * q + 512],
                        ws_down[512 * q:512 * q + 512, :])
            d_ffn[kbuf].inc(_fmload(g, wgb[kbuf], srcs[0], 8), 16)
            d_ffn[kbuf].inc(_fmload(g, wub[kbuf], srcs[1], 8), 16)
            d_ffn[kbuf].inc(_fmload(g, wdb[kbuf], srcs[2], 4), 16)

        eq_list = [(e, q) for e in range(NEXP + 1) for q in range(4)]
        eq_loaded = {}
        eq_consumed = {}

        def f(g):
            for i in range(2):
                wload(g, *eq_list[i])
                eq_loaded[i] = d_ffn[i % 2].n
        g_(f)

        def f(s2):
            s2.wait_ge(d_kv.h, wb_built)
            d_wb[0].inc(s2.dma_start(out=wbcast[0][:], in_=W_wb[0]), 16)
        s_(f)
        wb_loaded = {0: d_wb[0].n}
        exp_mult_ve = {}
        c_gate, c_up, c_silu, c_m1, c_m2, c_down, c_add = ({} for _ in range(7))
        prev_add_ve = ve.n

        for i, (e, q) in enumerate(eq_list):
            if q == 0 and e + 1 < NEXP:
                def f(s2, e=e):
                    s2.wait_ge(d_kv.h, wb_built)
                    if e >= 1:
                        s2.wait_ge(ve.h, exp_mult_ve[e - 1])
                    d_wb[(e + 1) % 2].inc(
                        s2.dma_start(out=wbcast[(e + 1) % 2][:], in_=W_wb[e + 1]), 16)
                s_(f)
                wb_loaded[e + 1] = d_wb[(e + 1) % 2].n

            pe_c, se_c, ve_c = pe.n, se.n, ve.n
            for tq in range(4):
                for sl in range(4):
                    pe_c += 1; c_gate[(i, tq, sl)] = pe_c
                    pe_c += 1; c_up[(i, tq, sl)] = pe_c
                    se_c += 1; c_silu[(i, tq, sl)] = se_c
                    ve_c += 1; c_m1[(i, tq, sl)] = ve_c
                    ve_c += 1; c_m2[(i, tq, sl)] = ve_c
                for hh in range(2):
                    pe_c += 1; c_down[(i, tq, hh)] = pe_c
                    ve_c += 1; c_add[(i, tq, hh)] = ve_c

            def f(t, i=i, e=e, pav=prev_add_ve):
                t.wait_ge(d_ffn[i % 2].h, eq_loaded[i])
                if i == 0:
                    t.wait_ge(ve.h, xnownT_ready)
                for tq in range(4):
                    for sl in range(4):
                        kb2 = sl % 2
                        if (i, tq, sl - 2) in c_silu if sl >= 2 else False:
                            pass
                        if sl >= 2:
                            t.wait_ge(se.h, c_silu[(i, tq, sl - 2)])
                        elif tq >= 1:
                            t.wait_ge(se.h, c_silu[(i, tq - 1, sl + 2)])
                        elif i >= 1:
                            t.wait_ge(se.h, c_silu[(i - 1, 3, sl + 2)])
                        buf = wgb[i % 2]
                        for dc in range(8):
                            ins = t.matmul(pb[kb2][:, :],
                                           buf[:, dc * 512 + sl * 128:dc * 512 + sl * 128 + 128],
                                           xn_ownT[:, dc * TOK + tq * 512:dc * TOK + tq * 512 + 512],
                                           start=(dc == 0), stop=(dc == 7))
                        pe.inc(ins)
                        assert pe.n == c_gate[(i, tq, sl)]
                        if sl >= 2:
                            t.wait_ge(ve.h, c_m1[(i, tq, sl - 2)])
                        elif tq >= 1:
                            t.wait_ge(ve.h, c_m1[(i, tq - 1, sl + 2)])
                        elif i >= 1:
                            t.wait_ge(ve.h, c_m1[(i - 1, 3, sl + 2)])
                        buf = wub[i % 2]
                        for dc in range(8):
                            ins = t.matmul(pb[2 + kb2][:, :],
                                           buf[:, dc * 512 + sl * 128:dc * 512 + sl * 128 + 128],
                                           xn_ownT[:, dc * TOK + tq * 512:dc * TOK + tq * 512 + 512],
                                           start=(dc == 0), stop=(dc == 7))
                        pe.inc(ins)
                        assert pe.n == c_up[(i, tq, sl)]
                    for hh in range(2):
                        t.wait_ge(ve.h, c_m2[(i, tq, 3)])
                        if hh == 1:
                            t.wait_ge(ve.h, c_add[(i, tq, 0)])
                        elif tq >= 1:
                            t.wait_ge(ve.h, c_add[(i, tq - 1, 1)])
                        else:
                            t.wait_ge(ve.h, pav)
                        for sl in range(4):
                            for c in range(4):
                                ins = t.matmul(
                                    pdown[:, c, :],
                                    wdb[i % 2][:, sl * DM + (4 * hh + c) * 128:
                                               sl * DM + (4 * hh + c) * 128 + 128],
                                    h_q[tq % 2][:, sl * 512:sl * 512 + 512],
                                    start=(sl == 0), stop=(sl == 3))
                        pe.inc(ins)
                        assert pe.n == c_down[(i, tq, hh)]
            t_(f)

            def f(a, i=i):
                for tq in range(4):
                    for sl in range(4):
                        kb2 = sl % 2
                        a.wait_ge(pe.h, c_gate[(i, tq, sl)])
                        if sl >= 2:
                            a.wait_ge(ve.h, c_m1[(i, tq, sl - 2)])
                        elif tq >= 1:
                            a.wait_ge(ve.h, c_m1[(i, tq - 1, sl + 2)])
                        elif i >= 1:
                            a.wait_ge(ve.h, c_m1[(i - 1, 3, sl + 2)])
                        se.inc(a.activation(hsil[kb2][:], pb[kb2][:, :], AF.Silu,
                                            bias=0.0, scale=1.0))
                        assert se.n == c_silu[(i, tq, sl)]
            a_(f)

            def f(v, i=i, e=e):
                for tq in range(4):
                    tqp = tq % 2
                    for sl in range(4):
                        kb2 = sl % 2
                        v.wait_ge(se.h, c_silu[(i, tq, sl)])
                        v.wait_ge(pe.h, c_up[(i, tq, sl)])
                        v.wait_ge(ve.h, ve.n)
                        ve.inc(v.tensor_tensor(h32[:], hsil[kb2][:], pb[2 + kb2][:, :],
                                               op=ALU.mult))
                        assert ve.n == c_m1[(i, tq, sl)]
                        v.wait_ge(ve.h, ve.n)
                        if tq >= 2:
                            v.wait_ge(pe.h, c_down[(i, tq - 2, 1)])
                        elif i >= 1 and tq < 2:
                            v.wait_ge(pe.h, c_down[(i - 1, tq + 2, 1)])
                        if e < NEXP:
                            v.wait_ge(d_wb[e % 2].h, wb_loaded[e])
                            ve.inc(v.tensor_tensor(h_q[tqp][:, sl * 512:sl * 512 + 512],
                                                   h32[:], wbcast[e % 2][:, tq * 512:tq * 512 + 512],
                                                   op=ALU.mult))
                        else:
                            ve.inc(v.tensor_copy(h_q[tqp][:, sl * 512:sl * 512 + 512],
                                                 h32[:]))
                        assert ve.n == c_m2[(i, tq, sl)]
                    for hh in range(2):
                        v.wait_ge(pe.h, c_down[(i, tq, hh)])
                        accv = acc[:].rearrange("p (c t2) -> p c t2", c=8)[
                            :, 4 * hh:4 * hh + 4, tq * 512:tq * 512 + 512]
                        ve.inc(v.tensor_tensor(accv, accv, pdown[:, :, :], op=ALU.add))
                        assert ve.n == c_add[(i, tq, hh)]
            v_(f)

            prev_add_ve = c_add[(i, 3, 1)]
            exp_mult_ve[e] = c_m2[(i, 3, 3)]
            eq_consumed[i] = pe.n
            if i + 2 < len(eq_list):
                def f(g, i=i):
                    g.wait_ge(pe.h, eq_consumed[i])
                    wload(g, *eq_list[i + 2])
                    eq_loaded[i + 2] = d_ffn[i % 2].n
                g_(f)
        st_Dp.close()

    st_D.close()
    st_BD.close()

    ffn_done_ve = ve.n

    def f(g):
        g.wait_ge(ve.h, ffn_done_ve)
        d_acc.inc(g.dma_start(out=acc[:].rearrange("p (c t2) -> p c t2", c=8),
                              in_=xT_own.rearrange("(c p) t2 -> p c t2", p=128),
                              accum_op=ALU.add), 16)
    g_(f)

    def f(s):
        s.wait_ge(d_acc.h, d_acc.n)
        d_out.inc(s.dma_start(out=outT.rearrange("(c p) t2 -> p c t2", p=128),
                              in_=acc[:].rearrange("p (c t2) -> p c t2", c=8)), 16)
        s.wait_ge(d_out.h, d_out.n)
    s_(f)

    es.close()
    return nc


_CACHED = {}


def _get_program():
    if "nc" not in _CACHED:
        _CACHED["nc"] = build_program()
    return _CACHED["nc"]


def kernel(**inputs):
    x = np.asarray(inputs["L0_attn_x"], dtype=np.float32)[0]
    xT = np.ascontiguousarray(x.T)
    x_live = np.ascontiguousarray(x[:LIVE])

    base = {
        "x_liv": x_live,
        "mapping": np.ascontiguousarray(np.asarray(inputs["L0_attn_mapping"], dtype=np.int32)),
        "Wqkv": np.ascontiguousarray(np.asarray(inputs["L0_attn_Wqkv"], dtype=np.float32)),
        "bqkv": np.ascontiguousarray(np.asarray(inputs["L0_attn_bqkv"], dtype=np.float32)),
        "Wproj": np.ascontiguousarray(np.asarray(inputs["L0_attn_Wproj"], dtype=np.float32)),
        "bproj": np.ascontiguousarray(np.asarray(inputs["L0_attn_bproj"], dtype=np.float32)),
        "gate_w": np.ascontiguousarray(np.asarray(inputs["L0_ffn_gate_w"], dtype=np.float32)),
        "w_gate": np.ascontiguousarray(np.asarray(inputs["L0_ffn_w_gate"], dtype=np.float32)),
        "w_up": np.ascontiguousarray(np.asarray(inputs["L0_ffn_w_up"], dtype=np.float32)),
        "w_down": np.ascontiguousarray(np.asarray(inputs["L0_ffn_w_down"], dtype=np.float32)),
        "ws_gate": np.ascontiguousarray(np.asarray(inputs["L0_ffn_ws_gate"], dtype=np.float32)),
        "ws_up": np.ascontiguousarray(np.asarray(inputs["L0_ffn_ws_up"], dtype=np.float32)),
        "ws_down": np.ascontiguousarray(np.asarray(inputs["L0_ffn_ws_down"], dtype=np.float32)),
    }

    in_maps = []
    p = np.arange(128)[:, None]
    g = np.arange(16)[None, :]
    for c in range(NCORES):
        qp = np.minimum(c * TOK + 128 * g + p, LIVE - 1).astype(np.int32)
        m = dict(base)
        m["x_own"] = np.ascontiguousarray(x[c * TOK:(c + 1) * TOK])
        m["xT_own"] = np.ascontiguousarray(xT[:, c * TOK:(c + 1) * TOK])
        m["qpos"] = np.ascontiguousarray(qp)
        in_maps.append(m)

    nc = _get_program()
    res = run_bass_kernel_spmd(nc, in_maps, core_ids=list(range(NCORES)))
    out = np.empty((M, DM), dtype=np.float32)
    for c in range(NCORES):
        out[c * TOK:(c + 1) * TOK] = res.results[c]["outT"].T
    return out.reshape(1, M, DM)


# version 04
# speedup vs baseline: 1.3747x; 1.0115x over previous
import numpy as np
from contextlib import ExitStack

import concourse.bass as bass
import concourse.mybir as mybir
from concourse.bass_utils import run_bass_kernel_spmd
from concourse.bass import IndirectOffsetOnAxis

F32 = mybir.dt.float32
BF16 = mybir.dt.bfloat16
I32 = mybir.dt.int32
AF = mybir.ActivationFunctionType
ALU = mybir.AluOpType
AX = mybir.AxisListType

NCORES = 8
M = 16384
DM = 1024
LIVE = 4096
TOK = 2048
NH = 16
NKV = 1024
NSEG = 16
VSEGW = 16 * 65
DFF = 2048
NEXP = 8
NT = 4
TW = 512
NBLK = 8


class Sem:
    def __init__(self, es, nc, name):
        self.h = es.enter_context(nc.semaphore(name))
        self.n = 0

    def inc(self, instr, k=1):
        instr.then_inc(self.h, k)
        self.n += k
        return instr


def _fmload(eng, dst, src, nch):
    return eng.dma_start(out=dst[:].rearrange("p (c o) -> p c o", c=nch),
                         in_=src.rearrange("(c p) o -> p c o", p=128))


def build_program(phases=("A", "C", "B", "D", "E")):
    nc = bass.Bass("TRN2", target_bir_lowering=False, enable_asserts=False)

    x_liv = nc.dram_tensor("x_liv", [LIVE, DM], F32, kind="ExternalInput").ap()
    x_own = nc.dram_tensor("x_own", [TOK, DM], F32, kind="ExternalInput").ap()
    xT_own = nc.dram_tensor("xT_own", [DM, TOK], F32, kind="ExternalInput").ap()
    mapping = nc.dram_tensor("mapping", [LIVE], I32, kind="ExternalInput").ap()
    qpos = nc.dram_tensor("qpos", [128, 16], I32, kind="ExternalInput").ap()
    Wqkv = nc.dram_tensor("Wqkv", [DM, 3 * DM], F32, kind="ExternalInput").ap()
    bqkv = nc.dram_tensor("bqkv", [3 * DM], F32, kind="ExternalInput").ap()
    Wproj = nc.dram_tensor("Wproj", [DM, DM], F32, kind="ExternalInput").ap()
    bproj = nc.dram_tensor("bproj", [DM], F32, kind="ExternalInput").ap()
    gate_w = nc.dram_tensor("gate_w", [DM, NEXP], F32, kind="ExternalInput").ap()
    w_gate = nc.dram_tensor("w_gate", [NEXP, DM, DFF], F32, kind="ExternalInput").ap()
    w_up = nc.dram_tensor("w_up", [NEXP, DM, DFF], F32, kind="ExternalInput").ap()
    w_down = nc.dram_tensor("w_down", [NEXP, DFF, DM], F32, kind="ExternalInput").ap()
    ws_gate = nc.dram_tensor("ws_gate", [DM, DFF], F32, kind="ExternalInput").ap()
    ws_up = nc.dram_tensor("ws_up", [DM, DFF], F32, kind="ExternalInput").ap()
    ws_down = nc.dram_tensor("ws_down", [DFF, DM], F32, kind="ExternalInput").ap()
    outT = nc.dram_tensor("outT", [DM, TOK], F32, kind="ExternalOutput").ap()
    K_dram = nc.dram_tensor("K_dram", [128 * NSEG, 512], BF16).ap()
    V_dram = nc.dram_tensor("V_dram", [64 * NSEG, VSEGW], BF16).ap()
    W_wb = nc.dram_tensor("W_wb", [NEXP, 128, TOK], BF16).ap()

    es = ExitStack()

    def sbuf(name, shape, dt):
        return es.enter_context(nc.sbuf_tensor(name, shape, dt))

    identf = sbuf("identf", [128, 128], F32)
    identb = sbuf("identb", [128, 128], BF16)
    ones64f = sbuf("ones64f", [1, 64], F32)
    ones128f = sbuf("ones128f", [1, 128], F32)
    ones128b = sbuf("ones128b", [1, 128], BF16)
    iota16f = sbuf("iota16f", [128, 1], F32)
    qpos_sb = sbuf("qpos_sb", [128, 16], I32)
    kvidx = sbuf("kvidx", [128, 8], I32)
    qmap = sbuf("qmap", [128, 16], I32)
    segraw = sbuf("segraw", [1, 32], I32)
    segf = sbuf("segf", [1, 32], F32)
    idxK = sbuf("idxK", [128, 32], I32)
    bqkv_sb = sbuf("bqkv_sb", [128, 24], F32)
    bqs = sbuf("bqs", [128, 8], F32)
    bproj_sb = sbuf("bproj_sb", [128, 8], F32)
    acc = sbuf("acc", [128, 8 * TOK], F32)
    NLN = 4
    bn6 = [sbuf(f"bn6_{i}", [128, 12], F32) for i in range(NLN)]
    bn2 = [sbuf(f"bn2_{i}", [128, 2], F32) for i in range(NLN)]
    rstd = [sbuf(f"rstd{i}", [128, 1], F32) for i in range(NLN)]
    stg = [sbuf(f"stg{i}", [128, DM], F32) for i in range(NLN)]
    stgn = [sbuf(f"stgn{i}", [128, DM], BF16) for i in range(NLN)]

    pb = [es.enter_context(nc.psum_tensor(f"pb{i}", [128, 512], F32)) for i in range(4)]
    st_TB = ExitStack()
    pb += [st_TB.enter_context(nc.psum_tensor(f"pb{i}", [128, 512], F32)) for i in range(4, 6)]
    tb = [st_TB.enter_context(nc.psum_tensor(f"tb{i}", [128, 1024], BF16)) for i in range(2)]

    d_in = Sem(es, nc, "d_in")
    d_iny = Sem(es, nc, "d_iny")
    d_idx = Sem(es, nc, "d_idx")
    d_g = [Sem(es, nc, f"d_g{i}") for i in range(4)]
    d_kst = [Sem(es, nc, f"d_kst{i}") for i in range(2)]
    d_own = [Sem(es, nc, f"d_own{i}") for i in range(4)]
    d_kv = Sem(es, nc, "d_kv")
    d_wtf = Sem(es, nc, "d_wtf")
    d_wb = [Sem(es, nc, f"d_wb{i}") for i in range(2)]
    d_ffn = [Sem(es, nc, f"d_ffn{i}") for i in range(2)]
    d_bv = Sem(es, nc, "d_bv")
    d_acc = Sem(es, nc, "d_acc")
    d_out = Sem(es, nc, "d_out")
    pe = Sem(es, nc, "pe")
    ve = Sem(es, nc, "ve")
    se = Sem(es, nc, "se")
    gp = Sem(es, nc, "gp")

    blk = es.enter_context(nc.Block())
    g_, s_, t_, v_, a_ = blk.gpsimd, blk.sync, blk.tensor, blk.vector, blk.scalar

    def f(g):
        gp.inc(g.iota(identf[:], pattern=[[1, 128]], base=0, channel_multiplier=-1,
               allow_small_or_imprecise_dtypes=True))
        gp.inc(g.iota(iota16f[:], pattern=[[0, 1]], base=0, channel_multiplier=16,
               allow_small_or_imprecise_dtypes=True))
        gp.inc(g.memset(ones64f[:], 1.0))
        gp.inc(g.memset(ones128f[:], 1.0))
        gp.inc(g.memset(ones128b[:], 1.0))
        d_in.inc(g.dma_start(
            out=kvidx[:],
            in_=mapping.rearrange("(gi p f) -> p gi f", gi=8, p=128)[:, :, 0:1].squeeze(2),
            allow_slow_non_contiguous=True), 16)
    g_(f)

    def f(s):
        d_iny.inc(s.dma_start(out=qpos_sb[:], in_=qpos), 16)
        d_iny.inc(s.dma_start(out=bqkv_sb[:], in_=bqkv.rearrange("(c p) -> p c", p=128),
                              allow_slow_non_contiguous=True), 16)
        d_iny.inc(s.dma_start(out=bproj_sb[:], in_=bproj.rearrange("(c p) -> p c", p=128),
                              allow_slow_non_contiguous=True), 16)
    s_(f)
    setup_iny = d_iny.n
    setup_in = d_in.n

    def f(v):
        v.wait_ge(gp.h, 1)
        ve.inc(v.tensor_scalar(identf[:], identf[:], 0.0, None, op0=ALU.is_equal))
        v.wait_ge(ve.h, ve.n)
        ve.inc(v.tensor_copy(identb[:], identf[:]))
        v.wait_ge(d_iny.h, setup_iny)
        ve.inc(v.tensor_scalar(bqs[:], bqkv_sb[:, 0:8], 0.125, None, op0=ALU.mult))
    v_(f)
    ident_ready = ve.n

    def f(g):
        g.wait_ge(d_iny.h, setup_iny)
        for gi in range(16):
            d_idx.inc(g.indirect_dma_start(
                out=qmap[:, gi:gi + 1], out_offset=None,
                in_=mapping[None, :],
                in_offset=IndirectOffsetOnAxis(qpos_sb[:, gi:gi + 1], 1)), 16)
        g.wait_ge(d_idx.h, d_idx.n)
        d_idx.inc(g.dma_start(out=segraw[:], in_=qmap[0:128:64, 0:16]), 16)
    g_(f)
    qmap_done = d_idx.n

    def f(v):
        v.wait_ge(d_idx.h, d_idx.n)
        ve.inc(v.tensor_scalar(segraw[:], segraw[:], 8, None, op0=ALU.arith_shift_right))
        v.wait_ge(ve.h, ve.n)
        ve.inc(v.tensor_copy(segf[:], segraw[:]))
    v_(f)
    segf_ready = ve.n

    def f(t):
        t.wait_ge(ve.h, segf_ready)
        t.wait_ge(gp.h, 4)
        pe.inc(t.matmul(pb[0][:, 0:32], ones128f[:], segf[:], start=True, stop=True))
    t_(f)
    def f(v):
        v.wait_ge(pe.h, pe.n)
        ve.inc(v.tensor_scalar(pb[0][:, 0:32], pb[0][:, 0:32], 1.0, iota16f[:],
                               op0=ALU.mult, op1=ALU.add))
        v.wait_ge(ve.h, ve.n)
        ve.inc(v.tensor_copy(idxK[:], pb[0][:, 0:32]))
    v_(f)
    idx_ready = ve.n

    all_sems = [d_in, d_iny, d_idx, d_g[0], d_g[1], d_g[2], d_g[3], d_kst[0], d_kst[1],
                d_own[0], d_own[1], d_own[2], d_own[3], d_kv, d_bv, d_wtf, d_wb[0], d_wb[1], d_ffn[0], d_ffn[1],
                d_acc, d_out, pe, ve, se, gp]

    def phase_barrier():
        vals = [(sm.h, sm.n) for sm in all_sems if sm.n > 0]
        for section in (g_, s_, t_, v_, a_):
            def f(e, vals=vals):
                for h, n in vals:
                    e.wait_ge(h, n)
            section(f)

    def blkcol(b):
        return 16 * (b & 1) + (b >> 1)

    ln_sqrt_se = [0] * 4
    ln_norm_ve = [0] * 4
    tb_free = [0, 0]

    def emit_ln_tr_batch(items):
        n = len(items)
        ve_c, se_c, pe_c = ve.n, se.n, pe.n
        p_rec, p_sqrt, p_norm = {}, {}, {}
        for idx, (k, gw, dfm, dcol, cstr) in enumerate(items):
            ve_c += 5; p_rec[idx] = ve_c
        for idx in range(n):
            se_c += 1; p_sqrt[idx] = se_c
        for idx in range(n):
            ve_c += 1; p_norm[idx] = ve_c
        p_tp, p_tc = {}, {}
        for idx in range(n):
            for ch in range(8):
                pe_c += 1; p_tp[(idx, ch)] = pe_c
                ve_c += 1; p_tc[(idx, ch)] = ve_c

        def f(v):
            for idx, (k, gw, dfm, dcol, cstr) in enumerate(items):
                v.wait_ge(*gw)
                v.wait_ge(se.h, ln_sqrt_se[k])
                ve.inc(v.bn_stats(bn6[k][:, 0:6], stg[k][:, 0:512]))
                ve.inc(v.bn_stats(bn6[k][:, 6:12], stg[k][:, 512:1024]))
                v.wait_ge(ve.h, ve.n)
                ve.inc(v.bn_aggr(bn2[k][:], bn6[k][:]))
                v.wait_ge(ve.h, ve.n)
                ve.inc(v.tensor_scalar(rstd[k][:], bn2[k][:, 1:2], 1e-5, None, op0=ALU.add))
                v.wait_ge(ve.h, ve.n)
                ve.inc(v.reciprocal(rstd[k][:], rstd[k][:]))
                assert ve.n == p_rec[idx]
        v_(f)
        def f(a):
            for idx, (k, gw, dfm, dcol, cstr) in enumerate(items):
                a.wait_ge(ve.h, p_rec[idx])
                se.inc(a.activation(rstd[k][:], rstd[k][:], AF.Sqrt, bias=0.0, scale=1.0))
                assert se.n == p_sqrt[idx]
        a_(f)
        def f(v):
            for idx, (k, gw, dfm, dcol, cstr) in enumerate(items):
                v.wait_ge(se.h, p_sqrt[idx])
                ve.inc(v.tensor_scalar(stgn[k][:], stg[k][:], bn2[k][:, 0:1], rstd[k][:],
                                       op0=ALU.subtract, op1=ALU.mult))
                assert ve.n == p_norm[idx]
        v_(f)
        def f(t):
            for idx, (k, gw, dfm, dcol, cstr) in enumerate(items):
                for ch in range(8):
                    gi2 = idx * 8 + ch
                    if ch == 0:
                        t.wait_ge(ve.h, p_norm[idx])
                    if gi2 >= 2:
                        t.wait_ge(ve.h, p_tc[((gi2 - 2) // 8, (gi2 - 2) % 8)])
                    else:
                        t.wait_ge(ve.h, max(tb_free[gi2 % 2], ident_ready))
                    pe.inc(t.transpose(tb[gi2 % 2][0:128, 0:128],
                                       stgn[k][:, ch * 128:ch * 128 + 128], identb[:]))
                    assert pe.n == p_tp[(idx, ch)]
        t_(f)
        def f(v):
            for idx, (k, gw, dfm, dcol, cstr) in enumerate(items):
                for ch in range(8):
                    gi2 = idx * 8 + ch
                    v.wait_ge(pe.h, p_tp[(idx, ch)])
                    ve.inc(v.tensor_copy(dfm[:, dcol + ch * cstr:dcol + ch * cstr + 128],
                                         tb[gi2 % 2][0:128, 0:128]))
                    assert ve.n == p_tc[(idx, ch)]
        v_(f)
        for idx, (k, gw, dfm, dcol, cstr) in enumerate(items):
            ln_sqrt_se[k] = p_sqrt[idx]
            ln_norm_ve[k] = p_norm[idx]
        tb_free[0] = p_tc[(n - 1, 6)]
        tb_free[1] = p_tc[(n - 1, 7)]
        return ve.n

    st_A = ExitStack()
    wk_sb = st_A.enter_context(nc.sbuf_tensor("wk_sb", [128, 8 * DM], BF16))
    wv_sb = st_A.enter_context(nc.sbuf_tensor("wv_sb", [128, 8 * DM], BF16))
    xn_kvT = st_A.enter_context(nc.sbuf_tensor("xn_kvT", [128, 8 * NKV], BF16))
    bv_b = st_A.enter_context(nc.sbuf_tensor("bv_b", [64, DM], F32))
    bv_row = st_A.enter_context(nc.sbuf_tensor("bv_row", [1, DM], F32))
    K_T2 = st_A.enter_context(nc.sbuf_tensor("K_T2", [128, NSEG * 512], BF16))
    V_seg = st_A.enter_context(nc.sbuf_tensor("V_seg", [64, NSEG * VSEGW], BF16))

    if "A" in phases:
        def f(g):
            d_in.inc(_fmload(g, wk_sb, Wqkv[:, DM:2 * DM], 8), 16)
            d_in.inc(_fmload(g, wv_sb, Wqkv[:, 2 * DM:3 * DM], 8), 16)
            vs4 = V_seg[:].rearrange("p (s h d) -> p s h d", s=NSEG, h=16)
            gp.inc(g.memset(vs4[:, :, :, 64:65], 1.0))
        g_(f)
        wkwv_loaded = d_in.n
        vones_gp = gp.n

        def f(s2):
            d_bv.inc(s2.dma_start(out=bv_row[:], in_=bqkv[None, 2 * DM:3 * DM]), 16)
        s_(f)

        def f(t):
            t.wait_ge(d_bv.h, d_bv.n)
            t.wait_ge(gp.h, 4)
            pe.inc(t.matmul(pb[1][0:64, :], ones64f[:], bv_row[0:1, 0:512],
                            start=True, stop=True))
            pe.inc(t.matmul(pb[2][0:64, :], ones64f[:], bv_row[0:1, 512:1024],
                            start=True, stop=True))
        t_(f)
        def f(v):
            v.wait_ge(pe.h, pe.n)
            ve.inc(v.tensor_copy(bv_b[:, 0:512], pb[1][0:64, :]))
            ve.inc(v.tensor_copy(bv_b[:, 512:1024], pb[2][0:64, :]))
        v_(f)
        bvb_ready = ve.n

        for bat in range(2):
            items = []
            for sub in range(4):
                gi = bat * 4 + sub
                k = gi % NLN
                def f(g, gi=gi, k=k):
                    g.wait_ge(d_in.h, d_in.n)
                    g.wait_ge(ve.h, ln_norm_ve[k])
                    d_g[k].inc(g.indirect_dma_start(
                        out=stg[k][:], out_offset=None,
                        in_=x_liv,
                        in_offset=IndirectOffsetOnAxis(kvidx[:, gi:gi + 1], 0)), 16)
                g_(f)
                items.append((k, (d_g[k].h, d_g[k].n), xn_kvT, gi * 128, NKV))
            emit_ln_tr_batch(items)
        xnkvT_ready = ve.n

        k_evacs = []
        for kc in range(8):
            for half in range(2):
                i = kc * 2 + half
                bank = pb[i % 4]
                def f(t, kc=kc, half=half, bank=bank, i=i):
                    t.wait_ge(d_in.h, wkwv_loaded)
                    t.wait_ge(ve.h, max(xnkvT_ready, idx_ready))
                    if i >= 4:
                        t.wait_ge(ve.h, k_evacs[i - 4])
                    for dc in range(8):
                        ins = t.matmul(bank[:, :],
                                       wk_sb[:, dc * DM + kc * 128:dc * DM + kc * 128 + 128],
                                       xn_kvT[:, dc * NKV + half * 512:dc * NKV + half * 512 + 512],
                                       start=(dc == 0), stop=(dc == 7))
                    pe.inc(ins)
                t_(f)
                def f(v, kc=kc, half=half, bank=bank):
                    v.wait_ge(pe.h, pe.n)
                    dst = K_T2[:].rearrange("p (sg r) -> p sg r", sg=NSEG)[
                        :, half * 8:half * 8 + 8, kc * 64:kc * 64 + 64]
                    srcp = bank[:, :].rearrange("p (sg s) -> p sg s", sg=8)
                    ve.inc(v.tensor_scalar(dst, srcp, bqkv_sb[:, 8 + kc:9 + kc], None,
                                           op0=ALU.add))
                v_(f)
                k_evacs.append(ve.n)
        KT_done = ve.n

        v_evacs = []
        for sg in range(NSEG):
            for vh in range(2):
                i = sg * 2 + vh
                bank = pb[4 + (i % 2)]
                def f(t, sg=sg, vh=vh, bank=bank, i=i):
                    t.wait_ge(ve.h, max(xnkvT_ready, bvb_ready))
                    if i >= 2:
                        t.wait_ge(ve.h, v_evacs[i - 2])
                    for dc in range(8):
                        ins = t.matmul(bank[0:64, :],
                                       xn_kvT[:, dc * NKV + sg * 64:dc * NKV + sg * 64 + 64],
                                       wv_sb[:, dc * DM + vh * 512:dc * DM + vh * 512 + 512],
                                       start=(dc == 0), stop=(dc == 7))
                    pe.inc(ins)
                t_(f)
                def f(v, sg=sg, vh=vh, bank=bank):
                    v.wait_ge(pe.h, pe.n)
                    base = sg * VSEGW + vh * 520
                    dst = V_seg[0:64, base:base + 520].rearrange(
                        "p (h d) -> p h d", h=8)[:, :, 0:64]
                    srcv = bank[0:64, :].rearrange("p (h d) -> p h d", h=8)
                    ve.inc(v.scalar_tensor_tensor(
                        dst, srcv, 0.0,
                        bv_b[:, vh * 512:vh * 512 + 512].rearrange("p (h d) -> p h d", h=8),
                        op0=ALU.bypass, op1=ALU.add))
                v_(f)
                v_evacs.append(ve.n)
        Vseg_done = ve.n

        def f(g):
            g.wait_ge(ve.h, KT_done)
            d_kv.inc(g.dma_start(out=K_dram.rearrange("(p e) r -> p e r", p=128),
                                 in_=K_T2[:].rearrange("p (e r) -> p e r", e=NSEG)), 16)
            g.wait_ge(ve.h, Vseg_done)
            g.wait_ge(gp.h, vones_gp)
            d_kv.inc(g.dma_start(out=V_dram.rearrange("(p e) r -> p e r", p=64),
                                 in_=V_seg[:].rearrange("p (e r) -> p e r", e=NSEG)), 16)
        g_(f)
        kv_dram_done = d_kv.n
    else:
        kv_dram_done = d_kv.n
        def f(v):
            ve.inc(v.memset(acc[:], 0.0))
        v_(f)

    st_A.close()
    phase_barrier()

    st_C = ExitStack()
    wq_sb = st_C.enter_context(nc.sbuf_tensor("wq_sb", [128, 8 * DM], BF16))
    wp_sb = st_C.enter_context(nc.sbuf_tensor("wp_sb", [128, 8 * DM], BF16))
    xnqT = st_C.enter_context(nc.sbuf_tensor("xnqT", [128, 8 * TW], BF16))
    QT = st_C.enter_context(nc.sbuf_tensor("QT", [128, 8 * TW], BF16))
    attn_tm = st_C.enter_context(nc.sbuf_tensor("attn_tm", [64, 2 * DM], BF16))
    attn_fm = st_C.enter_context(nc.sbuf_tensor("attn_fm", [128, 8 * TW], BF16))
    Kst = [st_C.enter_context(nc.sbuf_tensor(f"Kst{i}", [128, NBLK * 512], BF16)) for i in range(2)]
    Vst = [st_C.enter_context(nc.sbuf_tensor(f"Vst{i}", [64, NBLK * VSEGW], BF16)) for i in range(2)]
    P_sb = [st_C.enter_context(nc.sbuf_tensor(f"P_sb{i}", [64, 64], BF16)) for i in range(3)]
    negC = [st_C.enter_context(nc.sbuf_tensor(f"negC{i}", [64, 1], F32)) for i in range(2)]
    aU = [st_C.enter_context(nc.sbuf_tensor(f"aU{i}", [64, VSEGW], BF16)) for i in range(2)]
    rec16 = st_C.enter_context(nc.sbuf_tensor("rec16", [64, 16], F32))

    if "C" in phases:
        def f(g):
            d_in.inc(_fmload(g, wq_sb, Wqkv[:, 0:DM], 8), 16)
            d_in.inc(_fmload(g, wp_sb, Wproj, 8), 16)
        g_(f)
        wqwp_loaded = d_in.n

        prev_proj_ve = ve.n
        pipe_pe_hist = [0, 0]
        for j in range(NT):
            kb = j % 2
            items = []
            for gsub in range(4):
                k = gsub % NLN
                gi = 4 * j + gsub
                def f(g, gi=gi, k=k):
                    g.wait_ge(d_idx.h, qmap_done)
                    g.wait_ge(ve.h, ln_norm_ve[k])
                    d_g[k].inc(g.indirect_dma_start(
                        out=stg[k][:], out_offset=None,
                        in_=x_liv,
                        in_offset=IndirectOffsetOnAxis(qmap[:, gi:gi + 1], 0)), 16)
                g_(f)
                items.append((k, (d_g[k].h, d_g[k].n), xnqT, gsub * 128, TW))
            def f(g, j=j, kb=kb, guard=pipe_pe_hist[0]):
                g.wait_ge(d_kv.h, kv_dram_done)
                g.wait_ge(ve.h, idx_ready)
                g.wait_ge(pe.h, guard)
                for bb in range(NBLK):
                    col = blkcol(8 * j + bb)
                    d_kst[kb].inc(g.indirect_dma_start(
                        out=Kst[kb][:, bb * 512:bb * 512 + 512], out_offset=None,
                        in_=K_dram,
                        in_offset=IndirectOffsetOnAxis(idxK[:, col:col + 1], 0)), 16)
                    d_kst[kb].inc(g.indirect_dma_start(
                        out=Vst[kb][0:64, bb * VSEGW:bb * VSEGW + VSEGW], out_offset=None,
                        in_=V_dram,
                        in_offset=IndirectOffsetOnAxis(idxK[0:64, col:col + 1], 0)), 16)
            g_(f)
            kv_st_done = d_kst[kb].n
            emit_ln_tr_batch(items)
            xnqT_ready = ve.n

            q_evacs = []
            for qc in range(8):
                bank = pb[qc % 2]
                def f(t, qc=qc, bank=bank):
                    t.wait_ge(ve.h, xnqT_ready)
                    t.wait_ge(d_in.h, wqwp_loaded)
                    if len(q_evacs) >= 2:
                        t.wait_ge(ve.h, q_evacs[qc - 2])
                    for dc in range(8):
                        ins = t.matmul(bank[:, :],
                                       wq_sb[:, dc * DM + qc * 128:dc * DM + qc * 128 + 128],
                                       xnqT[:, dc * TW:dc * TW + 512],
                                       start=(dc == 0), stop=(dc == 7))
                    pe.inc(ins)
                t_(f)
                def f(v, qc=qc, bank=bank):
                    v.wait_ge(pe.h, pe.n)
                    ve.inc(v.tensor_scalar(QT[:, qc * TW:qc * TW + 512], bank[:, :],
                                           bqkv_sb[:, qc:qc + 1], 0.125,
                                           op0=ALU.add, op1=ALU.mult))
                v_(f)
                q_evacs.append(ve.n)
            QT_ready = ve.n

            NIT = NBLK * NH
            c_sc, c_rd, c_ex, c_av, c_ea = ({} for _ in range(5))
            c_no = {}
            c_tp = {}
            c_tc = {}
            pe_c, ve_c, se_c = pe.n, ve.n, se.n
            NSLOT = NIT + 6 + 16
            for slot in range(NSLOT):
                if slot < NIT:
                    pe_c += 1; c_sc[slot] = pe_c
                if 0 <= slot - 1 < NIT:
                    ve_c += 1; c_rd[slot - 1] = ve_c
                if 0 <= slot - 2 < NIT:
                    se_c += 1; c_ex[slot - 2] = se_c
                if 0 <= slot - 3 < NIT:
                    pe_c += 1; c_av[slot - 3] = pe_c
                if 0 <= slot - 4 < NIT:
                    i4 = slot - 4
                    ve_c += 1; c_ea[i4] = ve_c
                    if i4 % NH == NH - 1:
                        ve_c += 3; c_no[i4 // NH] = ve_c
                if slot >= 5 and (slot - 5) % NH == NH - 1:
                    bb2 = (slot - 5) // NH
                    if 0 <= bb2 < NBLK:
                        for ch in range(8):
                            pe_c += 1; c_tp[(bb2, ch)] = pe_c
                            ve_c += 1; c_tc[(bb2, ch)] = ve_c
            prev_se = se.n

            def f(t, j=j, kb=kb, QT_ready=QT_ready, kv_st_done=kv_st_done,
                  ppv=prev_proj_ve, prev_se=prev_se, tbf=tuple(tb_free)):
                for slot in range(NSLOT):
                    i = slot
                    if i < NIT:
                        bb, h = divmod(i, NH)
                        if i == 0:
                            t.wait_ge(ve.h, QT_ready)
                            t.wait_ge(d_kst[kb].h, kv_st_done)
                            t.wait_ge(ve.h, ppv)
                            t.wait_ge(se.h, prev_se)
                        if i >= 2:
                            t.wait_ge(ve.h, c_rd[i - 2])
                            t.wait_ge(se.h, c_ex[i - 2])
                        lhs = Kst[kb][(h & 1) * 64:(h & 1) * 64 + 64,
                                      bb * 512 + (h >> 1) * 64:bb * 512 + (h >> 1) * 64 + 64]
                        rhs = QT[(h & 1) * 64:(h & 1) * 64 + 64,
                                 (h >> 1) * TW + 64 * bb:(h >> 1) * TW + 64 * bb + 64]
                        pe.inc(t.matmul(pb[2 + (i % 2)][0:64, 0:64], lhs, rhs,
                                        start=True, stop=True))
                        assert pe.n == c_sc[i]
                    i = slot - 3
                    if 0 <= i < NIT:
                        bb, h = divmod(i, NH)
                        t.wait_ge(se.h, c_ex[i])
                        if i >= 2:
                            t.wait_ge(ve.h, c_ea[i - 2])
                        rhs = Vst[kb][0:64, bb * VSEGW + 65 * h:bb * VSEGW + 65 * h + 65]
                        pe.inc(t.matmul(pb[(i % 2)][0:64, 0:65], P_sb[i % 3][:], rhs,
                                        start=True, stop=True))
                        assert pe.n == c_av[i]
                    if slot >= 5 and (slot - 5) % NH == NH - 1:
                        bb2 = (slot - 5) // NH
                        if 0 <= bb2 < NBLK:
                            for ch in range(8):
                                if ch == 0:
                                    t.wait_ge(ve.h, c_no[bb2])
                                if ch >= 2:
                                    t.wait_ge(ve.h, c_tc[(bb2, ch - 2)])
                                elif bb2 >= 1:
                                    t.wait_ge(ve.h, c_tc[(bb2 - 1, 6 + ch)])
                                elif ch < 2:
                                    t.wait_ge(ve.h, tbf[ch])
                                pe.inc(t.transpose(
                                    tb[ch % 2][0:128, 0:64],
                                    attn_tm[0:64, (bb2 % 2) * DM + ch * 128:
                                            (bb2 % 2) * DM + ch * 128 + 128],
                                    identb[0:64, 0:64]))
                                assert pe.n == c_tp[(bb2, ch)]
            t_(f)
            pipe_pe_hist = [pipe_pe_hist[1], pe.n]

            def f(v, kb=kb):
                for slot in range(NSLOT):
                    i = slot - 1
                    if 0 <= i < NIT:
                        v.wait_ge(pe.h, c_sc[i])
                        if i >= 2:
                            v.wait_ge(se.h, c_ex[i - 2])
                        ve.inc(v.reduce_max(negC[i % 2][:], pb[2 + (i % 2)][0:64, 0:64],
                                            axis=AX.X, negate=True))
                        assert ve.n == c_rd[i]
                    i = slot - 4
                    if 0 <= i < NIT:
                        bb, h = divmod(i, NH)
                        v.wait_ge(pe.h, c_av[i])
                        ve.inc(v.tensor_copy(aU[bb % 2][:, 65 * h:65 * h + 65],
                                             pb[(i % 2)][0:64, 0:65]))
                        assert ve.n == c_ea[i]
                        if h == NH - 1:
                            if bb >= 2:
                                v.wait_ge(pe.h, c_tp[(bb - 2, 7)])
                            v.wait_ge(ve.h, ve.n)
                            a3 = aU[bb % 2][:].rearrange("p (hh c) -> p hh c", hh=16)
                            ve.inc(v.tensor_scalar(rec16[:], a3[:, :, 64:65].squeeze(2),
                                                   1e-6, None, op0=ALU.add))
                            v.wait_ge(ve.h, ve.n)
                            ve.inc(v.reciprocal(rec16[:], rec16[:]))
                            v.wait_ge(ve.h, ve.n)
                            dstb = attn_tm[0:64, (bb % 2) * DM:(bb % 2) * DM + DM].rearrange(
                                "p (hh d) -> p hh d", hh=16)
                            ve.inc(v.tensor_tensor(
                                dstb, a3[:, :, 0:64],
                                rec16[:].unsqueeze(2).broadcast_to([64, 16, 64]),
                                op=ALU.mult))
                            assert ve.n == c_no[bb]
                    if slot >= 5 and (slot - 5) % NH == NH - 1:
                        bb2 = (slot - 5) // NH
                        if 0 <= bb2 < NBLK:
                            for ch in range(8):
                                v.wait_ge(pe.h, c_tp[(bb2, ch)])
                                ve.inc(v.tensor_copy(
                                    attn_fm[:, ch * TW + bb2 * 64:ch * TW + bb2 * 64 + 64],
                                    tb[ch % 2][0:128, 0:64]))
                                assert ve.n == c_tc[(bb2, ch)]
            v_(f)
            tb_free[0] = c_tc[(NBLK - 1, 6)]
            tb_free[1] = c_tc[(NBLK - 1, 7)]
            attn_fm_ready = ve.n

            def f(a):
                for slot in range(NSLOT):
                    i = slot - 2
                    if 0 <= i < NIT:
                        a.wait_ge(ve.h, c_rd[i])
                        if i >= 3:
                            a.wait_ge(pe.h, c_av[i - 3])
                        se.inc(a.activation(P_sb[i % 3][:], pb[2 + (i % 2)][0:64, 0:64],
                                            AF.Exp, bias=negC[i % 2][:], scale=1.0))
                        assert se.n == c_ex[i]
            a_(f)

            p_evacs = []
            for oc in range(8):
                bank = pb[2 + (oc % 2)]
                def f(t, oc=oc, bank=bank):
                    t.wait_ge(ve.h, attn_fm_ready)
                    if len(p_evacs) >= 2:
                        t.wait_ge(ve.h, p_evacs[oc - 2])
                    for fc in range(8):
                        ins = t.matmul(bank[:, :],
                                       wp_sb[:, fc * DM + oc * 128:fc * DM + oc * 128 + 128],
                                       attn_fm[:, fc * TW:fc * TW + 512],
                                       start=(fc == 0), stop=(fc == 7))
                    pe.inc(ins)
                t_(f)
                def f(v, oc=oc, bank=bank, j=j):
                    v.wait_ge(pe.h, pe.n)
                    ve.inc(v.tensor_scalar(acc[:, oc * TOK + j * TW:oc * TOK + j * TW + 512],
                                           bank[:, :], bproj_sb[:, oc:oc + 1], None,
                                           op0=ALU.add))
                v_(f)
                p_evacs.append(ve.n)
            prev_proj_ve = ve.n
        attn_done_ve = ve.n
    else:
        attn_done_ve = ve.n

    st_C.close()
    phase_barrier()

    st_BD = ExitStack()
    xn_ownT = st_BD.enter_context(nc.sbuf_tensor("xn_ownT", [128, 8 * TOK], BF16))

    if "B" in phases or "D" in phases:
        for bat in range(4):
            items = []
            for sub in range(4):
                tt = bat * 4 + sub
                k = tt % NLN
                def f(s, tt=tt, k=k):
                    s.wait_ge(ve.h, ln_norm_ve[k])
                    d_own[k].inc(s.dma_start(out=stg[k][:],
                                             in_=x_own[tt * 128:tt * 128 + 128, :]), 16)
                s_(f)
                items.append((k, (d_own[k].h, d_own[k].n), xn_ownT, tt * 128, TOK))
            emit_ln_tr_batch(items)
    xnownT_ready = ve.n

    st_D = ExitStack()
    gate_sb = st_D.enter_context(nc.sbuf_tensor("gate_sb", [128, 8 * NEXP], BF16))
    w_tm = st_D.enter_context(nc.sbuf_tensor("w_tm", [128, 16 * NEXP], F32))
    sgs = st_D.enter_context(nc.sbuf_tensor("sgs", [128, NEXP], F32))
    sgm = st_D.enter_context(nc.sbuf_tensor("sgm", [128, 1], F32))
    sgm2 = st_D.enter_context(nc.sbuf_tensor("sgm2", [128, 1], F32))
    sgtmp = st_D.enter_context(nc.sbuf_tensor("sgtmp", [128, NEXP], F32))
    wbcast = [st_D.enter_context(nc.sbuf_tensor(f"wbcast{i}", [128, TOK], BF16)) for i in range(2)]
    h_q = [st_D.enter_context(nc.sbuf_tensor(f"h_q{i}", [128, 4 * 512], BF16)) for i in range(2)]
    h32 = st_D.enter_context(nc.sbuf_tensor("h32", [128, 512], F32))
    hsil = [st_D.enter_context(nc.sbuf_tensor(f"hsil{i}", [128, 512], BF16)) for i in range(2)]
    wgb = [st_D.enter_context(nc.sbuf_tensor(f"wgb{i}", [128, 8 * 512], BF16)) for i in range(2)]
    wub = [st_D.enter_context(nc.sbuf_tensor(f"wub{i}", [128, 8 * 512], BF16)) for i in range(2)]
    wdb = [st_D.enter_context(nc.sbuf_tensor(f"wdb{i}", [128, 4 * DM], BF16)) for i in range(2)]

    if "D" in phases:
        def f(g):
            d_in.inc(_fmload(g, gate_sb, gate_w, 8), 16)
        g_(f)
        gate_loaded = d_in.n

        zg_done = []
        for tt in range(16):
            bank = pb[tt % 2]
            def f(t, tt=tt, bank=bank):
                t.wait_ge(d_in.h, gate_loaded)
                t.wait_ge(ve.h, xnownT_ready)
                if len(zg_done) >= 2:
                    t.wait_ge(ve.h, zg_done[tt - 2])
                for dc in range(8):
                    ins = t.matmul(bank[:, 0:NEXP],
                                   xn_ownT[:, dc * TOK + tt * 128:dc * TOK + tt * 128 + 128],
                                   gate_sb[:, dc * NEXP:dc * NEXP + NEXP],
                                   start=(dc == 0), stop=(dc == 7))
                pe.inc(ins)
            t_(f)
            def f(v, tt=tt, bank=bank):
                v.wait_ge(pe.h, pe.n)
                v.wait_ge(ve.h, ve.n)
                v.wait_ge(se.h, se.n)
                ve.inc(v.reduce_max(sgm[:], bank[:, 0:NEXP], axis=AX.X, negate=True))
            v_(f)
            def f(a, bank=bank):
                a.wait_ge(ve.h, ve.n)
                se.inc(a.activation(sgs[:], bank[:, 0:NEXP], AF.Exp, bias=sgm[:], scale=1.0))
            a_(f)
            def f(v, tt=tt):
                v.wait_ge(se.h, se.n)
                ve.inc(v.reduce_sum(sgm[:], sgs[:], axis=AX.X))
                v.wait_ge(ve.h, ve.n)
                ve.inc(v.reciprocal(sgm[:], sgm[:]))
                v.wait_ge(ve.h, ve.n)
                ve.inc(v.tensor_scalar(sgs[:], sgs[:], sgm[:], None, op0=ALU.mult))
                v.wait_ge(ve.h, ve.n)
                ve.inc(v.reduce_max(sgm[:], sgs[:], axis=AX.X))
                v.wait_ge(ve.h, ve.n)
                ve.inc(v.tensor_scalar(sgtmp[:], sgs[:], sgm[:], None, op0=ALU.is_equal))
                v.wait_ge(ve.h, ve.n)
                ve.inc(v.scalar_tensor_tensor(sgtmp[:], sgtmp[:], -1e30, sgs[:],
                                              op0=ALU.mult, op1=ALU.add))
                v.wait_ge(ve.h, ve.n)
                ve.inc(v.reduce_max(sgm2[:], sgtmp[:], axis=AX.X))
                v.wait_ge(ve.h, ve.n)
                ve.inc(v.tensor_scalar(sgtmp[:], sgs[:], sgm2[:], None, op0=ALU.is_ge))
                v.wait_ge(ve.h, ve.n)
                ve.inc(v.tensor_mul(sgtmp[:], sgtmp[:], sgs[:]))
                ve.inc(v.tensor_add(sgm[:], sgm[:], sgm2[:]))
                v.wait_ge(ve.h, ve.n)
                ve.inc(v.tensor_scalar(sgm[:], sgm[:], 1e-20, None, op0=ALU.add))
                v.wait_ge(ve.h, ve.n)
                ve.inc(v.reciprocal(sgm[:], sgm[:]))
                v.wait_ge(ve.h, ve.n)
                ve.inc(v.tensor_scalar(w_tm[:, tt * NEXP:tt * NEXP + NEXP], sgtmp[:],
                                       sgm[:], None, op0=ALU.mult))
            v_(f)
            zg_done.append(ve.n)

        st_WB = ExitStack()
        wT8 = st_WB.enter_context(nc.sbuf_tensor("wT8", [NEXP, TOK], BF16))
        wTflat = st_WB.enter_context(nc.sbuf_tensor("wTflat", [1, TOK], BF16))
        wbstage = st_WB.enter_context(nc.sbuf_tensor("wbstage", [128, TOK], BF16))

        for tt in range(16):
            def f(t, tt=tt):
                t.wait_ge(ve.h, ve.n)
                pe.inc(t.transpose(pb[2][0:NEXP, 0:128],
                                   w_tm[:, tt * NEXP:tt * NEXP + NEXP], identf[:]))
            t_(f)
            def f(v, tt=tt):
                v.wait_ge(pe.h, pe.n)
                ve.inc(v.tensor_copy(wT8[:, tt * 128:tt * 128 + 128], pb[2][0:NEXP, 0:128]))
            v_(f)
        wT8_ready = ve.n

        for e in range(NEXP):
            def f(g, e=e):
                g.wait_ge(ve.h, wT8_ready)
                g.wait_ge(pe.h, pe.n)
                d_wtf.inc(g.dma_start(out=wTflat[:], in_=wT8[e:e + 1, :]), 16)
            g_(f)
            wtf_done = d_wtf.n
            for tq in range(4):
                bank = pb[4 + tq % 2]
                def f(t, tq=tq, bank=bank, wtf_done=wtf_done):
                    t.wait_ge(d_wtf.h, wtf_done)
                    t.wait_ge(gp.h, 5)
                    t.wait_ge(ve.h, ve.n)
                    t.wait_ge(d_kv.h, d_kv.n)
                    pe.inc(t.matmul(bank[:, :], ones128b[:],
                                    wTflat[0:1, tq * 512:tq * 512 + 512],
                                    start=True, stop=True))
                t_(f)
                def f(v, tq=tq, bank=bank):
                    v.wait_ge(pe.h, pe.n)
                    ve.inc(v.tensor_copy(wbstage[:, tq * 512:tq * 512 + 512], bank[:, :]))
                v_(f)
            def f(g, e=e):
                g.wait_ge(ve.h, ve.n)
                d_kv.inc(g.dma_start(out=W_wb[e], in_=wbstage[:]), 16)
            g_(f)
        wb_built = d_kv.n

        st_WB.close()
        st_TB.close()
        st_Dp = ExitStack()
        pdown = st_Dp.enter_context(nc.psum_tensor("pdown", [128, 4, 512], F32))

        def wload(g, e, q):
            kbuf = (e * 4 + q) % 2
            if e < NEXP:
                srcs = (w_gate[e, :, 512 * q:512 * q + 512],
                        w_up[e, :, 512 * q:512 * q + 512],
                        w_down[e, 512 * q:512 * q + 512, :])
            else:
                srcs = (ws_gate[:, 512 * q:512 * q + 512],
                        ws_up[:, 512 * q:512 * q + 512],
                        ws_down[512 * q:512 * q + 512, :])
            d_ffn[kbuf].inc(_fmload(g, wgb[kbuf], srcs[0], 8), 16)
            d_ffn[kbuf].inc(_fmload(g, wub[kbuf], srcs[1], 8), 16)
            d_ffn[kbuf].inc(_fmload(g, wdb[kbuf], srcs[2], 4), 16)

        eq_list = [(e, q) for e in range(NEXP + 1) for q in range(4)]
        eq_loaded = {}
        eq_consumed = {}

        def f(g):
            for i in range(2):
                wload(g, *eq_list[i])
                eq_loaded[i] = d_ffn[i % 2].n
        g_(f)

        def f(s2):
            s2.wait_ge(d_kv.h, wb_built)
            d_wb[0].inc(s2.dma_start(out=wbcast[0][:], in_=W_wb[0]), 16)
        s_(f)
        wb_loaded = {0: d_wb[0].n}
        exp_mult_ve = {}
        c_gate, c_up, c_silu, c_m1, c_m2, c_down, c_add = ({} for _ in range(7))
        prev_add_ve = ve.n

        for i, (e, q) in enumerate(eq_list):
            if q == 0 and e + 1 < NEXP:
                def f(s2, e=e):
                    s2.wait_ge(d_kv.h, wb_built)
                    if e >= 1:
                        s2.wait_ge(ve.h, exp_mult_ve[e - 1])
                    d_wb[(e + 1) % 2].inc(
                        s2.dma_start(out=wbcast[(e + 1) % 2][:], in_=W_wb[e + 1]), 16)
                s_(f)
                wb_loaded[e + 1] = d_wb[(e + 1) % 2].n

            pe_c, se_c, ve_c = pe.n, se.n, ve.n
            for tq in range(4):
                for sl in range(4):
                    pe_c += 1; c_gate[(i, tq, sl)] = pe_c
                    pe_c += 1; c_up[(i, tq, sl)] = pe_c
                    se_c += 1; c_silu[(i, tq, sl)] = se_c
                    ve_c += 1; c_m1[(i, tq, sl)] = ve_c
                    ve_c += 1; c_m2[(i, tq, sl)] = ve_c
                for hh in range(2):
                    pe_c += 1; c_down[(i, tq, hh)] = pe_c
                    ve_c += 1; c_add[(i, tq, hh)] = ve_c

            def f(t, i=i, e=e, pav=prev_add_ve):
                t.wait_ge(d_ffn[i % 2].h, eq_loaded[i])
                if i == 0:
                    t.wait_ge(ve.h, xnownT_ready)
                for tq in range(4):
                    for sl in range(4):
                        kb2 = sl % 2
                        if (i, tq, sl - 2) in c_silu if sl >= 2 else False:
                            pass
                        if sl >= 2:
                            t.wait_ge(se.h, c_silu[(i, tq, sl - 2)])
                        elif tq >= 1:
                            t.wait_ge(se.h, c_silu[(i, tq - 1, sl + 2)])
                        elif i >= 1:
                            t.wait_ge(se.h, c_silu[(i - 1, 3, sl + 2)])
                        buf = wgb[i % 2]
                        for dc in range(8):
                            ins = t.matmul(pb[kb2][:, :],
                                           buf[:, dc * 512 + sl * 128:dc * 512 + sl * 128 + 128],
                                           xn_ownT[:, dc * TOK + tq * 512:dc * TOK + tq * 512 + 512],
                                           start=(dc == 0), stop=(dc == 7))
                        pe.inc(ins)
                        assert pe.n == c_gate[(i, tq, sl)]
                        if sl >= 2:
                            t.wait_ge(ve.h, c_m1[(i, tq, sl - 2)])
                        elif tq >= 1:
                            t.wait_ge(ve.h, c_m1[(i, tq - 1, sl + 2)])
                        elif i >= 1:
                            t.wait_ge(ve.h, c_m1[(i - 1, 3, sl + 2)])
                        buf = wub[i % 2]
                        for dc in range(8):
                            ins = t.matmul(pb[2 + kb2][:, :],
                                           buf[:, dc * 512 + sl * 128:dc * 512 + sl * 128 + 128],
                                           xn_ownT[:, dc * TOK + tq * 512:dc * TOK + tq * 512 + 512],
                                           start=(dc == 0), stop=(dc == 7))
                        pe.inc(ins)
                        assert pe.n == c_up[(i, tq, sl)]
                    for hh in range(2):
                        t.wait_ge(ve.h, c_m2[(i, tq, 3)])
                        if hh == 1:
                            t.wait_ge(ve.h, c_add[(i, tq, 0)])
                        elif tq >= 1:
                            t.wait_ge(ve.h, c_add[(i, tq - 1, 1)])
                        else:
                            t.wait_ge(ve.h, pav)
                        for sl in range(4):
                            for c in range(4):
                                ins = t.matmul(
                                    pdown[:, c, :],
                                    wdb[i % 2][:, sl * DM + (4 * hh + c) * 128:
                                               sl * DM + (4 * hh + c) * 128 + 128],
                                    h_q[tq % 2][:, sl * 512:sl * 512 + 512],
                                    start=(sl == 0), stop=(sl == 3))
                        pe.inc(ins)
                        assert pe.n == c_down[(i, tq, hh)]
            t_(f)

            def f(a, i=i):
                for tq in range(4):
                    for sl in range(4):
                        kb2 = sl % 2
                        a.wait_ge(pe.h, c_gate[(i, tq, sl)])
                        if sl >= 2:
                            a.wait_ge(ve.h, c_m1[(i, tq, sl - 2)])
                        elif tq >= 1:
                            a.wait_ge(ve.h, c_m1[(i, tq - 1, sl + 2)])
                        elif i >= 1:
                            a.wait_ge(ve.h, c_m1[(i - 1, 3, sl + 2)])
                        se.inc(a.activation(hsil[kb2][:], pb[kb2][:, :], AF.Silu,
                                            bias=0.0, scale=1.0))
                        assert se.n == c_silu[(i, tq, sl)]
            a_(f)

            def f(v, i=i, e=e):
                for tq in range(4):
                    tqp = tq % 2
                    for sl in range(4):
                        kb2 = sl % 2
                        v.wait_ge(se.h, c_silu[(i, tq, sl)])
                        v.wait_ge(pe.h, c_up[(i, tq, sl)])
                        v.wait_ge(ve.h, ve.n)
                        ve.inc(v.tensor_tensor(h32[:], hsil[kb2][:], pb[2 + kb2][:, :],
                                               op=ALU.mult))
                        assert ve.n == c_m1[(i, tq, sl)]
                        v.wait_ge(ve.h, ve.n)
                        if tq >= 2:
                            v.wait_ge(pe.h, c_down[(i, tq - 2, 1)])
                        elif i >= 1 and tq < 2:
                            v.wait_ge(pe.h, c_down[(i - 1, tq + 2, 1)])
                        if e < NEXP:
                            v.wait_ge(d_wb[e % 2].h, wb_loaded[e])
                            ve.inc(v.tensor_tensor(h_q[tqp][:, sl * 512:sl * 512 + 512],
                                                   h32[:], wbcast[e % 2][:, tq * 512:tq * 512 + 512],
                                                   op=ALU.mult))
                        else:
                            ve.inc(v.tensor_copy(h_q[tqp][:, sl * 512:sl * 512 + 512],
                                                 h32[:]))
                        assert ve.n == c_m2[(i, tq, sl)]
                    for hh in range(2):
                        v.wait_ge(pe.h, c_down[(i, tq, hh)])
                        accv = acc[:].rearrange("p (c t2) -> p c t2", c=8)[
                            :, 4 * hh:4 * hh + 4, tq * 512:tq * 512 + 512]
                        ve.inc(v.tensor_tensor(accv, accv, pdown[:, :, :], op=ALU.add))
                        assert ve.n == c_add[(i, tq, hh)]
            v_(f)

            prev_add_ve = c_add[(i, 3, 1)]
            exp_mult_ve[e] = c_m2[(i, 3, 3)]
            eq_consumed[i] = pe.n
            if i + 2 < len(eq_list):
                def f(g, i=i):
                    g.wait_ge(pe.h, eq_consumed[i])
                    wload(g, *eq_list[i + 2])
                    eq_loaded[i + 2] = d_ffn[i % 2].n
                g_(f)
        st_Dp.close()

    st_D.close()
    st_BD.close()

    ffn_done_ve = ve.n

    def f(g):
        g.wait_ge(ve.h, ffn_done_ve)
        d_acc.inc(g.dma_start(out=acc[:].rearrange("p (c t2) -> p c t2", c=8),
                              in_=xT_own.rearrange("(c p) t2 -> p c t2", p=128),
                              accum_op=ALU.add), 16)
    g_(f)

    def f(s):
        s.wait_ge(d_acc.h, d_acc.n)
        d_out.inc(s.dma_start(out=outT.rearrange("(c p) t2 -> p c t2", p=128),
                              in_=acc[:].rearrange("p (c t2) -> p c t2", c=8)), 16)
        s.wait_ge(d_out.h, d_out.n)
    s_(f)

    es.close()
    return nc


_CACHED = {}


def _get_program():
    if "nc" not in _CACHED:
        _CACHED["nc"] = build_program()
    return _CACHED["nc"]


def kernel(**inputs):
    x = np.asarray(inputs["L0_attn_x"], dtype=np.float32)[0]
    xT = np.ascontiguousarray(x.T)
    x_live = np.ascontiguousarray(x[:LIVE])

    base = {
        "x_liv": x_live,
        "mapping": np.ascontiguousarray(np.asarray(inputs["L0_attn_mapping"], dtype=np.int32)),
        "Wqkv": np.ascontiguousarray(np.asarray(inputs["L0_attn_Wqkv"], dtype=np.float32)),
        "bqkv": np.ascontiguousarray(np.asarray(inputs["L0_attn_bqkv"], dtype=np.float32)),
        "Wproj": np.ascontiguousarray(np.asarray(inputs["L0_attn_Wproj"], dtype=np.float32)),
        "bproj": np.ascontiguousarray(np.asarray(inputs["L0_attn_bproj"], dtype=np.float32)),
        "gate_w": np.ascontiguousarray(np.asarray(inputs["L0_ffn_gate_w"], dtype=np.float32)),
        "w_gate": np.ascontiguousarray(np.asarray(inputs["L0_ffn_w_gate"], dtype=np.float32)),
        "w_up": np.ascontiguousarray(np.asarray(inputs["L0_ffn_w_up"], dtype=np.float32)),
        "w_down": np.ascontiguousarray(np.asarray(inputs["L0_ffn_w_down"], dtype=np.float32)),
        "ws_gate": np.ascontiguousarray(np.asarray(inputs["L0_ffn_ws_gate"], dtype=np.float32)),
        "ws_up": np.ascontiguousarray(np.asarray(inputs["L0_ffn_ws_up"], dtype=np.float32)),
        "ws_down": np.ascontiguousarray(np.asarray(inputs["L0_ffn_ws_down"], dtype=np.float32)),
    }

    in_maps = []
    p = np.arange(128)[:, None]
    g = np.arange(16)[None, :]
    for c in range(NCORES):
        qp = np.minimum(c * TOK + 128 * g + p, LIVE - 1).astype(np.int32)
        m = dict(base)
        m["x_own"] = np.ascontiguousarray(x[c * TOK:(c + 1) * TOK])
        m["xT_own"] = np.ascontiguousarray(xT[:, c * TOK:(c + 1) * TOK])
        m["qpos"] = np.ascontiguousarray(qp)
        in_maps.append(m)

    nc = _get_program()
    res = run_bass_kernel_spmd(nc, in_maps, core_ids=list(range(NCORES)))
    out = np.empty((M, DM), dtype=np.float32)
    for c in range(NCORES):
        out[c * TOK:(c + 1) * TOK] = res.results[c]["outT"].T
    return out.reshape(1, M, DM)


# version 05
# speedup vs baseline: 1.5932x; 1.1589x over previous
import numpy as np
from contextlib import ExitStack

import concourse.bass as bass
import concourse.mybir as mybir
from concourse.bass_utils import run_bass_kernel_spmd
from concourse.bass import IndirectOffsetOnAxis

F32 = mybir.dt.float32
BF16 = mybir.dt.bfloat16
I32 = mybir.dt.int32
AF = mybir.ActivationFunctionType
ALU = mybir.AluOpType
AX = mybir.AxisListType

NCORES = 8
M = 16384
DM = 1024
LIVE = 4096
TOK = 2048
NH = 16
NKV = 1024
NSEG = 16
VSEGW = 16 * 65
DFF = 2048
NEXP = 8
NT = 4
TW = 512
NBLK = 8


class Sem:
    def __init__(self, es, nc, name):
        self.h = es.enter_context(nc.semaphore(name))
        self.n = 0

    def inc(self, instr, k=1):
        instr.then_inc(self.h, k)
        self.n += k
        return instr


def _fmload(eng, dst, src, nch):
    return eng.dma_start(out=dst[:].rearrange("p (c o) -> p c o", c=nch),
                         in_=src.rearrange("(c p) o -> p c o", p=128))


def build_program(phases=("A", "C", "B", "D", "E")):
    nc = bass.Bass("TRN2", target_bir_lowering=False, enable_asserts=False)

    x_liv = nc.dram_tensor("x_liv", [LIVE, DM], F32, kind="ExternalInput").ap()
    x_own = nc.dram_tensor("x_own", [TOK, DM], F32, kind="ExternalInput").ap()
    xT_own = nc.dram_tensor("xT_own", [DM, TOK], F32, kind="ExternalInput").ap()
    mapping = nc.dram_tensor("mapping", [LIVE], I32, kind="ExternalInput").ap()
    qpos = nc.dram_tensor("qpos", [128, 16], I32, kind="ExternalInput").ap()
    Wqkv = nc.dram_tensor("Wqkv", [DM, 3 * DM], F32, kind="ExternalInput").ap()
    bqkv = nc.dram_tensor("bqkv", [3 * DM], F32, kind="ExternalInput").ap()
    Wproj = nc.dram_tensor("Wproj", [DM, DM], F32, kind="ExternalInput").ap()
    bproj = nc.dram_tensor("bproj", [DM], F32, kind="ExternalInput").ap()
    gate_w = nc.dram_tensor("gate_w", [DM, NEXP], F32, kind="ExternalInput").ap()
    w_gate = nc.dram_tensor("w_gate", [NEXP, DM, DFF], F32, kind="ExternalInput").ap()
    w_up = nc.dram_tensor("w_up", [NEXP, DM, DFF], F32, kind="ExternalInput").ap()
    w_down = nc.dram_tensor("w_down", [NEXP, DFF, DM], F32, kind="ExternalInput").ap()
    ws_gate = nc.dram_tensor("ws_gate", [DM, DFF], F32, kind="ExternalInput").ap()
    ws_up = nc.dram_tensor("ws_up", [DM, DFF], F32, kind="ExternalInput").ap()
    ws_down = nc.dram_tensor("ws_down", [DFF, DM], F32, kind="ExternalInput").ap()
    outT = nc.dram_tensor("outT", [DM, TOK], F32, kind="ExternalOutput").ap()
    K_dram = nc.dram_tensor("K_dram", [128 * NSEG, 512], BF16).ap()
    V_dram = nc.dram_tensor("V_dram", [64 * NSEG, VSEGW], BF16).ap()
    W_wb = nc.dram_tensor("W_wb", [NEXP, 128, TOK], BF16).ap()

    es = ExitStack()

    def sbuf(name, shape, dt):
        return es.enter_context(nc.sbuf_tensor(name, shape, dt))

    identf = sbuf("identf", [128, 128], F32)
    identb = sbuf("identb", [128, 128], BF16)
    ones64f = sbuf("ones64f", [1, 64], F32)
    ones128f = sbuf("ones128f", [1, 128], F32)
    ones128b = sbuf("ones128b", [1, 128], BF16)
    iota16f = sbuf("iota16f", [128, 1], F32)
    qpos_sb = sbuf("qpos_sb", [128, 16], I32)
    kvidx = sbuf("kvidx", [128, 8], I32)
    qmap = sbuf("qmap", [128, 16], I32)
    segraw = sbuf("segraw", [1, 32], I32)
    segf = sbuf("segf", [1, 32], F32)
    idxK = sbuf("idxK", [128, 32], I32)
    bqkv_sb = sbuf("bqkv_sb", [128, 24], F32)
    bqs = sbuf("bqs", [128, 8], F32)
    bproj_sb = sbuf("bproj_sb", [128, 8], F32)
    acc = sbuf("acc", [128, 8 * TOK], F32)
    NLN = 4
    bn6 = [sbuf(f"bn6_{i}", [128, 12], F32) for i in range(NLN)]
    bn2 = [sbuf(f"bn2_{i}", [128, 2], F32) for i in range(NLN)]
    rstd = [sbuf(f"rstd{i}", [128, 1], F32) for i in range(NLN)]
    stg = [sbuf(f"stg{i}", [128, DM], F32) for i in range(NLN)]
    stgn = [sbuf(f"stgn{i}", [128, DM], BF16) for i in range(NLN)]

    pb = [es.enter_context(nc.psum_tensor(f"pb{i}", [128, 512], F32)) for i in range(4)]
    st_TB = ExitStack()
    pb += [st_TB.enter_context(nc.psum_tensor(f"pb{i}", [128, 512], F32)) for i in range(4, 6)]
    tb = [st_TB.enter_context(nc.psum_tensor(f"tb{i}", [128, 1024], BF16)) for i in range(2)]

    d_in = Sem(es, nc, "d_in")
    d_iny = Sem(es, nc, "d_iny")
    d_idx = Sem(es, nc, "d_idx")
    d_g = [Sem(es, nc, f"d_g{i}") for i in range(4)]
    d_kst = [Sem(es, nc, f"d_kst{i}") for i in range(2)]
    d_own = [Sem(es, nc, f"d_own{i}") for i in range(4)]
    d_kv = Sem(es, nc, "d_kv")
    d_wtf = Sem(es, nc, "d_wtf")
    d_wb = [Sem(es, nc, f"d_wb{i}") for i in range(2)]
    d_ffn = [Sem(es, nc, f"d_ffn{i}") for i in range(2)]
    d_bv = Sem(es, nc, "d_bv")
    d_acc = Sem(es, nc, "d_acc")
    d_out = Sem(es, nc, "d_out")
    pe = Sem(es, nc, "pe")
    ve = Sem(es, nc, "ve")
    se = Sem(es, nc, "se")
    gp = Sem(es, nc, "gp")

    blk = es.enter_context(nc.Block())
    g_, s_, t_, v_, a_ = blk.gpsimd, blk.sync, blk.tensor, blk.vector, blk.scalar

    def f(g):
        gp.inc(g.iota(identf[:], pattern=[[1, 128]], base=0, channel_multiplier=-1,
               allow_small_or_imprecise_dtypes=True))
        gp.inc(g.iota(iota16f[:], pattern=[[0, 1]], base=0, channel_multiplier=16,
               allow_small_or_imprecise_dtypes=True))
        gp.inc(g.memset(ones64f[:], 1.0))
        gp.inc(g.memset(ones128f[:], 1.0))
        gp.inc(g.memset(ones128b[:], 1.0))
        d_in.inc(g.dma_start(
            out=kvidx[:],
            in_=mapping.rearrange("(gi p f) -> p gi f", gi=8, p=128)[:, :, 0:1].squeeze(2),
            allow_slow_non_contiguous=True), 16)
    g_(f)

    def f(s):
        d_iny.inc(s.dma_start(out=qpos_sb[:], in_=qpos), 16)
        d_iny.inc(s.dma_start(out=bqkv_sb[:], in_=bqkv.rearrange("(c p) -> p c", p=128),
                              allow_slow_non_contiguous=True), 16)
        d_iny.inc(s.dma_start(out=bproj_sb[:], in_=bproj.rearrange("(c p) -> p c", p=128),
                              allow_slow_non_contiguous=True), 16)
    s_(f)
    setup_iny = d_iny.n
    setup_in = d_in.n

    def f(v):
        v.wait_ge(gp.h, 1)
        ve.inc(v.tensor_scalar(identf[:], identf[:], 0.0, None, op0=ALU.is_equal))
        v.wait_ge(ve.h, ve.n)
        ve.inc(v.tensor_copy(identb[:], identf[:]))
        v.wait_ge(d_iny.h, setup_iny)
        ve.inc(v.tensor_scalar(bqs[:], bqkv_sb[:, 0:8], 0.125, None, op0=ALU.mult))
    v_(f)
    ident_ready = ve.n

    def f(g):
        g.wait_ge(d_iny.h, setup_iny)
        for gi in range(16):
            d_idx.inc(g.indirect_dma_start(
                out=qmap[:, gi:gi + 1], out_offset=None,
                in_=mapping[None, :],
                in_offset=IndirectOffsetOnAxis(qpos_sb[:, gi:gi + 1], 1)), 16)
        g.wait_ge(d_idx.h, d_idx.n)
        d_idx.inc(g.dma_start(out=segraw[:], in_=qmap[0:128:64, 0:16]), 16)
    g_(f)
    qmap_done = d_idx.n

    def f(v):
        v.wait_ge(d_idx.h, d_idx.n)
        ve.inc(v.tensor_scalar(segraw[:], segraw[:], 8, None, op0=ALU.arith_shift_right))
        v.wait_ge(ve.h, ve.n)
        ve.inc(v.tensor_copy(segf[:], segraw[:]))
    v_(f)
    segf_ready = ve.n

    def f(t):
        t.wait_ge(ve.h, segf_ready)
        t.wait_ge(gp.h, 4)
        pe.inc(t.matmul(pb[0][:, 0:32], ones128f[:], segf[:], start=True, stop=True))
    t_(f)
    def f(v):
        v.wait_ge(pe.h, pe.n)
        ve.inc(v.tensor_scalar(pb[0][:, 0:32], pb[0][:, 0:32], 1.0, iota16f[:],
                               op0=ALU.mult, op1=ALU.add))
        v.wait_ge(ve.h, ve.n)
        ve.inc(v.tensor_copy(idxK[:], pb[0][:, 0:32]))
    v_(f)
    idx_ready = ve.n

    all_sems = [d_in, d_iny, d_idx, d_g[0], d_g[1], d_g[2], d_g[3], d_kst[0], d_kst[1],
                d_own[0], d_own[1], d_own[2], d_own[3], d_kv, d_bv, d_wtf, d_wb[0], d_wb[1], d_ffn[0], d_ffn[1],
                d_acc, d_out, pe, ve, se, gp]

    def phase_barrier():
        vals = [(sm.h, sm.n) for sm in all_sems if sm.n > 0]
        for section in (g_, s_, t_, v_, a_):
            def f(e, vals=vals):
                for h, n in vals:
                    e.wait_ge(h, n)
            section(f)

    def blkcol(b):
        return 16 * (b & 1) + (b >> 1)

    ln_sqrt_se = [0] * 4
    ln_norm_ve = [0] * 4
    tb_free = [0, 0]

    def emit_ln_tr_batch(items):
        n = len(items)
        ve_c, se_c, pe_c = ve.n, se.n, pe.n
        p_rec, p_sqrt, p_norm = {}, {}, {}
        for idx, (k, gw, dfm, dcol, cstr) in enumerate(items):
            ve_c += 5; p_rec[idx] = ve_c
        for idx in range(n):
            se_c += 1; p_sqrt[idx] = se_c
        for idx in range(n):
            ve_c += 1; p_norm[idx] = ve_c
        p_tp, p_tc = {}, {}
        for idx in range(n):
            for ch in range(8):
                pe_c += 1; p_tp[(idx, ch)] = pe_c
                ve_c += 1; p_tc[(idx, ch)] = ve_c

        def f(v):
            for idx, (k, gw, dfm, dcol, cstr) in enumerate(items):
                v.wait_ge(*gw)
                v.wait_ge(se.h, ln_sqrt_se[k])
                ve.inc(v.bn_stats(bn6[k][:, 0:6], stg[k][:, 0:512]))
                ve.inc(v.bn_stats(bn6[k][:, 6:12], stg[k][:, 512:1024]))
                v.wait_ge(ve.h, ve.n)
                ve.inc(v.bn_aggr(bn2[k][:], bn6[k][:]))
                v.wait_ge(ve.h, ve.n)
                ve.inc(v.tensor_scalar(rstd[k][:], bn2[k][:, 1:2], 1e-5, None, op0=ALU.add))
                v.wait_ge(ve.h, ve.n)
                ve.inc(v.reciprocal(rstd[k][:], rstd[k][:]))
                assert ve.n == p_rec[idx]
        v_(f)
        def f(a):
            for idx, (k, gw, dfm, dcol, cstr) in enumerate(items):
                a.wait_ge(ve.h, p_rec[idx])
                se.inc(a.activation(rstd[k][:], rstd[k][:], AF.Sqrt, bias=0.0, scale=1.0))
                assert se.n == p_sqrt[idx]
        a_(f)
        def f(v):
            for idx, (k, gw, dfm, dcol, cstr) in enumerate(items):
                v.wait_ge(se.h, p_sqrt[idx])
                ve.inc(v.tensor_scalar(stgn[k][:], stg[k][:], bn2[k][:, 0:1], rstd[k][:],
                                       op0=ALU.subtract, op1=ALU.mult))
                assert ve.n == p_norm[idx]
        v_(f)
        def f(t):
            for idx, (k, gw, dfm, dcol, cstr) in enumerate(items):
                for ch in range(8):
                    gi2 = idx * 8 + ch
                    if ch == 0:
                        t.wait_ge(ve.h, p_norm[idx])
                    if gi2 >= 2:
                        t.wait_ge(ve.h, p_tc[((gi2 - 2) // 8, (gi2 - 2) % 8)])
                    else:
                        t.wait_ge(ve.h, max(tb_free[gi2 % 2], ident_ready))
                    pe.inc(t.transpose(tb[gi2 % 2][0:128, 0:128],
                                       stgn[k][:, ch * 128:ch * 128 + 128], identb[:]))
                    assert pe.n == p_tp[(idx, ch)]
        t_(f)
        def f(v):
            for idx, (k, gw, dfm, dcol, cstr) in enumerate(items):
                for ch in range(8):
                    gi2 = idx * 8 + ch
                    v.wait_ge(pe.h, p_tp[(idx, ch)])
                    ve.inc(v.tensor_copy(dfm[:, dcol + ch * cstr:dcol + ch * cstr + 128],
                                         tb[gi2 % 2][0:128, 0:128]))
                    assert ve.n == p_tc[(idx, ch)]
        v_(f)
        for idx, (k, gw, dfm, dcol, cstr) in enumerate(items):
            ln_sqrt_se[k] = p_sqrt[idx]
            ln_norm_ve[k] = p_norm[idx]
        tb_free[0] = p_tc[(n - 1, 6)]
        tb_free[1] = p_tc[(n - 1, 7)]
        return ve.n

    st_A = ExitStack()
    wk_sb = st_A.enter_context(nc.sbuf_tensor("wk_sb", [128, 8 * DM], BF16))
    wv_sb = st_A.enter_context(nc.sbuf_tensor("wv_sb", [128, 8 * DM], BF16))
    xn_kvT = st_A.enter_context(nc.sbuf_tensor("xn_kvT", [128, 8 * NKV], BF16))
    bv_b = st_A.enter_context(nc.sbuf_tensor("bv_b", [64, DM], F32))
    bv_row = st_A.enter_context(nc.sbuf_tensor("bv_row", [1, DM], F32))
    K_T2 = st_A.enter_context(nc.sbuf_tensor("K_T2", [128, NSEG * 512], BF16))
    V_seg = st_A.enter_context(nc.sbuf_tensor("V_seg", [64, NSEG * VSEGW], BF16))

    if "A" in phases:
        def f(g):
            d_in.inc(_fmload(g, wk_sb, Wqkv[:, DM:2 * DM], 8), 16)
            d_in.inc(_fmload(g, wv_sb, Wqkv[:, 2 * DM:3 * DM], 8), 16)
            vs4 = V_seg[:].rearrange("p (s h d) -> p s h d", s=NSEG, h=16)
            gp.inc(g.memset(vs4[:, :, :, 64:65], 1.0))
        g_(f)
        wkwv_loaded = d_in.n
        vones_gp = gp.n

        def f(s2):
            d_bv.inc(s2.dma_start(out=bv_row[:], in_=bqkv[None, 2 * DM:3 * DM]), 16)
        s_(f)

        def f(t):
            t.wait_ge(d_bv.h, d_bv.n)
            t.wait_ge(gp.h, 4)
            pe.inc(t.matmul(pb[1][0:64, :], ones64f[:], bv_row[0:1, 0:512],
                            start=True, stop=True))
            pe.inc(t.matmul(pb[2][0:64, :], ones64f[:], bv_row[0:1, 512:1024],
                            start=True, stop=True))
        t_(f)
        def f(v):
            v.wait_ge(pe.h, pe.n)
            ve.inc(v.tensor_copy(bv_b[:, 0:512], pb[1][0:64, :]))
            ve.inc(v.tensor_copy(bv_b[:, 512:1024], pb[2][0:64, :]))
        v_(f)
        bvb_ready = ve.n

        for bat in range(2):
            items = []
            for sub in range(4):
                gi = bat * 4 + sub
                k = gi % NLN
                def f(g, gi=gi, k=k):
                    g.wait_ge(d_in.h, d_in.n)
                    g.wait_ge(ve.h, ln_norm_ve[k])
                    d_g[k].inc(g.indirect_dma_start(
                        out=stg[k][:], out_offset=None,
                        in_=x_liv,
                        in_offset=IndirectOffsetOnAxis(kvidx[:, gi:gi + 1], 0)), 16)
                g_(f)
                items.append((k, (d_g[k].h, d_g[k].n), xn_kvT, gi * 128, NKV))
            emit_ln_tr_batch(items)
        xnkvT_ready = ve.n

        k_evacs = []
        for kc in range(8):
            for half in range(2):
                i = kc * 2 + half
                bank = pb[i % 4]
                def f(t, kc=kc, half=half, bank=bank, i=i):
                    t.wait_ge(d_in.h, wkwv_loaded)
                    t.wait_ge(ve.h, max(xnkvT_ready, idx_ready))
                    if i >= 4:
                        t.wait_ge(ve.h, k_evacs[i - 4])
                    for dc in range(8):
                        ins = t.matmul(bank[:, :],
                                       wk_sb[:, dc * DM + kc * 128:dc * DM + kc * 128 + 128],
                                       xn_kvT[:, dc * NKV + half * 512:dc * NKV + half * 512 + 512],
                                       start=(dc == 0), stop=(dc == 7))
                    pe.inc(ins)
                t_(f)
                def f(v, kc=kc, half=half, bank=bank):
                    v.wait_ge(pe.h, pe.n)
                    dst = K_T2[:].rearrange("p (sg r) -> p sg r", sg=NSEG)[
                        :, half * 8:half * 8 + 8, kc * 64:kc * 64 + 64]
                    srcp = bank[:, :].rearrange("p (sg s) -> p sg s", sg=8)
                    ve.inc(v.tensor_scalar(dst, srcp, bqkv_sb[:, 8 + kc:9 + kc], None,
                                           op0=ALU.add))
                v_(f)
                k_evacs.append(ve.n)
        KT_done = ve.n

        v_evacs = []
        for sg in range(NSEG):
            for vh in range(2):
                i = sg * 2 + vh
                bank = pb[4 + (i % 2)]
                def f(t, sg=sg, vh=vh, bank=bank, i=i):
                    t.wait_ge(ve.h, max(xnkvT_ready, bvb_ready))
                    if i >= 2:
                        t.wait_ge(ve.h, v_evacs[i - 2])
                    for dc in range(8):
                        ins = t.matmul(bank[0:64, :],
                                       xn_kvT[:, dc * NKV + sg * 64:dc * NKV + sg * 64 + 64],
                                       wv_sb[:, dc * DM + vh * 512:dc * DM + vh * 512 + 512],
                                       start=(dc == 0), stop=(dc == 7))
                    pe.inc(ins)
                t_(f)
                def f(v, sg=sg, vh=vh, bank=bank):
                    v.wait_ge(pe.h, pe.n)
                    base = sg * VSEGW + vh * 520
                    dst = V_seg[0:64, base:base + 520].rearrange(
                        "p (h d) -> p h d", h=8)[:, :, 0:64]
                    srcv = bank[0:64, :].rearrange("p (h d) -> p h d", h=8)
                    ve.inc(v.scalar_tensor_tensor(
                        dst, srcv, 0.0,
                        bv_b[:, vh * 512:vh * 512 + 512].rearrange("p (h d) -> p h d", h=8),
                        op0=ALU.bypass, op1=ALU.add))
                v_(f)
                v_evacs.append(ve.n)
        Vseg_done = ve.n

        def f(g):
            g.wait_ge(ve.h, KT_done)
            d_kv.inc(g.dma_start(out=K_dram.rearrange("(p e) r -> p e r", p=128),
                                 in_=K_T2[:].rearrange("p (e r) -> p e r", e=NSEG)), 16)
            g.wait_ge(ve.h, Vseg_done)
            g.wait_ge(gp.h, vones_gp)
            d_kv.inc(g.dma_start(out=V_dram.rearrange("(p e) r -> p e r", p=64),
                                 in_=V_seg[:].rearrange("p (e r) -> p e r", e=NSEG)), 16)
        g_(f)
        kv_dram_done = d_kv.n
    else:
        kv_dram_done = d_kv.n
        def f(v):
            ve.inc(v.memset(acc[:], 0.0))
        v_(f)

    st_A.close()
    phase_barrier()

    st_C = ExitStack()
    wq_sb = st_C.enter_context(nc.sbuf_tensor("wq_sb", [128, 8 * DM], BF16))
    wp_sb = st_C.enter_context(nc.sbuf_tensor("wp_sb", [128, 8 * DM], BF16))
    xnqT = st_C.enter_context(nc.sbuf_tensor("xnqT", [128, 8 * TW], BF16))
    QT = st_C.enter_context(nc.sbuf_tensor("QT", [128, 8 * TW], BF16))
    attn_tm = st_C.enter_context(nc.sbuf_tensor("attn_tm", [64, 2 * DM], BF16))
    attn_fm = st_C.enter_context(nc.sbuf_tensor("attn_fm", [128, 8 * TW], BF16))
    Kst = [st_C.enter_context(nc.sbuf_tensor(f"Kst{i}", [128, NBLK * 512], BF16)) for i in range(2)]
    Vst = [st_C.enter_context(nc.sbuf_tensor(f"Vst{i}", [64, NBLK * VSEGW], BF16)) for i in range(2)]
    P_sb = [st_C.enter_context(nc.sbuf_tensor(f"P_sb{i}", [64, 64], BF16)) for i in range(3)]
    negC = [st_C.enter_context(nc.sbuf_tensor(f"negC{i}", [64, 1], F32)) for i in range(2)]
    aU = [st_C.enter_context(nc.sbuf_tensor(f"aU{i}", [64, VSEGW], BF16)) for i in range(2)]
    rec16 = st_C.enter_context(nc.sbuf_tensor("rec16", [64, 16], F32))

    if "C" in phases:
        def f(g):
            d_in.inc(_fmload(g, wq_sb, Wqkv[:, 0:DM], 8), 16)
            d_in.inc(_fmload(g, wp_sb, Wproj, 8), 16)
        g_(f)
        wqwp_loaded = d_in.n

        prev_proj_ve = ve.n
        pipe_pe_hist = [0, 0]
        for j in range(NT):
            kb = j % 2
            items = []
            for gsub in range(4):
                k = gsub % NLN
                gi = 4 * j + gsub
                def f(g, gi=gi, k=k):
                    g.wait_ge(d_idx.h, qmap_done)
                    g.wait_ge(ve.h, ln_norm_ve[k])
                    d_g[k].inc(g.indirect_dma_start(
                        out=stg[k][:], out_offset=None,
                        in_=x_liv,
                        in_offset=IndirectOffsetOnAxis(qmap[:, gi:gi + 1], 0)), 16)
                g_(f)
                items.append((k, (d_g[k].h, d_g[k].n), xnqT, gsub * 128, TW))
            def f(g, j=j, kb=kb, guard=pipe_pe_hist[0]):
                g.wait_ge(d_kv.h, kv_dram_done)
                g.wait_ge(ve.h, idx_ready)
                g.wait_ge(pe.h, guard)
                for bb in range(NBLK):
                    col = blkcol(8 * j + bb)
                    d_kst[kb].inc(g.indirect_dma_start(
                        out=Kst[kb][:, bb * 512:bb * 512 + 512], out_offset=None,
                        in_=K_dram,
                        in_offset=IndirectOffsetOnAxis(idxK[:, col:col + 1], 0)), 16)
                    d_kst[kb].inc(g.indirect_dma_start(
                        out=Vst[kb][0:64, bb * VSEGW:bb * VSEGW + VSEGW], out_offset=None,
                        in_=V_dram,
                        in_offset=IndirectOffsetOnAxis(idxK[0:64, col:col + 1], 0)), 16)
            g_(f)
            kv_st_done = d_kst[kb].n
            emit_ln_tr_batch(items)
            xnqT_ready = ve.n

            q_evacs = []
            for qc in range(8):
                bank = pb[qc % 2]
                def f(t, qc=qc, bank=bank):
                    t.wait_ge(ve.h, xnqT_ready)
                    t.wait_ge(d_in.h, wqwp_loaded)
                    if len(q_evacs) >= 2:
                        t.wait_ge(ve.h, q_evacs[qc - 2])
                    for dc in range(8):
                        ins = t.matmul(bank[:, :],
                                       wq_sb[:, dc * DM + qc * 128:dc * DM + qc * 128 + 128],
                                       xnqT[:, dc * TW:dc * TW + 512],
                                       start=(dc == 0), stop=(dc == 7))
                    pe.inc(ins)
                t_(f)
                def f(v, qc=qc, bank=bank):
                    v.wait_ge(pe.h, pe.n)
                    ve.inc(v.tensor_scalar(QT[:, qc * TW:qc * TW + 512], bank[:, :],
                                           bqkv_sb[:, qc:qc + 1], 0.125,
                                           op0=ALU.add, op1=ALU.mult))
                v_(f)
                q_evacs.append(ve.n)
            QT_ready = ve.n

            NIT = NBLK * NH
            c_sc, c_rd, c_ex, c_av, c_ea = ({} for _ in range(5))
            c_no = {}
            c_tp = {}
            c_tc = {}
            pe_c, ve_c, se_c = pe.n, ve.n, se.n
            NSLOT = NIT + 6 + 16
            for slot in range(NSLOT):
                if slot < NIT:
                    pe_c += 1; c_sc[slot] = pe_c
                if 0 <= slot - 1 < NIT:
                    ve_c += 1; c_rd[slot - 1] = ve_c
                if 0 <= slot - 2 < NIT:
                    se_c += 1; c_ex[slot - 2] = se_c
                if 0 <= slot - 3 < NIT:
                    pe_c += 1; c_av[slot - 3] = pe_c
                if 0 <= slot - 4 < NIT:
                    i4 = slot - 4
                    ve_c += 1; c_ea[i4] = ve_c
                    if i4 % NH == NH - 1:
                        ve_c += 3; c_no[i4 // NH] = ve_c
                if slot >= 5 and (slot - 5) % NH == NH - 1:
                    bb2 = (slot - 5) // NH
                    if 0 <= bb2 < NBLK:
                        for ch in range(8):
                            pe_c += 1; c_tp[(bb2, ch)] = pe_c
                            ve_c += 1; c_tc[(bb2, ch)] = ve_c
            prev_se = se.n

            def f(t, j=j, kb=kb, QT_ready=QT_ready, kv_st_done=kv_st_done,
                  ppv=prev_proj_ve, prev_se=prev_se, tbf=tuple(tb_free)):
                for slot in range(NSLOT):
                    i = slot
                    if i < NIT:
                        bb, h = divmod(i, NH)
                        if i == 0:
                            t.wait_ge(ve.h, QT_ready)
                            t.wait_ge(d_kst[kb].h, kv_st_done)
                            t.wait_ge(ve.h, ppv)
                            t.wait_ge(se.h, prev_se)
                        if i >= 2:
                            t.wait_ge(ve.h, c_rd[i - 2])
                            t.wait_ge(se.h, c_ex[i - 2])
                        lhs = Kst[kb][(h & 1) * 64:(h & 1) * 64 + 64,
                                      bb * 512 + (h >> 1) * 64:bb * 512 + (h >> 1) * 64 + 64]
                        rhs = QT[(h & 1) * 64:(h & 1) * 64 + 64,
                                 (h >> 1) * TW + 64 * bb:(h >> 1) * TW + 64 * bb + 64]
                        pe.inc(t.matmul(pb[2 + (i % 2)][0:64, 0:64], lhs, rhs,
                                        start=True, stop=True))
                        assert pe.n == c_sc[i]
                    i = slot - 3
                    if 0 <= i < NIT:
                        bb, h = divmod(i, NH)
                        t.wait_ge(se.h, c_ex[i])
                        if i >= 2:
                            t.wait_ge(ve.h, c_ea[i - 2])
                        rhs = Vst[kb][0:64, bb * VSEGW + 65 * h:bb * VSEGW + 65 * h + 65]
                        pe.inc(t.matmul(pb[(i % 2)][0:64, 0:65], P_sb[i % 3][:], rhs,
                                        start=True, stop=True))
                        assert pe.n == c_av[i]
                    if slot >= 5 and (slot - 5) % NH == NH - 1:
                        bb2 = (slot - 5) // NH
                        if 0 <= bb2 < NBLK:
                            for ch in range(8):
                                if ch == 0:
                                    t.wait_ge(ve.h, c_no[bb2])
                                if ch >= 2:
                                    t.wait_ge(ve.h, c_tc[(bb2, ch - 2)])
                                elif bb2 >= 1:
                                    t.wait_ge(ve.h, c_tc[(bb2 - 1, 6 + ch)])
                                elif ch < 2:
                                    t.wait_ge(ve.h, tbf[ch])
                                pe.inc(t.transpose(
                                    tb[ch % 2][0:128, 0:64],
                                    attn_tm[0:64, (bb2 % 2) * DM + ch * 128:
                                            (bb2 % 2) * DM + ch * 128 + 128],
                                    identb[0:64, 0:64]))
                                assert pe.n == c_tp[(bb2, ch)]
            t_(f)
            pipe_pe_hist = [pipe_pe_hist[1], pe.n]

            def f(v, kb=kb):
                for slot in range(NSLOT):
                    i = slot - 1
                    if 0 <= i < NIT:
                        v.wait_ge(pe.h, c_sc[i])
                        if i >= 2:
                            v.wait_ge(se.h, c_ex[i - 2])
                        ve.inc(v.reduce_max(negC[i % 2][:], pb[2 + (i % 2)][0:64, 0:64],
                                            axis=AX.X, negate=True))
                        assert ve.n == c_rd[i]
                    i = slot - 4
                    if 0 <= i < NIT:
                        bb, h = divmod(i, NH)
                        v.wait_ge(pe.h, c_av[i])
                        ve.inc(v.tensor_copy(aU[bb % 2][:, 65 * h:65 * h + 65],
                                             pb[(i % 2)][0:64, 0:65]))
                        assert ve.n == c_ea[i]
                        if h == NH - 1:
                            if bb >= 2:
                                v.wait_ge(pe.h, c_tp[(bb - 2, 7)])
                            v.wait_ge(ve.h, ve.n)
                            a3 = aU[bb % 2][:].rearrange("p (hh c) -> p hh c", hh=16)
                            ve.inc(v.tensor_scalar(rec16[:], a3[:, :, 64:65].squeeze(2),
                                                   1e-6, None, op0=ALU.add))
                            v.wait_ge(ve.h, ve.n)
                            ve.inc(v.reciprocal(rec16[:], rec16[:]))
                            v.wait_ge(ve.h, ve.n)
                            dstb = attn_tm[0:64, (bb % 2) * DM:(bb % 2) * DM + DM].rearrange(
                                "p (hh d) -> p hh d", hh=16)
                            ve.inc(v.tensor_tensor(
                                dstb, a3[:, :, 0:64],
                                rec16[:].unsqueeze(2).broadcast_to([64, 16, 64]),
                                op=ALU.mult))
                            assert ve.n == c_no[bb]
                    if slot >= 5 and (slot - 5) % NH == NH - 1:
                        bb2 = (slot - 5) // NH
                        if 0 <= bb2 < NBLK:
                            for ch in range(8):
                                v.wait_ge(pe.h, c_tp[(bb2, ch)])
                                ve.inc(v.tensor_copy(
                                    attn_fm[:, ch * TW + bb2 * 64:ch * TW + bb2 * 64 + 64],
                                    tb[ch % 2][0:128, 0:64]))
                                assert ve.n == c_tc[(bb2, ch)]
            v_(f)
            tb_free[0] = c_tc[(NBLK - 1, 6)]
            tb_free[1] = c_tc[(NBLK - 1, 7)]
            attn_fm_ready = ve.n

            def f(a):
                for slot in range(NSLOT):
                    i = slot - 2
                    if 0 <= i < NIT:
                        a.wait_ge(ve.h, c_rd[i])
                        if i >= 3:
                            a.wait_ge(pe.h, c_av[i - 3])
                        se.inc(a.activation(P_sb[i % 3][:], pb[2 + (i % 2)][0:64, 0:64],
                                            AF.Exp, bias=negC[i % 2][:], scale=1.0))
                        assert se.n == c_ex[i]
            a_(f)

            p_evacs = []
            for oc in range(8):
                bank = pb[2 + (oc % 2)]
                def f(t, oc=oc, bank=bank):
                    t.wait_ge(ve.h, attn_fm_ready)
                    if len(p_evacs) >= 2:
                        t.wait_ge(ve.h, p_evacs[oc - 2])
                    for fc in range(8):
                        ins = t.matmul(bank[:, :],
                                       wp_sb[:, fc * DM + oc * 128:fc * DM + oc * 128 + 128],
                                       attn_fm[:, fc * TW:fc * TW + 512],
                                       start=(fc == 0), stop=(fc == 7))
                    pe.inc(ins)
                t_(f)
                def f(v, oc=oc, bank=bank, j=j):
                    v.wait_ge(pe.h, pe.n)
                    ve.inc(v.tensor_scalar(acc[:, oc * TOK + j * TW:oc * TOK + j * TW + 512],
                                           bank[:, :], bproj_sb[:, oc:oc + 1], None,
                                           op0=ALU.add))
                v_(f)
                p_evacs.append(ve.n)
            prev_proj_ve = ve.n
        attn_done_ve = ve.n
    else:
        attn_done_ve = ve.n

    st_C.close()
    phase_barrier()

    st_BD = ExitStack()
    xn_ownT = st_BD.enter_context(nc.sbuf_tensor("xn_ownT", [128, 8 * TOK], BF16))

    if "B" in phases or "D" in phases:
        for bat in range(4):
            items = []
            for sub in range(4):
                tt = bat * 4 + sub
                k = tt % NLN
                def f(s, tt=tt, k=k):
                    s.wait_ge(ve.h, ln_norm_ve[k])
                    d_own[k].inc(s.dma_start(out=stg[k][:],
                                             in_=x_own[tt * 128:tt * 128 + 128, :]), 16)
                s_(f)
                items.append((k, (d_own[k].h, d_own[k].n), xn_ownT, tt * 128, TOK))
            emit_ln_tr_batch(items)
    xnownT_ready = ve.n

    st_D = ExitStack()
    gate_sb = st_D.enter_context(nc.sbuf_tensor("gate_sb", [128, 8 * NEXP], BF16))
    w_tm = st_D.enter_context(nc.sbuf_tensor("w_tm", [128, 16 * NEXP], F32))
    sgs = st_D.enter_context(nc.sbuf_tensor("sgs", [128, NEXP], F32))
    sgm = st_D.enter_context(nc.sbuf_tensor("sgm", [128, 1], F32))
    sgm2 = st_D.enter_context(nc.sbuf_tensor("sgm2", [128, 1], F32))
    sgtmp = st_D.enter_context(nc.sbuf_tensor("sgtmp", [128, NEXP], F32))
    wbcast = [st_D.enter_context(nc.sbuf_tensor(f"wbcast{i}", [128, TOK], BF16)) for i in range(2)]
    h_q = [st_D.enter_context(nc.sbuf_tensor(f"h_q{i}", [128, 4 * 512], BF16)) for i in range(2)]
    h32 = st_D.enter_context(nc.sbuf_tensor("h32", [128, 512], F32))
    hsil = [st_D.enter_context(nc.sbuf_tensor(f"hsil{i}", [128, 512], BF16)) for i in range(2)]
    wgb = [st_D.enter_context(nc.sbuf_tensor(f"wgb{i}", [128, 8 * 512], BF16)) for i in range(2)]
    wub = [st_D.enter_context(nc.sbuf_tensor(f"wub{i}", [128, 8 * 512], BF16)) for i in range(2)]
    wdb = [st_D.enter_context(nc.sbuf_tensor(f"wdb{i}", [128, 4 * DM], BF16)) for i in range(2)]

    if "D" in phases:
        def f(g):
            d_in.inc(_fmload(g, gate_sb, gate_w, 8), 16)
        g_(f)
        gate_loaded = d_in.n

        zg_done = []
        for tt in range(16):
            bank = pb[tt % 2]
            def f(t, tt=tt, bank=bank):
                t.wait_ge(d_in.h, gate_loaded)
                t.wait_ge(ve.h, xnownT_ready)
                if len(zg_done) >= 2:
                    t.wait_ge(ve.h, zg_done[tt - 2])
                for dc in range(8):
                    ins = t.matmul(bank[:, 0:NEXP],
                                   xn_ownT[:, dc * TOK + tt * 128:dc * TOK + tt * 128 + 128],
                                   gate_sb[:, dc * NEXP:dc * NEXP + NEXP],
                                   start=(dc == 0), stop=(dc == 7))
                pe.inc(ins)
            t_(f)
            def f(v, tt=tt, bank=bank):
                v.wait_ge(pe.h, pe.n)
                v.wait_ge(ve.h, ve.n)
                v.wait_ge(se.h, se.n)
                ve.inc(v.reduce_max(sgm[:], bank[:, 0:NEXP], axis=AX.X, negate=True))
            v_(f)
            def f(a, bank=bank):
                a.wait_ge(ve.h, ve.n)
                se.inc(a.activation(sgs[:], bank[:, 0:NEXP], AF.Exp, bias=sgm[:], scale=1.0))
            a_(f)
            def f(v, tt=tt):
                v.wait_ge(se.h, se.n)
                ve.inc(v.reduce_sum(sgm[:], sgs[:], axis=AX.X))
                v.wait_ge(ve.h, ve.n)
                ve.inc(v.reciprocal(sgm[:], sgm[:]))
                v.wait_ge(ve.h, ve.n)
                ve.inc(v.tensor_scalar(sgs[:], sgs[:], sgm[:], None, op0=ALU.mult))
                v.wait_ge(ve.h, ve.n)
                ve.inc(v.reduce_max(sgm[:], sgs[:], axis=AX.X))
                v.wait_ge(ve.h, ve.n)
                ve.inc(v.tensor_scalar(sgtmp[:], sgs[:], sgm[:], None, op0=ALU.is_equal))
                v.wait_ge(ve.h, ve.n)
                ve.inc(v.scalar_tensor_tensor(sgtmp[:], sgtmp[:], -1e30, sgs[:],
                                              op0=ALU.mult, op1=ALU.add))
                v.wait_ge(ve.h, ve.n)
                ve.inc(v.reduce_max(sgm2[:], sgtmp[:], axis=AX.X))
                v.wait_ge(ve.h, ve.n)
                ve.inc(v.tensor_scalar(sgtmp[:], sgs[:], sgm2[:], None, op0=ALU.is_ge))
                v.wait_ge(ve.h, ve.n)
                ve.inc(v.tensor_mul(sgtmp[:], sgtmp[:], sgs[:]))
                ve.inc(v.tensor_add(sgm[:], sgm[:], sgm2[:]))
                v.wait_ge(ve.h, ve.n)
                ve.inc(v.tensor_scalar(sgm[:], sgm[:], 1e-20, None, op0=ALU.add))
                v.wait_ge(ve.h, ve.n)
                ve.inc(v.reciprocal(sgm[:], sgm[:]))
                v.wait_ge(ve.h, ve.n)
                ve.inc(v.tensor_scalar(w_tm[:, tt * NEXP:tt * NEXP + NEXP], sgtmp[:],
                                       sgm[:], None, op0=ALU.mult))
            v_(f)
            zg_done.append(ve.n)

        st_WB = ExitStack()
        wT8 = st_WB.enter_context(nc.sbuf_tensor("wT8", [NEXP, TOK], BF16))
        wTflat = st_WB.enter_context(nc.sbuf_tensor("wTflat", [1, TOK], BF16))
        wbstage = st_WB.enter_context(nc.sbuf_tensor("wbstage", [128, TOK], BF16))

        for tt in range(16):
            def f(t, tt=tt):
                t.wait_ge(ve.h, ve.n)
                pe.inc(t.transpose(pb[2][0:NEXP, 0:128],
                                   w_tm[:, tt * NEXP:tt * NEXP + NEXP], identf[:]))
            t_(f)
            def f(v, tt=tt):
                v.wait_ge(pe.h, pe.n)
                ve.inc(v.tensor_copy(wT8[:, tt * 128:tt * 128 + 128], pb[2][0:NEXP, 0:128]))
            v_(f)
        wT8_ready = ve.n

        for e in range(NEXP):
            def f(g, e=e):
                g.wait_ge(ve.h, wT8_ready)
                g.wait_ge(pe.h, pe.n)
                d_wtf.inc(g.dma_start(out=wTflat[:], in_=wT8[e:e + 1, :]), 16)
            g_(f)
            wtf_done = d_wtf.n
            for tq in range(4):
                bank = pb[4 + tq % 2]
                def f(t, tq=tq, bank=bank, wtf_done=wtf_done):
                    t.wait_ge(d_wtf.h, wtf_done)
                    t.wait_ge(gp.h, 5)
                    t.wait_ge(ve.h, ve.n)
                    t.wait_ge(d_kv.h, d_kv.n)
                    pe.inc(t.matmul(bank[:, :], ones128b[:],
                                    wTflat[0:1, tq * 512:tq * 512 + 512],
                                    start=True, stop=True))
                t_(f)
                def f(v, tq=tq, bank=bank):
                    v.wait_ge(pe.h, pe.n)
                    ve.inc(v.tensor_copy(wbstage[:, tq * 512:tq * 512 + 512], bank[:, :]))
                v_(f)
            def f(g, e=e):
                g.wait_ge(ve.h, ve.n)
                d_kv.inc(g.dma_start(out=W_wb[e], in_=wbstage[:]), 16)
            g_(f)
        wb_built = d_kv.n

        st_WB.close()
        st_TB.close()
        st_Dp = ExitStack()
        pdown = st_Dp.enter_context(nc.psum_tensor("pdown", [128, 4, 512], F32))

        def wload(g, e, q):
            kbuf = (e * 4 + q) % 2
            if e < NEXP:
                srcs = (w_gate[e, :, 512 * q:512 * q + 512],
                        w_up[e, :, 512 * q:512 * q + 512],
                        w_down[e, 512 * q:512 * q + 512, :])
            else:
                srcs = (ws_gate[:, 512 * q:512 * q + 512],
                        ws_up[:, 512 * q:512 * q + 512],
                        ws_down[512 * q:512 * q + 512, :])
            d_ffn[kbuf].inc(_fmload(g, wgb[kbuf], srcs[0], 8), 16)
            d_ffn[kbuf].inc(_fmload(g, wub[kbuf], srcs[1], 8), 16)
            d_ffn[kbuf].inc(_fmload(g, wdb[kbuf], srcs[2], 4), 16)

        eq_list = [(e, q) for e in range(NEXP + 1) for q in range(4)]
        eq_loaded = {}
        eq_consumed = {}

        def f(g):
            for i in range(2):
                wload(g, *eq_list[i])
                eq_loaded[i] = d_ffn[i % 2].n
        g_(f)

        def f(s2):
            s2.wait_ge(d_kv.h, wb_built)
            d_wb[0].inc(s2.dma_start(out=wbcast[0][:], in_=W_wb[0]), 16)
        s_(f)
        wb_loaded = {0: d_wb[0].n}
        exp_mult_ve = {}
        c_gate, c_up, c_silu, c_m1, c_m2, c_down, c_add = ({} for _ in range(7))
        prev_add_ve = ve.n

        for i, (e, q) in enumerate(eq_list):
            if q == 0 and e + 1 < NEXP:
                def f(s2, e=e):
                    s2.wait_ge(d_kv.h, wb_built)
                    if e >= 1:
                        s2.wait_ge(ve.h, exp_mult_ve[e - 1])
                    d_wb[(e + 1) % 2].inc(
                        s2.dma_start(out=wbcast[(e + 1) % 2][:], in_=W_wb[e + 1]), 16)
                s_(f)
                wb_loaded[e + 1] = d_wb[(e + 1) % 2].n

            pe_c, se_c, ve_c = pe.n, se.n, ve.n
            PE_ORDER = ([("gu", 0, sl) for sl in range(4)]
                        + [("gu", 1, 0), ("gu", 1, 1), ("dn", 0, 0),
                           ("gu", 1, 2), ("gu", 1, 3), ("dn", 0, 1),
                           ("gu", 2, 0), ("gu", 2, 1), ("dn", 1, 0),
                           ("gu", 2, 2), ("gu", 2, 3), ("dn", 1, 1),
                           ("gu", 3, 0), ("gu", 3, 1), ("dn", 2, 0),
                           ("gu", 3, 2), ("gu", 3, 3), ("dn", 2, 1),
                           ("dn", 3, 0), ("dn", 3, 1)])
            for kind, a, b in PE_ORDER:
                if kind == "gu":
                    pe_c += 1; c_gate[(i, a, b)] = pe_c
                    pe_c += 1; c_up[(i, a, b)] = pe_c
                else:
                    pe_c += 1; c_down[(i, a, b)] = pe_c
            VE_ORDER = []
            for tq in range(4):
                for sl in range(4):
                    VE_ORDER.append(("m", tq, sl))
                    if tq >= 1 and sl == 1:
                        VE_ORDER.append(("a", tq - 1, 0))
                    if tq >= 1 and sl == 3:
                        VE_ORDER.append(("a", tq - 1, 1))
            VE_ORDER += [("a", 3, 0), ("a", 3, 1)]
            for tq in range(4):
                for sl in range(4):
                    se_c += 1; c_silu[(i, tq, sl)] = se_c
            for kind, a0, b0 in VE_ORDER:
                if kind == "m":
                    ve_c += 1; c_m1[(i, a0, b0)] = ve_c
                    ve_c += 1; c_m2[(i, a0, b0)] = ve_c
                else:
                    ve_c += 1; c_add[(i, a0, b0)] = ve_c

            def emit_gu(t, i, tq, sls):
                for sl in sls:
                    kb2 = sl % 2
                    if sl >= 2:
                        t.wait_ge(se.h, c_silu[(i, tq, sl - 2)])
                    elif tq >= 1:
                        t.wait_ge(se.h, c_silu[(i, tq - 1, sl + 2)])
                    elif i >= 1:
                        t.wait_ge(se.h, c_silu[(i - 1, 3, sl + 2)])
                    buf = wgb[i % 2]
                    for dc in range(8):
                        ins = t.matmul(pb[kb2][:, :],
                                       buf[:, dc * 512 + sl * 128:dc * 512 + sl * 128 + 128],
                                       xn_ownT[:, dc * TOK + tq * 512:dc * TOK + tq * 512 + 512],
                                       start=(dc == 0), stop=(dc == 7))
                    pe.inc(ins)
                    assert pe.n == c_gate[(i, tq, sl)]
                    if sl >= 2:
                        t.wait_ge(ve.h, c_m1[(i, tq, sl - 2)])
                    elif tq >= 1:
                        t.wait_ge(ve.h, c_m1[(i, tq - 1, sl + 2)])
                    elif i >= 1:
                        t.wait_ge(ve.h, c_m1[(i - 1, 3, sl + 2)])
                    buf = wub[i % 2]
                    for dc in range(8):
                        ins = t.matmul(pb[2 + kb2][:, :],
                                       buf[:, dc * 512 + sl * 128:dc * 512 + sl * 128 + 128],
                                       xn_ownT[:, dc * TOK + tq * 512:dc * TOK + tq * 512 + 512],
                                       start=(dc == 0), stop=(dc == 7))
                    pe.inc(ins)
                    assert pe.n == c_up[(i, tq, sl)]

            def emit_dn(t, i, tq, hh, pav):
                t.wait_ge(ve.h, c_m2[(i, tq, 3)])
                if hh == 1:
                    t.wait_ge(ve.h, c_add[(i, tq, 0)])
                elif tq >= 1:
                    t.wait_ge(ve.h, c_add[(i, tq - 1, 1)])
                else:
                    t.wait_ge(ve.h, pav)
                for sl in range(4):
                    for c in range(4):
                        ins = t.matmul(
                            pdown[:, c, :],
                            wdb[i % 2][:, sl * DM + (4 * hh + c) * 128:
                                       sl * DM + (4 * hh + c) * 128 + 128],
                            h_q[tq % 2][:, sl * 512:sl * 512 + 512],
                            start=(sl == 0), stop=(sl == 3))
                pe.inc(ins)
                assert pe.n == c_down[(i, tq, hh)]

            def f(t, i=i, e=e, pav=prev_add_ve):
                t.wait_ge(d_ffn[i % 2].h, eq_loaded[i])
                if i == 0:
                    t.wait_ge(ve.h, xnownT_ready)
                emit_gu(t, i, 0, [0, 1, 2, 3])
                emit_gu(t, i, 1, [0, 1])
                emit_dn(t, i, 0, 0, pav)
                emit_gu(t, i, 1, [2, 3])
                emit_dn(t, i, 0, 1, pav)
                emit_gu(t, i, 2, [0, 1])
                emit_dn(t, i, 1, 0, pav)
                emit_gu(t, i, 2, [2, 3])
                emit_dn(t, i, 1, 1, pav)
                emit_gu(t, i, 3, [0, 1])
                emit_dn(t, i, 2, 0, pav)
                emit_gu(t, i, 3, [2, 3])
                emit_dn(t, i, 2, 1, pav)
                emit_dn(t, i, 3, 0, pav)
                emit_dn(t, i, 3, 1, pav)
            t_(f)

            def f(a, i=i):
                for tq in range(4):
                    for sl in range(4):
                        kb2 = sl % 2
                        a.wait_ge(pe.h, c_gate[(i, tq, sl)])
                        if sl >= 2:
                            a.wait_ge(ve.h, c_m1[(i, tq, sl - 2)])
                        elif tq >= 1:
                            a.wait_ge(ve.h, c_m1[(i, tq - 1, sl + 2)])
                        elif i >= 1:
                            a.wait_ge(ve.h, c_m1[(i - 1, 3, sl + 2)])
                        se.inc(a.activation(hsil[kb2][:], pb[kb2][:, :], AF.Silu,
                                            bias=0.0, scale=1.0))
                        assert se.n == c_silu[(i, tq, sl)]
            a_(f)

            def f(v, i=i, e=e, VE_ORDER=VE_ORDER):
                for kind, a0, b0 in VE_ORDER:
                    if kind == "m":
                        tq, sl = a0, b0
                        tqp = tq % 2
                        kb2 = sl % 2
                        v.wait_ge(se.h, c_silu[(i, tq, sl)])
                        v.wait_ge(pe.h, c_up[(i, tq, sl)])
                        v.wait_ge(ve.h, ve.n)
                        ve.inc(v.tensor_tensor(h32[:], hsil[kb2][:], pb[2 + kb2][:, :],
                                               op=ALU.mult))
                        assert ve.n == c_m1[(i, tq, sl)]
                        v.wait_ge(ve.h, ve.n)
                        if tq >= 2:
                            v.wait_ge(pe.h, c_down[(i, tq - 2, 1)])
                        elif i >= 1 and tq < 2:
                            v.wait_ge(pe.h, c_down[(i - 1, tq + 2, 1)])
                        if e < NEXP:
                            v.wait_ge(d_wb[e % 2].h, wb_loaded[e])
                            ve.inc(v.tensor_tensor(h_q[tqp][:, sl * 512:sl * 512 + 512],
                                                   h32[:], wbcast[e % 2][:, tq * 512:tq * 512 + 512],
                                                   op=ALU.mult))
                        else:
                            ve.inc(v.tensor_copy(h_q[tqp][:, sl * 512:sl * 512 + 512],
                                                 h32[:]))
                        assert ve.n == c_m2[(i, tq, sl)]
                    else:
                        tq, hh = a0, b0
                        v.wait_ge(pe.h, c_down[(i, tq, hh)])
                        accv = acc[:].rearrange("p (c t2) -> p c t2", c=8)[
                            :, 4 * hh:4 * hh + 4, tq * 512:tq * 512 + 512]
                        ve.inc(v.tensor_tensor(accv, accv, pdown[:, :, :], op=ALU.add))
                        assert ve.n == c_add[(i, tq, hh)]
            v_(f)

            prev_add_ve = c_add[(i, 3, 1)]
            exp_mult_ve[e] = c_m2[(i, 3, 3)]
            eq_consumed[i] = pe.n
            if i + 2 < len(eq_list):
                def f(g, i=i):
                    g.wait_ge(pe.h, eq_consumed[i])
                    wload(g, *eq_list[i + 2])
                    eq_loaded[i + 2] = d_ffn[i % 2].n
                g_(f)
        st_Dp.close()

    st_D.close()
    st_BD.close()

    ffn_done_ve = ve.n

    def f(g):
        g.wait_ge(ve.h, ffn_done_ve)
        d_acc.inc(g.dma_start(out=acc[:].rearrange("p (c t2) -> p c t2", c=8),
                              in_=xT_own.rearrange("(c p) t2 -> p c t2", p=128),
                              accum_op=ALU.add), 16)
    g_(f)

    def f(s):
        s.wait_ge(d_acc.h, d_acc.n)
        d_out.inc(s.dma_start(out=outT.rearrange("(c p) t2 -> p c t2", p=128),
                              in_=acc[:].rearrange("p (c t2) -> p c t2", c=8)), 16)
        s.wait_ge(d_out.h, d_out.n)
    s_(f)

    es.close()
    return nc


_CACHED = {}


def _get_program():
    if "nc" not in _CACHED:
        _CACHED["nc"] = build_program()
    return _CACHED["nc"]


def kernel(**inputs):
    x = np.asarray(inputs["L0_attn_x"], dtype=np.float32)[0]
    xT = np.ascontiguousarray(x.T)
    x_live = np.ascontiguousarray(x[:LIVE])

    base = {
        "x_liv": x_live,
        "mapping": np.ascontiguousarray(np.asarray(inputs["L0_attn_mapping"], dtype=np.int32)),
        "Wqkv": np.ascontiguousarray(np.asarray(inputs["L0_attn_Wqkv"], dtype=np.float32)),
        "bqkv": np.ascontiguousarray(np.asarray(inputs["L0_attn_bqkv"], dtype=np.float32)),
        "Wproj": np.ascontiguousarray(np.asarray(inputs["L0_attn_Wproj"], dtype=np.float32)),
        "bproj": np.ascontiguousarray(np.asarray(inputs["L0_attn_bproj"], dtype=np.float32)),
        "gate_w": np.ascontiguousarray(np.asarray(inputs["L0_ffn_gate_w"], dtype=np.float32)),
        "w_gate": np.ascontiguousarray(np.asarray(inputs["L0_ffn_w_gate"], dtype=np.float32)),
        "w_up": np.ascontiguousarray(np.asarray(inputs["L0_ffn_w_up"], dtype=np.float32)),
        "w_down": np.ascontiguousarray(np.asarray(inputs["L0_ffn_w_down"], dtype=np.float32)),
        "ws_gate": np.ascontiguousarray(np.asarray(inputs["L0_ffn_ws_gate"], dtype=np.float32)),
        "ws_up": np.ascontiguousarray(np.asarray(inputs["L0_ffn_ws_up"], dtype=np.float32)),
        "ws_down": np.ascontiguousarray(np.asarray(inputs["L0_ffn_ws_down"], dtype=np.float32)),
    }

    in_maps = []
    p = np.arange(128)[:, None]
    g = np.arange(16)[None, :]
    for c in range(NCORES):
        qp = np.minimum(c * TOK + 128 * g + p, LIVE - 1).astype(np.int32)
        m = dict(base)
        m["x_own"] = np.ascontiguousarray(x[c * TOK:(c + 1) * TOK])
        m["xT_own"] = np.ascontiguousarray(xT[:, c * TOK:(c + 1) * TOK])
        m["qpos"] = np.ascontiguousarray(qp)
        in_maps.append(m)

    nc = _get_program()
    res = run_bass_kernel_spmd(nc, in_maps, core_ids=list(range(NCORES)))
    out = np.empty((M, DM), dtype=np.float32)
    for c in range(NCORES):
        out[c * TOK:(c + 1) * TOK] = res.results[c]["outT"].T
    return out.reshape(1, M, DM)


# version 06
# speedup vs baseline: 1.6365x; 1.0272x over previous
import numpy as np
from contextlib import ExitStack

import concourse.bass as bass
import concourse.mybir as mybir
from concourse.bass_utils import run_bass_kernel_spmd
from concourse.bass import IndirectOffsetOnAxis

F32 = mybir.dt.float32
BF16 = mybir.dt.bfloat16
I32 = mybir.dt.int32
AF = mybir.ActivationFunctionType
ALU = mybir.AluOpType
AX = mybir.AxisListType

NCORES = 8
M = 16384
DM = 1024
LIVE = 4096
TOK = 2048
NH = 16
NKV = 1024
NSEG = 16
VSEGW = 16 * 65
DFF = 2048
NEXP = 8
NT = 4
TW = 512
NBLK = 8


class Sem:
    def __init__(self, es, nc, name):
        self.h = es.enter_context(nc.semaphore(name))
        self.n = 0

    def inc(self, instr, k=1):
        instr.then_inc(self.h, k)
        self.n += k
        return instr


def _fmload(eng, dst, src, nch):
    return eng.dma_start(out=dst[:].rearrange("p (c o) -> p c o", c=nch),
                         in_=src.rearrange("(c p) o -> p c o", p=128))


def build_program(phases=("A", "C", "B", "D", "E")):
    nc = bass.Bass("TRN2", target_bir_lowering=False, enable_asserts=False)

    x_liv = nc.dram_tensor("x_liv", [LIVE, DM], F32, kind="ExternalInput").ap()
    x_own = nc.dram_tensor("x_own", [TOK, DM], F32, kind="ExternalInput").ap()
    xT_own = nc.dram_tensor("xT_own", [DM, TOK], F32, kind="ExternalInput").ap()
    mapping = nc.dram_tensor("mapping", [LIVE], I32, kind="ExternalInput").ap()
    qpos = nc.dram_tensor("qpos", [128, 16], I32, kind="ExternalInput").ap()
    Wqkv = nc.dram_tensor("Wqkv", [DM, 3 * DM], F32, kind="ExternalInput").ap()
    bqkv = nc.dram_tensor("bqkv", [3 * DM], F32, kind="ExternalInput").ap()
    Wproj = nc.dram_tensor("Wproj", [DM, DM], F32, kind="ExternalInput").ap()
    bproj = nc.dram_tensor("bproj", [DM], F32, kind="ExternalInput").ap()
    gate_w = nc.dram_tensor("gate_w", [DM, NEXP], F32, kind="ExternalInput").ap()
    w_gate = nc.dram_tensor("w_gate", [NEXP, DM, DFF], F32, kind="ExternalInput").ap()
    w_up = nc.dram_tensor("w_up", [NEXP, DM, DFF], F32, kind="ExternalInput").ap()
    w_down = nc.dram_tensor("w_down", [NEXP, DFF, DM], F32, kind="ExternalInput").ap()
    ws_gate = nc.dram_tensor("ws_gate", [DM, DFF], F32, kind="ExternalInput").ap()
    ws_up = nc.dram_tensor("ws_up", [DM, DFF], F32, kind="ExternalInput").ap()
    ws_down = nc.dram_tensor("ws_down", [DFF, DM], F32, kind="ExternalInput").ap()
    outT = nc.dram_tensor("outT", [DM, TOK], F32, kind="ExternalOutput").ap()
    K_dram = nc.dram_tensor("K_dram", [128 * NSEG, 512], BF16).ap()
    V_dram = nc.dram_tensor("V_dram", [64 * NSEG, VSEGW], BF16).ap()
    W_wb = nc.dram_tensor("W_wb", [NEXP, 128, TOK], BF16).ap()

    es = ExitStack()

    def sbuf(name, shape, dt):
        return es.enter_context(nc.sbuf_tensor(name, shape, dt))

    identf = sbuf("identf", [128, 128], F32)
    identb = sbuf("identb", [128, 128], BF16)
    ones64f = sbuf("ones64f", [1, 64], F32)
    ones128f = sbuf("ones128f", [1, 128], F32)
    ones128b = sbuf("ones128b", [1, 128], BF16)
    iota16f = sbuf("iota16f", [128, 1], F32)
    qpos_sb = sbuf("qpos_sb", [128, 16], I32)
    kvidx = sbuf("kvidx", [128, 8], I32)
    qmap = sbuf("qmap", [128, 16], I32)
    segraw = sbuf("segraw", [1, 32], I32)
    segf = sbuf("segf", [1, 32], F32)
    idxK = sbuf("idxK", [128, 32], I32)
    bqkv_sb = sbuf("bqkv_sb", [128, 24], F32)
    bqs = sbuf("bqs", [128, 8], F32)
    bproj_sb = sbuf("bproj_sb", [128, 8], F32)
    acc = sbuf("acc", [128, 8 * TOK], F32)
    NLN = 4
    bn6 = [sbuf(f"bn6_{i}", [128, 12], F32) for i in range(NLN)]
    bn2 = [sbuf(f"bn2_{i}", [128, 2], F32) for i in range(NLN)]
    rstd = [sbuf(f"rstd{i}", [128, 1], F32) for i in range(NLN)]
    stg = [sbuf(f"stg{i}", [128, DM], BF16) for i in range(NLN)]
    stgn = [sbuf(f"stgn{i}", [128, DM], BF16) for i in range(NLN)]

    pb = [es.enter_context(nc.psum_tensor(f"pb{i}", [128, 512], F32)) for i in range(4)]
    st_TB = ExitStack()
    pb += [st_TB.enter_context(nc.psum_tensor(f"pb{i}", [128, 512], F32)) for i in range(4, 6)]
    tb = [st_TB.enter_context(nc.psum_tensor(f"tb{i}", [128, 1024], BF16)) for i in range(2)]

    d_in = Sem(es, nc, "d_in")
    d_iny = Sem(es, nc, "d_iny")
    d_idx = Sem(es, nc, "d_idx")
    d_g = [Sem(es, nc, f"d_g{i}") for i in range(4)]
    d_kst = [Sem(es, nc, f"d_kst{i}") for i in range(2)]
    d_own = [Sem(es, nc, f"d_own{i}") for i in range(4)]
    d_kv = Sem(es, nc, "d_kv")
    d_wtf = [Sem(es, nc, f"d_wtf{i}") for i in range(2)]
    d_wb = [Sem(es, nc, f"d_wb{i}") for i in range(2)]
    d_ffn = [Sem(es, nc, f"d_ffn{i}") for i in range(2)]
    d_bv = Sem(es, nc, "d_bv")
    d_acc = Sem(es, nc, "d_acc")
    d_out = Sem(es, nc, "d_out")
    pe = Sem(es, nc, "pe")
    ve = Sem(es, nc, "ve")
    se = Sem(es, nc, "se")
    gp = Sem(es, nc, "gp")

    blk = es.enter_context(nc.Block())
    g_, s_, t_, v_, a_ = blk.gpsimd, blk.sync, blk.tensor, blk.vector, blk.scalar

    def f(g):
        gp.inc(g.iota(identf[:], pattern=[[1, 128]], base=0, channel_multiplier=-1,
               allow_small_or_imprecise_dtypes=True))
        gp.inc(g.iota(iota16f[:], pattern=[[0, 1]], base=0, channel_multiplier=16,
               allow_small_or_imprecise_dtypes=True))
        gp.inc(g.memset(ones64f[:], 1.0))
        gp.inc(g.memset(ones128f[:], 1.0))
        gp.inc(g.memset(ones128b[:], 1.0))
        d_in.inc(g.dma_start(
            out=kvidx[:],
            in_=mapping.rearrange("(gi p f) -> p gi f", gi=8, p=128)[:, :, 0:1].squeeze(2),
            allow_slow_non_contiguous=True), 16)
    g_(f)

    def f(s):
        d_iny.inc(s.dma_start(out=qpos_sb[:], in_=qpos), 16)
        d_iny.inc(s.dma_start(out=bqkv_sb[:], in_=bqkv.rearrange("(c p) -> p c", p=128),
                              allow_slow_non_contiguous=True), 16)
        d_iny.inc(s.dma_start(out=bproj_sb[:], in_=bproj.rearrange("(c p) -> p c", p=128),
                              allow_slow_non_contiguous=True), 16)
    s_(f)
    setup_iny = d_iny.n
    setup_in = d_in.n

    def f(v):
        v.wait_ge(gp.h, 1)
        ve.inc(v.tensor_scalar(identf[:], identf[:], 0.0, None, op0=ALU.is_equal))
        v.wait_ge(ve.h, ve.n)
        ve.inc(v.tensor_copy(identb[:], identf[:]))
        v.wait_ge(d_iny.h, setup_iny)
        ve.inc(v.tensor_scalar(bqs[:], bqkv_sb[:, 0:8], 0.125, None, op0=ALU.mult))
    v_(f)
    ident_ready = ve.n

    def f(g):
        g.wait_ge(d_iny.h, setup_iny)
        for gi in range(16):
            d_idx.inc(g.indirect_dma_start(
                out=qmap[:, gi:gi + 1], out_offset=None,
                in_=mapping[None, :],
                in_offset=IndirectOffsetOnAxis(qpos_sb[:, gi:gi + 1], 1)), 16)
        g.wait_ge(d_idx.h, d_idx.n)
        d_idx.inc(g.dma_start(out=segraw[:], in_=qmap[0:128:64, 0:16]), 16)
    g_(f)
    qmap_done = d_idx.n

    def f(v):
        v.wait_ge(d_idx.h, d_idx.n)
        ve.inc(v.tensor_scalar(segraw[:], segraw[:], 8, None, op0=ALU.arith_shift_right))
        v.wait_ge(ve.h, ve.n)
        ve.inc(v.tensor_copy(segf[:], segraw[:]))
    v_(f)
    segf_ready = ve.n

    def f(t):
        t.wait_ge(ve.h, segf_ready)
        t.wait_ge(gp.h, 4)
        pe.inc(t.matmul(pb[0][:, 0:32], ones128f[:], segf[:], start=True, stop=True))
    t_(f)
    def f(v):
        v.wait_ge(pe.h, pe.n)
        ve.inc(v.tensor_scalar(pb[0][:, 0:32], pb[0][:, 0:32], 1.0, iota16f[:],
                               op0=ALU.mult, op1=ALU.add))
        v.wait_ge(ve.h, ve.n)
        ve.inc(v.tensor_copy(idxK[:], pb[0][:, 0:32]))
    v_(f)
    idx_ready = ve.n

    all_sems = [d_in, d_iny, d_idx, d_g[0], d_g[1], d_g[2], d_g[3], d_kst[0], d_kst[1],
                d_own[0], d_own[1], d_own[2], d_own[3], d_kv, d_bv, d_wtf[0], d_wtf[1], d_wb[0], d_wb[1], d_ffn[0], d_ffn[1],
                d_acc, d_out, pe, ve, se, gp]

    def phase_barrier():
        vals = [(sm.h, sm.n) for sm in all_sems if sm.n > 0]
        for section in (g_, s_, t_, v_, a_):
            def f(e, vals=vals):
                for h, n in vals:
                    e.wait_ge(h, n)
            section(f)

    def blkcol(b):
        return 16 * (b & 1) + (b >> 1)

    ln_sqrt_se = [0] * 4
    ln_norm_ve = [0] * 4
    tb_free = [0, 0]

    def emit_ln_tr_batch(items):
        n = len(items)
        ve_c, se_c, pe_c = ve.n, se.n, pe.n
        p_rec, p_sqrt, p_norm = {}, {}, {}
        for idx, (k, gw, dfm, dcol, cstr) in enumerate(items):
            ve_c += 5; p_rec[idx] = ve_c
        for idx in range(n):
            se_c += 1; p_sqrt[idx] = se_c
        for idx in range(n):
            ve_c += 1; p_norm[idx] = ve_c
        p_tp, p_tc = {}, {}
        for idx in range(n):
            for ch in range(8):
                pe_c += 1; p_tp[(idx, ch)] = pe_c
                ve_c += 1; p_tc[(idx, ch)] = ve_c

        def f(v):
            for idx, (k, gw, dfm, dcol, cstr) in enumerate(items):
                v.wait_ge(*gw)
                v.wait_ge(se.h, ln_sqrt_se[k])
                ve.inc(v.bn_stats(bn6[k][:, 0:6], stg[k][:, 0:512]))
                ve.inc(v.bn_stats(bn6[k][:, 6:12], stg[k][:, 512:1024]))
                v.wait_ge(ve.h, ve.n)
                ve.inc(v.bn_aggr(bn2[k][:], bn6[k][:]))
                v.wait_ge(ve.h, ve.n)
                ve.inc(v.tensor_scalar(rstd[k][:], bn2[k][:, 1:2], 1e-5, None, op0=ALU.add))
                v.wait_ge(ve.h, ve.n)
                ve.inc(v.reciprocal(rstd[k][:], rstd[k][:]))
                assert ve.n == p_rec[idx]
        v_(f)
        def f(a):
            for idx, (k, gw, dfm, dcol, cstr) in enumerate(items):
                a.wait_ge(ve.h, p_rec[idx])
                se.inc(a.activation(rstd[k][:], rstd[k][:], AF.Sqrt, bias=0.0, scale=1.0))
                assert se.n == p_sqrt[idx]
        a_(f)
        def f(v):
            for idx, (k, gw, dfm, dcol, cstr) in enumerate(items):
                v.wait_ge(se.h, p_sqrt[idx])
                ve.inc(v.tensor_scalar(stgn[k][:], stg[k][:], bn2[k][:, 0:1], rstd[k][:],
                                       op0=ALU.subtract, op1=ALU.mult))
                assert ve.n == p_norm[idx]
        v_(f)
        def f(t):
            for idx, (k, gw, dfm, dcol, cstr) in enumerate(items):
                for ch in range(8):
                    gi2 = idx * 8 + ch
                    if ch == 0:
                        t.wait_ge(ve.h, p_norm[idx])
                    if gi2 >= 2:
                        t.wait_ge(ve.h, p_tc[((gi2 - 2) // 8, (gi2 - 2) % 8)])
                    else:
                        t.wait_ge(ve.h, max(tb_free[gi2 % 2], ident_ready))
                    pe.inc(t.transpose(tb[gi2 % 2][0:128, 0:128],
                                       stgn[k][:, ch * 128:ch * 128 + 128], identb[:]))
                    assert pe.n == p_tp[(idx, ch)]
        t_(f)
        def f(v):
            for idx, (k, gw, dfm, dcol, cstr) in enumerate(items):
                for ch in range(8):
                    gi2 = idx * 8 + ch
                    v.wait_ge(pe.h, p_tp[(idx, ch)])
                    ve.inc(v.tensor_copy(dfm[:, dcol + ch * cstr:dcol + ch * cstr + 128],
                                         tb[gi2 % 2][0:128, 0:128]))
                    assert ve.n == p_tc[(idx, ch)]
        v_(f)
        for idx, (k, gw, dfm, dcol, cstr) in enumerate(items):
            ln_sqrt_se[k] = p_sqrt[idx]
            ln_norm_ve[k] = p_norm[idx]
        tb_free[0] = p_tc[(n - 1, 6)]
        tb_free[1] = p_tc[(n - 1, 7)]
        return ve.n

    st_A = ExitStack()
    wk_sb = st_A.enter_context(nc.sbuf_tensor("wk_sb", [128, 8 * DM], BF16))
    wv_sb = st_A.enter_context(nc.sbuf_tensor("wv_sb", [128, 8 * DM], BF16))
    xn_kvT = st_A.enter_context(nc.sbuf_tensor("xn_kvT", [128, 8 * NKV], BF16))
    bv_b = st_A.enter_context(nc.sbuf_tensor("bv_b", [64, DM], F32))
    bv_row = st_A.enter_context(nc.sbuf_tensor("bv_row", [1, DM], F32))
    K_T2 = st_A.enter_context(nc.sbuf_tensor("K_T2", [128, NSEG * 512], BF16))
    V_seg = st_A.enter_context(nc.sbuf_tensor("V_seg", [64, NSEG * VSEGW], BF16))

    if "A" in phases:
        def f(g):
            d_in.inc(_fmload(g, wk_sb, Wqkv[:, DM:2 * DM], 8), 16)
            d_in.inc(_fmload(g, wv_sb, Wqkv[:, 2 * DM:3 * DM], 8), 16)
            vs4 = V_seg[:].rearrange("p (s h d) -> p s h d", s=NSEG, h=16)
            gp.inc(g.memset(vs4[:, :, :, 64:65], 1.0))
        g_(f)
        wkwv_loaded = d_in.n
        vones_gp = gp.n

        def f(s2):
            d_bv.inc(s2.dma_start(out=bv_row[:], in_=bqkv[None, 2 * DM:3 * DM]), 16)
        s_(f)

        def f(t):
            t.wait_ge(d_bv.h, d_bv.n)
            t.wait_ge(gp.h, 4)
            pe.inc(t.matmul(pb[1][0:64, :], ones64f[:], bv_row[0:1, 0:512],
                            start=True, stop=True))
            pe.inc(t.matmul(pb[2][0:64, :], ones64f[:], bv_row[0:1, 512:1024],
                            start=True, stop=True))
        t_(f)
        def f(v):
            v.wait_ge(pe.h, pe.n)
            ve.inc(v.tensor_copy(bv_b[:, 0:512], pb[1][0:64, :]))
            ve.inc(v.tensor_copy(bv_b[:, 512:1024], pb[2][0:64, :]))
        v_(f)
        bvb_ready = ve.n

        for bat in range(2):
            items = []
            for sub in range(4):
                gi = bat * 4 + sub
                k = gi % NLN
                def f(g, gi=gi, k=k):
                    g.wait_ge(d_in.h, d_in.n)
                    g.wait_ge(ve.h, ln_norm_ve[k])
                    d_g[k].inc(g.indirect_dma_start(
                        out=stg[k][:], out_offset=None,
                        in_=x_liv,
                        in_offset=IndirectOffsetOnAxis(kvidx[:, gi:gi + 1], 0)), 16)
                g_(f)
                items.append((k, (d_g[k].h, d_g[k].n), xn_kvT, gi * 128, NKV))
            emit_ln_tr_batch(items)
        xnkvT_ready = ve.n

        k_evacs = []
        for kc in range(8):
            for half in range(2):
                i = kc * 2 + half
                bank = pb[i % 4]
                def f(t, kc=kc, half=half, bank=bank, i=i):
                    t.wait_ge(d_in.h, wkwv_loaded)
                    t.wait_ge(ve.h, max(xnkvT_ready, idx_ready))
                    if i >= 4:
                        t.wait_ge(ve.h, k_evacs[i - 4])
                    for dc in range(8):
                        ins = t.matmul(bank[:, :],
                                       wk_sb[:, dc * DM + kc * 128:dc * DM + kc * 128 + 128],
                                       xn_kvT[:, dc * NKV + half * 512:dc * NKV + half * 512 + 512],
                                       start=(dc == 0), stop=(dc == 7))
                    pe.inc(ins)
                t_(f)
                def f(v, kc=kc, half=half, bank=bank):
                    v.wait_ge(pe.h, pe.n)
                    dst = K_T2[:].rearrange("p (sg r) -> p sg r", sg=NSEG)[
                        :, half * 8:half * 8 + 8, kc * 64:kc * 64 + 64]
                    srcp = bank[:, :].rearrange("p (sg s) -> p sg s", sg=8)
                    ve.inc(v.tensor_scalar(dst, srcp, bqkv_sb[:, 8 + kc:9 + kc], None,
                                           op0=ALU.add))
                v_(f)
                k_evacs.append(ve.n)
        KT_done = ve.n

        v_evacs = []
        for sg in range(NSEG):
            for vh in range(2):
                i = sg * 2 + vh
                bank = pb[4 + (i % 2)]
                def f(t, sg=sg, vh=vh, bank=bank, i=i):
                    t.wait_ge(ve.h, max(xnkvT_ready, bvb_ready))
                    if i >= 2:
                        t.wait_ge(ve.h, v_evacs[i - 2])
                    for dc in range(8):
                        ins = t.matmul(bank[0:64, :],
                                       xn_kvT[:, dc * NKV + sg * 64:dc * NKV + sg * 64 + 64],
                                       wv_sb[:, dc * DM + vh * 512:dc * DM + vh * 512 + 512],
                                       start=(dc == 0), stop=(dc == 7))
                    pe.inc(ins)
                t_(f)
                def f(v, sg=sg, vh=vh, bank=bank):
                    v.wait_ge(pe.h, pe.n)
                    base = sg * VSEGW + vh * 520
                    dst = V_seg[0:64, base:base + 520].rearrange(
                        "p (h d) -> p h d", h=8)[:, :, 0:64]
                    srcv = bank[0:64, :].rearrange("p (h d) -> p h d", h=8)
                    ve.inc(v.scalar_tensor_tensor(
                        dst, srcv, 0.0,
                        bv_b[:, vh * 512:vh * 512 + 512].rearrange("p (h d) -> p h d", h=8),
                        op0=ALU.bypass, op1=ALU.add))
                v_(f)
                v_evacs.append(ve.n)
        Vseg_done = ve.n

        def f(g):
            g.wait_ge(ve.h, KT_done)
            d_kv.inc(g.dma_start(out=K_dram.rearrange("(p e) r -> p e r", p=128),
                                 in_=K_T2[:].rearrange("p (e r) -> p e r", e=NSEG)), 16)
            g.wait_ge(ve.h, Vseg_done)
            g.wait_ge(gp.h, vones_gp)
            d_kv.inc(g.dma_start(out=V_dram.rearrange("(p e) r -> p e r", p=64),
                                 in_=V_seg[:].rearrange("p (e r) -> p e r", e=NSEG)), 16)
        g_(f)
        kv_dram_done = d_kv.n
    else:
        kv_dram_done = d_kv.n
        def f(v):
            ve.inc(v.memset(acc[:], 0.0))
        v_(f)

    st_A.close()
    phase_barrier()

    st_C = ExitStack()
    wq_sb = st_C.enter_context(nc.sbuf_tensor("wq_sb", [128, 8 * DM], BF16))
    wp_sb = st_C.enter_context(nc.sbuf_tensor("wp_sb", [128, 8 * DM], BF16))
    xnqT = st_C.enter_context(nc.sbuf_tensor("xnqT", [128, 8 * TW], BF16))
    QT = st_C.enter_context(nc.sbuf_tensor("QT", [128, 8 * TW], BF16))
    attn_tm = st_C.enter_context(nc.sbuf_tensor("attn_tm", [64, 2 * DM], BF16))
    attn_fm = st_C.enter_context(nc.sbuf_tensor("attn_fm", [128, 8 * TW], BF16))
    Kst = [st_C.enter_context(nc.sbuf_tensor(f"Kst{i}", [128, NBLK * 512], BF16)) for i in range(2)]
    Vst = [st_C.enter_context(nc.sbuf_tensor(f"Vst{i}", [64, NBLK * VSEGW], BF16)) for i in range(2)]
    P_sb = [st_C.enter_context(nc.sbuf_tensor(f"P_sb{i}", [64, 64], BF16)) for i in range(3)]
    negC = [st_C.enter_context(nc.sbuf_tensor(f"negC{i}", [64, 1], F32)) for i in range(2)]
    aU = [st_C.enter_context(nc.sbuf_tensor(f"aU{i}", [64, VSEGW], BF16)) for i in range(2)]
    rec16 = st_C.enter_context(nc.sbuf_tensor("rec16", [64, 16], F32))

    if "C" in phases:
        def f(g):
            d_in.inc(_fmload(g, wq_sb, Wqkv[:, 0:DM], 8), 16)
            d_in.inc(_fmload(g, wp_sb, Wproj, 8), 16)
        g_(f)
        wqwp_loaded = d_in.n

        prev_proj_ve = ve.n
        pipe_pe_hist = [0, 0]
        for j in range(NT):
            kb = j % 2
            items = []
            for gsub in range(4):
                k = gsub % NLN
                gi = 4 * j + gsub
                def f(g, gi=gi, k=k):
                    g.wait_ge(d_idx.h, qmap_done)
                    g.wait_ge(ve.h, ln_norm_ve[k])
                    d_g[k].inc(g.indirect_dma_start(
                        out=stg[k][:], out_offset=None,
                        in_=x_liv,
                        in_offset=IndirectOffsetOnAxis(qmap[:, gi:gi + 1], 0)), 16)
                g_(f)
                items.append((k, (d_g[k].h, d_g[k].n), xnqT, gsub * 128, TW))
            def f(g, j=j, kb=kb, guard=pipe_pe_hist[0]):
                g.wait_ge(d_kv.h, kv_dram_done)
                g.wait_ge(ve.h, idx_ready)
                g.wait_ge(pe.h, guard)
                for bb in range(NBLK):
                    col = blkcol(8 * j + bb)
                    d_kst[kb].inc(g.indirect_dma_start(
                        out=Kst[kb][:, bb * 512:bb * 512 + 512], out_offset=None,
                        in_=K_dram,
                        in_offset=IndirectOffsetOnAxis(idxK[:, col:col + 1], 0)), 16)
                    d_kst[kb].inc(g.indirect_dma_start(
                        out=Vst[kb][0:64, bb * VSEGW:bb * VSEGW + VSEGW], out_offset=None,
                        in_=V_dram,
                        in_offset=IndirectOffsetOnAxis(idxK[0:64, col:col + 1], 0)), 16)
            g_(f)
            kv_st_done = d_kst[kb].n
            emit_ln_tr_batch(items)
            xnqT_ready = ve.n

            q_evacs = []
            for qc in range(8):
                bank = pb[qc % 2]
                def f(t, qc=qc, bank=bank):
                    t.wait_ge(ve.h, xnqT_ready)
                    t.wait_ge(d_in.h, wqwp_loaded)
                    if len(q_evacs) >= 2:
                        t.wait_ge(ve.h, q_evacs[qc - 2])
                    for dc in range(8):
                        ins = t.matmul(bank[:, :],
                                       wq_sb[:, dc * DM + qc * 128:dc * DM + qc * 128 + 128],
                                       xnqT[:, dc * TW:dc * TW + 512],
                                       start=(dc == 0), stop=(dc == 7))
                    pe.inc(ins)
                t_(f)
                def f(v, qc=qc, bank=bank):
                    v.wait_ge(pe.h, pe.n)
                    ve.inc(v.tensor_scalar(QT[:, qc * TW:qc * TW + 512], bank[:, :],
                                           bqkv_sb[:, qc:qc + 1], 0.125,
                                           op0=ALU.add, op1=ALU.mult))
                v_(f)
                q_evacs.append(ve.n)
            QT_ready = ve.n

            NIT = NBLK * NH
            c_sc, c_rd, c_ex, c_av, c_ea = ({} for _ in range(5))
            c_no = {}
            c_tp = {}
            c_tc = {}
            pe_c, ve_c, se_c = pe.n, ve.n, se.n
            NSLOT = NIT + 6 + 16
            for slot in range(NSLOT):
                if slot < NIT:
                    pe_c += 1; c_sc[slot] = pe_c
                if 0 <= slot - 1 < NIT:
                    ve_c += 1; c_rd[slot - 1] = ve_c
                if 0 <= slot - 2 < NIT:
                    se_c += 1; c_ex[slot - 2] = se_c
                if 0 <= slot - 3 < NIT:
                    pe_c += 1; c_av[slot - 3] = pe_c
                if 0 <= slot - 4 < NIT:
                    i4 = slot - 4
                    ve_c += 1; c_ea[i4] = ve_c
                    if i4 % NH == NH - 1:
                        ve_c += 3; c_no[i4 // NH] = ve_c
                if slot >= 5 and (slot - 5) % NH == NH - 1:
                    bb2 = (slot - 5) // NH
                    if 0 <= bb2 < NBLK:
                        for ch in range(8):
                            pe_c += 1; c_tp[(bb2, ch)] = pe_c
                            ve_c += 1; c_tc[(bb2, ch)] = ve_c
            prev_se = se.n

            def f(t, j=j, kb=kb, QT_ready=QT_ready, kv_st_done=kv_st_done,
                  ppv=prev_proj_ve, prev_se=prev_se, tbf=tuple(tb_free)):
                for slot in range(NSLOT):
                    i = slot
                    if i < NIT:
                        bb, h = divmod(i, NH)
                        if i == 0:
                            t.wait_ge(ve.h, QT_ready)
                            t.wait_ge(d_kst[kb].h, kv_st_done)
                            t.wait_ge(ve.h, ppv)
                            t.wait_ge(se.h, prev_se)
                        if i >= 2:
                            t.wait_ge(ve.h, c_rd[i - 2])
                            t.wait_ge(se.h, c_ex[i - 2])
                        lhs = Kst[kb][(h & 1) * 64:(h & 1) * 64 + 64,
                                      bb * 512 + (h >> 1) * 64:bb * 512 + (h >> 1) * 64 + 64]
                        rhs = QT[(h & 1) * 64:(h & 1) * 64 + 64,
                                 (h >> 1) * TW + 64 * bb:(h >> 1) * TW + 64 * bb + 64]
                        pe.inc(t.matmul(pb[2 + (i % 2)][0:64, 0:64], lhs, rhs,
                                        start=True, stop=True))
                        assert pe.n == c_sc[i]
                    i = slot - 3
                    if 0 <= i < NIT:
                        bb, h = divmod(i, NH)
                        t.wait_ge(se.h, c_ex[i])
                        if i >= 2:
                            t.wait_ge(ve.h, c_ea[i - 2])
                        rhs = Vst[kb][0:64, bb * VSEGW + 65 * h:bb * VSEGW + 65 * h + 65]
                        pe.inc(t.matmul(pb[(i % 2)][0:64, 0:65], P_sb[i % 3][:], rhs,
                                        start=True, stop=True))
                        assert pe.n == c_av[i]
                    if slot >= 5 and (slot - 5) % NH == NH - 1:
                        bb2 = (slot - 5) // NH
                        if 0 <= bb2 < NBLK:
                            for ch in range(8):
                                if ch == 0:
                                    t.wait_ge(ve.h, c_no[bb2])
                                if ch >= 2:
                                    t.wait_ge(ve.h, c_tc[(bb2, ch - 2)])
                                elif bb2 >= 1:
                                    t.wait_ge(ve.h, c_tc[(bb2 - 1, 6 + ch)])
                                elif ch < 2:
                                    t.wait_ge(ve.h, tbf[ch])
                                pe.inc(t.transpose(
                                    tb[ch % 2][0:128, 0:64],
                                    attn_tm[0:64, (bb2 % 2) * DM + ch * 128:
                                            (bb2 % 2) * DM + ch * 128 + 128],
                                    identb[0:64, 0:64]))
                                assert pe.n == c_tp[(bb2, ch)]
            t_(f)
            pipe_pe_hist = [pipe_pe_hist[1], pe.n]

            def f(v, kb=kb):
                for slot in range(NSLOT):
                    i = slot - 1
                    if 0 <= i < NIT:
                        v.wait_ge(pe.h, c_sc[i])
                        if i >= 2:
                            v.wait_ge(se.h, c_ex[i - 2])
                        ve.inc(v.reduce_max(negC[i % 2][:], pb[2 + (i % 2)][0:64, 0:64],
                                            axis=AX.X, negate=True))
                        assert ve.n == c_rd[i]
                    i = slot - 4
                    if 0 <= i < NIT:
                        bb, h = divmod(i, NH)
                        v.wait_ge(pe.h, c_av[i])
                        ve.inc(v.tensor_copy(aU[bb % 2][:, 65 * h:65 * h + 65],
                                             pb[(i % 2)][0:64, 0:65]))
                        assert ve.n == c_ea[i]
                        if h == NH - 1:
                            if bb >= 2:
                                v.wait_ge(pe.h, c_tp[(bb - 2, 7)])
                            v.wait_ge(ve.h, ve.n)
                            a3 = aU[bb % 2][:].rearrange("p (hh c) -> p hh c", hh=16)
                            ve.inc(v.tensor_scalar(rec16[:], a3[:, :, 64:65].squeeze(2),
                                                   1e-6, None, op0=ALU.add))
                            v.wait_ge(ve.h, ve.n)
                            ve.inc(v.reciprocal(rec16[:], rec16[:]))
                            v.wait_ge(ve.h, ve.n)
                            dstb = attn_tm[0:64, (bb % 2) * DM:(bb % 2) * DM + DM].rearrange(
                                "p (hh d) -> p hh d", hh=16)
                            ve.inc(v.tensor_tensor(
                                dstb, a3[:, :, 0:64],
                                rec16[:].unsqueeze(2).broadcast_to([64, 16, 64]),
                                op=ALU.mult))
                            assert ve.n == c_no[bb]
                    if slot >= 5 and (slot - 5) % NH == NH - 1:
                        bb2 = (slot - 5) // NH
                        if 0 <= bb2 < NBLK:
                            for ch in range(8):
                                v.wait_ge(pe.h, c_tp[(bb2, ch)])
                                ve.inc(v.tensor_copy(
                                    attn_fm[:, ch * TW + bb2 * 64:ch * TW + bb2 * 64 + 64],
                                    tb[ch % 2][0:128, 0:64]))
                                assert ve.n == c_tc[(bb2, ch)]
            v_(f)
            tb_free[0] = c_tc[(NBLK - 1, 6)]
            tb_free[1] = c_tc[(NBLK - 1, 7)]
            attn_fm_ready = ve.n

            def f(a):
                for slot in range(NSLOT):
                    i = slot - 2
                    if 0 <= i < NIT:
                        a.wait_ge(ve.h, c_rd[i])
                        if i >= 3:
                            a.wait_ge(pe.h, c_av[i - 3])
                        se.inc(a.activation(P_sb[i % 3][:], pb[2 + (i % 2)][0:64, 0:64],
                                            AF.Exp, bias=negC[i % 2][:], scale=1.0))
                        assert se.n == c_ex[i]
            a_(f)

            p_evacs = []
            for oc in range(8):
                bank = pb[2 + (oc % 2)]
                def f(t, oc=oc, bank=bank):
                    t.wait_ge(ve.h, attn_fm_ready)
                    if len(p_evacs) >= 2:
                        t.wait_ge(ve.h, p_evacs[oc - 2])
                    for fc in range(8):
                        ins = t.matmul(bank[:, :],
                                       wp_sb[:, fc * DM + oc * 128:fc * DM + oc * 128 + 128],
                                       attn_fm[:, fc * TW:fc * TW + 512],
                                       start=(fc == 0), stop=(fc == 7))
                    pe.inc(ins)
                t_(f)
                def f(v, oc=oc, bank=bank, j=j):
                    v.wait_ge(pe.h, pe.n)
                    ve.inc(v.tensor_scalar(acc[:, oc * TOK + j * TW:oc * TOK + j * TW + 512],
                                           bank[:, :], bproj_sb[:, oc:oc + 1], None,
                                           op0=ALU.add))
                v_(f)
                p_evacs.append(ve.n)
            prev_proj_ve = ve.n
        attn_done_ve = ve.n
    else:
        attn_done_ve = ve.n

    st_C.close()
    phase_barrier()

    st_BD = ExitStack()
    xn_ownT = st_BD.enter_context(nc.sbuf_tensor("xn_ownT", [128, 8 * TOK], BF16))

    if "B" in phases or "D" in phases:
        for bat in range(4):
            items = []
            for sub in range(4):
                tt = bat * 4 + sub
                k = tt % NLN
                def f(g, tt=tt, k=k):
                    g.wait_ge(ve.h, ln_norm_ve[k])
                    d_own[k].inc(g.dma_start(out=stg[k][:],
                                             in_=x_own[tt * 128:tt * 128 + 128, :]), 16)
                g_(f)
                items.append((k, (d_own[k].h, d_own[k].n), xn_ownT, tt * 128, TOK))
            emit_ln_tr_batch(items)
    xnownT_ready = ve.n

    st_D = ExitStack()
    gate_sb = st_D.enter_context(nc.sbuf_tensor("gate_sb", [128, 8 * NEXP], BF16))
    w_tm = st_D.enter_context(nc.sbuf_tensor("w_tm", [128, 16 * NEXP], F32))
    sgs = st_D.enter_context(nc.sbuf_tensor("sgs", [128, NEXP], F32))
    sgm = st_D.enter_context(nc.sbuf_tensor("sgm", [128, 1], F32))
    sgm2 = st_D.enter_context(nc.sbuf_tensor("sgm2", [128, 1], F32))
    sgtmp = st_D.enter_context(nc.sbuf_tensor("sgtmp", [128, NEXP], F32))
    wbcast = [st_D.enter_context(nc.sbuf_tensor(f"wbcast{i}", [128, TOK], BF16)) for i in range(2)]
    h_q = [st_D.enter_context(nc.sbuf_tensor(f"h_q{i}", [128, 4 * 512], BF16)) for i in range(2)]
    h32 = st_D.enter_context(nc.sbuf_tensor("h32", [128, 512], F32))
    hsil = [st_D.enter_context(nc.sbuf_tensor(f"hsil{i}", [128, 512], BF16)) for i in range(2)]
    wgb = [st_D.enter_context(nc.sbuf_tensor(f"wgb{i}", [128, 8 * 512], BF16)) for i in range(2)]
    wub = [st_D.enter_context(nc.sbuf_tensor(f"wub{i}", [128, 8 * 512], BF16)) for i in range(2)]
    wdb = [st_D.enter_context(nc.sbuf_tensor(f"wdb{i}", [128, 4 * DM], BF16)) for i in range(2)]

    if "D" in phases:
        def f(g):
            d_in.inc(_fmload(g, gate_sb, gate_w, 8), 16)
        g_(f)
        gate_loaded = d_in.n

        def wload(g, e, q):
            kbuf = (e * 4 + q) % 2
            if e < NEXP:
                srcs = (w_gate[e, :, 512 * q:512 * q + 512],
                        w_up[e, :, 512 * q:512 * q + 512],
                        w_down[e, 512 * q:512 * q + 512, :])
            else:
                srcs = (ws_gate[:, 512 * q:512 * q + 512],
                        ws_up[:, 512 * q:512 * q + 512],
                        ws_down[512 * q:512 * q + 512, :])
            d_ffn[kbuf].inc(_fmload(g, wgb[kbuf], srcs[0], 8), 16)
            d_ffn[kbuf].inc(_fmload(g, wub[kbuf], srcs[1], 8), 16)
            d_ffn[kbuf].inc(_fmload(g, wdb[kbuf], srcs[2], 4), 16)

        eq_list = [(e, q) for e in range(NEXP + 1) for q in range(4)]
        eq_loaded = {}
        eq_consumed = {}

        def f(g):
            for i in range(2):
                wload(g, *eq_list[i])
                eq_loaded[i] = d_ffn[i % 2].n
        g_(f)

        zall = st_D.enter_context(nc.sbuf_tensor("zall", [128, 128], F32))
        zexp = st_D.enter_context(nc.sbuf_tensor("zexp", [128, 128], F32))
        ztmp = st_D.enter_context(nc.sbuf_tensor("ztmp", [128, 128], F32))
        g16a = st_D.enter_context(nc.sbuf_tensor("g16a", [128, 16], F32))
        g16b = st_D.enter_context(nc.sbuf_tensor("g16b", [128, 16], F32))

        zg_done = []
        for tt in range(16):
            bank = pb[tt % 4]
            def f(t, tt=tt, bank=bank):
                t.wait_ge(d_in.h, gate_loaded)
                t.wait_ge(ve.h, xnownT_ready)
                if len(zg_done) >= 4:
                    t.wait_ge(ve.h, zg_done[tt - 4])
                for dc in range(8):
                    ins = t.matmul(bank[:, 0:NEXP],
                                   xn_ownT[:, dc * TOK + tt * 128:dc * TOK + tt * 128 + 128],
                                   gate_sb[:, dc * NEXP:dc * NEXP + NEXP],
                                   start=(dc == 0), stop=(dc == 7))
                pe.inc(ins)
            t_(f)
            def f(v, tt=tt, bank=bank):
                v.wait_ge(pe.h, pe.n)
                ve.inc(v.tensor_copy(zall[:, tt * NEXP:tt * NEXP + NEXP], bank[:, 0:NEXP]))
            v_(f)
            zg_done.append(ve.n)

        def bc16(ap16):
            return ap16[:].unsqueeze(2).broadcast_to([128, 16, NEXP])

        zv = zall[:].rearrange("p (h e) -> p h e", h=16)
        ze = zexp[:].rearrange("p (h e) -> p h e", h=16)
        zt = ztmp[:].rearrange("p (h e) -> p h e", h=16)
        def f(v):
            v.wait_ge(ve.h, ve.n)
            ve.inc(v.reduce_max(g16a[:], zv, axis=AX.X, negate=True))
            v.wait_ge(ve.h, ve.n)
            ve.inc(v.tensor_tensor(zv, zv, bc16(g16a), op=ALU.add))
        v_(f)
        g_sub_done = ve.n
        def f(a):
            a.wait_ge(ve.h, g_sub_done)
            se.inc(a.activation(zexp[:], zall[:], AF.Exp, bias=0.0, scale=1.0))
        a_(f)
        def f(v):
            v.wait_ge(se.h, se.n)
            ve.inc(v.reduce_sum(g16a[:], ze, axis=AX.X))
            v.wait_ge(ve.h, ve.n)
            ve.inc(v.reciprocal(g16a[:], g16a[:]))
            v.wait_ge(ve.h, ve.n)
            ve.inc(v.tensor_tensor(ze, ze, bc16(g16a), op=ALU.mult))
            v.wait_ge(ve.h, ve.n)
            ve.inc(v.reduce_max(g16a[:], ze, axis=AX.X))
            v.wait_ge(ve.h, ve.n)
            ve.inc(v.tensor_tensor(zt, ze, bc16(g16a), op=ALU.is_equal))
            v.wait_ge(ve.h, ve.n)
            ve.inc(v.scalar_tensor_tensor(ztmp[:], ztmp[:], -1e30, zexp[:],
                                          op0=ALU.mult, op1=ALU.add))
            v.wait_ge(ve.h, ve.n)
            ve.inc(v.reduce_max(g16b[:], zt, axis=AX.X))
            v.wait_ge(ve.h, ve.n)
            ve.inc(v.tensor_tensor(zt, ze, bc16(g16b), op=ALU.is_ge))
            v.wait_ge(ve.h, ve.n)
            ve.inc(v.tensor_mul(ztmp[:], ztmp[:], zexp[:]))
            ve.inc(v.tensor_add(g16a[:], g16a[:], g16b[:]))
            v.wait_ge(ve.h, ve.n)
            ve.inc(v.tensor_scalar(g16a[:], g16a[:], 1e-20, None, op0=ALU.add))
            v.wait_ge(ve.h, ve.n)
            ve.inc(v.reciprocal(g16a[:], g16a[:]))
            v.wait_ge(ve.h, ve.n)
            ve.inc(v.tensor_tensor(w_tm[:].rearrange("p (h e) -> p h e", h=16), zt,
                                   bc16(g16a), op=ALU.mult))
        v_(f)

        st_WB = ExitStack()
        wT8 = st_WB.enter_context(nc.sbuf_tensor("wT8", [NEXP, TOK], BF16))
        wTflat = [st_WB.enter_context(nc.sbuf_tensor(f"wTflat{i}", [1, TOK], BF16)) for i in range(2)]
        wbstage = st_WB.enter_context(nc.sbuf_tensor("wbstage", [128, TOK], BF16))

        for tt in range(16):
            def f(t, tt=tt):
                t.wait_ge(ve.h, ve.n)
                pe.inc(t.transpose(pb[2][0:NEXP, 0:128],
                                   w_tm[:, tt * NEXP:tt * NEXP + NEXP], identf[:]))
            t_(f)
            def f(v, tt=tt):
                v.wait_ge(pe.h, pe.n)
                ve.inc(v.tensor_copy(wT8[:, tt * 128:tt * 128 + 128], pb[2][0:NEXP, 0:128]))
            v_(f)
        wT8_ready = ve.n

        def f(g):
            g.wait_ge(ve.h, wT8_ready)
            d_wtf[0].inc(g.dma_start(out=wTflat[0][:], in_=wT8[0:1, :]), 16)
        g_(f)
        wtf_done = {0: d_wtf[0].n}
        mm_done = {}
        for e in range(NEXP):
            for tq in range(4):
                bank = pb[4 + tq % 2]
                def f(t, e=e, tq=tq, bank=bank):
                    t.wait_ge(d_wtf[e % 2].h, wtf_done[e])
                    t.wait_ge(gp.h, 5)
                    t.wait_ge(ve.h, ve.n)
                    t.wait_ge(d_kv.h, d_kv.n)
                    pe.inc(t.matmul(bank[:, :], ones128b[:],
                                    wTflat[e % 2][0:1, tq * 512:tq * 512 + 512],
                                    start=True, stop=True))
                t_(f)
                def f(v, tq=tq, bank=bank):
                    v.wait_ge(pe.h, pe.n)
                    ve.inc(v.tensor_copy(wbstage[:, tq * 512:tq * 512 + 512], bank[:, :]))
                v_(f)
            mm_done[e] = pe.n
            if e + 1 < NEXP:
                def f(g, e=e):
                    if e >= 1:
                        g.wait_ge(pe.h, mm_done[e - 1])
                    d_wtf[(e + 1) % 2].inc(
                        g.dma_start(out=wTflat[(e + 1) % 2][:], in_=wT8[e + 1:e + 2, :]), 16)
                g_(f)
                wtf_done[e + 1] = d_wtf[(e + 1) % 2].n
            def f(g, e=e):
                g.wait_ge(ve.h, ve.n)
                d_kv.inc(g.dma_start(out=W_wb[e], in_=wbstage[:]), 16)
            g_(f)
        wb_built = d_kv.n

        st_WB.close()
        st_TB.close()
        st_Dp = ExitStack()
        pdown = st_Dp.enter_context(nc.psum_tensor("pdown", [128, 4, 512], F32))

        def f(s2):
            s2.wait_ge(d_kv.h, wb_built)
            d_wb[0].inc(s2.dma_start(out=wbcast[0][:], in_=W_wb[0]), 16)
        s_(f)
        wb_loaded = {0: d_wb[0].n}
        exp_mult_ve = {}
        c_gate, c_up, c_silu, c_m1, c_m2, c_down, c_add = ({} for _ in range(7))
        prev_add_ve = ve.n

        for i, (e, q) in enumerate(eq_list):
            if q == 0 and e + 1 < NEXP:
                def f(s2, e=e):
                    s2.wait_ge(d_kv.h, wb_built)
                    if e >= 1:
                        s2.wait_ge(ve.h, exp_mult_ve[e - 1])
                    d_wb[(e + 1) % 2].inc(
                        s2.dma_start(out=wbcast[(e + 1) % 2][:], in_=W_wb[e + 1]), 16)
                s_(f)
                wb_loaded[e + 1] = d_wb[(e + 1) % 2].n

            pe_c, se_c, ve_c = pe.n, se.n, ve.n
            PE_ORDER = ([("gu", 0, sl) for sl in range(4)]
                        + [("gu", 1, 0), ("gu", 1, 1), ("dn", 0, 0),
                           ("gu", 1, 2), ("gu", 1, 3), ("dn", 0, 1),
                           ("gu", 2, 0), ("gu", 2, 1), ("dn", 1, 0),
                           ("gu", 2, 2), ("gu", 2, 3), ("dn", 1, 1),
                           ("gu", 3, 0), ("gu", 3, 1), ("dn", 2, 0),
                           ("gu", 3, 2), ("gu", 3, 3), ("dn", 2, 1),
                           ("dn", 3, 0), ("dn", 3, 1)])
            for kind, a, b in PE_ORDER:
                if kind == "gu":
                    pe_c += 1; c_gate[(i, a, b)] = pe_c
                    pe_c += 1; c_up[(i, a, b)] = pe_c
                else:
                    pe_c += 1; c_down[(i, a, b)] = pe_c
            VE_ORDER = []
            for tq in range(4):
                for sl in range(4):
                    VE_ORDER.append(("m", tq, sl))
                    if tq >= 1 and sl == 1:
                        VE_ORDER.append(("a", tq - 1, 0))
                    if tq >= 1 and sl == 3:
                        VE_ORDER.append(("a", tq - 1, 1))
            VE_ORDER += [("a", 3, 0), ("a", 3, 1)]
            for tq in range(4):
                for sl in range(4):
                    se_c += 1; c_silu[(i, tq, sl)] = se_c
            for kind, a0, b0 in VE_ORDER:
                if kind == "m":
                    ve_c += 1; c_m1[(i, a0, b0)] = ve_c
                    ve_c += 1; c_m2[(i, a0, b0)] = ve_c
                else:
                    ve_c += 1; c_add[(i, a0, b0)] = ve_c

            def emit_gu(t, i, tq, sls):
                for sl in sls:
                    kb2 = sl % 2
                    if sl >= 2:
                        t.wait_ge(se.h, c_silu[(i, tq, sl - 2)])
                    elif tq >= 1:
                        t.wait_ge(se.h, c_silu[(i, tq - 1, sl + 2)])
                    elif i >= 1:
                        t.wait_ge(se.h, c_silu[(i - 1, 3, sl + 2)])
                    buf = wgb[i % 2]
                    for dc in range(8):
                        ins = t.matmul(pb[kb2][:, :],
                                       buf[:, dc * 512 + sl * 128:dc * 512 + sl * 128 + 128],
                                       xn_ownT[:, dc * TOK + tq * 512:dc * TOK + tq * 512 + 512],
                                       start=(dc == 0), stop=(dc == 7))
                    pe.inc(ins)
                    assert pe.n == c_gate[(i, tq, sl)]
                    if sl >= 2:
                        t.wait_ge(ve.h, c_m1[(i, tq, sl - 2)])
                    elif tq >= 1:
                        t.wait_ge(ve.h, c_m1[(i, tq - 1, sl + 2)])
                    elif i >= 1:
                        t.wait_ge(ve.h, c_m1[(i - 1, 3, sl + 2)])
                    buf = wub[i % 2]
                    for dc in range(8):
                        ins = t.matmul(pb[2 + kb2][:, :],
                                       buf[:, dc * 512 + sl * 128:dc * 512 + sl * 128 + 128],
                                       xn_ownT[:, dc * TOK + tq * 512:dc * TOK + tq * 512 + 512],
                                       start=(dc == 0), stop=(dc == 7))
                    pe.inc(ins)
                    assert pe.n == c_up[(i, tq, sl)]

            def emit_dn(t, i, tq, hh, pav):
                t.wait_ge(ve.h, c_m2[(i, tq, 3)])
                if hh == 1:
                    t.wait_ge(ve.h, c_add[(i, tq, 0)])
                elif tq >= 1:
                    t.wait_ge(ve.h, c_add[(i, tq - 1, 1)])
                else:
                    t.wait_ge(ve.h, pav)
                for sl in range(4):
                    for c in range(4):
                        ins = t.matmul(
                            pdown[:, c, :],
                            wdb[i % 2][:, sl * DM + (4 * hh + c) * 128:
                                       sl * DM + (4 * hh + c) * 128 + 128],
                            h_q[tq % 2][:, sl * 512:sl * 512 + 512],
                            start=(sl == 0), stop=(sl == 3))
                pe.inc(ins)
                assert pe.n == c_down[(i, tq, hh)]

            def f(t, i=i, e=e, pav=prev_add_ve):
                t.wait_ge(d_ffn[i % 2].h, eq_loaded[i])
                if i == 0:
                    t.wait_ge(ve.h, xnownT_ready)
                emit_gu(t, i, 0, [0, 1, 2, 3])
                emit_gu(t, i, 1, [0, 1])
                emit_dn(t, i, 0, 0, pav)
                emit_gu(t, i, 1, [2, 3])
                emit_dn(t, i, 0, 1, pav)
                emit_gu(t, i, 2, [0, 1])
                emit_dn(t, i, 1, 0, pav)
                emit_gu(t, i, 2, [2, 3])
                emit_dn(t, i, 1, 1, pav)
                emit_gu(t, i, 3, [0, 1])
                emit_dn(t, i, 2, 0, pav)
                emit_gu(t, i, 3, [2, 3])
                emit_dn(t, i, 2, 1, pav)
                emit_dn(t, i, 3, 0, pav)
                emit_dn(t, i, 3, 1, pav)
            t_(f)

            def f(a, i=i):
                for tq in range(4):
                    for sl in range(4):
                        kb2 = sl % 2
                        a.wait_ge(pe.h, c_gate[(i, tq, sl)])
                        if sl >= 2:
                            a.wait_ge(ve.h, c_m1[(i, tq, sl - 2)])
                        elif tq >= 1:
                            a.wait_ge(ve.h, c_m1[(i, tq - 1, sl + 2)])
                        elif i >= 1:
                            a.wait_ge(ve.h, c_m1[(i - 1, 3, sl + 2)])
                        se.inc(a.activation(hsil[kb2][:], pb[kb2][:, :], AF.Silu,
                                            bias=0.0, scale=1.0))
                        assert se.n == c_silu[(i, tq, sl)]
            a_(f)

            def f(v, i=i, e=e, VE_ORDER=VE_ORDER):
                for kind, a0, b0 in VE_ORDER:
                    if kind == "m":
                        tq, sl = a0, b0
                        tqp = tq % 2
                        kb2 = sl % 2
                        v.wait_ge(se.h, c_silu[(i, tq, sl)])
                        v.wait_ge(pe.h, c_up[(i, tq, sl)])
                        v.wait_ge(ve.h, ve.n)
                        ve.inc(v.tensor_tensor(h32[:], hsil[kb2][:], pb[2 + kb2][:, :],
                                               op=ALU.mult))
                        assert ve.n == c_m1[(i, tq, sl)]
                        v.wait_ge(ve.h, ve.n)
                        if tq >= 2:
                            v.wait_ge(pe.h, c_down[(i, tq - 2, 1)])
                        elif i >= 1 and tq < 2:
                            v.wait_ge(pe.h, c_down[(i - 1, tq + 2, 1)])
                        if e < NEXP:
                            v.wait_ge(d_wb[e % 2].h, wb_loaded[e])
                            ve.inc(v.tensor_tensor(h_q[tqp][:, sl * 512:sl * 512 + 512],
                                                   h32[:], wbcast[e % 2][:, tq * 512:tq * 512 + 512],
                                                   op=ALU.mult))
                        else:
                            ve.inc(v.tensor_copy(h_q[tqp][:, sl * 512:sl * 512 + 512],
                                                 h32[:]))
                        assert ve.n == c_m2[(i, tq, sl)]
                    else:
                        tq, hh = a0, b0
                        v.wait_ge(pe.h, c_down[(i, tq, hh)])
                        accv = acc[:].rearrange("p (c t2) -> p c t2", c=8)[
                            :, 4 * hh:4 * hh + 4, tq * 512:tq * 512 + 512]
                        ve.inc(v.tensor_tensor(accv, accv, pdown[:, :, :], op=ALU.add))
                        assert ve.n == c_add[(i, tq, hh)]
            v_(f)

            prev_add_ve = c_add[(i, 3, 1)]
            exp_mult_ve[e] = c_m2[(i, 3, 3)]
            eq_consumed[i] = pe.n
            if i + 2 < len(eq_list):
                def f(g, i=i):
                    g.wait_ge(pe.h, eq_consumed[i])
                    wload(g, *eq_list[i + 2])
                    eq_loaded[i + 2] = d_ffn[i % 2].n
                g_(f)
        st_Dp.close()

    st_D.close()
    st_BD.close()

    ffn_done_ve = ve.n

    def f(g):
        g.wait_ge(ve.h, ffn_done_ve)
        d_acc.inc(g.dma_start(out=acc[:].rearrange("p (c t2) -> p c t2", c=8),
                              in_=xT_own.rearrange("(c p) t2 -> p c t2", p=128),
                              accum_op=ALU.add), 16)
    g_(f)

    def f(s):
        s.wait_ge(d_acc.h, d_acc.n)
        d_out.inc(s.dma_start(out=outT.rearrange("(c p) t2 -> p c t2", p=128),
                              in_=acc[:].rearrange("p (c t2) -> p c t2", c=8)), 16)
        s.wait_ge(d_out.h, d_out.n)
    s_(f)

    es.close()
    return nc


_CACHED = {}


def _get_program():
    if "nc" not in _CACHED:
        _CACHED["nc"] = build_program()
    return _CACHED["nc"]


def kernel(**inputs):
    x = np.asarray(inputs["L0_attn_x"], dtype=np.float32)[0]
    xT = np.ascontiguousarray(x.T)
    x_live = np.ascontiguousarray(x[:LIVE])

    base = {
        "x_liv": x_live,
        "mapping": np.ascontiguousarray(np.asarray(inputs["L0_attn_mapping"], dtype=np.int32)),
        "Wqkv": np.ascontiguousarray(np.asarray(inputs["L0_attn_Wqkv"], dtype=np.float32)),
        "bqkv": np.ascontiguousarray(np.asarray(inputs["L0_attn_bqkv"], dtype=np.float32)),
        "Wproj": np.ascontiguousarray(np.asarray(inputs["L0_attn_Wproj"], dtype=np.float32)),
        "bproj": np.ascontiguousarray(np.asarray(inputs["L0_attn_bproj"], dtype=np.float32)),
        "gate_w": np.ascontiguousarray(np.asarray(inputs["L0_ffn_gate_w"], dtype=np.float32)),
        "w_gate": np.ascontiguousarray(np.asarray(inputs["L0_ffn_w_gate"], dtype=np.float32)),
        "w_up": np.ascontiguousarray(np.asarray(inputs["L0_ffn_w_up"], dtype=np.float32)),
        "w_down": np.ascontiguousarray(np.asarray(inputs["L0_ffn_w_down"], dtype=np.float32)),
        "ws_gate": np.ascontiguousarray(np.asarray(inputs["L0_ffn_ws_gate"], dtype=np.float32)),
        "ws_up": np.ascontiguousarray(np.asarray(inputs["L0_ffn_ws_up"], dtype=np.float32)),
        "ws_down": np.ascontiguousarray(np.asarray(inputs["L0_ffn_ws_down"], dtype=np.float32)),
    }

    in_maps = []
    p = np.arange(128)[:, None]
    g = np.arange(16)[None, :]
    for c in range(NCORES):
        qp = np.minimum(c * TOK + 128 * g + p, LIVE - 1).astype(np.int32)
        m = dict(base)
        m["x_own"] = np.ascontiguousarray(x[c * TOK:(c + 1) * TOK])
        m["xT_own"] = np.ascontiguousarray(xT[:, c * TOK:(c + 1) * TOK])
        m["qpos"] = np.ascontiguousarray(qp)
        in_maps.append(m)

    nc = _get_program()
    res = run_bass_kernel_spmd(nc, in_maps, core_ids=list(range(NCORES)))
    out = np.empty((M, DM), dtype=np.float32)
    for c in range(NCORES):
        out[c * TOK:(c + 1) * TOK] = res.results[c]["outT"].T
    return out.reshape(1, M, DM)


# version 07
# speedup vs baseline: 1.6418x; 1.0032x over previous
import numpy as np
from contextlib import ExitStack

import concourse.bass as bass
import concourse.mybir as mybir
from concourse.bass_utils import run_bass_kernel_spmd
from concourse.bass import IndirectOffsetOnAxis

F32 = mybir.dt.float32
BF16 = mybir.dt.bfloat16
I32 = mybir.dt.int32
AF = mybir.ActivationFunctionType
ALU = mybir.AluOpType
AX = mybir.AxisListType

NCORES = 8
M = 16384
DM = 1024
LIVE = 4096
TOK = 2048
NH = 16
NKV = 1024
NSEG = 16
VSEGW = 16 * 65
DFF = 2048
NEXP = 8
NT = 4
TW = 512
NBLK = 8


class Sem:
    def __init__(self, es, nc, name):
        self.h = es.enter_context(nc.semaphore(name))
        self.n = 0

    def inc(self, instr, k=1):
        instr.then_inc(self.h, k)
        self.n += k
        return instr


def _fmload(eng, dst, src, nch):
    return eng.dma_start(out=dst[:].rearrange("p (c o) -> p c o", c=nch),
                         in_=src.rearrange("(c p) o -> p c o", p=128))


def build_program(phases=("A", "C", "B", "D", "E")):
    nc = bass.Bass("TRN2", target_bir_lowering=False, enable_asserts=False)

    x_liv = nc.dram_tensor("x_liv", [LIVE, DM], F32, kind="ExternalInput").ap()
    x_own = nc.dram_tensor("x_own", [TOK, DM], F32, kind="ExternalInput").ap()
    xT_own = nc.dram_tensor("xT_own", [DM, TOK], F32, kind="ExternalInput").ap()
    mapping = nc.dram_tensor("mapping", [LIVE], I32, kind="ExternalInput").ap()
    qpos = nc.dram_tensor("qpos", [128, 16], I32, kind="ExternalInput").ap()
    Wqkv = nc.dram_tensor("Wqkv", [DM, 3 * DM], F32, kind="ExternalInput").ap()
    bqkv = nc.dram_tensor("bqkv", [3 * DM], F32, kind="ExternalInput").ap()
    Wproj = nc.dram_tensor("Wproj", [DM, DM], F32, kind="ExternalInput").ap()
    bproj = nc.dram_tensor("bproj", [DM], F32, kind="ExternalInput").ap()
    gate_w = nc.dram_tensor("gate_w", [DM, NEXP], F32, kind="ExternalInput").ap()
    w_gate = nc.dram_tensor("w_gate", [NEXP, DM, DFF], F32, kind="ExternalInput").ap()
    w_up = nc.dram_tensor("w_up", [NEXP, DM, DFF], F32, kind="ExternalInput").ap()
    w_down = nc.dram_tensor("w_down", [NEXP, DFF, DM], F32, kind="ExternalInput").ap()
    ws_gate = nc.dram_tensor("ws_gate", [DM, DFF], F32, kind="ExternalInput").ap()
    ws_up = nc.dram_tensor("ws_up", [DM, DFF], F32, kind="ExternalInput").ap()
    ws_down = nc.dram_tensor("ws_down", [DFF, DM], F32, kind="ExternalInput").ap()
    outT = nc.dram_tensor("outT", [DM, TOK], F32, kind="ExternalOutput").ap()
    K_dram = nc.dram_tensor("K_dram", [128 * NSEG, 512], BF16).ap()
    V_dram = nc.dram_tensor("V_dram", [64 * NSEG, VSEGW], BF16).ap()
    W_wb = nc.dram_tensor("W_wb", [NEXP, 128, TOK], BF16).ap()

    es = ExitStack()

    def sbuf(name, shape, dt):
        return es.enter_context(nc.sbuf_tensor(name, shape, dt))

    identf = sbuf("identf", [128, 128], F32)
    identb = sbuf("identb", [128, 128], BF16)
    ones64f = sbuf("ones64f", [1, 64], F32)
    ones128f = sbuf("ones128f", [1, 128], F32)
    ones128b = sbuf("ones128b", [1, 128], BF16)
    iota16f = sbuf("iota16f", [128, 1], F32)
    qpos_sb = sbuf("qpos_sb", [128, 16], I32)
    kvidx = sbuf("kvidx", [128, 8], I32)
    qmap = sbuf("qmap", [128, 16], I32)
    segraw = sbuf("segraw", [1, 32], I32)
    segf = sbuf("segf", [1, 32], F32)
    idxK = sbuf("idxK", [128, 32], I32)
    bqkv_sb = sbuf("bqkv_sb", [128, 24], F32)
    bqs = sbuf("bqs", [128, 8], F32)
    bproj_sb = sbuf("bproj_sb", [128, 8], F32)
    acc = sbuf("acc", [128, 8 * TOK], F32)
    NLN = 4
    bn6 = [sbuf(f"bn6_{i}", [128, 12], F32) for i in range(NLN)]
    bn2 = [sbuf(f"bn2_{i}", [128, 2], F32) for i in range(NLN)]
    rstd = [sbuf(f"rstd{i}", [128, 1], F32) for i in range(NLN)]
    stg = [sbuf(f"stg{i}", [128, DM], BF16) for i in range(NLN)]
    stgn = [sbuf(f"stgn{i}", [128, DM], BF16) for i in range(NLN)]

    pb = [es.enter_context(nc.psum_tensor(f"pb{i}", [128, 512], F32)) for i in range(4)]
    st_TB = ExitStack()
    pb += [st_TB.enter_context(nc.psum_tensor(f"pb{i}", [128, 512], F32)) for i in range(4, 6)]
    tb = [st_TB.enter_context(nc.psum_tensor(f"tb{i}", [128, 1024], BF16)) for i in range(2)]

    d_in = Sem(es, nc, "d_in")
    d_iny = Sem(es, nc, "d_iny")
    d_idx = Sem(es, nc, "d_idx")
    d_g = [Sem(es, nc, f"d_g{i}") for i in range(4)]
    d_kst = [Sem(es, nc, f"d_kst{i}") for i in range(2)]
    d_own = [Sem(es, nc, f"d_own{i}") for i in range(4)]
    d_kv = Sem(es, nc, "d_kv")
    d_wtf = [Sem(es, nc, f"d_wtf{i}") for i in range(2)]
    d_wb = [Sem(es, nc, f"d_wb{i}") for i in range(2)]
    d_ffn = [Sem(es, nc, f"d_ffn{i}") for i in range(2)]
    d_bv = Sem(es, nc, "d_bv")
    d_kvi = Sem(es, nc, "d_kvi")
    d_acc = Sem(es, nc, "d_acc")
    d_out = Sem(es, nc, "d_out")
    pe = Sem(es, nc, "pe")
    ve = Sem(es, nc, "ve")
    se = Sem(es, nc, "se")
    gp = Sem(es, nc, "gp")

    blk = es.enter_context(nc.Block())
    g_, s_, t_, v_, a_ = blk.gpsimd, blk.sync, blk.tensor, blk.vector, blk.scalar

    def f(g):
        gp.inc(g.iota(identf[:], pattern=[[1, 128]], base=0, channel_multiplier=-1,
               allow_small_or_imprecise_dtypes=True))
        gp.inc(g.iota(iota16f[:], pattern=[[0, 1]], base=0, channel_multiplier=16,
               allow_small_or_imprecise_dtypes=True))
        gp.inc(g.memset(ones64f[:], 1.0))
        gp.inc(g.memset(ones128f[:], 1.0))
        gp.inc(g.memset(ones128b[:], 1.0))
        d_kvi.inc(g.dma_start(
            out=kvidx[:],
            in_=mapping.rearrange("(gi p f) -> p gi f", gi=8, p=128)[:, :, 0:1].squeeze(2),
            allow_slow_non_contiguous=True), 16)
    g_(f)

    def f(s):
        d_iny.inc(s.dma_start(out=qpos_sb[:], in_=qpos), 16)
        d_iny.inc(s.dma_start(out=bqkv_sb[:], in_=bqkv.rearrange("(c p) -> p c", p=128),
                              allow_slow_non_contiguous=True), 16)
        d_iny.inc(s.dma_start(out=bproj_sb[:], in_=bproj.rearrange("(c p) -> p c", p=128),
                              allow_slow_non_contiguous=True), 16)
    s_(f)
    setup_iny = d_iny.n
    setup_in = d_in.n

    def f(v):
        v.wait_ge(gp.h, 1)
        ve.inc(v.tensor_scalar(identf[:], identf[:], 0.0, None, op0=ALU.is_equal))
        v.wait_ge(ve.h, ve.n)
        ve.inc(v.tensor_copy(identb[:], identf[:]))
        v.wait_ge(d_iny.h, setup_iny)
        ve.inc(v.tensor_scalar(bqs[:], bqkv_sb[:, 0:8], 0.125, None, op0=ALU.mult))
    v_(f)
    ident_ready = ve.n

    def f(g):
        g.wait_ge(d_iny.h, setup_iny)
        for gi in range(16):
            d_idx.inc(g.indirect_dma_start(
                out=qmap[:, gi:gi + 1], out_offset=None,
                in_=mapping[None, :],
                in_offset=IndirectOffsetOnAxis(qpos_sb[:, gi:gi + 1], 1)), 16)
        g.wait_ge(d_idx.h, d_idx.n)
        d_idx.inc(g.dma_start(out=segraw[:], in_=qmap[0:128:64, 0:16]), 16)
    g_(f)
    qmap_done = d_idx.n

    def f(v):
        v.wait_ge(d_idx.h, d_idx.n)
        ve.inc(v.tensor_scalar(segraw[:], segraw[:], 8, None, op0=ALU.arith_shift_right))
        v.wait_ge(ve.h, ve.n)
        ve.inc(v.tensor_copy(segf[:], segraw[:]))
    v_(f)
    segf_ready = ve.n

    def f(t):
        t.wait_ge(ve.h, segf_ready)
        t.wait_ge(gp.h, 4)
        pe.inc(t.matmul(pb[0][:, 0:32], ones128f[:], segf[:], start=True, stop=True))
    t_(f)
    def f(v):
        v.wait_ge(pe.h, pe.n)
        ve.inc(v.tensor_scalar(pb[0][:, 0:32], pb[0][:, 0:32], 1.0, iota16f[:],
                               op0=ALU.mult, op1=ALU.add))
        v.wait_ge(ve.h, ve.n)
        ve.inc(v.tensor_copy(idxK[:], pb[0][:, 0:32]))
    v_(f)
    idx_ready = ve.n

    all_sems = [d_in, d_iny, d_idx, d_g[0], d_g[1], d_g[2], d_g[3], d_kst[0], d_kst[1],
                d_own[0], d_own[1], d_own[2], d_own[3], d_kv, d_bv, d_kvi, d_wtf[0], d_wtf[1], d_wb[0], d_wb[1], d_ffn[0], d_ffn[1],
                d_acc, d_out, pe, ve, se, gp]

    def phase_barrier():
        vals = [(sm.h, sm.n) for sm in all_sems if sm.n > 0]
        for section in (g_, s_, t_, v_, a_):
            def f(e, vals=vals):
                for h, n in vals:
                    e.wait_ge(h, n)
            section(f)

    def blkcol(b):
        return 16 * (b & 1) + (b >> 1)

    ln_sqrt_se = [0] * 4
    ln_norm_ve = [0] * 4
    tb_free = [0, 0]

    def emit_ln_batch(items):
        n = len(items)
        ve_c, se_c = ve.n, se.n
        p_rec, p_sqrt, p_norm = {}, {}, {}
        for idx in range(n):
            ve_c += 5; p_rec[idx] = ve_c
        for idx in range(n):
            se_c += 1; p_sqrt[idx] = se_c
        for idx in range(n):
            ve_c += 1; p_norm[idx] = ve_c

        def f(v):
            for idx, (k, gw) in enumerate(items):
                v.wait_ge(*gw)
                v.wait_ge(se.h, ln_sqrt_se[k])
                ve.inc(v.bn_stats(bn6[k][:, 0:6], stg[k][:, 0:512]))
                ve.inc(v.bn_stats(bn6[k][:, 6:12], stg[k][:, 512:1024]))
                v.wait_ge(ve.h, ve.n)
                ve.inc(v.bn_aggr(bn2[k][:], bn6[k][:]))
                v.wait_ge(ve.h, ve.n)
                ve.inc(v.tensor_scalar(rstd[k][:], bn2[k][:, 1:2], 1e-5, None, op0=ALU.add))
                v.wait_ge(ve.h, ve.n)
                ve.inc(v.reciprocal(rstd[k][:], rstd[k][:]))
                assert ve.n == p_rec[idx]
        v_(f)
        def f(a):
            for idx, (k, gw) in enumerate(items):
                a.wait_ge(ve.h, p_rec[idx])
                se.inc(a.activation(rstd[k][:], rstd[k][:], AF.Sqrt, bias=0.0, scale=1.0))
                assert se.n == p_sqrt[idx]
        a_(f)
        def f(v):
            for idx, (k, gw) in enumerate(items):
                v.wait_ge(se.h, p_sqrt[idx])
                ve.inc(v.tensor_scalar(stgn[k][:], stg[k][:], bn2[k][:, 0:1], rstd[k][:],
                                       op0=ALU.subtract, op1=ALU.mult))
                assert ve.n == p_norm[idx]
        v_(f)
        for idx, (k, gw) in enumerate(items):
            ln_sqrt_se[k] = p_sqrt[idx]
            ln_norm_ve[k] = p_norm[idx]
        return p_norm

    def emit_tr_batch(items, p_norm, extra_pe_wait=0):
        n = len(items)
        pe_c, ve_c = pe.n, ve.n
        p_tp, p_tc = {}, {}
        for idx in range(n):
            for ch in range(8):
                pe_c += 1; p_tp[(idx, ch)] = pe_c
                ve_c += 1; p_tc[(idx, ch)] = ve_c

        def f(t):
            for idx, (k, dfm, dcol, cstr) in enumerate(items):
                for ch in range(8):
                    gi2 = idx * 8 + ch
                    if ch == 0:
                        t.wait_ge(ve.h, p_norm[idx])
                    if gi2 >= 2:
                        t.wait_ge(ve.h, p_tc[((gi2 - 2) // 8, (gi2 - 2) % 8)])
                    else:
                        t.wait_ge(ve.h, max(tb_free[gi2 % 2], ident_ready))
                    pe.inc(t.transpose(tb[gi2 % 2][0:128, 0:128],
                                       stgn[k][:, ch * 128:ch * 128 + 128], identb[:]))
                    assert pe.n == p_tp[(idx, ch)]
        t_(f)
        def f(v):
            for idx, (k, dfm, dcol, cstr) in enumerate(items):
                for ch in range(8):
                    gi2 = idx * 8 + ch
                    if gi2 == 0 and extra_pe_wait:
                        v.wait_ge(pe.h, extra_pe_wait)
                    v.wait_ge(pe.h, p_tp[(idx, ch)])
                    ve.inc(v.tensor_copy(dfm[:, dcol + ch * cstr:dcol + ch * cstr + 128],
                                         tb[gi2 % 2][0:128, 0:128]))
                    assert ve.n == p_tc[(idx, ch)]
        v_(f)
        tb_free[0] = p_tc[(n - 1, 6)]
        tb_free[1] = p_tc[(n - 1, 7)]
        return ve.n

    def emit_ln_tr_batch(items):
        p_norm = emit_ln_batch([(k, gw) for (k, gw, dfm, dcol, cstr) in items])
        return emit_tr_batch([(k, dfm, dcol, cstr) for (k, gw, dfm, dcol, cstr) in items],
                             p_norm)

    st_A = ExitStack()
    wk_sb = st_A.enter_context(nc.sbuf_tensor("wk_sb", [128, 8 * DM], BF16))
    wv_sb = st_A.enter_context(nc.sbuf_tensor("wv_sb", [128, 8 * DM], BF16))
    xn_kvT = st_A.enter_context(nc.sbuf_tensor("xn_kvT", [128, 8 * NKV], BF16))
    bv_b = st_A.enter_context(nc.sbuf_tensor("bv_b", [64, DM], F32))
    bv_row = st_A.enter_context(nc.sbuf_tensor("bv_row", [1, DM], F32))
    K_T2 = st_A.enter_context(nc.sbuf_tensor("K_T2", [128, NSEG * 512], BF16))
    V_seg = st_A.enter_context(nc.sbuf_tensor("V_seg", [64, NSEG * VSEGW], BF16))

    if "A" in phases:
        def f(g):
            d_in.inc(_fmload(g, wk_sb, Wqkv[:, DM:2 * DM], 8), 16)
            d_in.inc(_fmload(g, wv_sb, Wqkv[:, 2 * DM:3 * DM], 8), 16)
            vs4 = V_seg[:].rearrange("p (s h d) -> p s h d", s=NSEG, h=16)
            gp.inc(g.memset(vs4[:, :, :, 64:65], 1.0))
        g_(f)
        wkwv_loaded = d_in.n
        vones_gp = gp.n

        def f(s2):
            d_bv.inc(s2.dma_start(out=bv_row[:], in_=bqkv[None, 2 * DM:3 * DM]), 16)
        s_(f)

        def f(t):
            t.wait_ge(d_bv.h, d_bv.n)
            t.wait_ge(gp.h, 4)
            pe.inc(t.matmul(pb[1][0:64, :], ones64f[:], bv_row[0:1, 0:512],
                            start=True, stop=True))
            pe.inc(t.matmul(pb[2][0:64, :], ones64f[:], bv_row[0:1, 512:1024],
                            start=True, stop=True))
        t_(f)
        def f(v):
            v.wait_ge(pe.h, pe.n)
            ve.inc(v.tensor_copy(bv_b[:, 0:512], pb[1][0:64, :]))
            ve.inc(v.tensor_copy(bv_b[:, 512:1024], pb[2][0:64, :]))
        v_(f)
        bvb_ready = ve.n

        def emit_kv_gathers(bat):
            for sub in range(4):
                gi = bat * 4 + sub
                k = gi % NLN
                def f(g, gi=gi, k=k):
                    g.wait_ge(d_kvi.h, 16)
                    g.wait_ge(ve.h, ln_norm_ve[k])
                    d_g[k].inc(g.indirect_dma_start(
                        out=stg[k][:], out_offset=None,
                        in_=x_liv,
                        in_offset=IndirectOffsetOnAxis(kvidx[:, gi:gi + 1], 0)), 16)
                g_(f)

        def emit_K_half(half, ready_ve):
            evs = []
            for kc in range(8):
                bank = pb[kc % 4]
                def f(t, kc=kc, bank=bank, half=half, ready_ve=ready_ve):
                    t.wait_ge(d_in.h, wkwv_loaded)
                    t.wait_ge(ve.h, max(ready_ve, idx_ready))
                    if len(evs) >= 4:
                        t.wait_ge(ve.h, evs[kc - 4])
                    for dc in range(8):
                        ins = t.matmul(bank[:, :],
                                       wk_sb[:, dc * DM + kc * 128:dc * DM + kc * 128 + 128],
                                       xn_kvT[:, dc * NKV + half * 512:dc * NKV + half * 512 + 512],
                                       start=(dc == 0), stop=(dc == 7))
                    pe.inc(ins)
                t_(f)
                def f(v, kc=kc, bank=bank, half=half):
                    v.wait_ge(pe.h, pe.n)
                    dst = K_T2[:].rearrange("p (sg r) -> p sg r", sg=NSEG)[
                        :, half * 8:half * 8 + 8, kc * 64:kc * 64 + 64]
                    srcp = bank[:, :].rearrange("p (sg s) -> p sg s", sg=8)
                    ve.inc(v.tensor_scalar(dst, srcp, bqkv_sb[:, 8 + kc:9 + kc], None,
                                           op0=ALU.add))
                v_(f)
                evs.append(ve.n)
            return ve.n

        def emit_V_half(half, ready_ve):
            evs = []
            for sgi in range(8):
                sg = half * 8 + sgi
                for vh in range(2):
                    i = sgi * 2 + vh
                    bank = pb[4 + (i % 2)]
                    def f(t, sg=sg, vh=vh, bank=bank, i=i, ready_ve=ready_ve):
                        t.wait_ge(ve.h, max(ready_ve, bvb_ready))
                        if i >= 2:
                            t.wait_ge(ve.h, evs[i - 2])
                        for dc in range(8):
                            ins = t.matmul(bank[0:64, :],
                                           xn_kvT[:, dc * NKV + sg * 64:dc * NKV + sg * 64 + 64],
                                           wv_sb[:, dc * DM + vh * 512:dc * DM + vh * 512 + 512],
                                           start=(dc == 0), stop=(dc == 7))
                        pe.inc(ins)
                    t_(f)
                    def f(v, sg=sg, vh=vh, bank=bank):
                        v.wait_ge(pe.h, pe.n)
                        base = sg * VSEGW + vh * 520
                        dst = V_seg[0:64, base:base + 520].rearrange(
                            "p (h d) -> p h d", h=8)[:, :, 0:64]
                        srcv = bank[0:64, :].rearrange("p (h d) -> p h d", h=8)
                        ve.inc(v.scalar_tensor_tensor(
                            dst, srcv, 0.0,
                            bv_b[:, vh * 512:vh * 512 + 512].rearrange("p (h d) -> p h d", h=8),
                            op0=ALU.bypass, op1=ALU.add))
                    v_(f)
                    evs.append(ve.n)
            return ve.n

        emit_kv_gathers(0)
        p0 = emit_ln_batch([(gi % NLN, (d_g[gi % NLN].h, d_g[gi % NLN].n))
                            for gi in range(4)])
        emit_tr_batch([(gi % NLN, xn_kvT, gi * 128, NKV) for gi in range(4)], p0)
        half0_ready = ve.n
        emit_kv_gathers(1)
        p1 = emit_ln_batch([(gi % NLN, (d_g[gi % NLN].h, d_g[gi % NLN].n))
                            for gi in range(4, 8)])
        kt0 = emit_K_half(0, half0_ready)
        emit_tr_batch([(gi % NLN, xn_kvT, gi * 128, NKV) for gi in range(4, 8)], p1)
        half1_ready = ve.n
        v0 = emit_V_half(0, half0_ready)
        kt1 = emit_K_half(1, half1_ready)
        v1 = emit_V_half(1, half1_ready)
        KT_done = kt1
        Vseg_done = v1

        def f(g):
            g.wait_ge(ve.h, KT_done)
            d_kv.inc(g.dma_start(out=K_dram.rearrange("(p e) r -> p e r", p=128),
                                 in_=K_T2[:].rearrange("p (e r) -> p e r", e=NSEG)), 16)
            g.wait_ge(ve.h, Vseg_done)
            g.wait_ge(gp.h, vones_gp)
            d_kv.inc(g.dma_start(out=V_dram.rearrange("(p e) r -> p e r", p=64),
                                 in_=V_seg[:].rearrange("p (e r) -> p e r", e=NSEG)), 16)
        g_(f)
        kv_dram_done = d_kv.n
    else:
        kv_dram_done = d_kv.n
        def f(v):
            ve.inc(v.memset(acc[:], 0.0))
        v_(f)

    st_A.close()
    phase_barrier()

    st_C = ExitStack()
    wq_sb = st_C.enter_context(nc.sbuf_tensor("wq_sb", [128, 8 * DM], BF16))
    wp_sb = st_C.enter_context(nc.sbuf_tensor("wp_sb", [128, 8 * DM], BF16))
    xnqT = st_C.enter_context(nc.sbuf_tensor("xnqT", [128, 8 * TW], BF16))
    QT = st_C.enter_context(nc.sbuf_tensor("QT", [128, 8 * TW], BF16))
    attn_tm = st_C.enter_context(nc.sbuf_tensor("attn_tm", [64, 2 * DM], BF16))
    attn_fm = st_C.enter_context(nc.sbuf_tensor("attn_fm", [128, 8 * TW], BF16))
    Kst = [st_C.enter_context(nc.sbuf_tensor(f"Kst{i}", [128, NBLK * 512], BF16)) for i in range(2)]
    Vst = [st_C.enter_context(nc.sbuf_tensor(f"Vst{i}", [64, NBLK * VSEGW], BF16)) for i in range(2)]
    P_sb = [st_C.enter_context(nc.sbuf_tensor(f"P_sb{i}", [64, 64], BF16)) for i in range(3)]
    negC = [st_C.enter_context(nc.sbuf_tensor(f"negC{i}", [64, 1], F32)) for i in range(2)]
    aU = [st_C.enter_context(nc.sbuf_tensor(f"aU{i}", [64, VSEGW], BF16)) for i in range(2)]
    rec16 = st_C.enter_context(nc.sbuf_tensor("rec16", [64, 16], F32))

    if "C" in phases:
        def f(g):
            d_in.inc(_fmload(g, wq_sb, Wqkv[:, 0:DM], 8), 16)
            d_in.inc(_fmload(g, wp_sb, Wproj, 8), 16)
        g_(f)
        wqwp_loaded = d_in.n

        prev_proj_ve = ve.n
        kst_free = [0, 0]
        prev_se_hist = 0
        qproj_mm_end = 0

        def emit_c_gathers(j):
            kb = j % 2
            for gsub in range(4):
                k = gsub % NLN
                gi = 4 * j + gsub
                def f(g, gi=gi, k=k):
                    g.wait_ge(d_idx.h, qmap_done)
                    g.wait_ge(ve.h, ln_norm_ve[k])
                    d_g[k].inc(g.indirect_dma_start(
                        out=stg[k][:], out_offset=None,
                        in_=x_liv,
                        in_offset=IndirectOffsetOnAxis(qmap[:, gi:gi + 1], 0)), 16)
                g_(f)
            def f(g, j=j, kb=kb, guard=kst_free[kb]):
                g.wait_ge(d_kv.h, kv_dram_done)
                g.wait_ge(ve.h, idx_ready)
                g.wait_ge(pe.h, guard)
                for bb in range(NBLK):
                    col = blkcol(8 * j + bb)
                    d_kst[kb].inc(g.indirect_dma_start(
                        out=Kst[kb][:, bb * 512:bb * 512 + 512], out_offset=None,
                        in_=K_dram,
                        in_offset=IndirectOffsetOnAxis(idxK[:, col:col + 1], 0)), 16)
                    d_kst[kb].inc(g.indirect_dma_start(
                        out=Vst[kb][0:64, bb * VSEGW:bb * VSEGW + VSEGW], out_offset=None,
                        in_=V_dram,
                        in_offset=IndirectOffsetOnAxis(idxK[0:64, col:col + 1], 0)), 16)
            g_(f)
            return d_kst[kb].n

        def emit_c_lntr(j):
            gws = []
            for gsub in range(4):
                k = gsub % NLN
                gws.append((k, (d_g[k].h, d_g[k].n)))
            p_norm = emit_ln_batch(gws)
            return emit_tr_batch([(k, xnqT, gsub * 128, TW)
                                  for gsub, (k, gw) in enumerate(gws)],
                                 p_norm, extra_pe_wait=qproj_mm_end)

        def emit_c_qproj(j, xnqT_ready, prev_ea_se, prev_sc_pe):
            q_evacs = []
            for qc in range(8):
                bank = pb[qc % 2]
                def f(t, qc=qc, bank=bank, xnqT_ready=xnqT_ready, prev_ea_se=prev_ea_se):
                    t.wait_ge(ve.h, xnqT_ready)
                    t.wait_ge(d_in.h, wqwp_loaded)
                    if qc < 2:
                        t.wait_ge(se.h, prev_ea_se)
                    if len(q_evacs) >= 2:
                        t.wait_ge(ve.h, q_evacs[qc - 2])
                    for dc in range(8):
                        ins = t.matmul(bank[:, :],
                                       wq_sb[:, dc * DM + qc * 128:dc * DM + qc * 128 + 128],
                                       xnqT[:, dc * TW:dc * TW + 512],
                                       start=(dc == 0), stop=(dc == 7))
                    pe.inc(ins)
                t_(f)
                def f(v, qc=qc, bank=bank, prev_sc_pe=prev_sc_pe):
                    if qc == 0:
                        v.wait_ge(pe.h, prev_sc_pe)
                    v.wait_ge(pe.h, pe.n)
                    ve.inc(v.tensor_scalar(QT[:, qc * TW:qc * TW + 512], bank[:, :],
                                           bqkv_sb[:, qc:qc + 1], 0.125,
                                           op0=ALU.add, op1=ALU.mult))
                v_(f)
                q_evacs.append(ve.n)
            return ve.n, pe.n

        def emit_c_pipe(j, QT_ready, kv_st_done):
            nonlocal_vals = {}
            kb = j % 2
            NIT = NBLK * NH
            c_sc, c_rd, c_ex, c_av, c_ea = ({} for _ in range(5))
            c_no = {}
            c_tp = {}
            c_tc = {}
            pe_c, ve_c, se_c = pe.n, ve.n, se.n
            NSLOT = NIT + 6 + 16
            for slot in range(NSLOT):
                if slot < NIT:
                    pe_c += 1; c_sc[slot] = pe_c
                if 0 <= slot - 1 < NIT:
                    ve_c += 1; c_rd[slot - 1] = ve_c
                if 0 <= slot - 2 < NIT:
                    se_c += 1; c_ex[slot - 2] = se_c
                if 0 <= slot - 3 < NIT:
                    pe_c += 1; c_av[slot - 3] = pe_c
                if 0 <= slot - 4 < NIT:
                    i4 = slot - 4
                    se_c += 1; c_ea[i4] = se_c
                    if i4 % NH == NH - 1:
                        ve_c += 3; c_no[i4 // NH] = ve_c
                if slot >= 5 and (slot - 5) % NH == NH - 1:
                    bb2 = (slot - 5) // NH
                    if 0 <= bb2 < NBLK:
                        for ch in range(8):
                            pe_c += 1; c_tp[(bb2, ch)] = pe_c
                            ve_c += 1; c_tc[(bb2, ch)] = ve_c

            def f(t, j=j, kb=kb, QT_ready=QT_ready, kv_st_done=kv_st_done,
                  ppv=prev_proj_ve, prev_se=prev_se_hist, tbf=tuple(tb_free)):
                for slot in range(NSLOT):
                    i = slot
                    if i < NIT:
                        bb, h = divmod(i, NH)
                        if i == 0:
                            t.wait_ge(ve.h, QT_ready)
                            t.wait_ge(d_kst[kb].h, kv_st_done)
                            t.wait_ge(ve.h, ppv)
                            t.wait_ge(se.h, prev_se)
                        if i >= 2:
                            t.wait_ge(ve.h, c_rd[i - 2])
                            t.wait_ge(se.h, c_ex[i - 2])
                        lhs = Kst[kb][(h & 1) * 64:(h & 1) * 64 + 64,
                                      bb * 512 + (h >> 1) * 64:bb * 512 + (h >> 1) * 64 + 64]
                        rhs = QT[(h & 1) * 64:(h & 1) * 64 + 64,
                                 (h >> 1) * TW + 64 * bb:(h >> 1) * TW + 64 * bb + 64]
                        pe.inc(t.matmul(pb[2 + (i % 2)][0:64, 0:64], lhs, rhs,
                                        start=True, stop=True))
                        assert pe.n == c_sc[i]
                    i = slot - 3
                    if 0 <= i < NIT:
                        bb, h = divmod(i, NH)
                        t.wait_ge(se.h, c_ex[i])
                        if i >= 2:
                            t.wait_ge(se.h, c_ea[i - 2])
                        rhs = Vst[kb][0:64, bb * VSEGW + 65 * h:bb * VSEGW + 65 * h + 65]
                        pe.inc(t.matmul(pb[(i % 2)][0:64, 0:65], P_sb[i % 3][:], rhs,
                                        start=True, stop=True))
                        assert pe.n == c_av[i]
                    if slot >= 5 and (slot - 5) % NH == NH - 1:
                        bb2 = (slot - 5) // NH
                        if 0 <= bb2 < NBLK:
                            for ch in range(8):
                                if ch == 0:
                                    t.wait_ge(ve.h, c_no[bb2])
                                if ch >= 2:
                                    t.wait_ge(ve.h, c_tc[(bb2, ch - 2)])
                                elif bb2 >= 1:
                                    t.wait_ge(ve.h, c_tc[(bb2 - 1, 6 + ch)])
                                elif ch < 2:
                                    t.wait_ge(ve.h, tbf[ch])
                                pe.inc(t.transpose(
                                    tb[ch % 2][0:128, 0:64],
                                    attn_tm[0:64, (bb2 % 2) * DM + ch * 128:
                                            (bb2 % 2) * DM + ch * 128 + 128],
                                    identb[0:64, 0:64]))
                                assert pe.n == c_tp[(bb2, ch)]
            t_(f)

            def f(v, kb=kb):
                for slot in range(NSLOT):
                    i = slot - 1
                    if 0 <= i < NIT:
                        v.wait_ge(pe.h, c_sc[i])
                        if i >= 2:
                            v.wait_ge(se.h, c_ex[i - 2])
                        ve.inc(v.reduce_max(negC[i % 2][:], pb[2 + (i % 2)][0:64, 0:64],
                                            axis=AX.X, negate=True))
                        assert ve.n == c_rd[i]
                    i = slot - 4
                    if 0 <= i < NIT:
                        bb, h = divmod(i, NH)
                        if h == NH - 1:
                            v.wait_ge(se.h, c_ea[i])
                            if bb >= 2:
                                v.wait_ge(pe.h, c_tp[(bb - 2, 7)])
                            a3 = aU[bb % 2][:].rearrange("p (hh c) -> p hh c", hh=16)
                            ve.inc(v.tensor_scalar(rec16[:], a3[:, :, 64:65].squeeze(2),
                                                   1e-6, None, op0=ALU.add))
                            v.wait_ge(ve.h, ve.n)
                            ve.inc(v.reciprocal(rec16[:], rec16[:]))
                            v.wait_ge(ve.h, ve.n)
                            dstb = attn_tm[0:64, (bb % 2) * DM:(bb % 2) * DM + DM].rearrange(
                                "p (hh d) -> p hh d", hh=16)
                            ve.inc(v.tensor_tensor(
                                dstb, a3[:, :, 0:64],
                                rec16[:].unsqueeze(2).broadcast_to([64, 16, 64]),
                                op=ALU.mult))
                            assert ve.n == c_no[bb]
                    if slot >= 5 and (slot - 5) % NH == NH - 1:
                        bb2 = (slot - 5) // NH
                        if 0 <= bb2 < NBLK:
                            for ch in range(8):
                                v.wait_ge(pe.h, c_tp[(bb2, ch)])
                                ve.inc(v.tensor_copy(
                                    attn_fm[:, ch * TW + bb2 * 64:ch * TW + bb2 * 64 + 64],
                                    tb[ch % 2][0:128, 0:64]))
                                assert ve.n == c_tc[(bb2, ch)]
            v_(f)
            tb_free[0] = c_tc[(NBLK - 1, 6)]
            tb_free[1] = c_tc[(NBLK - 1, 7)]
            attn_fm_ready = ve.n

            def f(a):
                for slot in range(NSLOT):
                    i = slot - 2
                    if 0 <= i < NIT:
                        a.wait_ge(ve.h, c_rd[i])
                        if i >= 3:
                            a.wait_ge(pe.h, c_av[i - 3])
                        se.inc(a.activation(P_sb[i % 3][:], pb[2 + (i % 2)][0:64, 0:64],
                                            AF.Exp, bias=negC[i % 2][:], scale=1.0))
                        assert se.n == c_ex[i]
                    i = slot - 4
                    if 0 <= i < NIT:
                        bb, h = divmod(i, NH)
                        a.wait_ge(pe.h, c_av[i])
                        se.inc(a.activation(aU[bb % 2][:, 65 * h:65 * h + 65],
                                            pb[(i % 2)][0:64, 0:65], AF.Identity,
                                            bias=0.0, scale=1.0))
                        assert se.n == c_ea[i]
            a_(f)

            p_evacs = []
            for oc in range(8):
                bank = pb[2 + (oc % 2)]
                def f(t, oc=oc, bank=bank, attn_fm_ready=attn_fm_ready):
                    t.wait_ge(ve.h, attn_fm_ready)
                    if len(p_evacs) >= 2:
                        t.wait_ge(ve.h, p_evacs[oc - 2])
                    for fc in range(8):
                        ins = t.matmul(bank[:, :],
                                       wp_sb[:, fc * DM + oc * 128:fc * DM + oc * 128 + 128],
                                       attn_fm[:, fc * TW:fc * TW + 512],
                                       start=(fc == 0), stop=(fc == 7))
                    pe.inc(ins)
                t_(f)
                def f(v, oc=oc, bank=bank, j=j):
                    v.wait_ge(pe.h, pe.n)
                    ve.inc(v.tensor_scalar(acc[:, oc * TOK + j * TW:oc * TOK + j * TW + 512],
                                           bank[:, :], bproj_sb[:, oc:oc + 1], None,
                                           op0=ALU.add))
                v_(f)
                p_evacs.append(ve.n)
            return (c_ea[NIT - 1], c_sc[NIT - 1], c_av[NIT - 1], ve.n)

        kvst = {0: emit_c_gathers(0)}
        xr = emit_c_lntr(0)
        qt, qproj_mm_end = emit_c_qproj(0, xr, 0, 0)
        qts = {0: qt}
        for j in range(NT):
            if j + 1 < NT:
                kvst[j + 1] = emit_c_gathers(j + 1)
                xr = emit_c_lntr(j + 1)
            ea_last, sc_last, av_last, pv = emit_c_pipe(j, qts[j], kvst[j])
            prev_proj_ve = pv
            prev_se_hist = ea_last
            kst_free[j % 2] = av_last
            if j + 1 < NT:
                qt, qproj_mm_end = emit_c_qproj(j + 1, xr, ea_last, sc_last)
                qts[j + 1] = qt
        attn_done_ve = ve.n
        attn_done_ve = ve.n
    else:
        attn_done_ve = ve.n

    st_C.close()
    phase_barrier()

    st_BD = ExitStack()
    xn_ownT = st_BD.enter_context(nc.sbuf_tensor("xn_ownT", [128, 8 * TOK], BF16))

    if "B" in phases or "D" in phases:
        for bat in range(4):
            items = []
            for sub in range(4):
                tt = bat * 4 + sub
                k = tt % NLN
                def f(g, tt=tt, k=k):
                    g.wait_ge(ve.h, ln_norm_ve[k])
                    d_own[k].inc(g.dma_start(out=stg[k][:],
                                             in_=x_own[tt * 128:tt * 128 + 128, :]), 16)
                g_(f)
                items.append((k, (d_own[k].h, d_own[k].n), xn_ownT, tt * 128, TOK))
            emit_ln_tr_batch(items)
    xnownT_ready = ve.n

    st_D = ExitStack()
    gate_sb = st_D.enter_context(nc.sbuf_tensor("gate_sb", [128, 8 * NEXP], BF16))
    w_tm = st_D.enter_context(nc.sbuf_tensor("w_tm", [128, 16 * NEXP], F32))
    sgs = st_D.enter_context(nc.sbuf_tensor("sgs", [128, NEXP], F32))
    sgm = st_D.enter_context(nc.sbuf_tensor("sgm", [128, 1], F32))
    sgm2 = st_D.enter_context(nc.sbuf_tensor("sgm2", [128, 1], F32))
    sgtmp = st_D.enter_context(nc.sbuf_tensor("sgtmp", [128, NEXP], F32))
    wbcast = [st_D.enter_context(nc.sbuf_tensor(f"wbcast{i}", [128, TOK], BF16)) for i in range(2)]
    h_q = [st_D.enter_context(nc.sbuf_tensor(f"h_q{i}", [128, 4 * 512], BF16)) for i in range(2)]
    h32 = st_D.enter_context(nc.sbuf_tensor("h32", [128, 512], F32))
    hsil = [st_D.enter_context(nc.sbuf_tensor(f"hsil{i}", [128, 512], BF16)) for i in range(2)]
    wgb = [st_D.enter_context(nc.sbuf_tensor(f"wgb{i}", [128, 8 * 512], BF16)) for i in range(2)]
    wub = [st_D.enter_context(nc.sbuf_tensor(f"wub{i}", [128, 8 * 512], BF16)) for i in range(2)]
    wdb = [st_D.enter_context(nc.sbuf_tensor(f"wdb{i}", [128, 4 * DM], BF16)) for i in range(2)]

    if "D" in phases:
        def f(g):
            d_in.inc(_fmload(g, gate_sb, gate_w, 8), 16)
        g_(f)
        gate_loaded = d_in.n

        def wload(g, e, q):
            kbuf = (e * 4 + q) % 2
            if e < NEXP:
                srcs = (w_gate[e, :, 512 * q:512 * q + 512],
                        w_up[e, :, 512 * q:512 * q + 512],
                        w_down[e, 512 * q:512 * q + 512, :])
            else:
                srcs = (ws_gate[:, 512 * q:512 * q + 512],
                        ws_up[:, 512 * q:512 * q + 512],
                        ws_down[512 * q:512 * q + 512, :])
            d_ffn[kbuf].inc(_fmload(g, wgb[kbuf], srcs[0], 8), 16)
            d_ffn[kbuf].inc(_fmload(g, wub[kbuf], srcs[1], 8), 16)
            d_ffn[kbuf].inc(_fmload(g, wdb[kbuf], srcs[2], 4), 16)

        eq_list = [(e, q) for e in range(NEXP + 1) for q in range(4)]
        eq_loaded = {}
        eq_consumed = {}

        def f(g):
            for i in range(2):
                wload(g, *eq_list[i])
                eq_loaded[i] = d_ffn[i % 2].n
        g_(f)

        zall = st_D.enter_context(nc.sbuf_tensor("zall", [128, 128], F32))
        zexp = st_D.enter_context(nc.sbuf_tensor("zexp", [128, 128], F32))
        ztmp = st_D.enter_context(nc.sbuf_tensor("ztmp", [128, 128], F32))
        g16a = st_D.enter_context(nc.sbuf_tensor("g16a", [128, 16], F32))
        g16b = st_D.enter_context(nc.sbuf_tensor("g16b", [128, 16], F32))

        zg_done = []
        for tt in range(16):
            bank = pb[tt % 4]
            def f(t, tt=tt, bank=bank):
                t.wait_ge(d_in.h, gate_loaded)
                t.wait_ge(ve.h, xnownT_ready)
                if len(zg_done) >= 4:
                    t.wait_ge(ve.h, zg_done[tt - 4])
                for dc in range(8):
                    ins = t.matmul(bank[:, 0:NEXP],
                                   xn_ownT[:, dc * TOK + tt * 128:dc * TOK + tt * 128 + 128],
                                   gate_sb[:, dc * NEXP:dc * NEXP + NEXP],
                                   start=(dc == 0), stop=(dc == 7))
                pe.inc(ins)
            t_(f)
            def f(v, tt=tt, bank=bank):
                v.wait_ge(pe.h, pe.n)
                ve.inc(v.tensor_copy(zall[:, tt * NEXP:tt * NEXP + NEXP], bank[:, 0:NEXP]))
            v_(f)
            zg_done.append(ve.n)

        def bc16(ap16):
            return ap16[:].unsqueeze(2).broadcast_to([128, 16, NEXP])

        zv = zall[:].rearrange("p (h e) -> p h e", h=16)
        ze = zexp[:].rearrange("p (h e) -> p h e", h=16)
        zt = ztmp[:].rearrange("p (h e) -> p h e", h=16)
        def f(v):
            v.wait_ge(ve.h, ve.n)
            ve.inc(v.reduce_max(g16a[:], zv, axis=AX.X, negate=True))
            v.wait_ge(ve.h, ve.n)
            ve.inc(v.tensor_tensor(zv, zv, bc16(g16a), op=ALU.add))
        v_(f)
        g_sub_done = ve.n
        def f(a):
            a.wait_ge(ve.h, g_sub_done)
            se.inc(a.activation(zexp[:], zall[:], AF.Exp, bias=0.0, scale=1.0))
        a_(f)
        def f(v):
            v.wait_ge(se.h, se.n)
            ve.inc(v.reduce_sum(g16a[:], ze, axis=AX.X))
            v.wait_ge(ve.h, ve.n)
            ve.inc(v.reciprocal(g16a[:], g16a[:]))
            v.wait_ge(ve.h, ve.n)
            ve.inc(v.tensor_tensor(ze, ze, bc16(g16a), op=ALU.mult))
            v.wait_ge(ve.h, ve.n)
            ve.inc(v.reduce_max(g16a[:], ze, axis=AX.X))
            v.wait_ge(ve.h, ve.n)
            ve.inc(v.tensor_tensor(zt, ze, bc16(g16a), op=ALU.is_equal))
            v.wait_ge(ve.h, ve.n)
            ve.inc(v.scalar_tensor_tensor(ztmp[:], ztmp[:], -1e30, zexp[:],
                                          op0=ALU.mult, op1=ALU.add))
            v.wait_ge(ve.h, ve.n)
            ve.inc(v.reduce_max(g16b[:], zt, axis=AX.X))
            v.wait_ge(ve.h, ve.n)
            ve.inc(v.tensor_tensor(zt, ze, bc16(g16b), op=ALU.is_ge))
            v.wait_ge(ve.h, ve.n)
            ve.inc(v.tensor_mul(ztmp[:], ztmp[:], zexp[:]))
            ve.inc(v.tensor_add(g16a[:], g16a[:], g16b[:]))
            v.wait_ge(ve.h, ve.n)
            ve.inc(v.tensor_scalar(g16a[:], g16a[:], 1e-20, None, op0=ALU.add))
            v.wait_ge(ve.h, ve.n)
            ve.inc(v.reciprocal(g16a[:], g16a[:]))
            v.wait_ge(ve.h, ve.n)
            ve.inc(v.tensor_tensor(w_tm[:].rearrange("p (h e) -> p h e", h=16), zt,
                                   bc16(g16a), op=ALU.mult))
        v_(f)

        st_WB = ExitStack()
        wT8 = st_WB.enter_context(nc.sbuf_tensor("wT8", [NEXP, TOK], BF16))
        wTflat = [st_WB.enter_context(nc.sbuf_tensor(f"wTflat{i}", [1, TOK], BF16)) for i in range(2)]
        wbstage = st_WB.enter_context(nc.sbuf_tensor("wbstage", [128, TOK], BF16))

        for tt in range(16):
            def f(t, tt=tt):
                t.wait_ge(ve.h, ve.n)
                pe.inc(t.transpose(pb[2][0:NEXP, 0:128],
                                   w_tm[:, tt * NEXP:tt * NEXP + NEXP], identf[:]))
            t_(f)
            def f(v, tt=tt):
                v.wait_ge(pe.h, pe.n)
                ve.inc(v.tensor_copy(wT8[:, tt * 128:tt * 128 + 128], pb[2][0:NEXP, 0:128]))
            v_(f)
        wT8_ready = ve.n

        def f(g):
            g.wait_ge(ve.h, wT8_ready)
            d_wtf[0].inc(g.dma_start(out=wTflat[0][:], in_=wT8[0:1, :]), 16)
        g_(f)
        wtf_done = {0: d_wtf[0].n}
        mm_done = {}
        for e in range(NEXP):
            for tq in range(4):
                bank = pb[4 + tq % 2]
                def f(t, e=e, tq=tq, bank=bank):
                    t.wait_ge(d_wtf[e % 2].h, wtf_done[e])
                    t.wait_ge(gp.h, 5)
                    t.wait_ge(ve.h, ve.n)
                    t.wait_ge(d_kv.h, d_kv.n)
                    pe.inc(t.matmul(bank[:, :], ones128b[:],
                                    wTflat[e % 2][0:1, tq * 512:tq * 512 + 512],
                                    start=True, stop=True))
                t_(f)
                def f(v, tq=tq, bank=bank):
                    v.wait_ge(pe.h, pe.n)
                    ve.inc(v.tensor_copy(wbstage[:, tq * 512:tq * 512 + 512], bank[:, :]))
                v_(f)
            mm_done[e] = pe.n
            if e + 1 < NEXP:
                def f(g, e=e):
                    if e >= 1:
                        g.wait_ge(pe.h, mm_done[e - 1])
                    d_wtf[(e + 1) % 2].inc(
                        g.dma_start(out=wTflat[(e + 1) % 2][:], in_=wT8[e + 1:e + 2, :]), 16)
                g_(f)
                wtf_done[e + 1] = d_wtf[(e + 1) % 2].n
            def f(g, e=e):
                g.wait_ge(ve.h, ve.n)
                d_kv.inc(g.dma_start(out=W_wb[e], in_=wbstage[:]), 16)
            g_(f)
        wb_built = d_kv.n

        st_WB.close()
        st_TB.close()
        st_Dp = ExitStack()
        pdown = st_Dp.enter_context(nc.psum_tensor("pdown", [128, 4, 512], F32))

        def f(s2):
            s2.wait_ge(d_kv.h, wb_built)
            d_wb[0].inc(s2.dma_start(out=wbcast[0][:], in_=W_wb[0]), 16)
        s_(f)
        wb_loaded = {0: d_wb[0].n}
        exp_mult_ve = {}
        c_gate, c_up, c_silu, c_m1, c_m2, c_down, c_add = ({} for _ in range(7))
        prev_add_ve = ve.n

        for i, (e, q) in enumerate(eq_list):
            if q == 0 and e + 1 < NEXP:
                def f(s2, e=e):
                    s2.wait_ge(d_kv.h, wb_built)
                    if e >= 1:
                        s2.wait_ge(ve.h, exp_mult_ve[e - 1])
                    d_wb[(e + 1) % 2].inc(
                        s2.dma_start(out=wbcast[(e + 1) % 2][:], in_=W_wb[e + 1]), 16)
                s_(f)
                wb_loaded[e + 1] = d_wb[(e + 1) % 2].n

            pe_c, se_c, ve_c = pe.n, se.n, ve.n
            PE_ORDER = ([("gu", 0, sl) for sl in range(4)]
                        + [("gu", 1, 0), ("gu", 1, 1), ("dn", 0, 0),
                           ("gu", 1, 2), ("gu", 1, 3), ("dn", 0, 1),
                           ("gu", 2, 0), ("gu", 2, 1), ("dn", 1, 0),
                           ("gu", 2, 2), ("gu", 2, 3), ("dn", 1, 1),
                           ("gu", 3, 0), ("gu", 3, 1), ("dn", 2, 0),
                           ("gu", 3, 2), ("gu", 3, 3), ("dn", 2, 1),
                           ("dn", 3, 0), ("dn", 3, 1)])
            for kind, a, b in PE_ORDER:
                if kind == "gu":
                    pe_c += 1; c_gate[(i, a, b)] = pe_c
                    pe_c += 1; c_up[(i, a, b)] = pe_c
                else:
                    pe_c += 1; c_down[(i, a, b)] = pe_c
            VE_ORDER = []
            for tq in range(4):
                for sl in range(4):
                    VE_ORDER.append(("m", tq, sl))
                    if tq >= 1 and sl == 1:
                        VE_ORDER.append(("a", tq - 1, 0))
                    if tq >= 1 and sl == 3:
                        VE_ORDER.append(("a", tq - 1, 1))
            VE_ORDER += [("a", 3, 0), ("a", 3, 1)]
            for tq in range(4):
                for sl in range(4):
                    se_c += 1; c_silu[(i, tq, sl)] = se_c
            for kind, a0, b0 in VE_ORDER:
                if kind == "m":
                    ve_c += 1; c_m1[(i, a0, b0)] = ve_c
                    ve_c += 1; c_m2[(i, a0, b0)] = ve_c
                else:
                    ve_c += 1; c_add[(i, a0, b0)] = ve_c

            def emit_gu(t, i, tq, sls):
                for sl in sls:
                    kb2 = sl % 2
                    if sl >= 2:
                        t.wait_ge(se.h, c_silu[(i, tq, sl - 2)])
                    elif tq >= 1:
                        t.wait_ge(se.h, c_silu[(i, tq - 1, sl + 2)])
                    elif i >= 1:
                        t.wait_ge(se.h, c_silu[(i - 1, 3, sl + 2)])
                    buf = wgb[i % 2]
                    for dc in range(8):
                        ins = t.matmul(pb[kb2][:, :],
                                       buf[:, dc * 512 + sl * 128:dc * 512 + sl * 128 + 128],
                                       xn_ownT[:, dc * TOK + tq * 512:dc * TOK + tq * 512 + 512],
                                       start=(dc == 0), stop=(dc == 7))
                    pe.inc(ins)
                    assert pe.n == c_gate[(i, tq, sl)]
                    if sl >= 2:
                        t.wait_ge(ve.h, c_m1[(i, tq, sl - 2)])
                    elif tq >= 1:
                        t.wait_ge(ve.h, c_m1[(i, tq - 1, sl + 2)])
                    elif i >= 1:
                        t.wait_ge(ve.h, c_m1[(i - 1, 3, sl + 2)])
                    buf = wub[i % 2]
                    for dc in range(8):
                        ins = t.matmul(pb[2 + kb2][:, :],
                                       buf[:, dc * 512 + sl * 128:dc * 512 + sl * 128 + 128],
                                       xn_ownT[:, dc * TOK + tq * 512:dc * TOK + tq * 512 + 512],
                                       start=(dc == 0), stop=(dc == 7))
                    pe.inc(ins)
                    assert pe.n == c_up[(i, tq, sl)]

            def emit_dn(t, i, tq, hh, pav):
                t.wait_ge(ve.h, c_m2[(i, tq, 3)])
                if hh == 1:
                    t.wait_ge(ve.h, c_add[(i, tq, 0)])
                elif tq >= 1:
                    t.wait_ge(ve.h, c_add[(i, tq - 1, 1)])
                else:
                    t.wait_ge(ve.h, pav)
                for sl in range(4):
                    for c in range(4):
                        ins = t.matmul(
                            pdown[:, c, :],
                            wdb[i % 2][:, sl * DM + (4 * hh + c) * 128:
                                       sl * DM + (4 * hh + c) * 128 + 128],
                            h_q[tq % 2][:, sl * 512:sl * 512 + 512],
                            start=(sl == 0), stop=(sl == 3))
                pe.inc(ins)
                assert pe.n == c_down[(i, tq, hh)]

            def f(t, i=i, e=e, pav=prev_add_ve):
                t.wait_ge(d_ffn[i % 2].h, eq_loaded[i])
                if i == 0:
                    t.wait_ge(ve.h, xnownT_ready)
                emit_gu(t, i, 0, [0, 1, 2, 3])
                emit_gu(t, i, 1, [0, 1])
                emit_dn(t, i, 0, 0, pav)
                emit_gu(t, i, 1, [2, 3])
                emit_dn(t, i, 0, 1, pav)
                emit_gu(t, i, 2, [0, 1])
                emit_dn(t, i, 1, 0, pav)
                emit_gu(t, i, 2, [2, 3])
                emit_dn(t, i, 1, 1, pav)
                emit_gu(t, i, 3, [0, 1])
                emit_dn(t, i, 2, 0, pav)
                emit_gu(t, i, 3, [2, 3])
                emit_dn(t, i, 2, 1, pav)
                emit_dn(t, i, 3, 0, pav)
                emit_dn(t, i, 3, 1, pav)
            t_(f)

            def f(a, i=i):
                for tq in range(4):
                    for sl in range(4):
                        kb2 = sl % 2
                        a.wait_ge(pe.h, c_gate[(i, tq, sl)])
                        if sl >= 2:
                            a.wait_ge(ve.h, c_m1[(i, tq, sl - 2)])
                        elif tq >= 1:
                            a.wait_ge(ve.h, c_m1[(i, tq - 1, sl + 2)])
                        elif i >= 1:
                            a.wait_ge(ve.h, c_m1[(i - 1, 3, sl + 2)])
                        se.inc(a.activation(hsil[kb2][:], pb[kb2][:, :], AF.Silu,
                                            bias=0.0, scale=1.0))
                        assert se.n == c_silu[(i, tq, sl)]
            a_(f)

            def f(v, i=i, e=e, VE_ORDER=VE_ORDER):
                for kind, a0, b0 in VE_ORDER:
                    if kind == "m":
                        tq, sl = a0, b0
                        tqp = tq % 2
                        kb2 = sl % 2
                        v.wait_ge(se.h, c_silu[(i, tq, sl)])
                        v.wait_ge(pe.h, c_up[(i, tq, sl)])
                        v.wait_ge(ve.h, ve.n)
                        ve.inc(v.tensor_tensor(h32[:], hsil[kb2][:], pb[2 + kb2][:, :],
                                               op=ALU.mult))
                        assert ve.n == c_m1[(i, tq, sl)]
                        v.wait_ge(ve.h, ve.n)
                        if tq >= 2:
                            v.wait_ge(pe.h, c_down[(i, tq - 2, 1)])
                        elif i >= 1 and tq < 2:
                            v.wait_ge(pe.h, c_down[(i - 1, tq + 2, 1)])
                        if e < NEXP:
                            v.wait_ge(d_wb[e % 2].h, wb_loaded[e])
                            ve.inc(v.tensor_tensor(h_q[tqp][:, sl * 512:sl * 512 + 512],
                                                   h32[:], wbcast[e % 2][:, tq * 512:tq * 512 + 512],
                                                   op=ALU.mult))
                        else:
                            ve.inc(v.tensor_copy(h_q[tqp][:, sl * 512:sl * 512 + 512],
                                                 h32[:]))
                        assert ve.n == c_m2[(i, tq, sl)]
                    else:
                        tq, hh = a0, b0
                        v.wait_ge(pe.h, c_down[(i, tq, hh)])
                        accv = acc[:].rearrange("p (c t2) -> p c t2", c=8)[
                            :, 4 * hh:4 * hh + 4, tq * 512:tq * 512 + 512]
                        ve.inc(v.tensor_tensor(accv, accv, pdown[:, :, :], op=ALU.add))
                        assert ve.n == c_add[(i, tq, hh)]
            v_(f)

            prev_add_ve = c_add[(i, 3, 1)]
            exp_mult_ve[e] = c_m2[(i, 3, 3)]
            eq_consumed[i] = pe.n
            if i + 2 < len(eq_list):
                def f(g, i=i):
                    g.wait_ge(pe.h, eq_consumed[i])
                    wload(g, *eq_list[i + 2])
                    eq_loaded[i + 2] = d_ffn[i % 2].n
                g_(f)
        st_Dp.close()

    st_D.close()
    st_BD.close()

    ffn_done_ve = ve.n

    def f(g):
        g.wait_ge(ve.h, ffn_done_ve)
        d_acc.inc(g.dma_start(out=acc[:].rearrange("p (c t2) -> p c t2", c=8),
                              in_=xT_own.rearrange("(c p) t2 -> p c t2", p=128),
                              accum_op=ALU.add), 16)
    g_(f)

    def f(s):
        s.wait_ge(d_acc.h, d_acc.n)
        d_out.inc(s.dma_start(out=outT.rearrange("(c p) t2 -> p c t2", p=128),
                              in_=acc[:].rearrange("p (c t2) -> p c t2", c=8)), 16)
        s.wait_ge(d_out.h, d_out.n)
    s_(f)

    es.close()
    return nc


_CACHED = {}


def _get_program():
    if "nc" not in _CACHED:
        _CACHED["nc"] = build_program()
    return _CACHED["nc"]


def kernel(**inputs):
    x = np.asarray(inputs["L0_attn_x"], dtype=np.float32)[0]
    xT = np.ascontiguousarray(x.T)
    x_live = np.ascontiguousarray(x[:LIVE])

    base = {
        "x_liv": x_live,
        "mapping": np.ascontiguousarray(np.asarray(inputs["L0_attn_mapping"], dtype=np.int32)),
        "Wqkv": np.ascontiguousarray(np.asarray(inputs["L0_attn_Wqkv"], dtype=np.float32)),
        "bqkv": np.ascontiguousarray(np.asarray(inputs["L0_attn_bqkv"], dtype=np.float32)),
        "Wproj": np.ascontiguousarray(np.asarray(inputs["L0_attn_Wproj"], dtype=np.float32)),
        "bproj": np.ascontiguousarray(np.asarray(inputs["L0_attn_bproj"], dtype=np.float32)),
        "gate_w": np.ascontiguousarray(np.asarray(inputs["L0_ffn_gate_w"], dtype=np.float32)),
        "w_gate": np.ascontiguousarray(np.asarray(inputs["L0_ffn_w_gate"], dtype=np.float32)),
        "w_up": np.ascontiguousarray(np.asarray(inputs["L0_ffn_w_up"], dtype=np.float32)),
        "w_down": np.ascontiguousarray(np.asarray(inputs["L0_ffn_w_down"], dtype=np.float32)),
        "ws_gate": np.ascontiguousarray(np.asarray(inputs["L0_ffn_ws_gate"], dtype=np.float32)),
        "ws_up": np.ascontiguousarray(np.asarray(inputs["L0_ffn_ws_up"], dtype=np.float32)),
        "ws_down": np.ascontiguousarray(np.asarray(inputs["L0_ffn_ws_down"], dtype=np.float32)),
    }

    in_maps = []
    p = np.arange(128)[:, None]
    g = np.arange(16)[None, :]
    for c in range(NCORES):
        qp = np.minimum(c * TOK + 128 * g + p, LIVE - 1).astype(np.int32)
        m = dict(base)
        m["x_own"] = np.ascontiguousarray(x[c * TOK:(c + 1) * TOK])
        m["xT_own"] = np.ascontiguousarray(xT[:, c * TOK:(c + 1) * TOK])
        m["qpos"] = np.ascontiguousarray(qp)
        in_maps.append(m)

    nc = _get_program()
    res = run_bass_kernel_spmd(nc, in_maps, core_ids=list(range(NCORES)))
    out = np.empty((M, DM), dtype=np.float32)
    for c in range(NCORES):
        out[c * TOK:(c + 1) * TOK] = res.results[c]["outT"].T
    return out.reshape(1, M, DM)


# version 08
# speedup vs baseline: 1.7447x; 1.0627x over previous
import numpy as np
from contextlib import ExitStack

import concourse.bass as bass
import concourse.mybir as mybir
from concourse.bass_utils import run_bass_kernel_spmd
from concourse.bass import IndirectOffsetOnAxis

F32 = mybir.dt.float32
BF16 = mybir.dt.bfloat16
I32 = mybir.dt.int32
AF = mybir.ActivationFunctionType
ALU = mybir.AluOpType
AX = mybir.AxisListType

NCORES = 8
M = 16384
DM = 1024
LIVE = 4096
TOK = 2048
NH = 16
NKV = 1024
NSEG = 16
VSEGW = 16 * 65
DFF = 2048
NEXP = 8
NT = 4
TW = 512
NBLK = 8


class Sem:
    def __init__(self, es, nc, name):
        self.h = es.enter_context(nc.semaphore(name))
        self.n = 0

    def inc(self, instr, k=1):
        instr.then_inc(self.h, k)
        self.n += k
        return instr


def _fmload(eng, dst, src, nch):
    return eng.dma_start(out=dst[:].rearrange("p (c o) -> p c o", c=nch),
                         in_=src.rearrange("(c p) o -> p c o", p=128))


def build_program(phases=("A", "C", "B", "D", "E")):
    nc = bass.Bass("TRN2", target_bir_lowering=False, enable_asserts=False)

    x_liv = nc.dram_tensor("x_liv", [LIVE, DM], F32, kind="ExternalInput").ap()
    x_own = nc.dram_tensor("x_own", [TOK, DM], F32, kind="ExternalInput").ap()
    xT_own = nc.dram_tensor("xT_own", [DM, TOK], F32, kind="ExternalInput").ap()
    mapping = nc.dram_tensor("mapping", [LIVE], I32, kind="ExternalInput").ap()
    qpos = nc.dram_tensor("qpos", [128, 16], I32, kind="ExternalInput").ap()
    Wqkv = nc.dram_tensor("Wqkv", [DM, 3 * DM], F32, kind="ExternalInput").ap()
    bqkv = nc.dram_tensor("bqkv", [3 * DM], F32, kind="ExternalInput").ap()
    Wproj = nc.dram_tensor("Wproj", [DM, DM], F32, kind="ExternalInput").ap()
    bproj = nc.dram_tensor("bproj", [DM], F32, kind="ExternalInput").ap()
    gate_w = nc.dram_tensor("gate_w", [DM, NEXP], F32, kind="ExternalInput").ap()
    w_gate = nc.dram_tensor("w_gate", [NEXP, DM, DFF], F32, kind="ExternalInput").ap()
    w_up = nc.dram_tensor("w_up", [NEXP, DM, DFF], F32, kind="ExternalInput").ap()
    w_down = nc.dram_tensor("w_down", [NEXP, DFF, DM], F32, kind="ExternalInput").ap()
    ws_gate = nc.dram_tensor("ws_gate", [DM, DFF], F32, kind="ExternalInput").ap()
    ws_up = nc.dram_tensor("ws_up", [DM, DFF], F32, kind="ExternalInput").ap()
    ws_down = nc.dram_tensor("ws_down", [DFF, DM], F32, kind="ExternalInput").ap()
    outT = nc.dram_tensor("outT", [DM, TOK], F32, kind="ExternalOutput").ap()
    K_dram = nc.dram_tensor("K_dram", [128 * NSEG, 512], BF16).ap()
    V_dram = nc.dram_tensor("V_dram", [64 * NSEG, VSEGW], BF16).ap()
    W_wb = nc.dram_tensor("W_wb", [NEXP, 128, TOK], BF16).ap()

    es = ExitStack()

    def sbuf(name, shape, dt):
        return es.enter_context(nc.sbuf_tensor(name, shape, dt))

    identf = sbuf("identf", [128, 128], F32)
    identb = sbuf("identb", [128, 128], BF16)
    ones64f = sbuf("ones64f", [1, 64], F32)
    ones128f = sbuf("ones128f", [1, 128], F32)
    ones128b = sbuf("ones128b", [1, 128], BF16)
    iota16f = sbuf("iota16f", [128, 1], F32)
    qpos_sb = sbuf("qpos_sb", [128, 16], I32)
    kvidx = sbuf("kvidx", [128, 8], I32)
    qmap = sbuf("qmap", [128, 16], I32)
    segraw = sbuf("segraw", [1, 32], I32)
    segf = sbuf("segf", [1, 32], F32)
    idxK = sbuf("idxK", [128, 32], I32)
    bqkv_sb = sbuf("bqkv_sb", [128, 24], F32)
    bqs = sbuf("bqs", [128, 8], F32)
    bproj_sb = sbuf("bproj_sb", [128, 8], F32)
    acc = sbuf("acc", [128, 8 * TOK], F32)
    NLN = 4
    bn6 = [sbuf(f"bn6_{i}", [128, 12], F32) for i in range(NLN)]
    bn2 = [sbuf(f"bn2_{i}", [128, 2], F32) for i in range(NLN)]
    rstd = [sbuf(f"rstd{i}", [128, 1], F32) for i in range(NLN)]
    stg = [sbuf(f"stg{i}", [128, DM], BF16) for i in range(NLN)]
    stgn = [sbuf(f"stgn{i}", [128, DM], BF16) for i in range(NLN)]

    pb = [es.enter_context(nc.psum_tensor(f"pb{i}", [128, 512], F32)) for i in range(4)]
    st_TB = ExitStack()
    pb += [st_TB.enter_context(nc.psum_tensor(f"pb{i}", [128, 512], F32)) for i in range(4, 6)]
    tb = [st_TB.enter_context(nc.psum_tensor(f"tb{i}", [128, 1024], BF16)) for i in range(2)]

    d_in = Sem(es, nc, "d_in")
    d_iny = Sem(es, nc, "d_iny")
    d_idx = Sem(es, nc, "d_idx")
    d_g = [Sem(es, nc, f"d_g{i}") for i in range(4)]
    d_kst = [Sem(es, nc, f"d_kst{i}") for i in range(2)]
    d_own = [Sem(es, nc, f"d_own{i}") for i in range(4)]
    d_kv = Sem(es, nc, "d_kv")
    d_wtf = [Sem(es, nc, f"d_wtf{i}") for i in range(2)]
    d_wb = [Sem(es, nc, f"d_wb{i}") for i in range(2)]
    d_ffn = [Sem(es, nc, f"d_ffn{i}") for i in range(2)]
    d_bv = Sem(es, nc, "d_bv")
    d_kvi = Sem(es, nc, "d_kvi")
    d_acc = [Sem(es, nc, f"d_acc{i}") for i in range(4)]
    d_out = Sem(es, nc, "d_out")
    pe = Sem(es, nc, "pe")
    ve = Sem(es, nc, "ve")
    se = Sem(es, nc, "se")
    gp = Sem(es, nc, "gp")

    blk = es.enter_context(nc.Block())
    g_, s_, t_, v_, a_ = blk.gpsimd, blk.sync, blk.tensor, blk.vector, blk.scalar

    def f(g):
        gp.inc(g.iota(identf[:], pattern=[[1, 128]], base=0, channel_multiplier=-1,
               allow_small_or_imprecise_dtypes=True))
        gp.inc(g.iota(iota16f[:], pattern=[[0, 1]], base=0, channel_multiplier=16,
               allow_small_or_imprecise_dtypes=True))
        gp.inc(g.memset(ones64f[:], 1.0))
        gp.inc(g.memset(ones128f[:], 1.0))
        gp.inc(g.memset(ones128b[:], 1.0))
        d_kvi.inc(g.dma_start(
            out=kvidx[:],
            in_=mapping.rearrange("(gi p f) -> p gi f", gi=8, p=128)[:, :, 0:1].squeeze(2),
            allow_slow_non_contiguous=True), 16)
    g_(f)

    def f(s):
        d_iny.inc(s.dma_start(out=qpos_sb[:], in_=qpos), 16)
        d_iny.inc(s.dma_start(out=bqkv_sb[:], in_=bqkv.rearrange("(c p) -> p c", p=128),
                              allow_slow_non_contiguous=True), 16)
        d_iny.inc(s.dma_start(out=bproj_sb[:], in_=bproj.rearrange("(c p) -> p c", p=128),
                              allow_slow_non_contiguous=True), 16)
    s_(f)
    setup_iny = d_iny.n
    setup_in = d_in.n

    def f(v):
        v.wait_ge(gp.h, 1)
        ve.inc(v.tensor_scalar(identf[:], identf[:], 0.0, None, op0=ALU.is_equal))
        v.wait_ge(ve.h, ve.n)
        ve.inc(v.tensor_copy(identb[:], identf[:]))
        v.wait_ge(d_iny.h, setup_iny)
        ve.inc(v.tensor_scalar(bqs[:], bqkv_sb[:, 0:8], 0.125, None, op0=ALU.mult))
    v_(f)
    ident_ready = ve.n

    def f(g):
        g.wait_ge(d_iny.h, setup_iny)
        for gi in range(16):
            d_idx.inc(g.indirect_dma_start(
                out=qmap[:, gi:gi + 1], out_offset=None,
                in_=mapping[None, :],
                in_offset=IndirectOffsetOnAxis(qpos_sb[:, gi:gi + 1], 1)), 16)
        g.wait_ge(d_idx.h, d_idx.n)
        d_idx.inc(g.dma_start(out=segraw[:], in_=qmap[0:128:64, 0:16]), 16)
    g_(f)
    qmap_done = d_idx.n

    def f(v):
        v.wait_ge(d_idx.h, d_idx.n)
        ve.inc(v.tensor_scalar(segraw[:], segraw[:], 8, None, op0=ALU.arith_shift_right))
        v.wait_ge(ve.h, ve.n)
        ve.inc(v.tensor_copy(segf[:], segraw[:]))
    v_(f)
    segf_ready = ve.n

    def f(t):
        t.wait_ge(ve.h, segf_ready)
        t.wait_ge(gp.h, 4)
        pe.inc(t.matmul(pb[0][:, 0:32], ones128f[:], segf[:], start=True, stop=True))
    t_(f)
    def f(v):
        v.wait_ge(pe.h, pe.n)
        ve.inc(v.tensor_scalar(pb[0][:, 0:32], pb[0][:, 0:32], 1.0, iota16f[:],
                               op0=ALU.mult, op1=ALU.add))
        v.wait_ge(ve.h, ve.n)
        ve.inc(v.tensor_copy(idxK[:], pb[0][:, 0:32]))
    v_(f)
    idx_ready = ve.n

    all_sems = [d_in, d_iny, d_idx, d_g[0], d_g[1], d_g[2], d_g[3], d_kst[0], d_kst[1],
                d_own[0], d_own[1], d_own[2], d_own[3], d_kv, d_bv, d_kvi, d_wtf[0], d_wtf[1], d_wb[0], d_wb[1], d_ffn[0], d_ffn[1],
                *d_acc, d_out, pe, ve, se, gp]

    def phase_barrier():
        vals = [(sm.h, sm.n) for sm in all_sems if sm.n > 0]
        for section in (g_, s_, t_, v_, a_):
            def f(e, vals=vals):
                for h, n in vals:
                    e.wait_ge(h, n)
            section(f)

    def blkcol(b):
        return 16 * (b & 1) + (b >> 1)

    ln_sqrt_se = [0] * 4
    ln_norm_ve = [0] * 4
    tb_free = [0, 0]

    def emit_ln_batch(items):
        n = len(items)
        ve_c, se_c = ve.n, se.n
        p_rec, p_sqrt, p_norm = {}, {}, {}
        for idx in range(n):
            ve_c += 5; p_rec[idx] = ve_c
        for idx in range(n):
            se_c += 1; p_sqrt[idx] = se_c
        for idx in range(n):
            ve_c += 1; p_norm[idx] = ve_c

        def f(v):
            for idx, (k, gw) in enumerate(items):
                v.wait_ge(*gw)
                v.wait_ge(se.h, ln_sqrt_se[k])
                ve.inc(v.bn_stats(bn6[k][:, 0:6], stg[k][:, 0:512]))
                ve.inc(v.bn_stats(bn6[k][:, 6:12], stg[k][:, 512:1024]))
                v.wait_ge(ve.h, ve.n)
                ve.inc(v.bn_aggr(bn2[k][:], bn6[k][:]))
                v.wait_ge(ve.h, ve.n)
                ve.inc(v.tensor_scalar(rstd[k][:], bn2[k][:, 1:2], 1e-5, None, op0=ALU.add))
                v.wait_ge(ve.h, ve.n)
                ve.inc(v.reciprocal(rstd[k][:], rstd[k][:]))
                assert ve.n == p_rec[idx]
        v_(f)
        def f(a):
            for idx, (k, gw) in enumerate(items):
                a.wait_ge(ve.h, p_rec[idx])
                se.inc(a.activation(rstd[k][:], rstd[k][:], AF.Sqrt, bias=0.0, scale=1.0))
                assert se.n == p_sqrt[idx]
        a_(f)
        def f(v):
            for idx, (k, gw) in enumerate(items):
                v.wait_ge(se.h, p_sqrt[idx])
                ve.inc(v.tensor_scalar(stgn[k][:], stg[k][:], bn2[k][:, 0:1], rstd[k][:],
                                       op0=ALU.subtract, op1=ALU.mult))
                assert ve.n == p_norm[idx]
        v_(f)
        for idx, (k, gw) in enumerate(items):
            ln_sqrt_se[k] = p_sqrt[idx]
            ln_norm_ve[k] = p_norm[idx]
        return p_norm

    def emit_tr_batch(items, p_norm, extra_pe_wait=0):
        n = len(items)
        pe_c, ve_c = pe.n, ve.n
        p_tp, p_tc = {}, {}
        for idx in range(n):
            for ch in range(8):
                pe_c += 1; p_tp[(idx, ch)] = pe_c
                ve_c += 1; p_tc[(idx, ch)] = ve_c

        def f(t):
            for idx, (k, dfm, dcol, cstr) in enumerate(items):
                for ch in range(8):
                    gi2 = idx * 8 + ch
                    if ch == 0:
                        t.wait_ge(ve.h, p_norm[idx])
                    if gi2 >= 2:
                        t.wait_ge(ve.h, p_tc[((gi2 - 2) // 8, (gi2 - 2) % 8)])
                    else:
                        t.wait_ge(ve.h, max(tb_free[gi2 % 2], ident_ready))
                    pe.inc(t.transpose(tb[gi2 % 2][0:128, 0:128],
                                       stgn[k][:, ch * 128:ch * 128 + 128], identb[:]))
                    assert pe.n == p_tp[(idx, ch)]
        t_(f)
        def f(v):
            for idx, (k, dfm, dcol, cstr) in enumerate(items):
                for ch in range(8):
                    gi2 = idx * 8 + ch
                    if gi2 == 0 and extra_pe_wait:
                        v.wait_ge(pe.h, extra_pe_wait)
                    v.wait_ge(pe.h, p_tp[(idx, ch)])
                    ve.inc(v.tensor_copy(dfm[:, dcol + ch * cstr:dcol + ch * cstr + 128],
                                         tb[gi2 % 2][0:128, 0:128]))
                    assert ve.n == p_tc[(idx, ch)]
        v_(f)
        tb_free[0] = p_tc[(n - 1, 6)]
        tb_free[1] = p_tc[(n - 1, 7)]
        return ve.n

    def emit_ln_tr_batch(items):
        p_norm = emit_ln_batch([(k, gw) for (k, gw, dfm, dcol, cstr) in items])
        return emit_tr_batch([(k, dfm, dcol, cstr) for (k, gw, dfm, dcol, cstr) in items],
                             p_norm)

    st_A = ExitStack()
    wk_sb = st_A.enter_context(nc.sbuf_tensor("wk_sb", [128, 8 * DM], BF16))
    wv_sb = st_A.enter_context(nc.sbuf_tensor("wv_sb", [128, 8 * DM], BF16))
    xn_kvT = st_A.enter_context(nc.sbuf_tensor("xn_kvT", [128, 8 * NKV], BF16))
    bv_b = st_A.enter_context(nc.sbuf_tensor("bv_b", [64, DM], F32))
    bv_row = st_A.enter_context(nc.sbuf_tensor("bv_row", [1, DM], F32))
    K_T2 = st_A.enter_context(nc.sbuf_tensor("K_T2", [128, NSEG * 512], BF16))
    V_seg = st_A.enter_context(nc.sbuf_tensor("V_seg", [64, NSEG * VSEGW], BF16))

    if "A" in phases:
        def f(g):
            d_in.inc(_fmload(g, wk_sb, Wqkv[:, DM:2 * DM], 8), 16)
            d_in.inc(_fmload(g, wv_sb, Wqkv[:, 2 * DM:3 * DM], 8), 16)
            vs4 = V_seg[:].rearrange("p (s h d) -> p s h d", s=NSEG, h=16)
            gp.inc(g.memset(vs4[:, :, :, 64:65], 1.0))
        g_(f)
        wkwv_loaded = d_in.n
        vones_gp = gp.n

        def f(s2):
            d_bv.inc(s2.dma_start(out=bv_row[:], in_=bqkv[None, 2 * DM:3 * DM]), 16)
        s_(f)

        def f(t):
            t.wait_ge(d_bv.h, d_bv.n)
            t.wait_ge(gp.h, 4)
            pe.inc(t.matmul(pb[1][0:64, :], ones64f[:], bv_row[0:1, 0:512],
                            start=True, stop=True))
            pe.inc(t.matmul(pb[2][0:64, :], ones64f[:], bv_row[0:1, 512:1024],
                            start=True, stop=True))
        t_(f)
        def f(v):
            v.wait_ge(pe.h, pe.n)
            ve.inc(v.tensor_copy(bv_b[:, 0:512], pb[1][0:64, :]))
            ve.inc(v.tensor_copy(bv_b[:, 512:1024], pb[2][0:64, :]))
        v_(f)
        bvb_ready = ve.n

        def emit_kv_gathers(bat):
            for sub in range(4):
                gi = bat * 4 + sub
                k = gi % NLN
                def f(g, gi=gi, k=k):
                    g.wait_ge(d_kvi.h, 16)
                    g.wait_ge(ve.h, ln_norm_ve[k])
                    d_g[k].inc(g.indirect_dma_start(
                        out=stg[k][:], out_offset=None,
                        in_=x_liv,
                        in_offset=IndirectOffsetOnAxis(kvidx[:, gi:gi + 1], 0)), 16)
                g_(f)

        def emit_K_half(half, ready_ve):
            evs = []
            for kc in range(8):
                bank = pb[kc % 4]
                def f(t, kc=kc, bank=bank, half=half, ready_ve=ready_ve):
                    t.wait_ge(d_in.h, wkwv_loaded)
                    t.wait_ge(ve.h, max(ready_ve, idx_ready))
                    if len(evs) >= 4:
                        t.wait_ge(ve.h, evs[kc - 4])
                    for dc in range(8):
                        ins = t.matmul(bank[:, :],
                                       wk_sb[:, dc * DM + kc * 128:dc * DM + kc * 128 + 128],
                                       xn_kvT[:, dc * NKV + half * 512:dc * NKV + half * 512 + 512],
                                       start=(dc == 0), stop=(dc == 7))
                    pe.inc(ins)
                t_(f)
                def f(v, kc=kc, bank=bank, half=half):
                    v.wait_ge(pe.h, pe.n)
                    dst = K_T2[:].rearrange("p (sg r) -> p sg r", sg=NSEG)[
                        :, half * 8:half * 8 + 8, kc * 64:kc * 64 + 64]
                    srcp = bank[:, :].rearrange("p (sg s) -> p sg s", sg=8)
                    ve.inc(v.tensor_scalar(dst, srcp, bqkv_sb[:, 8 + kc:9 + kc], None,
                                           op0=ALU.add))
                v_(f)
                evs.append(ve.n)
            return ve.n

        def emit_V_half(half, ready_ve):
            evs = []
            for sgi in range(8):
                sg = half * 8 + sgi
                for vh in range(2):
                    i = sgi * 2 + vh
                    bank = pb[4 + (i % 2)]
                    def f(t, sg=sg, vh=vh, bank=bank, i=i, ready_ve=ready_ve):
                        t.wait_ge(ve.h, max(ready_ve, bvb_ready))
                        if i >= 2:
                            t.wait_ge(ve.h, evs[i - 2])
                        for dc in range(8):
                            ins = t.matmul(bank[0:64, :],
                                           xn_kvT[:, dc * NKV + sg * 64:dc * NKV + sg * 64 + 64],
                                           wv_sb[:, dc * DM + vh * 512:dc * DM + vh * 512 + 512],
                                           start=(dc == 0), stop=(dc == 7))
                        pe.inc(ins)
                    t_(f)
                    def f(v, sg=sg, vh=vh, bank=bank):
                        v.wait_ge(pe.h, pe.n)
                        base = sg * VSEGW + vh * 520
                        dst = V_seg[0:64, base:base + 520].rearrange(
                            "p (h d) -> p h d", h=8)[:, :, 0:64]
                        srcv = bank[0:64, :].rearrange("p (h d) -> p h d", h=8)
                        ve.inc(v.scalar_tensor_tensor(
                            dst, srcv, 0.0,
                            bv_b[:, vh * 512:vh * 512 + 512].rearrange("p (h d) -> p h d", h=8),
                            op0=ALU.bypass, op1=ALU.add))
                    v_(f)
                    evs.append(ve.n)
            return ve.n

        emit_kv_gathers(0)
        p0 = emit_ln_batch([(gi % NLN, (d_g[gi % NLN].h, d_g[gi % NLN].n))
                            for gi in range(4)])
        emit_tr_batch([(gi % NLN, xn_kvT, gi * 128, NKV) for gi in range(4)], p0)
        half0_ready = ve.n
        emit_kv_gathers(1)
        p1 = emit_ln_batch([(gi % NLN, (d_g[gi % NLN].h, d_g[gi % NLN].n))
                            for gi in range(4, 8)])
        kt0 = emit_K_half(0, half0_ready)
        emit_tr_batch([(gi % NLN, xn_kvT, gi * 128, NKV) for gi in range(4, 8)], p1)
        half1_ready = ve.n
        v0 = emit_V_half(0, half0_ready)
        kt1 = emit_K_half(1, half1_ready)
        v1 = emit_V_half(1, half1_ready)
        KT_done = kt1
        Vseg_done = v1

        def f(g):
            g.wait_ge(ve.h, KT_done)
            d_kv.inc(g.dma_start(out=K_dram.rearrange("(p e) r -> p e r", p=128),
                                 in_=K_T2[:].rearrange("p (e r) -> p e r", e=NSEG)), 16)
            g.wait_ge(ve.h, Vseg_done)
            g.wait_ge(gp.h, vones_gp)
            d_kv.inc(g.dma_start(out=V_dram.rearrange("(p e) r -> p e r", p=64),
                                 in_=V_seg[:].rearrange("p (e r) -> p e r", e=NSEG)), 16)
        g_(f)
        kv_dram_done = d_kv.n
    else:
        kv_dram_done = d_kv.n
        def f(v):
            ve.inc(v.memset(acc[:], 0.0))
        v_(f)

    st_A.close()
    phase_barrier()

    st_C = ExitStack()
    wq_sb = st_C.enter_context(nc.sbuf_tensor("wq_sb", [128, 8 * DM], BF16))
    wp_sb = st_C.enter_context(nc.sbuf_tensor("wp_sb", [128, 8 * DM], BF16))
    xnqT = st_C.enter_context(nc.sbuf_tensor("xnqT", [128, 8 * TW], BF16))
    QT = st_C.enter_context(nc.sbuf_tensor("QT", [128, 8 * TW], BF16))
    attn_tm = st_C.enter_context(nc.sbuf_tensor("attn_tm", [64, 2 * DM], BF16))
    attn_fm = st_C.enter_context(nc.sbuf_tensor("attn_fm", [128, 8 * TW], BF16))
    Kst = [st_C.enter_context(nc.sbuf_tensor(f"Kst{i}", [128, NBLK * 512], BF16)) for i in range(2)]
    Vst = [st_C.enter_context(nc.sbuf_tensor(f"Vst{i}", [64, NBLK * VSEGW], BF16)) for i in range(2)]
    P_sb = [st_C.enter_context(nc.sbuf_tensor(f"P_sb{i}", [64, 64], BF16)) for i in range(3)]
    negC = [st_C.enter_context(nc.sbuf_tensor(f"negC{i}", [64, 1], F32)) for i in range(2)]
    aU = [st_C.enter_context(nc.sbuf_tensor(f"aU{i}", [64, VSEGW], BF16)) for i in range(2)]
    rec16 = st_C.enter_context(nc.sbuf_tensor("rec16", [64, 16], F32))

    if "C" in phases:
        def f(g):
            d_in.inc(_fmload(g, wq_sb, Wqkv[:, 0:DM], 8), 16)
            d_in.inc(_fmload(g, wp_sb, Wproj, 8), 16)
        g_(f)
        wqwp_loaded = d_in.n

        prev_proj_ve = ve.n
        kst_free = [0, 0]
        prev_se_hist = 0
        qproj_mm_end = 0

        def emit_c_gathers(j):
            kb = j % 2
            for gsub in range(4):
                k = gsub % NLN
                gi = 4 * j + gsub
                def f(g, gi=gi, k=k):
                    g.wait_ge(d_idx.h, qmap_done)
                    g.wait_ge(ve.h, ln_norm_ve[k])
                    d_g[k].inc(g.indirect_dma_start(
                        out=stg[k][:], out_offset=None,
                        in_=x_liv,
                        in_offset=IndirectOffsetOnAxis(qmap[:, gi:gi + 1], 0)), 16)
                g_(f)
            def f(g, j=j, kb=kb, guard=kst_free[kb]):
                g.wait_ge(d_kv.h, kv_dram_done)
                g.wait_ge(ve.h, idx_ready)
                g.wait_ge(pe.h, guard)
                for bb in range(NBLK):
                    col = blkcol(8 * j + bb)
                    d_kst[kb].inc(g.indirect_dma_start(
                        out=Kst[kb][:, bb * 512:bb * 512 + 512], out_offset=None,
                        in_=K_dram,
                        in_offset=IndirectOffsetOnAxis(idxK[:, col:col + 1], 0)), 16)
                    d_kst[kb].inc(g.indirect_dma_start(
                        out=Vst[kb][0:64, bb * VSEGW:bb * VSEGW + VSEGW], out_offset=None,
                        in_=V_dram,
                        in_offset=IndirectOffsetOnAxis(idxK[0:64, col:col + 1], 0)), 16)
            g_(f)
            return d_kst[kb].n

        def emit_c_lntr(j):
            gws = []
            for gsub in range(4):
                k = gsub % NLN
                gws.append((k, (d_g[k].h, d_g[k].n)))
            p_norm = emit_ln_batch(gws)
            return emit_tr_batch([(k, xnqT, gsub * 128, TW)
                                  for gsub, (k, gw) in enumerate(gws)],
                                 p_norm, extra_pe_wait=qproj_mm_end)

        def emit_c_qproj(j, xnqT_ready, prev_ea_se, prev_sc_pe):
            q_evacs = []
            for qc in range(8):
                bank = pb[qc % 2]
                def f(t, qc=qc, bank=bank, xnqT_ready=xnqT_ready, prev_ea_se=prev_ea_se):
                    t.wait_ge(ve.h, xnqT_ready)
                    t.wait_ge(d_in.h, wqwp_loaded)
                    if qc < 2:
                        t.wait_ge(se.h, prev_ea_se)
                    if len(q_evacs) >= 2:
                        t.wait_ge(ve.h, q_evacs[qc - 2])
                    for dc in range(8):
                        ins = t.matmul(bank[:, :],
                                       wq_sb[:, dc * DM + qc * 128:dc * DM + qc * 128 + 128],
                                       xnqT[:, dc * TW:dc * TW + 512],
                                       start=(dc == 0), stop=(dc == 7))
                    pe.inc(ins)
                t_(f)
                def f(v, qc=qc, bank=bank, prev_sc_pe=prev_sc_pe):
                    if qc == 0:
                        v.wait_ge(pe.h, prev_sc_pe)
                    v.wait_ge(pe.h, pe.n)
                    ve.inc(v.tensor_scalar(QT[:, qc * TW:qc * TW + 512], bank[:, :],
                                           bqkv_sb[:, qc:qc + 1], 0.125,
                                           op0=ALU.add, op1=ALU.mult))
                v_(f)
                q_evacs.append(ve.n)
            return ve.n, pe.n

        def emit_c_pipe(j, QT_ready, kv_st_done):
            nonlocal_vals = {}
            kb = j % 2
            NIT = NBLK * NH
            c_sc, c_rd, c_ex, c_av, c_ea = ({} for _ in range(5))
            c_no = {}
            c_tp = {}
            c_tc = {}
            pe_c, ve_c, se_c = pe.n, ve.n, se.n
            NSLOT = NIT + 6 + 16
            for slot in range(NSLOT):
                if slot < NIT:
                    pe_c += 1; c_sc[slot] = pe_c
                if 0 <= slot - 1 < NIT:
                    ve_c += 1; c_rd[slot - 1] = ve_c
                if 0 <= slot - 2 < NIT:
                    se_c += 1; c_ex[slot - 2] = se_c
                if 0 <= slot - 3 < NIT:
                    pe_c += 1; c_av[slot - 3] = pe_c
                if 0 <= slot - 4 < NIT:
                    i4 = slot - 4
                    se_c += 1; c_ea[i4] = se_c
                    if i4 % NH == NH - 1:
                        ve_c += 3; c_no[i4 // NH] = ve_c
                if slot >= 5 and (slot - 5) % NH == NH - 1:
                    bb2 = (slot - 5) // NH
                    if 0 <= bb2 < NBLK:
                        for ch in range(8):
                            pe_c += 1; c_tp[(bb2, ch)] = pe_c
                            ve_c += 1; c_tc[(bb2, ch)] = ve_c

            def f(t, j=j, kb=kb, QT_ready=QT_ready, kv_st_done=kv_st_done,
                  ppv=prev_proj_ve, prev_se=prev_se_hist, tbf=tuple(tb_free)):
                for slot in range(NSLOT):
                    i = slot
                    if i < NIT:
                        bb, h = divmod(i, NH)
                        if i == 0:
                            t.wait_ge(ve.h, QT_ready)
                            t.wait_ge(d_kst[kb].h, kv_st_done)
                            t.wait_ge(ve.h, ppv)
                            t.wait_ge(se.h, prev_se)
                        if i >= 2:
                            t.wait_ge(ve.h, c_rd[i - 2])
                            t.wait_ge(se.h, c_ex[i - 2])
                        lhs = Kst[kb][(h & 1) * 64:(h & 1) * 64 + 64,
                                      bb * 512 + (h >> 1) * 64:bb * 512 + (h >> 1) * 64 + 64]
                        rhs = QT[(h & 1) * 64:(h & 1) * 64 + 64,
                                 (h >> 1) * TW + 64 * bb:(h >> 1) * TW + 64 * bb + 64]
                        pe.inc(t.matmul(pb[2 + (i % 2)][0:64, 0:64], lhs, rhs,
                                        start=True, stop=True))
                        assert pe.n == c_sc[i]
                    i = slot - 3
                    if 0 <= i < NIT:
                        bb, h = divmod(i, NH)
                        t.wait_ge(se.h, c_ex[i])
                        if i >= 2:
                            t.wait_ge(se.h, c_ea[i - 2])
                        rhs = Vst[kb][0:64, bb * VSEGW + 65 * h:bb * VSEGW + 65 * h + 65]
                        pe.inc(t.matmul(pb[(i % 2)][0:64, 0:65], P_sb[i % 3][:], rhs,
                                        start=True, stop=True))
                        assert pe.n == c_av[i]
                    if slot >= 5 and (slot - 5) % NH == NH - 1:
                        bb2 = (slot - 5) // NH
                        if 0 <= bb2 < NBLK:
                            for ch in range(8):
                                if ch == 0:
                                    t.wait_ge(ve.h, c_no[bb2])
                                if ch >= 2:
                                    t.wait_ge(ve.h, c_tc[(bb2, ch - 2)])
                                elif bb2 >= 1:
                                    t.wait_ge(ve.h, c_tc[(bb2 - 1, 6 + ch)])
                                elif ch < 2:
                                    t.wait_ge(ve.h, tbf[ch])
                                pe.inc(t.transpose(
                                    tb[ch % 2][0:128, 0:64],
                                    attn_tm[0:64, (bb2 % 2) * DM + ch * 128:
                                            (bb2 % 2) * DM + ch * 128 + 128],
                                    identb[0:64, 0:64]))
                                assert pe.n == c_tp[(bb2, ch)]
            t_(f)

            def f(v, kb=kb):
                for slot in range(NSLOT):
                    i = slot - 1
                    if 0 <= i < NIT:
                        v.wait_ge(pe.h, c_sc[i])
                        if i >= 2:
                            v.wait_ge(se.h, c_ex[i - 2])
                        ve.inc(v.reduce_max(negC[i % 2][:], pb[2 + (i % 2)][0:64, 0:64],
                                            axis=AX.X, negate=True))
                        assert ve.n == c_rd[i]
                    i = slot - 4
                    if 0 <= i < NIT:
                        bb, h = divmod(i, NH)
                        if h == NH - 1:
                            v.wait_ge(se.h, c_ea[i])
                            if bb >= 2:
                                v.wait_ge(pe.h, c_tp[(bb - 2, 7)])
                            a3 = aU[bb % 2][:].rearrange("p (hh c) -> p hh c", hh=16)
                            ve.inc(v.tensor_scalar(rec16[:], a3[:, :, 64:65].squeeze(2),
                                                   1e-6, None, op0=ALU.add))
                            v.wait_ge(ve.h, ve.n)
                            ve.inc(v.reciprocal(rec16[:], rec16[:]))
                            v.wait_ge(ve.h, ve.n)
                            dstb = attn_tm[0:64, (bb % 2) * DM:(bb % 2) * DM + DM].rearrange(
                                "p (hh d) -> p hh d", hh=16)
                            ve.inc(v.tensor_tensor(
                                dstb, a3[:, :, 0:64],
                                rec16[:].unsqueeze(2).broadcast_to([64, 16, 64]),
                                op=ALU.mult))
                            assert ve.n == c_no[bb]
                    if slot >= 5 and (slot - 5) % NH == NH - 1:
                        bb2 = (slot - 5) // NH
                        if 0 <= bb2 < NBLK:
                            for ch in range(8):
                                v.wait_ge(pe.h, c_tp[(bb2, ch)])
                                ve.inc(v.tensor_copy(
                                    attn_fm[:, ch * TW + bb2 * 64:ch * TW + bb2 * 64 + 64],
                                    tb[ch % 2][0:128, 0:64]))
                                assert ve.n == c_tc[(bb2, ch)]
            v_(f)
            tb_free[0] = c_tc[(NBLK - 1, 6)]
            tb_free[1] = c_tc[(NBLK - 1, 7)]
            attn_fm_ready = ve.n

            def f(a):
                for slot in range(NSLOT):
                    i = slot - 2
                    if 0 <= i < NIT:
                        a.wait_ge(ve.h, c_rd[i])
                        if i >= 3:
                            a.wait_ge(pe.h, c_av[i - 3])
                        se.inc(a.activation(P_sb[i % 3][:], pb[2 + (i % 2)][0:64, 0:64],
                                            AF.Exp, bias=negC[i % 2][:], scale=1.0))
                        assert se.n == c_ex[i]
                    i = slot - 4
                    if 0 <= i < NIT:
                        bb, h = divmod(i, NH)
                        a.wait_ge(pe.h, c_av[i])
                        se.inc(a.activation(aU[bb % 2][:, 65 * h:65 * h + 65],
                                            pb[(i % 2)][0:64, 0:65], AF.Identity,
                                            bias=0.0, scale=1.0))
                        assert se.n == c_ea[i]
            a_(f)

            p_evacs = []
            for oc in range(8):
                bank = pb[2 + (oc % 2)]
                def f(t, oc=oc, bank=bank, attn_fm_ready=attn_fm_ready):
                    t.wait_ge(ve.h, attn_fm_ready)
                    if len(p_evacs) >= 2:
                        t.wait_ge(ve.h, p_evacs[oc - 2])
                    for fc in range(8):
                        ins = t.matmul(bank[:, :],
                                       wp_sb[:, fc * DM + oc * 128:fc * DM + oc * 128 + 128],
                                       attn_fm[:, fc * TW:fc * TW + 512],
                                       start=(fc == 0), stop=(fc == 7))
                    pe.inc(ins)
                t_(f)
                def f(v, oc=oc, bank=bank, j=j):
                    v.wait_ge(pe.h, pe.n)
                    ve.inc(v.tensor_scalar(acc[:, oc * TOK + j * TW:oc * TOK + j * TW + 512],
                                           bank[:, :], bproj_sb[:, oc:oc + 1], None,
                                           op0=ALU.add))
                v_(f)
                p_evacs.append(ve.n)
            return (c_ea[NIT - 1], c_sc[NIT - 1], c_av[NIT - 1], ve.n)

        kvst = {0: emit_c_gathers(0)}
        xr = emit_c_lntr(0)
        qt, qproj_mm_end = emit_c_qproj(0, xr, 0, 0)
        qts = {0: qt}
        for j in range(NT):
            if j + 1 < NT:
                kvst[j + 1] = emit_c_gathers(j + 1)
                xr = emit_c_lntr(j + 1)
            ea_last, sc_last, av_last, pv = emit_c_pipe(j, qts[j], kvst[j])
            prev_proj_ve = pv
            prev_se_hist = ea_last
            kst_free[j % 2] = av_last
            if j + 1 < NT:
                qt, qproj_mm_end = emit_c_qproj(j + 1, xr, ea_last, sc_last)
                qts[j + 1] = qt
        attn_done_ve = ve.n
        attn_done_ve = ve.n
    else:
        attn_done_ve = ve.n

    st_C.close()
    phase_barrier()

    st_BD = ExitStack()
    xn_ownT = st_BD.enter_context(nc.sbuf_tensor("xn_ownT", [128, 8 * TOK], BF16))

    if "B" in phases or "D" in phases:
        for bat in range(4):
            items = []
            for sub in range(4):
                tt = bat * 4 + sub
                k = tt % NLN
                def f(g, tt=tt, k=k):
                    g.wait_ge(ve.h, ln_norm_ve[k])
                    d_own[k].inc(g.dma_start(out=stg[k][:],
                                             in_=x_own[tt * 128:tt * 128 + 128, :]), 16)
                g_(f)
                items.append((k, (d_own[k].h, d_own[k].n), xn_ownT, tt * 128, TOK))
            emit_ln_tr_batch(items)
    xnownT_ready = ve.n

    st_D = ExitStack()
    gate_sb = st_D.enter_context(nc.sbuf_tensor("gate_sb", [128, 8 * NEXP], BF16))
    w_tm = st_D.enter_context(nc.sbuf_tensor("w_tm", [128, 16 * NEXP], F32))
    sgs = st_D.enter_context(nc.sbuf_tensor("sgs", [128, NEXP], F32))
    sgm = st_D.enter_context(nc.sbuf_tensor("sgm", [128, 1], F32))
    sgm2 = st_D.enter_context(nc.sbuf_tensor("sgm2", [128, 1], F32))
    sgtmp = st_D.enter_context(nc.sbuf_tensor("sgtmp", [128, NEXP], F32))
    wbcast = [st_D.enter_context(nc.sbuf_tensor(f"wbcast{i}", [128, TOK], BF16)) for i in range(2)]
    h_q = [st_D.enter_context(nc.sbuf_tensor(f"h_q{i}", [128, 4 * 512], BF16)) for i in range(2)]
    h32 = st_D.enter_context(nc.sbuf_tensor("h32", [128, 512], F32))
    hsil = [st_D.enter_context(nc.sbuf_tensor(f"hsil{i}", [128, 512], BF16)) for i in range(2)]
    wgb = [st_D.enter_context(nc.sbuf_tensor(f"wgb{i}", [128, 8 * 512], BF16)) for i in range(2)]
    wub = [st_D.enter_context(nc.sbuf_tensor(f"wub{i}", [128, 8 * 512], BF16)) for i in range(2)]
    wdb = [st_D.enter_context(nc.sbuf_tensor(f"wdb{i}", [128, 4 * DM], BF16)) for i in range(2)]

    if "D" in phases:
        def f(g):
            d_in.inc(_fmload(g, gate_sb, gate_w, 8), 16)
        g_(f)
        gate_loaded = d_in.n

        def wload(g, e, q):
            kbuf = (e * 4 + q) % 2
            if e < NEXP:
                srcs = (w_gate[e, :, 512 * q:512 * q + 512],
                        w_up[e, :, 512 * q:512 * q + 512],
                        w_down[e, 512 * q:512 * q + 512, :])
            else:
                srcs = (ws_gate[:, 512 * q:512 * q + 512],
                        ws_up[:, 512 * q:512 * q + 512],
                        ws_down[512 * q:512 * q + 512, :])
            d_ffn[kbuf].inc(_fmload(g, wgb[kbuf], srcs[0], 8), 16)
            d_ffn[kbuf].inc(_fmload(g, wub[kbuf], srcs[1], 8), 16)
            d_ffn[kbuf].inc(_fmload(g, wdb[kbuf], srcs[2], 4), 16)

        eq_list = [(e, q) for e in range(NEXP + 1) for q in range(4)]
        eq_loaded = {}
        eq_consumed = {}

        def f(g):
            for i in range(2):
                wload(g, *eq_list[i])
                eq_loaded[i] = d_ffn[i % 2].n
        g_(f)

        zall = st_D.enter_context(nc.sbuf_tensor("zall", [128, 128], F32))
        zexp = st_D.enter_context(nc.sbuf_tensor("zexp", [128, 128], F32))
        ztmp = st_D.enter_context(nc.sbuf_tensor("ztmp", [128, 128], F32))
        g16a = st_D.enter_context(nc.sbuf_tensor("g16a", [128, 16], F32))
        g16b = st_D.enter_context(nc.sbuf_tensor("g16b", [128, 16], F32))

        zg_done = []
        for tt in range(16):
            bank = pb[tt % 4]
            def f(t, tt=tt, bank=bank):
                t.wait_ge(d_in.h, gate_loaded)
                t.wait_ge(ve.h, xnownT_ready)
                if len(zg_done) >= 4:
                    t.wait_ge(ve.h, zg_done[tt - 4])
                for dc in range(8):
                    ins = t.matmul(bank[:, 0:NEXP],
                                   xn_ownT[:, dc * TOK + tt * 128:dc * TOK + tt * 128 + 128],
                                   gate_sb[:, dc * NEXP:dc * NEXP + NEXP],
                                   start=(dc == 0), stop=(dc == 7))
                pe.inc(ins)
            t_(f)
            def f(v, tt=tt, bank=bank):
                v.wait_ge(pe.h, pe.n)
                ve.inc(v.tensor_copy(zall[:, tt * NEXP:tt * NEXP + NEXP], bank[:, 0:NEXP]))
            v_(f)
            zg_done.append(ve.n)

        def bc16(ap16):
            return ap16[:].unsqueeze(2).broadcast_to([128, 16, NEXP])

        zv = zall[:].rearrange("p (h e) -> p h e", h=16)
        ze = zexp[:].rearrange("p (h e) -> p h e", h=16)
        zt = ztmp[:].rearrange("p (h e) -> p h e", h=16)
        def f(v):
            v.wait_ge(ve.h, ve.n)
            ve.inc(v.reduce_max(g16a[:], zv, axis=AX.X, negate=True))
            v.wait_ge(ve.h, ve.n)
            ve.inc(v.tensor_tensor(zv, zv, bc16(g16a), op=ALU.add))
        v_(f)
        g_sub_done = ve.n
        def f(a):
            a.wait_ge(ve.h, g_sub_done)
            se.inc(a.activation(zexp[:], zall[:], AF.Exp, bias=0.0, scale=1.0))
        a_(f)
        def f(v):
            v.wait_ge(se.h, se.n)
            ve.inc(v.reduce_sum(g16a[:], ze, axis=AX.X))
            v.wait_ge(ve.h, ve.n)
            ve.inc(v.reciprocal(g16a[:], g16a[:]))
            v.wait_ge(ve.h, ve.n)
            ve.inc(v.tensor_tensor(ze, ze, bc16(g16a), op=ALU.mult))
            v.wait_ge(ve.h, ve.n)
            ve.inc(v.reduce_max(g16a[:], ze, axis=AX.X))
            v.wait_ge(ve.h, ve.n)
            ve.inc(v.tensor_tensor(zt, ze, bc16(g16a), op=ALU.is_equal))
            v.wait_ge(ve.h, ve.n)
            ve.inc(v.scalar_tensor_tensor(ztmp[:], ztmp[:], -1e30, zexp[:],
                                          op0=ALU.mult, op1=ALU.add))
            v.wait_ge(ve.h, ve.n)
            ve.inc(v.reduce_max(g16b[:], zt, axis=AX.X))
            v.wait_ge(ve.h, ve.n)
            ve.inc(v.tensor_tensor(zt, ze, bc16(g16b), op=ALU.is_ge))
            v.wait_ge(ve.h, ve.n)
            ve.inc(v.tensor_mul(ztmp[:], ztmp[:], zexp[:]))
            ve.inc(v.tensor_add(g16a[:], g16a[:], g16b[:]))
            v.wait_ge(ve.h, ve.n)
            ve.inc(v.tensor_scalar(g16a[:], g16a[:], 1e-20, None, op0=ALU.add))
            v.wait_ge(ve.h, ve.n)
            ve.inc(v.reciprocal(g16a[:], g16a[:]))
            v.wait_ge(ve.h, ve.n)
            ve.inc(v.tensor_tensor(w_tm[:].rearrange("p (h e) -> p h e", h=16), zt,
                                   bc16(g16a), op=ALU.mult))
        v_(f)

        st_WB = ExitStack()
        wT8 = st_WB.enter_context(nc.sbuf_tensor("wT8", [NEXP, TOK], BF16))
        wTflat = [st_WB.enter_context(nc.sbuf_tensor(f"wTflat{i}", [1, TOK], BF16)) for i in range(2)]
        wbstage = st_WB.enter_context(nc.sbuf_tensor("wbstage", [128, TOK], BF16))

        for tt in range(16):
            def f(t, tt=tt):
                t.wait_ge(ve.h, ve.n)
                pe.inc(t.transpose(pb[2][0:NEXP, 0:128],
                                   w_tm[:, tt * NEXP:tt * NEXP + NEXP], identf[:]))
            t_(f)
            def f(v, tt=tt):
                v.wait_ge(pe.h, pe.n)
                ve.inc(v.tensor_copy(wT8[:, tt * 128:tt * 128 + 128], pb[2][0:NEXP, 0:128]))
            v_(f)
        wT8_ready = ve.n

        def f(g):
            g.wait_ge(ve.h, wT8_ready)
            d_wtf[0].inc(g.dma_start(out=wTflat[0][:], in_=wT8[0:1, :]), 16)
        g_(f)
        wtf_done = {0: d_wtf[0].n}
        mm_done = {}
        for e in range(NEXP):
            for tq in range(4):
                bank = pb[4 + tq % 2]
                def f(t, e=e, tq=tq, bank=bank):
                    t.wait_ge(d_wtf[e % 2].h, wtf_done[e])
                    t.wait_ge(gp.h, 5)
                    t.wait_ge(ve.h, ve.n)
                    t.wait_ge(d_kv.h, d_kv.n)
                    pe.inc(t.matmul(bank[:, :], ones128b[:],
                                    wTflat[e % 2][0:1, tq * 512:tq * 512 + 512],
                                    start=True, stop=True))
                t_(f)
                def f(v, tq=tq, bank=bank):
                    v.wait_ge(pe.h, pe.n)
                    ve.inc(v.tensor_copy(wbstage[:, tq * 512:tq * 512 + 512], bank[:, :]))
                v_(f)
            mm_done[e] = pe.n
            if e + 1 < NEXP:
                def f(g, e=e):
                    if e >= 1:
                        g.wait_ge(pe.h, mm_done[e - 1])
                    d_wtf[(e + 1) % 2].inc(
                        g.dma_start(out=wTflat[(e + 1) % 2][:], in_=wT8[e + 1:e + 2, :]), 16)
                g_(f)
                wtf_done[e + 1] = d_wtf[(e + 1) % 2].n
            def f(g, e=e):
                g.wait_ge(ve.h, ve.n)
                d_kv.inc(g.dma_start(out=W_wb[e], in_=wbstage[:]), 16)
            g_(f)
        wb_built = d_kv.n

        st_WB.close()
        st_TB.close()
        st_Dp = ExitStack()
        pdown = st_Dp.enter_context(nc.psum_tensor("pdown", [128, 4, 512], F32))

        def f(s2):
            s2.wait_ge(d_kv.h, wb_built)
            d_wb[0].inc(s2.dma_start(out=wbcast[0][:], in_=W_wb[0]), 16)
        s_(f)
        wb_loaded = {0: d_wb[0].n}
        exp_mult_ve = {}
        c_gate, c_up, c_silu, c_m1, c_m2, c_down, c_add = ({} for _ in range(7))
        prev_add_ve = ve.n

        last_add_tq = {}
        for i, (e, q) in enumerate(eq_list):
            if q == 0 and e + 1 < NEXP:
                def f(s2, e=e):
                    s2.wait_ge(d_kv.h, wb_built)
                    if e >= 1:
                        s2.wait_ge(ve.h, exp_mult_ve[e - 1])
                    d_wb[(e + 1) % 2].inc(
                        s2.dma_start(out=wbcast[(e + 1) % 2][:], in_=W_wb[e + 1]), 16)
                s_(f)
                wb_loaded[e + 1] = d_wb[(e + 1) % 2].n

            pe_c, se_c, ve_c = pe.n, se.n, ve.n
            PE_ORDER = ([("gu", 0, 0), ("gu", 0, 1)]
                        + ([("dnP", 3, 0)] if i >= 1 else [])
                        + [("gu", 0, 2), ("gu", 0, 3)]
                        + ([("dnP", 3, 1)] if i >= 1 else [])
                        + [("gu", 1, 0), ("gu", 1, 1), ("dn", 0, 0),
                           ("gu", 1, 2), ("gu", 1, 3), ("dn", 0, 1),
                           ("gu", 2, 0), ("gu", 2, 1), ("dn", 1, 0),
                           ("gu", 2, 2), ("gu", 2, 3), ("dn", 1, 1),
                           ("gu", 3, 0), ("gu", 3, 1), ("dn", 2, 0),
                           ("gu", 3, 2), ("gu", 3, 3), ("dn", 2, 1)])
            for kind, a, b in PE_ORDER:
                if kind == "gu":
                    pe_c += 1; c_gate[(i, a, b)] = pe_c
                    pe_c += 1; c_up[(i, a, b)] = pe_c
                elif kind == "dn":
                    pe_c += 1; c_down[(i, a, b)] = pe_c
                else:
                    pe_c += 1; c_down[(i - 1, a, b)] = pe_c
            VE_ORDER = []
            for tq in range(4):
                for sl in range(4):
                    VE_ORDER.append(("m", tq, sl))
                    if tq == 0 and i >= 1 and sl == 1:
                        VE_ORDER.append(("aP", 3, 0))
                    if tq == 0 and i >= 1 and sl == 3:
                        VE_ORDER.append(("aP", 3, 1))
                    if tq >= 1 and sl == 1:
                        VE_ORDER.append(("a", tq - 1, 0))
                    if tq >= 1 and sl == 3:
                        VE_ORDER.append(("a", tq - 1, 1))
            for tq in range(4):
                for sl in range(4):
                    se_c += 1; c_silu[(i, tq, sl)] = se_c
            for kind, a0, b0 in VE_ORDER:
                if kind == "m":
                    ve_c += 1; c_m1[(i, a0, b0)] = ve_c
                    ve_c += 1; c_m2[(i, a0, b0)] = ve_c
                elif kind == "a":
                    ve_c += 1; c_add[(i, a0, b0)] = ve_c
                else:
                    ve_c += 1; c_add[(i - 1, a0, b0)] = ve_c
            pav = c_add[(i - 1, 3, 1)] if i >= 1 else prev_add_ve

            def emit_gu(t, i, tq, sls):
                for sl in sls:
                    kb2 = sl % 2
                    if sl >= 2:
                        t.wait_ge(se.h, c_silu[(i, tq, sl - 2)])
                    elif tq >= 1:
                        t.wait_ge(se.h, c_silu[(i, tq - 1, sl + 2)])
                    elif i >= 1:
                        t.wait_ge(se.h, c_silu[(i - 1, 3, sl + 2)])
                    buf = wgb[i % 2]
                    for dc in range(8):
                        ins = t.matmul(pb[kb2][:, :],
                                       buf[:, dc * 512 + sl * 128:dc * 512 + sl * 128 + 128],
                                       xn_ownT[:, dc * TOK + tq * 512:dc * TOK + tq * 512 + 512],
                                       start=(dc == 0), stop=(dc == 7))
                    pe.inc(ins)
                    assert pe.n == c_gate[(i, tq, sl)]
                    if sl >= 2:
                        t.wait_ge(ve.h, c_m1[(i, tq, sl - 2)])
                    elif tq >= 1:
                        t.wait_ge(ve.h, c_m1[(i, tq - 1, sl + 2)])
                    elif i >= 1:
                        t.wait_ge(ve.h, c_m1[(i - 1, 3, sl + 2)])
                    buf = wub[i % 2]
                    for dc in range(8):
                        ins = t.matmul(pb[2 + kb2][:, :],
                                       buf[:, dc * 512 + sl * 128:dc * 512 + sl * 128 + 128],
                                       xn_ownT[:, dc * TOK + tq * 512:dc * TOK + tq * 512 + 512],
                                       start=(dc == 0), stop=(dc == 7))
                    pe.inc(ins)
                    assert pe.n == c_up[(i, tq, sl)]

            def emit_dn(t, i, tq, hh, pav):
                t.wait_ge(ve.h, c_m2[(i, tq, 3)])
                if hh == 1:
                    t.wait_ge(ve.h, c_add[(i, tq, 0)])
                elif tq >= 1:
                    t.wait_ge(ve.h, c_add[(i, tq - 1, 1)])
                else:
                    t.wait_ge(ve.h, pav)
                for sl in range(4):
                    for c in range(4):
                        ins = t.matmul(
                            pdown[:, c, :],
                            wdb[i % 2][:, sl * DM + (4 * hh + c) * 128:
                                       sl * DM + (4 * hh + c) * 128 + 128],
                            h_q[tq % 2][:, sl * 512:sl * 512 + 512],
                            start=(sl == 0), stop=(sl == 3))
                pe.inc(ins)
                assert pe.n == c_down[(i, tq, hh)]

            def emit_add(v, i, tq, hh):
                v.wait_ge(pe.h, c_down[(i, tq, hh)])
                accv = acc[:].rearrange("p (c t2) -> p c t2", c=8)[
                    :, 4 * hh:4 * hh + 4, tq * 512:tq * 512 + 512]
                ve.inc(v.tensor_tensor(accv, accv, pdown[:, :, :], op=ALU.add))
                assert ve.n == c_add[(i, tq, hh)]

            def f(t, i=i, e=e, pav=pav, PE_ORDER=PE_ORDER):
                t.wait_ge(d_ffn[i % 2].h, eq_loaded[i])
                if i == 0:
                    t.wait_ge(ve.h, xnownT_ready)
                for kind, a, b in PE_ORDER:
                    if kind == "gu":
                        emit_gu(t, i, a, [b])
                    elif kind == "dn":
                        emit_dn(t, i, a, b, pav)
                    else:
                        emit_dn(t, i - 1, a, b, pav)
            t_(f)

            def f(a, i=i):
                for tq in range(4):
                    for sl in range(4):
                        kb2 = sl % 2
                        a.wait_ge(pe.h, c_gate[(i, tq, sl)])
                        if sl >= 2:
                            a.wait_ge(ve.h, c_m1[(i, tq, sl - 2)])
                        elif tq >= 1:
                            a.wait_ge(ve.h, c_m1[(i, tq - 1, sl + 2)])
                        elif i >= 1:
                            a.wait_ge(ve.h, c_m1[(i - 1, 3, sl + 2)])
                        se.inc(a.activation(hsil[kb2][:], pb[kb2][:, :], AF.Silu,
                                            bias=0.0, scale=1.0))
                        assert se.n == c_silu[(i, tq, sl)]
            a_(f)

            def f(v, i=i, e=e, VE_ORDER=VE_ORDER):
                for kind, a0, b0 in VE_ORDER:
                    if kind == "m":
                        tq, sl = a0, b0
                        tqp = tq % 2
                        kb2 = sl % 2
                        v.wait_ge(se.h, c_silu[(i, tq, sl)])
                        v.wait_ge(pe.h, c_up[(i, tq, sl)])
                        v.wait_ge(ve.h, ve.n)
                        ve.inc(v.tensor_tensor(h32[:], hsil[kb2][:], pb[2 + kb2][:, :],
                                               op=ALU.mult))
                        assert ve.n == c_m1[(i, tq, sl)]
                        v.wait_ge(ve.h, ve.n)
                        if tq >= 2:
                            v.wait_ge(pe.h, c_down[(i, tq - 2, 1)])
                        elif i >= 1 and tq < 2:
                            v.wait_ge(pe.h, c_down[(i - 1, tq + 2, 1)])
                        if e < NEXP:
                            v.wait_ge(d_wb[e % 2].h, wb_loaded[e])
                            ve.inc(v.tensor_tensor(h_q[tqp][:, sl * 512:sl * 512 + 512],
                                                   h32[:], wbcast[e % 2][:, tq * 512:tq * 512 + 512],
                                                   op=ALU.mult))
                        else:
                            ve.inc(v.tensor_copy(h_q[tqp][:, sl * 512:sl * 512 + 512],
                                                 h32[:]))
                        assert ve.n == c_m2[(i, tq, sl)]
                    elif kind == "a":
                        emit_add(v, i, a0, b0)
                    else:
                        emit_add(v, i - 1, a0, b0)
            v_(f)

            exp_mult_ve[e] = c_m2[(i, 3, 3)]
            if i >= 1:
                last_add_tq[3] = c_add[(i - 1, 3, 1)]
                if i + 1 < len(eq_list):
                    def f(g, i=i):
                        g.wait_ge(pe.h, c_down[(i - 1, 3, 1)])
                        wload(g, *eq_list[i + 1])
                        eq_loaded[i + 1] = d_ffn[(i + 1) % 2].n
                    g_(f)

        iL = len(eq_list) - 1
        pe_c, ve_c = pe.n, ve.n
        pe_c += 1; c_down[(iL, 3, 0)] = pe_c
        pe_c += 1; c_down[(iL, 3, 1)] = pe_c
        ve_c += 1; c_add[(iL, 3, 0)] = ve_c
        ve_c += 1; c_add[(iL, 3, 1)] = ve_c

        def f(t, iL=iL):
            emit_dn(t, iL, 3, 0, 0)
            emit_dn(t, iL, 3, 1, 0)
        t_(f)

        def f(v, iL=iL):
            emit_add(v, iL, 3, 0)
            emit_add(v, iL, 3, 1)
        v_(f)
        for tq in range(4):
            last_add_tq[tq] = c_add[(iL, tq, 1)]
        st_Dp.close()

    st_D.close()
    st_BD.close()

    accq = acc[:].rearrange("p (c t2) -> p c t2", c=8)
    xq = xT_own.rearrange("(c p) t2 -> p c t2", p=128)
    oq = outT.rearrange("(c p) t2 -> p c t2", p=128)
    for tq in range(4):
        def f(g, tq=tq):
            g.wait_ge(ve.h, last_add_tq[tq])
            d_acc[tq].inc(g.dma_start(out=accq[:, :, tq * 512:tq * 512 + 512],
                                      in_=xq[:, :, tq * 512:tq * 512 + 512],
                                      accum_op=ALU.add), 16)
        g_(f)
        def f(s, tq=tq, need=d_acc[tq].n):
            s.wait_ge(d_acc[tq].h, need)
            d_out.inc(s.dma_start(out=oq[:, :, tq * 512:tq * 512 + 512],
                                  in_=accq[:, :, tq * 512:tq * 512 + 512]), 16)
        s_(f)

    def f(s):
        s.wait_ge(d_out.h, d_out.n)
    s_(f)

    es.close()
    return nc


_CACHED = {}


def _get_program():
    if "nc" not in _CACHED:
        _CACHED["nc"] = build_program()
    return _CACHED["nc"]


def kernel(**inputs):
    x = np.asarray(inputs["L0_attn_x"], dtype=np.float32)[0]
    xT = np.ascontiguousarray(x.T)
    x_live = np.ascontiguousarray(x[:LIVE])

    base = {
        "x_liv": x_live,
        "mapping": np.ascontiguousarray(np.asarray(inputs["L0_attn_mapping"], dtype=np.int32)),
        "Wqkv": np.ascontiguousarray(np.asarray(inputs["L0_attn_Wqkv"], dtype=np.float32)),
        "bqkv": np.ascontiguousarray(np.asarray(inputs["L0_attn_bqkv"], dtype=np.float32)),
        "Wproj": np.ascontiguousarray(np.asarray(inputs["L0_attn_Wproj"], dtype=np.float32)),
        "bproj": np.ascontiguousarray(np.asarray(inputs["L0_attn_bproj"], dtype=np.float32)),
        "gate_w": np.ascontiguousarray(np.asarray(inputs["L0_ffn_gate_w"], dtype=np.float32)),
        "w_gate": np.ascontiguousarray(np.asarray(inputs["L0_ffn_w_gate"], dtype=np.float32)),
        "w_up": np.ascontiguousarray(np.asarray(inputs["L0_ffn_w_up"], dtype=np.float32)),
        "w_down": np.ascontiguousarray(np.asarray(inputs["L0_ffn_w_down"], dtype=np.float32)),
        "ws_gate": np.ascontiguousarray(np.asarray(inputs["L0_ffn_ws_gate"], dtype=np.float32)),
        "ws_up": np.ascontiguousarray(np.asarray(inputs["L0_ffn_ws_up"], dtype=np.float32)),
        "ws_down": np.ascontiguousarray(np.asarray(inputs["L0_ffn_ws_down"], dtype=np.float32)),
    }

    in_maps = []
    p = np.arange(128)[:, None]
    g = np.arange(16)[None, :]
    for c in range(NCORES):
        qp = np.minimum(c * TOK + 128 * g + p, LIVE - 1).astype(np.int32)
        m = dict(base)
        m["x_own"] = np.ascontiguousarray(x[c * TOK:(c + 1) * TOK])
        m["xT_own"] = np.ascontiguousarray(xT[:, c * TOK:(c + 1) * TOK])
        m["qpos"] = np.ascontiguousarray(qp)
        in_maps.append(m)

    nc = _get_program()
    res = run_bass_kernel_spmd(nc, in_maps, core_ids=list(range(NCORES)))
    out = np.empty((M, DM), dtype=np.float32)
    for c in range(NCORES):
        out[c * TOK:(c + 1) * TOK] = res.results[c]["outT"].T
    return out.reshape(1, M, DM)
